# version 12
# baseline (speedup 1.0000x reference)
import numpy as np
import concourse.bass as bass
import concourse.mybir as mybir
from concourse.bass_utils import run_bass_kernel_spmd
from concourse.bass_core_rust import IndirectOffsetOnAxis

F32 = mybir.dt.float32
BF16 = mybir.dt.bfloat16
I32 = mybir.dt.int32
U32 = mybir.dt.uint32
AF = mybir.ActivationFunctionType
ALU = mybir.AluOpType
AXX = mybir.AxisListType.X

D = 2048
NCORES = 8
T = 1024
TB = 512
NTT = TB // 128
NCH = TB // 128
CH = 128
TT3 = TB + 3
EPS = 1e-5

ENGS = ("sync", "vector", "scalar", "tensor", "gpsimd")


class Prog:

    def __init__(self, nc):
        self.nc = nc
        self.steps = []
        self.sems = {}
        self.cnt = {}
        self._guards = []

    def sem(self, name):
        if name not in self.sems:
            g = self.nc.semaphore(name)
            self.sems[name] = g.__enter__()
            self._guards.append(g)
            self.cnt[name] = 0
        return self.sems[name]

    def sbuf(self, name, shape, dtype):
        g = self.nc.sbuf_tensor(name, shape, dtype)
        t = g.__enter__()
        self._guards.append(g)
        return t

    def psum(self, name, shape, dtype=F32):
        g = self.nc.psum_tensor(name, shape, dtype)
        t = g.__enter__()
        self._guards.append(g)
        return t

    def dram(self, name, shape, dtype, **kw):
        return self.nc.dram_tensor(name, shape, dtype, kind="Internal", **kw)

    AUTO = {"vector": "dve", "scalar": "act", "gpsimd": "gps"}

    def _selfsync(self, eng):
        a = self.AUTO.get(eng)
        if a is None:
            return
        cur = self.val(a)
        sw = getattr(self, "_selfwait", None)
        if sw is None:
            self._selfwait = sw = {}
        if cur > sw.get(eng, 0):
            self.wait(eng, a, cur)
            sw[eng] = cur

    def op(self, eng, fn, inc=None, n=1):
        self._selfsync(eng)
        if inc is None and eng in self.AUTO:
            inc = self.AUTO[eng]
            n = 1
        if inc:
            s = self.sem(inc)
            self.cnt[inc] += n
            self.steps.append((eng, lambda e, fn=fn, s=s, n=n: fn(e).then_inc(s, n)))
            return self.cnt[inc]
        self.steps.append((eng, fn))
        return None

    def _op_nosync(self, eng, fn, inc=None, n=1):
        if inc:
            s = self.sem(inc)
            self.cnt[inc] += n
            self.steps.append((eng, lambda e, fn=fn, s=s, n=n: fn(e).then_inc(s, n)))
            return self.cnt[inc]
        self.steps.append((eng, fn))
        return None

    DMA_SEMS = ("qa", "qw", "qw0", "qw1", "qa0", "qa1")

    def wait(self, eng, sem_name, val):
        if not val or val <= 0:
            return
        if sem_name in self.DMA_SEMS and val < self.cnt.get(sem_name, 0):
            val = self.cnt[sem_name]
        s = self.sem(sem_name)
        mw = getattr(self, "max_waited", None)
        if mw is None:
            self.max_waited = mw = {}
        mw[sem_name] = max(mw.get(sem_name, 0), val)
        self.steps.append((eng, lambda e, s=s, val=val: e.wait_ge(s, val)))

    def dma(self, q, out, in_, inc, **kw):
        mw = getattr(self, "max_waited", None) or {}
        isy = getattr(self, "issuer_synced", None)
        if isy is None:
            self.issuer_synced = isy = {}
        w = mw.get(inc, 0)
        if w > isy.get((q, inc), 0):
            s = self.sem(inc)
            self.steps.append((q, lambda e, s=s, val=w: e.wait_ge(s, val)))
            isy[(q, inc)] = w
        self._selfsync(q)
        return self._op_nosync(q, lambda e, out=out, in_=in_, kw=kw: e.dma_start(out=out, in_=in_, **kw), inc=inc, n=16)

    def val(self, s):
        return self.cnt.get(s, 0)

    def push(self):
        return len(self._guards)

    def pop(self, marker):
        for item in reversed(self._guards[marker:]):
            g = item[0] if isinstance(item, tuple) else item
            g.__exit__(None, None, None)
        del self._guards[marker:]

    def barrier(self):
        sems = [s for s in ("dve", "act", "pe", "gps", "qa", "qw", "cc") if s in self.cnt and self.cnt[s] > 0]
        for eng in ENGS:
            for s in sems:
                self.wait(eng, s, self.cnt[s])

    def realize(self):
        with self.nc.Block() as block:
            by = {k: [fn for e, fn in self.steps if e == k] for k in ENGS}
            block.sync(lambda e: [fn(e) for fn in by["sync"]])
            block.vector(lambda e: [fn(e) for fn in by["vector"]])
            block.scalar(lambda e: [fn(e) for fn in by["scalar"]])
            block.tensor(lambda e: [fn(e) for fn in by["tensor"]])
            block.gpsimd(lambda e: [fn(e) for fn in by["gpsimd"]])




def emit_idents(p, idb, idf):
    for t_ in (idb, idf):
        v = p.op("gpsimd", lambda e, t_=t_: e.memset(t_[:, :], 1.0), inc="gps")
        p.wait("gpsimd", "gps", v)
        p.op("gpsimd", lambda e, t_=t_: e.affine_select(
            out=t_[:, :], in_=t_[:, :], pattern=[[1, 128]], compare_op=ALU.is_equal,
            fill=0.0, base=0, channel_multiplier=-1), inc="gps")
    p.wait("tensor", "gps", p.val("gps"))



def stream_mm(p, nk, bufs, ncol, srcfn2, body):
    npair = (nk + 1) // 2
    p.wait("sync", "pe", p.val("pe"))
    p.wait("scalar", "pe", p.val("pe"))
    vws = {0: p.dma("sync", bufs[0].ap()[:, 0:2 * ncol].rearrange("p (i c) -> p i c", i=2), srcfn2(0), inc="qw0")}
    vpes = {}
    for kk in range(npair):
        par = kk % 2
        if kk + 1 < npair:
            if kk - 1 >= 0:
                p.wait(("sync", "scalar")[(kk + 1) % 2], "pe", vpes[kk - 1])
            vws[kk + 1] = p.dma(("sync", "scalar")[(kk + 1) % 2],
                                bufs[(kk + 1) % 2].ap()[:, 0:2 * ncol].rearrange("p (i c) -> p i c", i=2),
                                srcfn2(kk + 1), inc=f"qw{(kk + 1) % 2}")
        p.wait("tensor", f"qw{par}", vws[kk])
        for half in range(2):
            k = 2 * kk + half
            if k >= nk:
                break
            body(k, bufs[par].ap()[:, half * ncol:(half + 1) * ncol])
        vpes[kk] = p.val("pe")


class MambaCfg:
    def __init__(self, layer):
        self.layer = layer
        if layer == 0:
            self.PROJ, self.CONV, self.N = 8512, 4352, 128
            self.conv_tiles = 34
            self.dt_mt = 66
            self.last_w = 192
        else:
            self.PROJ, self.CONV, self.N = 8384, 4224, 64
            self.conv_tiles = 33
            self.dt_mt = 65
            self.last_w = 64


def emit_mamba(p, cfg, io, idb, idf, mode):
    L = cfg.layer
    N = cfg.N
    CT = cfg.conv_tiles
    sfx = f"_m{L}{mode[0]}_{io['tag']}"
    full = (mode == "full")

    arena1 = p.sbuf("ar1" + sfx, [128, 34 * TT3], BF16)
    hs_ap = lambda j, a, b: arena1[:, j * TT3 + a: j * TT3 + b]
    yT_ap = lambda k, a, b: arena1[:, k * TB + a: k * TB + b]
    arena2 = p.sbuf("ar2" + sfx, [128, 8192], BF16)
    uT = lambda k, a, b: arena2[:, k * TB + a: k * TB + b]
    ring = p.sbuf("ring" + sfx, [128, 5 * 4096], BF16)
    rg = lambda s: ring[:, (s % 5) * 4096:(s % 5) * 4096 + 4096]
    xdt = p.sbuf("xdt" + sfx, [128, 4096], BF16)
    if mode == "full":
        outb = ring[:, 0:4 * D]
        gate_s = p.sbuf("gts" + sfx, [128, 4 * TB], BF16)
        gtok = p.sbuf("gtok" + sfx, [128, TB], BF16)
    dt_raw = p.sbuf("dtr" + sfx, [64, TB], F32)
    dts = p.sbuf("dts" + sfx, [64, TB], F32)
    adt = p.sbuf("adt" + sfx, [64, TB], F32)
    R = p.sbuf("R" + sfx, [N, 4096], F32)
    R_bf = p.sbuf("Rbf" + sfx, [N, 4096], BF16)
    acs = p.sbuf("acs" + sfx, [64, CH], F32)
    acsT = p.sbuf("acsT" + sfx, [128, 64], F32)
    dec_hm = p.sbuf("dechm" + sfx, [64, CH], F32)
    decT = p.sbuf("decT" + sfx, [128, 64], F32)
    sdecT = p.sbuf("sdecT" + sfx, [128, 64], F32)
    gambc = p.sbuf("gam" + sfx, [128, 64], F32)
    acsf = p.sbuf("acsf" + sfx, [1, 32 * CH], F32)
    grow = p.sbuf("grow" + sfx, [1, 64], F32)
    gsum = p.sbuf("gsum" + sfx, [64, 2], F32)
    Btok = p.sbuf("Btok" + sfx, [128, N], BF16)
    Bd = p.sbuf("Bdd" + sfx, [128, N], BF16)
    Gm = p.sbuf("Gm" + sfx, [128, CH], BF16)
    Dsc = p.sbuf("Dsc" + sfx, [128, 512], F32)
    W4 = p.sbuf("W4" + sfx, [128, 512], BF16)
    ysc = p.sbuf("ysc" + sfx, [128, 64], F32)
    Dbc = p.sbuf("Dbc" + sfx, [128, 64], F32)
    if mode == "full":
        nw_bc = p.sbuf("nwbc" + sfx, [128, 4096], BF16)
    ones1f = p.sbuf("on1" + sfx, [1, 128], F32)
    ones64 = p.sbuf("on64" + sfx, [64, CH], F32)
    srow = p.sbuf("srow" + sfx, [1, 4224], F32)
    convw = p.sbuf("cw" + sfx, [128, 34 * 4], F32)
    convb = p.sbuf("cb" + sfx, [128, 34], F32)
    dtbias = p.sbuf("dtb" + sfx, [64, 1], F32)
    expA = p.sbuf("expA" + sfx, [64, 1], F32)
    cacc = p.sbuf("cacc" + sfx, [128, TB], F32)
    halo_sv = p.sbuf("halo" + sfx, [128, 34 * 3], BF16)
    lnS = p.sbuf("lnS" + sfx, [128, 8], F32)
    xtl = p.sbuf("xtl" + sfx, [128, D], F32)
    utok = p.sbuf("utok" + sfx, [128, D], BF16)
    sq2 = utok
    wb2 = [p.sbuf(f"wbu{i}" + sfx, [128, 1024], BF16) for i in range(2)]
    if mode == "full":
        wr2 = [p.sbuf(f"wru{i}" + sfx, [128, 1024], BF16) for i in range(2)]
        rms = p.sbuf("rms" + sfx, [128, NCH], F32)
    if full:
        mrow = p.sbuf("mrow" + sfx, [1, 8], F32)
        mcol = p.sbuf("mcol" + sfx, [128, 8], F32)
        grow3 = p.sbuf("grw3" + sfx, [1, 192], F32)
        mhc = p.sbuf("mhc" + sfx, [128, 8], F32)
        grow8 = xdt.ap().bitcast(F32)[0:1, 0:512]
        hstg = xdt[:, 1024:1024 + 34 * 3]
    if mode == "state":
        u3 = p.sbuf("u3" + sfx, [128, 16 * 3], BF16)
        x3 = p.sbuf("x3" + sfx, [3, D], F32)
        u3tok = p.sbuf("u3t" + sfx, [3, D], BF16)

    pb = [p.psum(f"pb{i}" + sfx, [128, 512], F32) for i in range(8)]
    pbb = [t.ap().bitcast(BF16) for t in pb]

    QA, QW = "qa", "qw"

    p.op("vector", lambda e: e.memset(ones1f[:, :], 1.0), inc="dve")
    p.op("vector", lambda e: e.memset(ones64[:, :], 1.0), inc="dve")
    p.op("vector", lambda e: e.memset(gsum[:, :], 0.0), inc="dve")
    if not getattr(p, "_consts_done", False):
        p._consts_done = True
        cstt = p.sbuf("cstt" + sfx, [128, 2], F32)
        p.op("vector", lambda e: e.memset(cstt[:, 0:1], EPS), inc="dve")
        p.op("vector", lambda e: e.memset(cstt[:, 1:2], 1.0), inc="dve")
        p.nc.const_aps.register(F32, EPS, cstt[:, 0:1])
        p.nc.const_aps.register(F32, 1.0, cstt[:, 1:2])
    cbt = p.sbuf("cbt" + sfx, [34, 128], F32)
    if L == 0:
        p.dma("gpsimd", convw.ap().rearrange("p (j k) -> p j k", k=4)[:, 0:34, :], io["conv_w"].rearrange("(j p) k -> p j k", p=128), inc=QA)
        p.dma("gpsimd", cbt[:, :], io["conv_b"].rearrange("(j p) -> j p", p=128), inc=QA)
    else:
        p.dma("gpsimd", convw.ap().rearrange("p (j k) -> p j k", k=4)[:, 0:32, :], io["conv_w"][0:4096, :].rearrange("(j p) k -> p j k", p=128), inc=QA)
        p.dma("gpsimd", convw[0:64, 128:132], io["conv_w"][4096:4160, :], inc=QA)
        p.dma("gpsimd", convw[0:64, 132:136], io["conv_w"][4160:4224, :], inc=QA)
        p.dma("gpsimd", cbt[0:32, :], io["conv_b"][0:4096].rearrange("(j p) -> j p", p=128), inc=QA)
        p.dma("gpsimd", cbt[32:33, 0:64], io["conv_b"][4096:4160].unsqueeze(0), inc=QA)
        p.dma("gpsimd", cbt[33:34, 0:64], io["conv_b"][4160:4224].unsqueeze(0), inc=QA)
    p.dma("gpsimd", dtbias[:, :], io["dt_bias"].unsqueeze(1), inc=QA)
    p.dma("gpsimd", expA[:, :], io["A_log"].unsqueeze(1), inc=QA)
    p.dma("gpsimd", srow[0:1, 0:64], io["Dv"].unsqueeze(0), inc=QA)
    p.dma("gpsimd", srow[0:1, 64:64 + 4096], io["norm_w"].unsqueeze(0), inc=QA)
    if full and not io.get("fused"):
        p.dma("gpsimd", mrow[:, 0:3], io["mrow"], inc=QA)
        p.dma("gpsimd", mcol[:, 0:4], io["mcol"], inc=QA)
        p.dma("gpsimd", grow3[0:1, :], io["g_in"], inc=QA)
        p.dma("gpsimd", halo_sv[:, :], io["halo_in"], inc=QA)
    if full and io.get("fused"):
        p.dma("gpsimd", mrow[:, 0:8], io["mrow8"], inc=QA)
        p.dma("gpsimd", mcol[:, 0:8], io["mcol8"], inc=QA)
        p.dma("gpsimd", mhc[:, 0:8], io["mh8"], inc=QA)
        p.dma("gpsimd", grow8[0:1, :], io["g_all"].rearrange("a b -> (a b)").unsqueeze(0), inc=QA)
    vc = p.val(QA)
    p.wait("scalar", QA, vc)
    p.op("scalar", lambda e: e.activation(out=expA[:, :], in_=expA[:, :], func=AF.Exp), inc="act")
    p.wait("tensor", QA, vc)
    p.wait("tensor", "dve", p.val("dve"))
    v_dbcmm = p.op("tensor", lambda e: e.matmul(out=pb[0][:, 0:64], lhsT=ones1f[:, :], rhs=srow[0:1, 0:64], start=True, stop=True), inc="pe")
    if full:
        for j in range(8):
            p.wait("tensor", "dve", p.val("dve"))
            p.op("tensor", lambda e, j=j: e.matmul(out=pb[1][:, :], lhsT=ones1f[:, :], rhs=srow[0:1, 64 + 512 * j:64 + 512 * (j + 1)], start=True, stop=True), inc="pe")
            v = p.val("pe")
            p.wait("vector", "pe", v)
            p.op("vector", lambda e, j=j: e.tensor_copy(out=nw_bc[:, 512 * j:512 * (j + 1)], in_=pb[1][:, :]), inc="dve")
    p.wait("vector", "pe", v_dbcmm)
    p.op("vector", lambda e: e.tensor_copy(out=Dbc[:, :], in_=pb[0][:, 0:64]), inc="dve")
    p.op("tensor", lambda e: e.transpose(out=pb[3][:, 0:34], in_=cbt[:, :], identity=idf[0:34, 0:34]), inc="pe")
    p.wait("vector", "pe", p.val("pe"))
    p.op("vector", lambda e: e.tensor_copy(out=convb[:, 0:34], in_=pb[3][:, 0:34]), inc="dve")
    if full and io.get("fused"):
        p.wait("vector", QA, vc)
        p.op("vector", lambda e: e.memset(halo_sv[:, :], 0.0), inc="dve")
        p.op("vector", lambda e: e.memset(R[:, :], 0.0), inc="dve")
        for j in range(8):
            vh = p.dma("gpsimd", hstg[:, :], io["halo_all"][128 * j:128 * (j + 1), :], inc=QA)
            p.wait("vector", QA, vh)
            p.op("vector", lambda e, j=j: e.scalar_tensor_tensor(
                out=halo_sv[:, :], in0=hstg[:, :], scalar=mhc[:, j:j + 1], in1=halo_sv[:, :],
                op0=ALU.mult, op1=ALU.add), inc="dve")
            p.op("vector", lambda e, j=j: e.tensor_scalar(out=grow[0:1, :], in0=grow8[0:1, 64 * j:64 * (j + 1)], scalar1=mrow[0:1, j:j + 1], scalar2=None, op0=ALU.mult), inc="dve")
            p.op("vector", lambda e, j=j: e.tensor_scalar(out=srow[0:1, 4160:4224], in0=mrow[0:1, j:j + 1].to_broadcast([1, 64]), scalar1=-1.0, scalar2=1.0, op0=ALU.mult, op1=ALU.add), inc="dve")
            p.op("vector", lambda e: e.tensor_add(out=grow[0:1, :], in0=grow[0:1, :], in1=srow[0:1, 4160:4224]), inc="dve")
            vg = p.val("dve")
            p.wait("tensor", "dve", vg)
            p.op("tensor", lambda e: e.matmul(out=pb[2][:, 0:64], lhsT=ones1f[:, :], rhs=grow[0:1, :], start=True, stop=True), inc="pe")
            vpe = p.val("pe")
            p.wait("vector", "pe", vpe)
            p.op("vector", lambda e: e.tensor_copy(out=gambc[:, :], in_=pb[2][:, 0:64]), inc="dve")
            p.op("vector", lambda e: e.tensor_tensor(
                out=R.ap().rearrange("p (h d) -> p h d", h=64),
                in0=R.ap().rearrange("p (h d) -> p h d", h=64),
                in1=gambc[0:N, :].unsqueeze(2).to_broadcast([N, 64, 64]), op=ALU.mult), inc="dve")
            p.wait("gpsimd", "dve", p.val("dve"))
            vS = p.dma("gpsimd", R_bf.ap().bitcast(F32)[0:N, 0:2048], io["S_all"][N * j:N * (j + 1), 0:2048], inc=QA)
            p.wait("vector", QA, vS)
            p.op("vector", lambda e, j=j: e.scalar_tensor_tensor(
                out=R[0:N, 0:2048], in0=R_bf.ap().bitcast(F32)[0:N, 0:2048], scalar=mcol[0:N, j:j + 1],
                in1=R[0:N, 0:2048], op0=ALU.mult, op1=ALU.add), inc="dve")
            p.wait("gpsimd", "dve", p.val("dve"))
            vS2 = p.dma("gpsimd", R_bf.ap().bitcast(F32)[0:N, 0:2048], io["S_all"][N * j:N * (j + 1), 2048:4096], inc=QA)
            p.wait("vector", QA, vS2)
            p.op("vector", lambda e, j=j: e.scalar_tensor_tensor(
                out=R[0:N, 2048:4096], in0=R_bf.ap().bitcast(F32)[0:N, 0:2048], scalar=mcol[0:N, j:j + 1],
                in1=R[0:N, 2048:4096], op0=ALU.mult, op1=ALU.add), inc="dve")
    elif full:
        p.wait("vector", QA, vc)
        p.op("vector", lambda e: e.tensor_scalar(out=halo_sv[:, :], in0=halo_sv[:, :], scalar1=mcol[:, 3:4], scalar2=None, op0=ALU.mult), inc="dve")
        p.op("vector", lambda e: e.memset(R[:, :], 0.0), inc="dve")
        for j in range(3):
            p.op("vector", lambda e, j=j: e.tensor_scalar(out=grow[0:1, :], in0=grow3[0:1, 64 * j:64 * (j + 1)], scalar1=mrow[0:1, j:j + 1], scalar2=None, op0=ALU.mult), inc="dve")
            p.op("vector", lambda e, j=j: e.tensor_scalar(out=srow[0:1, 4160:4224], in0=mrow[0:1, j:j + 1].to_broadcast([1, 64]), scalar1=-1.0, scalar2=1.0, op0=ALU.mult, op1=ALU.add), inc="dve")
            p.op("vector", lambda e: e.tensor_add(out=grow[0:1, :], in0=grow[0:1, :], in1=srow[0:1, 4160:4224]), inc="dve")
            vg = p.val("dve")
            p.wait("tensor", "dve", vg)
            p.op("tensor", lambda e: e.matmul(out=pb[2][:, 0:64], lhsT=ones1f[:, :], rhs=grow[0:1, :], start=True, stop=True), inc="pe")
            vpe = p.val("pe")
            p.wait("vector", "pe", vpe)
            p.op("vector", lambda e: e.tensor_copy(out=gambc[:, :], in_=pb[2][:, 0:64]), inc="dve")
            p.op("vector", lambda e: e.tensor_tensor(
                out=R.ap().rearrange("p (h d) -> p h d", h=64),
                in0=R.ap().rearrange("p (h d) -> p h d", h=64),
                in1=gambc[0:N, :].unsqueeze(2).to_broadcast([N, 64, 64]), op=ALU.mult), inc="dve")
            p.wait("gpsimd", "dve", p.val("dve"))
            vS = p.dma("gpsimd", R_bf.ap().bitcast(F32)[0:N, 0:2048], io["S_in"][j][0:N, 0:2048], inc=QA)
            p.wait("vector", QA, vS)
            p.op("vector", lambda e, j=j: e.scalar_tensor_tensor(
                out=R[0:N, 0:2048], in0=R_bf.ap().bitcast(F32)[0:N, 0:2048], scalar=mcol[0:N, j:j + 1],
                in1=R[0:N, 0:2048], op0=ALU.mult, op1=ALU.add), inc="dve")
            p.wait("gpsimd", "dve", p.val("dve"))
            vS2 = p.dma("gpsimd", R_bf.ap().bitcast(F32)[0:N, 0:2048], io["S_in"][j][0:N, 2048:4096], inc=QA)
            p.wait("vector", QA, vS2)
            p.op("vector", lambda e, j=j: e.scalar_tensor_tensor(
                out=R[0:N, 2048:4096], in0=R_bf.ap().bitcast(F32)[0:N, 0:2048], scalar=mcol[0:N, j:j + 1],
                in1=R[0:N, 2048:4096], op0=ALU.mult, op1=ALU.add), inc="dve")
    else:
        p.op("vector", lambda e: e.memset(R[:, :], 0.0), inc="dve")
        p.op("vector", lambda e: e.memset(halo_sv[:, :], 0.0), inc="dve")
    p.wait("tensor", "dve", p.val("dve"))

    if L == 0:
        units = [(128 * (32 + j), 128, ("hs", j, 128)) for j in range(34)]
        units.append((8448, 64, ("dt",)))
        conv_info = [(j, 128) for j in range(34)]
    else:
        units = [(128 * (32 + j), 128, ("hs", j, 128)) for j in range(32)]
        units += [(8192, 64, ("hs", 32, 64)), (8256, 64, ("hs", 33, 64)), (8320, 64, ("dt",))]
        conv_info = [(j, 128) for j in range(32)] + [(32, 64), (33, 64)]
    hs_groups = []
    i = 0
    while i < len(units):
        g = [units[i]]
        while (len(g) < 4 and i + len(g) < len(units)
               and sum(u[1] for u in g) + units[i + len(g)][1] <= 512
               and (L == 0 or (units[i][0] // 1152) == ((units[i + len(g)][0] + units[i + len(g)][1] - 1) // 1152))):
            g.append(units[i + len(g)])
        hs_groups.append(g)
        i += len(g)
    gate_groups = []
    i = 0
    while i < 32:
        e2 = min(i + 4, 32)
        if L == 1:
            e2 = min(e2, (i // 9 + 1) * 9)
        gate_groups.append(list(range(i, e2)))
        i = e2

    if mode == "state":
        if io.get("xh_all") is not None:
            mh3 = lnS[0:3, 0:8]
            xh3 = xtl[0:3, :]
            p.dma("gpsimd", mh3[:, :], io["mh8_3"], inc=QA)
            p.op("vector", lambda e: e.memset(x3[:, :], 0.0), inc="dve")
            p.wait("vector", QA, p.val(QA))
            for j in range(8):
                v3 = p.dma("gpsimd", xh3[:, :], io["xh_all"][3 * j:3 * (j + 1), :], inc=QA)
                p.wait("vector", QA, v3)
                p.op("vector", lambda e, j=j: e.scalar_tensor_tensor(
                    out=x3[:, :], in0=xh3[:, :], scalar=mh3[:, j:j + 1], in1=x3[:, :],
                    op0=ALU.mult, op1=ALU.add), inc="dve")
                p.wait("gpsimd", "dve", p.val("dve"))
            v3 = p.val(QA)
        else:
            v3 = p.dma("gpsimd", x3[:, :], io["x_halo"], inc=QA)
        p.wait("vector", QA, v3)
        p.op("vector", lambda e: e.tensor_reduce(out=lnS[0:3, 0:1], in_=x3[:, :], axis=AXX, op=ALU.add), inc="dve")
        p.wait("scalar", QA, v3)
        p.op("scalar", lambda e: e.activation(out=sq2[0:3, :], in_=x3[:, :], func=AF.Square), inc="act")
        p.wait("vector", "act", p.val("act"))
        p.op("vector", lambda e: e.tensor_scalar(out=lnS[0:3, 1:2], in0=lnS[0:3, 0:1], scalar1=0.0, scalar2=None, op0=ALU.mult), inc="dve")
        p.op("vector", lambda e: e.tensor_reduce(out=lnS[0:3, 1:2], in_=sq2[0:3, :], axis=AXX, op=ALU.add), inc="dve")
        p.op("vector", lambda e: e.tensor_scalar(out=lnS[0:3, 1:2], in0=lnS[0:3, 1:2], scalar1=1.0 / D, scalar2=None, op0=ALU.mult), inc="dve")
        p.op("vector", lambda e: e.tensor_scalar(out=lnS[0:3, 2:3], in0=lnS[0:3, 0:1], scalar1=1.0 / D, scalar2=None, op0=ALU.mult), inc="dve")
        p.op("vector", lambda e: e.tensor_scalar(out=lnS[0:3, 4:5], in0=lnS[0:3, 2:3], scalar1=lnS[0:3, 2:3], scalar2=None, op0=ALU.mult), inc="dve")
        p.op("vector", lambda e: e.tensor_sub(out=lnS[0:3, 4:5], in0=lnS[0:3, 1:2], in1=lnS[0:3, 4:5]), inc="dve")
        vd = p.val("dve")
        p.wait("scalar", "dve", vd)
        p.op("scalar", lambda e: e.activation(out=lnS[0:3, 5:6], in_=lnS[0:3, 4:5], func=AF.Ln, bias=EPS), inc="act")
        p.op("scalar", lambda e: e.activation(out=lnS[0:3, 6:7], in_=lnS[0:3, 5:6], func=AF.Exp, scale=-0.5), inc="act")
        va = p.val("act")
        p.wait("vector", "act", va)
        p.op("vector", lambda e: e.tensor_scalar(out=u3tok[:, :], in0=x3[:, :], scalar1=lnS[0:3, 2:3], scalar2=lnS[0:3, 6:7], op0=ALU.subtract, op1=ALU.mult), inc="dve")
        vd = p.val("dve")
        p.wait("tensor", "dve", vd)
        for j in range(16):
            p.op("tensor", lambda e, j=j: e.transpose(
                out=pbb[0][:, (j % 4) * 128:(j % 4) * 128 + 3], in_=u3tok[:, j * 128:(j + 1) * 128],
                identity=idb[0:3, 0:3]), inc=("pe" if j % 4 == 3 else None))
            if j % 4 == 3:
                v = p.val("pe")
                p.wait("scalar", "pe", v)
                for jj in range(j - 3, j + 1):
                    p.op("scalar", lambda e, jj=jj: e.copy(out=u3[:, jj * 3:(jj + 1) * 3], in_=pbb[0][:, (jj % 4) * 128:(jj % 4) * 128 + 3]), inc="act")
                p.wait("tensor", "act", p.val("act"))
        for gi, grp in enumerate(hs_groups):
            m0 = grp[0][0]
            ncol = sum(u[1] for u in grp)
            for k in range(16):
                vw = p.dma("sync", wb2[0][:, 0:ncol], io["w_in_slc"](k, m0, ncol), inc=QW)
                p.wait("tensor", QW, vw)
                co = 0
                for mi, (c0, w_, dest) in enumerate(grp):
                    p.op("tensor", lambda e, k=k, co=co, w_=w_, mi=mi: e.matmul(
                        out=pb[mi][0:w_, 0:3], lhsT=wb2[0][:, co:co + w_], rhs=u3[:, k * 3:(k + 1) * 3],
                        start=(k == 0), stop=(k == 15)), inc=("pe" if mi == len(grp) - 1 else None))
                    co += w_
                vpe = p.val("pe")
                p.wait("sync", "pe", vpe)
            vpe = p.val("pe")
            p.wait("scalar", "pe", vpe)
            for mi, (c0, w_, dest) in enumerate(grp):
                if dest[0] == "hs":
                    j, pn = dest[1], dest[2]
                    p.op("scalar", lambda e, j=j, mi=mi, pn=pn: e.copy(out=hs_ap(j, 0, 3)[0:pn, :], in_=pb[mi][0:pn, 0:3]), inc="act")
            p.wait("tensor", "act", p.val("act"))
        va = p.val("act")
        p.wait("vector", "act", va)
        for (j, pn) in conv_info:
            p.op("vector", lambda e, j=j, pn=pn: e.tensor_copy(out=halo_sv[0:pn, 3 * j:3 * j + 3], in_=hs_ap(j, 0, 3)[0:pn, :]), inc="dve")
        p.wait("scalar", "dve", p.val("dve"))

    for tb in range(2):
        t0 = tb * TB

        for it in range(NTT):
            p.wait("gpsimd", "dve", p.val("dve"))
            vx = p.dma("gpsimd", xtl[:, :], io["x_in"][t0 + it * 128: t0 + (it + 1) * 128, :], inc=QA)
            p.wait("vector", QA, vx)
            p.op("vector", lambda e: e.tensor_reduce(out=lnS[:, 0:1], in_=xtl[:, :], axis=AXX, op=ALU.add), inc="dve")
            p.wait("scalar", QA, vx)
            p.op("scalar", lambda e: e.activation(out=sq2[:, :], in_=xtl[:, :], func=AF.Square), inc="act")
            p.wait("vector", "act", p.val("act"))
            p.op("vector", lambda e: e.tensor_reduce(out=lnS[:, 1:2], in_=sq2[:, :], axis=AXX, op=ALU.add), inc="dve")
            p.op("vector", lambda e: e.tensor_scalar(out=lnS[:, 1:2], in0=lnS[:, 1:2], scalar1=1.0 / D, scalar2=None, op0=ALU.mult), inc="dve")
            p.op("vector", lambda e: e.tensor_scalar(out=lnS[:, 2:3], in0=lnS[:, 0:1], scalar1=1.0 / D, scalar2=None, op0=ALU.mult), inc="dve")
            p.op("vector", lambda e: e.tensor_scalar(out=lnS[:, 4:5], in0=lnS[:, 2:3], scalar1=lnS[:, 2:3], scalar2=None, op0=ALU.mult), inc="dve")
            p.op("vector", lambda e: e.tensor_sub(out=lnS[:, 4:5], in0=lnS[:, 1:2], in1=lnS[:, 4:5]), inc="dve")
            vd = p.val("dve")
            p.wait("scalar", "dve", vd)
            p.op("scalar", lambda e: e.activation(out=lnS[:, 5:6], in_=lnS[:, 4:5], func=AF.Ln, bias=EPS), inc="act")
            p.op("scalar", lambda e: e.activation(out=lnS[:, 6:7], in_=lnS[:, 5:6], func=AF.Exp, scale=-0.5), inc="act")
            va = p.val("act")
            p.wait("vector", "act", va)
            p.op("vector", lambda e: e.tensor_scalar(out=utok[:, :], in0=xtl[:, :], scalar1=lnS[:, 2:3], scalar2=lnS[:, 6:7], op0=ALU.subtract, op1=ALU.mult), inc="dve")
            vd = p.val("dve")
            p.wait("tensor", "dve", vd)
            for j in range(16):
                p.op("tensor", lambda e, j=j: e.transpose(
                    out=pbb[j % 4][:, 0:128], in_=utok[:, j * 128:(j + 1) * 128],
                    identity=idb[:, :]), inc=("pe" if j % 4 == 3 else None))
                if j % 4 == 3:
                    v = p.val("pe")
                    p.wait("scalar", "pe", v)
                    for jj in range(j - 3, j + 1):
                        p.op("scalar", lambda e, jj=jj, it=it: e.copy(
                            out=uT(jj, it * 128, it * 128 + 128), in_=pbb[jj % 4][:, 0:128]), inc="act")
                    p.wait("tensor", "act", p.val("act"))

        evac_val = {0: 0, 1: 0}
        for gi, grp in enumerate(hs_groups):
            m0 = grp[0][0]
            ncol = sum(u[1] for u in grp)
            pbase = 4 * (gi % 2)
            p.wait("tensor", "act", evac_val[gi % 2])

            def body(k, buf, grp=grp, pbase=pbase):
                co = 0
                for mi, (c0, w_, dest) in enumerate(grp):
                    p.op("tensor", lambda e, k=k, co=co, w_=w_, mi=mi, buf=buf, pbase=pbase: e.matmul(
                        out=pb[pbase + mi][0:w_, :],
                        lhsT=buf[:, co:co + w_], rhs=uT(k, 0, TB),
                        start=(k == 0), stop=(k == 15)), inc=("pe" if mi == len(grp) - 1 else None))
                    co += w_

            stream_mm(p, 16, wb2, ncol,
                      lambda kk, m0=m0, ncol=ncol: io["w_in_slc2"](kk, m0, ncol), body)
            vpe = p.val("pe")
            p.wait("scalar", "pe", vpe)
            for mi, (c0, w_, dest) in enumerate(grp):
                if dest[0] == "dt":
                    p.op("scalar", lambda e, mi=mi, pbase=pbase: e.copy(out=dt_raw[:, :], in_=pb[pbase + mi][0:64, :]), inc="act")
                else:
                    j, pn = dest[1], dest[2]
                    p.op("scalar", lambda e, j=j, mi=mi, pn=pn, pbase=pbase: e.copy(out=hs_ap(j, 3, 3 + TB)[0:pn, :], in_=pb[pbase + mi][0:pn, :]), inc="act")
            evac_val[gi % 2] = p.val("act")
        p.wait("tensor", "act", p.val("act"))

        va_hs = p.val("act")
        p.wait("vector", "act", va_hs)
        p.wait("vector", QA, p.val(QA))
        for (j, pn) in conv_info:
            p.op("vector", lambda e, j=j, pn=pn: e.tensor_copy(out=hs_ap(j, 0, 3)[0:pn, :], in_=halo_sv[0:pn, 3 * j:3 * j + 3]), inc="dve")
            p.op("vector", lambda e, j=j, pn=pn: e.tensor_copy(out=halo_sv[0:pn, 3 * j:3 * j + 3], in_=hs_ap(j, TB, TB + 3)[0:pn, :]), inc="dve")
            p.op("vector", lambda e, j=j, pn=pn: e.tensor_scalar(
                out=cacc[0:pn, :], in0=hs_ap(j, 0, TB)[0:pn, :], scalar1=convw[0:pn, 4 * j:4 * j + 1], scalar2=None, op0=ALU.mult), inc="dve")
            for kk in range(1, 4):
                p.op("vector", lambda e, j=j, kk=kk, pn=pn: e.scalar_tensor_tensor(
                    out=cacc[0:pn, :], in0=hs_ap(j, kk, kk + TB)[0:pn, :], scalar=convw[0:pn, 4 * j + kk:4 * j + kk + 1],
                    in1=cacc[0:pn, :], op0=ALU.mult, op1=ALU.add), inc="dve")
            vd = p.val("dve")
            p.wait("scalar", "dve", vd)
            p.op("scalar", lambda e, j=j, pn=pn: e.activation(
                out=hs_ap(j, 3, 3 + TB)[0:pn, :], in_=cacc[0:pn, :], func=AF.Silu, bias=convb[0:pn, j:j + 1]), inc="act")
            p.wait("vector", "act", p.val("act"))

        p.op("vector", lambda e: e.tensor_scalar(out=dts[:, :], in0=dt_raw[:, :], scalar1=dtbias[:, :], scalar2=None, op0=ALU.add), inc="dve")
        vd = p.val("dve")
        p.wait("scalar", "dve", vd)
        p.op("scalar", lambda e: e.activation(out=dt_raw[:, :], in_=dts[:, :], func=AF.Abs), inc="act")
        p.op("scalar", lambda e: e.activation(out=adt[:, :], in_=dt_raw[:, :], func=AF.Exp, scale=-1.0), inc="act")
        p.op("scalar", lambda e: e.activation(out=adt[:, :], in_=adt[:, :], func=AF.Ln, bias=1.0), inc="act")
        va = p.val("act")
        p.wait("vector", "act", va)
        p.op("vector", lambda e: e.tensor_scalar(out=dts[:, :], in0=dts[:, :], scalar1=0.0, scalar2=None, op0=ALU.max), inc="dve")
        p.op("vector", lambda e: e.tensor_add(out=dts[:, :], in0=dts[:, :], in1=adt[:, :]), inc="dve")
        p.op("vector", lambda e: e.tensor_scalar(out=adt[:, :], in0=dts[:, :], scalar1=expA[:, :], scalar2=-1.0, op0=ALU.mult, op1=ALU.mult), inc="dve")
        if mode == "state":
            p.op("vector", lambda e: e.tensor_reduce(out=gsum[:, 1:2], in_=adt[:, :], axis=AXX, op=ALU.add), inc="dve")
            p.op("vector", lambda e: e.tensor_add(out=gsum[:, 0:1], in0=gsum[:, 0:1], in1=gsum[:, 1:2]), inc="dve")
        vd_dt = p.val("dve")

        for c in range(NCH):
            cs = c * CH
            slot_x = (tb * NCH + c + 1) % 5
            slot_y = (tb * NCH + c) % 5
            p.wait("vector", "dve", vd_dt)
            p.op("vector", lambda e, cs=cs: e.tensor_tensor_scan(
                out=acs[:, :], data0=ones64[:, :], data1=adt[:, cs:cs + CH],
                initial=0.0, op0=ALU.mult, op1=ALU.add), inc="dve")
            vd_acs = p.val("dve")
            p.op("vector", lambda e: e.tensor_scalar(out=dec_hm[:, :], in0=acs[:, :], scalar1=acs[:, CH - 1:CH], scalar2=-1.0, op0=ALU.subtract, op1=ALU.mult), inc="dve")
            vd2 = p.val("dve")
            p.wait("scalar", "dve", vd2)
            p.op("scalar", lambda e: e.activation(out=dec_hm[:, :], in_=dec_hm[:, :], func=AF.Exp), inc="act")
            va_dec = p.val("act")
            p.wait("gpsimd", "dve", vd_acs)
            p.dma("gpsimd", io["acs_dram"].ap(), acs[:, :], inc=QA)
            p.wait("gpsimd", QA, p.val(QA))
            if full:
                p.dma("gpsimd", acsf[:, :], io["acs_dram"].ap()[0:32, :].rearrange("h t -> (h t)").unsqueeze(0), inc=QA)
            v_acsf = p.val(QA)
            p.dma("gpsimd", io["alast_dram"].ap().unsqueeze(1), acs[:, CH - 1:CH], inc=QA)
            p.wait("gpsimd", QA, p.val(QA))
            vgr = p.dma("gpsimd", grow[0:1, :], io["alast_dram"].ap().unsqueeze(0), inc=QA)
            p.wait("tensor", "dve", vd_acs)
            p.op("tensor", lambda e: e.transpose(out=pb[4][:, 0:64], in_=acs[:, :], identity=idf[0:64, 0:64]), inc="pe")
            p.wait("tensor", "act", va_dec)
            p.op("tensor", lambda e: e.transpose(out=pb[5][:, 0:64], in_=dec_hm[:, :], identity=idf[0:64, 0:64]), inc="pe")
            p.op("tensor", lambda e, cs=cs: e.transpose(out=pb[6][:, 0:64], in_=dts[:, cs:cs + CH], identity=idf[0:64, 0:64]), inc="pe")
            vpe = p.val("pe")
            p.wait("vector", "pe", vpe)
            p.op("vector", lambda e: e.tensor_copy(out=acsT[:, :], in_=pb[4][:, 0:64]), inc="dve")
            p.op("vector", lambda e: e.tensor_copy(out=decT[:, :], in_=pb[5][:, 0:64]), inc="dve")
            p.op("vector", lambda e: e.tensor_copy(out=Dsc[:, 0:64], in_=pb[6][:, 0:64]), inc="dve")
            vd_acsT = p.val("dve")
            p.wait("scalar", "dve", vd_acsT)
            if full:
                p.op("scalar", lambda e: e.activation(out=sdecT[:, :], in_=acsT[:, :], func=AF.Exp), inc="act")
            p.wait("scalar", QA, vgr)
            p.op("scalar", lambda e: e.activation(out=grow[:, :], in_=grow[:, :], func=AF.Exp), inc="act")
            va_g = p.val("act")
            p.wait("tensor", "act", va_g)
            p.op("tensor", lambda e: e.matmul(out=pb[7][:, 0:64], lhsT=ones1f[:, :], rhs=grow[:, :], start=True, stop=True), inc="pe")
            vpe_g = p.val("pe")
            p.wait("vector", "pe", vpe_g)
            p.op("vector", lambda e: e.tensor_copy(out=gambc[:, :], in_=pb[7][:, 0:64]), inc="dve")
            vd_gam = p.val("dve")
            p.wait("tensor", "dve", vd_gam)

            for jg in range(8):
                for j4 in range(4):
                    j = jg * 4 + j4
                    p.op("tensor", lambda e, j=j, j4=j4, cs=cs: e.transpose(
                        out=pbb[j4][:, 0:128], in_=hs_ap(j, 3 + cs, 3 + cs + CH),
                        identity=idb[:, :]), inc=("pe" if j4 == 3 else None))
                v = p.val("pe")
                p.wait("scalar", "pe", v)
                for j4 in range(4):
                    j = jg * 4 + j4
                    p.op("scalar", lambda e, j=j, j4=j4, slot_x=slot_x: e.copy(
                        out=rg(slot_x)[:, j * 128:(j + 1) * 128], in_=pbb[j4][:, 0:128]), inc="act")
                p.wait("tensor", "act", p.val("act"))
            if L == 0:
                p.op("tensor", lambda e, cs=cs: e.transpose(out=pbb[0][:, 0:128], in_=hs_ap(32, 3 + cs, 3 + cs + CH), identity=idb[:, :]), inc="pe")
            else:
                p.op("tensor", lambda e, cs=cs: e.transpose(out=pbb[0][:, 0:64], in_=hs_ap(32, 3 + cs, 3 + cs + CH)[0:64, :], identity=idb[0:64, 0:64]), inc="pe")
            vpe_b = p.val("pe")
            p.wait("scalar", "pe", vpe_b)
            p.op("scalar", lambda e: e.copy(out=Btok[:, 0:N], in_=pbb[0][:, 0:N]), inc="act")
            va_bt = p.val("act")
            p.wait("tensor", "act", va_bt)
            p.wait("vector", "act", va_bt)

            p.op("vector", lambda e, slot_x=slot_x: e.tensor_tensor(
                out=xdt.ap().rearrange("p (h d) -> p h d", h=64),
                in0=rg(slot_x).rearrange("p (h d) -> p h d", h=64),
                in1=Dsc[:, 0:64].unsqueeze(2).to_broadcast([128, 64, 64]), op=ALU.mult), inc="dve")
            p.op("vector", lambda e: e.tensor_copy(out=R_bf[:, :], in_=R[:, :]), inc="dve")
            vd_xdt = p.val("dve")
            p.wait("tensor", "dve", vd_xdt)

            if full:
                if L == 0:
                    p.op("tensor", lambda e, cs=cs: e.matmul(
                        out=pb[6][:, 0:128], lhsT=hs_ap(32, 3 + cs, 3 + cs + CH),
                        rhs=hs_ap(33, 3 + cs, 3 + cs + CH), start=True, stop=True), inc="pe")
                else:
                    p.op("tensor", lambda e, cs=cs: e.matmul(
                        out=pb[6][:, 0:128], lhsT=hs_ap(32, 3 + cs, 3 + cs + CH)[0:64, :],
                        rhs=hs_ap(33, 3 + cs, 3 + cs + CH)[0:64, :], start=True, stop=True), inc="pe")
                vpe_G = p.val("pe")
                p.wait("scalar", "pe", vpe_G)
                p.op("scalar", lambda e: e.copy(out=Gm[:, :], in_=pb[6][:, 0:128]), inc="act")
                va_G = p.val("act")
                p.wait("gpsimd", "act", va_G)
                p.op("gpsimd", lambda e: e.affine_select(
                    out=Gm[:, :], in_=Gm[:, :], pattern=[[1, CH]], compare_op=ALU.is_ge,
                    fill=0.0, base=0, channel_multiplier=-1), inc="gps")
                v_gm = p.val("gps")
                p.wait("vector", "gps", v_gm)
                p.wait("tensor", "gps", v_gm)
                p.wait("vector", QA, p.val(QA))

            for g in range(16 if full else 0):
                h0 = 4 * g
                if g == 8:
                    p.wait("gpsimd", "pe", p.val("pe"))
                    p.dma("gpsimd", acsf[:, :], io["acs_dram"].ap()[32:64, :].rearrange("h t -> (h t)").unsqueeze(0), inc=QA)
                    p.wait("tensor", QA, p.val(QA))
                p.op("tensor", lambda e, h0=h0: e.matmul(
                    out=pb[4][:, :], lhsT=ones1f[:, :], rhs=acsf[0:1, (h0 % 32) * CH:((h0 % 32) + 4) * CH],
                    start=True, stop=True), inc="pe")
                v_db = p.val("pe")
                p.wait("vector", "pe", v_db)
                p.op("vector", lambda e, h0=h0: e.tensor_tensor(
                    out=Dsc.ap().rearrange("p (q l) -> p q l", q=4),
                    in0=pb[4].ap().rearrange("p (q l) -> p q l", q=4),
                    in1=acsT[:, h0:h0 + 4].unsqueeze(2).to_broadcast([128, 4, CH]), op=ALU.subtract), inc="dve")
                p.op("vector", lambda e: e.tensor_scalar(out=Dsc[:, :], in0=Dsc[:, :], scalar1=30.0, scalar2=None, op0=ALU.min), inc="dve")
                vd = p.val("dve")
                p.wait("scalar", "dve", vd)
                p.op("scalar", lambda e: e.activation(out=W4[:, :], in_=Dsc[:, :], func=AF.Exp), inc="act")
                va = p.val("act")
                p.wait("vector", "act", va)
                p.op("vector", lambda e: e.tensor_tensor(
                    out=W4.ap().rearrange("p (q l) -> p q l", q=4),
                    in0=W4.ap().rearrange("p (q l) -> p q l", q=4),
                    in1=Gm[:, :].unsqueeze(1).to_broadcast([128, 4, CH]), op=ALU.mult), inc="dve")
                vd_w4 = p.val("dve")
                p.wait("tensor", "dve", vd_w4)
                for hh in range(4):
                    h = h0 + hh
                    p.op("tensor", lambda e, hh=hh, h=h: e.matmul(
                        out=pb[5][:, hh * 64:(hh + 1) * 64], lhsT=W4[:, hh * CH:hh * CH + CH],
                        rhs=xdt[:, h * 64:(h + 1) * 64], start=True, stop=True))
                    if L == 0:
                        p.op("tensor", lambda e, hh=hh, h=h, cs=cs: e.matmul(
                            out=pb[6][:, hh * 64:(hh + 1) * 64], lhsT=hs_ap(33, 3 + cs, 3 + cs + CH),
                            rhs=R_bf[:, h * 64:(h + 1) * 64], start=True, stop=True),
                            inc=("pe" if hh == 3 else None))
                    else:
                        p.op("tensor", lambda e, hh=hh, h=h, cs=cs: e.matmul(
                            out=pb[6][:, hh * 64:(hh + 1) * 64],
                            lhsT=hs_ap(33, 3 + cs, 3 + cs + CH)[0:64, :],
                            rhs=R_bf[:, h * 64:(h + 1) * 64], start=True, stop=True),
                            inc=("pe" if hh == 3 else None))
                vpe_y = p.val("pe")
                p.wait("vector", "pe", vpe_y)
                for hh in range(4):
                    h = h0 + hh
                    p.op("vector", lambda e, h=h: e.tensor_scalar(
                        out=Bd[:, 0:N], in0=Btok[:, 0:N], scalar1=decT[:, h:h + 1], scalar2=None, op0=ALU.mult), inc="dve")
                    vb = p.val("dve")
                    p.wait("tensor", "dve", vb)
                    p.op("tensor", lambda e, hh=hh, h=h: e.matmul(
                        out=pb[7][0:N, hh * 64:(hh + 1) * 64], lhsT=Bd[:, 0:N],
                        rhs=xdt[:, h * 64:(h + 1) * 64], start=True, stop=True), inc="pe")
                    vS = p.val("pe")
                    p.op("vector", lambda e, h=h, hh=hh, slot_x=slot_x: e.scalar_tensor_tensor(
                        out=ysc[:, :], in0=rg(slot_x)[:, h * 64:(h + 1) * 64], scalar=Dbc[:, h:h + 1],
                        in1=pb[5][:, hh * 64:(hh + 1) * 64], op0=ALU.mult, op1=ALU.add), inc="dve")
                    p.op("vector", lambda e, h=h, hh=hh, slot_y=slot_y: e.scalar_tensor_tensor(
                        out=rg(slot_y)[:, h * 64:(h + 1) * 64], in0=pb[6][:, hh * 64:(hh + 1) * 64],
                        scalar=sdecT[:, h:h + 1], in1=ysc[:, :], op0=ALU.mult, op1=ALU.add), inc="dve")
                    p.wait("vector", "pe", vS)
                    p.op("vector", lambda e, h=h: e.tensor_scalar(
                        out=R[0:N, h * 64:(h + 1) * 64], in0=R[0:N, h * 64:(h + 1) * 64],
                        scalar1=gambc[0:N, h:h + 1], scalar2=None, op0=ALU.mult), inc="dve")
                    p.op("vector", lambda e, h=h, hh=hh: e.tensor_add(
                        out=R[0:N, h * 64:(h + 1) * 64], in0=R[0:N, h * 64:(h + 1) * 64],
                        in1=pb[7][0:N, hh * 64:(hh + 1) * 64]), inc="dve")
                vd_grp = p.val("dve")
                p.wait("tensor", "dve", vd_grp)

            if not full:
                for g in range(16):
                    h0 = 4 * g
                    for hh in range(4):
                        h = h0 + hh
                        p.op("vector", lambda e, h=h: e.tensor_scalar(
                            out=Bd[:, 0:N], in0=Btok[:, 0:N], scalar1=decT[:, h:h + 1], scalar2=None, op0=ALU.mult), inc="dve")
                        vb = p.val("dve")
                        p.wait("tensor", "dve", vb)
                        p.op("tensor", lambda e, hh=hh, h=h: e.matmul(
                            out=pb[7][0:N, hh * 64:(hh + 1) * 64], lhsT=Bd[:, 0:N],
                            rhs=xdt[:, h * 64:(h + 1) * 64], start=True, stop=True), inc="pe")
                        vS = p.val("pe")
                        p.wait("vector", "pe", vS)
                        p.op("vector", lambda e, h=h: e.tensor_scalar(
                            out=R[0:N, h * 64:(h + 1) * 64], in0=R[0:N, h * 64:(h + 1) * 64],
                            scalar1=gambc[0:N, h:h + 1], scalar2=None, op0=ALU.mult), inc="dve")
                        p.op("vector", lambda e, h=h, hh=hh: e.tensor_add(
                            out=R[0:N, h * 64:(h + 1) * 64], in0=R[0:N, h * 64:(h + 1) * 64],
                            in1=pb[7][0:N, hh * 64:(hh + 1) * 64]), inc="dve")
                    vd_grp = p.val("dve")
                    p.wait("tensor", "dve", vd_grp)

        if not full:
            continue

        for gid, grp in enumerate(gate_groups):
            m0 = grp[0]
            ncol = 128 * len(grp)

            def body(k, buf, grp=grp):
                for mi, mt in enumerate(grp):
                    p.op("tensor", lambda e, k=k, mi=mi, buf=buf: e.matmul(
                        out=pb[mi][:, :], lhsT=buf[:, mi * 128:mi * 128 + 128], rhs=uT(k, 0, TB),
                        start=(k == 0), stop=(k == 15)), inc=("pe" if mi == len(grp) - 1 else None))

            stream_mm(p, 16, wb2, ncol,
                      lambda kk, m0=m0, ncol=ncol: io["w_in_slc2"](kk, m0 * 128, ncol), body)
            vpe = p.val("pe")
            p.wait("scalar", "pe", vpe)
            for mi, mt in enumerate(grp):
                p.op("scalar", lambda e, mi=mi: e.activation(
                    out=gate_s[:, mi * TB:(mi + 1) * TB], in_=pb[mi][:, :], func=AF.Silu), inc="act")
            va = p.val("act")
            p.wait("tensor", "act", va)
            for mi, mt in enumerate(grp):
                for cc in range(NCH):
                    if cc >= 2:
                        p.wait("tensor", "act", p.val("act"))
                    p.op("tensor", lambda e, mi=mi, cc=cc: e.transpose(
                        out=pbb[4 + cc % 2][:, 0:128], in_=gate_s[:, mi * TB + cc * 128: mi * TB + (cc + 1) * 128],
                        identity=idb[:, :]), inc="pe")
                    v = p.val("pe")
                    p.wait("scalar", "pe", v)
                    p.op("scalar", lambda e, cc=cc: e.copy(
                        out=gtok[:, cc * 128:(cc + 1) * 128], in_=pbb[4 + cc % 2][:, 0:128]), inc="act")
                va2 = p.val("act")
                p.wait("vector", "act", va2)
                for cc in range(NCH):
                    sy = (tb * NCH + cc) % 5
                    p.op("vector", lambda e, mt=mt, cc=cc, sy=sy: e.tensor_mul(
                        out=rg(sy)[:, mt * 128:(mt + 1) * 128], in0=rg(sy)[:, mt * 128:(mt + 1) * 128],
                        in1=gtok[:, cc * 128:(cc + 1) * 128]), inc="dve")
                vd = p.val("dve")
                p.wait("scalar", "dve", vd)
                p.wait("tensor", "dve", vd)

        for c in range(NCH):
            sy = (tb * NCH + c) % 5
            sd = (tb * NCH + c + 4) % 5
            p.op("scalar", lambda e, sy=sy, sd=sd: e.activation(out=rg(sd)[:, :], in_=rg(sy)[:, :], func=AF.Square), inc="act")
            p.wait("vector", "act", p.val("act"))
            p.op("vector", lambda e, sd=sd, c=c: e.tensor_reduce(out=rms[:, c:c + 1], in_=rg(sd)[:, :], axis=AXX, op=ALU.add), inc="dve")
            p.op("vector", lambda e, c=c: e.tensor_scalar(out=rms[:, c:c + 1], in0=rms[:, c:c + 1], scalar1=1.0 / 4096, scalar2=None, op0=ALU.mult), inc="dve")
            vd = p.val("dve")
            p.wait("scalar", "dve", vd)
            p.op("scalar", lambda e, c=c: e.activation(out=rms[:, c:c + 1], in_=rms[:, c:c + 1], func=AF.Ln, bias=EPS), inc="act")
            p.op("scalar", lambda e, c=c: e.activation(out=rms[:, c:c + 1], in_=rms[:, c:c + 1], func=AF.Exp, scale=-0.5), inc="act")
            va = p.val("act")
            p.wait("vector", "act", va)
            p.op("vector", lambda e, sy=sy, sd=sd, c=c: e.scalar_tensor_tensor(
                out=rg(sd)[:, :], in0=rg(sy)[:, :], scalar=rms[:, c:c + 1], in1=nw_bc[:, :],
                op0=ALU.mult, op1=ALU.mult), inc="dve")
            vd = p.val("dve")
            p.wait("tensor", "dve", vd)
            for jg in range(8):
                for j4 in range(4):
                    j = jg * 4 + j4
                    p.op("tensor", lambda e, j=j, j4=j4, sd=sd: e.transpose(
                        out=pbb[j4][:, 0:128], in_=rg(sd)[:, j * 128:(j + 1) * 128],
                        identity=idb[:, :]), inc=("pe" if j4 == 3 else None))
                v = p.val("pe")
                p.wait("scalar", "pe", v)
                for j4 in range(4):
                    j = jg * 4 + j4
                    p.op("scalar", lambda e, j=j, j4=j4, c=c: e.copy(
                        out=yT_ap(j, c * 128, (c + 1) * 128), in_=pbb[j4][:, 0:128]), inc="act")
                p.wait("tensor", "act", p.val("act"))

        va_yT = p.val("act")
        p.wait("tensor", "act", va_yT)
        evac_h = {0: 0, 1: 0}
        for dchunk in range(4):
            pbase = 4 * (dchunk % 2)
            p.wait("tensor", "act", evac_h[dchunk % 2])

            def body(k, buf, pbase=pbase):
                for tt in range(NTT):
                    p.op("tensor", lambda e, k=k, tt=tt, buf=buf, pbase=pbase: e.matmul(
                        out=pb[pbase + tt][:, :], lhsT=yT_ap(k, tt * 128, (tt + 1) * 128),
                        rhs=buf[:, :], start=(k == 0), stop=(k == 31)), inc=("pe" if tt == NTT - 1 else None))

            stream_mm(p, 32, wr2, 512,
                      lambda kk, dchunk=dchunk: io["w_out_slc2"](kk, dchunk), body)
            vpe = p.val("pe")
            p.wait("scalar", "pe", vpe)
            for tt in range(NTT):
                p.op("scalar", lambda e, tt=tt, dchunk=dchunk, pbase=pbase: e.copy(
                    out=outb[:, tt * D + dchunk * 512: tt * D + (dchunk + 1) * 512], in_=pb[pbase + tt][:, :]), inc="act")
            evac_h[dchunk % 2] = p.val("act")
        p.wait("tensor", "act", p.val("act"))
        va_out = p.val("act")
        p.wait("vector", "act", va_out)
        for tt in range(NTT):
            p.wait("gpsimd", "dve", p.val("dve"))
            vx = p.dma("gpsimd", xtl[:, :], io["x_in"][t0 + tt * 128: t0 + (tt + 1) * 128, :], inc=QA)
            p.op("vector", lambda e, tt=tt: e.tensor_reduce(out=lnS[:, 0:1], in_=outb[:, tt * D:(tt + 1) * D], axis=AXX, op=ALU.add), inc="dve")
            p.op("scalar", lambda e, tt=tt: e.activation(out=sq2[:, :], in_=outb[:, tt * D:(tt + 1) * D], func=AF.Square), inc="act")
            p.wait("vector", "act", p.val("act"))
            p.op("vector", lambda e: e.tensor_reduce(out=lnS[:, 1:2], in_=sq2[:, :], axis=AXX, op=ALU.add), inc="dve")
            p.op("vector", lambda e: e.tensor_scalar(out=lnS[:, 1:2], in0=lnS[:, 1:2], scalar1=1.0 / D, scalar2=None, op0=ALU.mult), inc="dve")
            p.op("vector", lambda e: e.tensor_scalar(out=lnS[:, 2:3], in0=lnS[:, 0:1], scalar1=1.0 / D, scalar2=None, op0=ALU.mult), inc="dve")
            p.op("vector", lambda e: e.tensor_scalar(out=lnS[:, 4:5], in0=lnS[:, 2:3], scalar1=lnS[:, 2:3], scalar2=None, op0=ALU.mult), inc="dve")
            p.op("vector", lambda e: e.tensor_sub(out=lnS[:, 4:5], in0=lnS[:, 1:2], in1=lnS[:, 4:5]), inc="dve")
            vd = p.val("dve")
            p.wait("scalar", "dve", vd)
            p.op("scalar", lambda e: e.activation(out=lnS[:, 5:6], in_=lnS[:, 4:5], func=AF.Ln, bias=EPS), inc="act")
            p.op("scalar", lambda e: e.activation(out=lnS[:, 6:7], in_=lnS[:, 5:6], func=AF.Exp, scale=-0.5), inc="act")
            va = p.val("act")
            p.wait("vector", "act", va)
            p.wait("vector", QA, vx)
            p.op("vector", lambda e, tt=tt: e.tensor_scalar(
                out=outb[:, tt * D:(tt + 1) * D],
                in0=outb[:, tt * D:(tt + 1) * D],
                scalar1=lnS[:, 2:3], scalar2=lnS[:, 6:7], op0=ALU.subtract, op1=ALU.mult), inc="dve")
            p.op("vector", lambda e, tt=tt: e.tensor_add(out=xtl[:, :], in0=xtl[:, :], in1=outb[:, tt * D:(tt + 1) * D]), inc="dve")
            vd = p.val("dve")
            p.wait("gpsimd", "dve", vd)
            p.dma("gpsimd", io["x_out"][t0 + tt * 128:t0 + (tt + 1) * 128, :], xtl[:, :], inc=QA)
            p.wait("gpsimd", QA, p.val(QA))

    if mode == "state":
        vd = p.val("dve")
        p.wait("scalar", "dve", vd)
        p.op("scalar", lambda e: e.activation(out=gsum[:, 0:1], in_=gsum[:, 0:1], func=AF.Exp), inc="act")
        va = p.val("act")
        p.wait("gpsimd", "act", va)
        p.wait("gpsimd", "dve", vd)
        p.dma("gpsimd", io["out_S"], R[0:N, :], inc=QA)
        p.dma("gpsimd", io["out_g"], gsum[:, 0:1], inc=QA)
        p.dma("gpsimd", io["out_halo"], halo_sv[:, :], inc=QA)
        p.wait("gpsimd", QA, p.val(QA))
    p.wait("gpsimd", QW, p.val(QW))



def l0_win_slc(w):
    def f(k, c0, ncol):
        return w[k * 128:(k + 1) * 128, c0: c0 + ncol]
    return f


def l0_win_slc2(w):
    def f(kk, c0, ncol):
        return w[kk * 256:(kk + 1) * 256, c0: c0 + ncol].rearrange("(i p) c -> p i c", p=128)
    return f


def l1_win_slc2(w):
    def f(kk, c0, ncol):
        r = c0 // 1152
        pl = c0 - r * 1152
        return w[2048 * r + kk * 256: 2048 * r + (kk + 1) * 256, pl: pl + ncol].rearrange("(i p) c -> p i c", p=128)
    return f


def l0_wout_slc2(w):
    def f(kk, dchunk):
        return w[kk * 256:(kk + 1) * 256, dchunk * 512:(dchunk + 1) * 512].rearrange("(i p) c -> p i c", p=128)
    return f


def l0_wout_slc(w):
    def f(k, dchunk):
        return w[k * 128:(k + 1) * 128, dchunk * 512:(dchunk + 1) * 512]
    return f


def l1_win_slc(w):
    def f(k, c0, ncol):
        r = c0 // 1152
        pl = c0 - r * 1152
        return w[2048 * r + k * 128: 2048 * r + (k + 1) * 128, pl: pl + ncol]
    return f



def _bf16():
    import ml_dtypes
    return ml_dtypes.bfloat16





def _mk_common_io(nc, cfg, pref=""):
    CT = cfg.conv_tiles
    io = {}
    io["conv_w"] = nc.dram_tensor(pref + "conv_w", [cfg.CONV, 4], F32, kind="ExternalInput").ap()
    io["conv_b"] = nc.dram_tensor(pref + "conv_b", [cfg.CONV], F32, kind="ExternalInput").ap()
    io["dt_bias"] = nc.dram_tensor(pref + "dt_bias", [64], F32, kind="ExternalInput").ap()
    io["A_log"] = nc.dram_tensor(pref + "A_log", [64], F32, kind="ExternalInput").ap()
    io["Dv"] = nc.dram_tensor(pref + "Dv", [64], F32, kind="ExternalInput").ap()
    io["norm_w"] = nc.dram_tensor(pref + "norm_w", [4096], F32, kind="ExternalInput").ap()
    return io


def run_mamba_unfused(layer, x_full, w_in_bf, w_out_bf, conv_w, conv_b, dt_bias, A_log, Dv, norm_w):
    cfg = MambaCfg(layer)
    CT = cfg.conv_tiles
    N = cfg.N
    win_shape = list(w_in_bf.shape)

    nc = bass.Bass("TRN2", target_bir_lowering=False)
    p = Prog(nc)
    io = _mk_common_io(nc, cfg)
    io["tag"] = "s"
    io["x_in"] = nc.dram_tensor("x_in", [T, D], F32, kind="ExternalInput").ap()
    io["x_halo"] = nc.dram_tensor("x_halo", [3, D], F32, kind="ExternalInput").ap()
    w_in = nc.dram_tensor("w_in", win_shape, BF16, kind="ExternalInput").ap()
    io["w_in_slc"] = l0_win_slc(w_in) if layer == 0 else l1_win_slc(w_in)
    io["w_in_slc2"] = l0_win_slc2(w_in) if layer == 0 else l1_win_slc2(w_in)
    io["acs_dram"] = nc.dram_tensor("acs_dram", [64, CH], F32, kind="Internal")
    io["alast_dram"] = nc.dram_tensor("alast_dram", [64], F32, kind="Internal")
    io["out_S"] = nc.dram_tensor("out_S", [N, 4096], F32, kind="ExternalOutput").ap()
    io["out_g"] = nc.dram_tensor("out_g", [64, 1], F32, kind="ExternalOutput").ap()
    io["out_halo"] = nc.dram_tensor("out_halo", [128, 34 * 3], BF16, kind="ExternalOutput").ap()
    idb = p.sbuf("idb", [128, 128], BF16)
    idf = p.sbuf("idf", [128, 128], F32)
    emit_idents(p, idb, idf)
    emit_mamba(p, cfg, io, idb, idf, "state")
    p.realize()

    in_maps = []
    for c in range(NCORES):
        b, q = c // 4, c % 4
        xh = np.zeros((3, D), np.float32) if q == 0 else np.ascontiguousarray(x_full[b, q * 1024 - 3:q * 1024])
        in_maps.append({
            "x_in": np.ascontiguousarray(x_full[b, q * 1024:(q + 1) * 1024]),
            "x_halo": xh, "w_in": w_in_bf,
            "conv_w": conv_w, "conv_b": conv_b, "dt_bias": dt_bias,
            "A_log": A_log, "Dv": Dv, "norm_w": norm_w,
        })
    res = run_bass_kernel_spmd(nc, in_maps, core_ids=list(range(NCORES)))
    Ss = [res.results[c]["out_S"] for c in range(NCORES)]
    gs = [res.results[c]["out_g"] for c in range(NCORES)]
    halos = [res.results[c]["out_halo"] for c in range(NCORES)]

    nc = bass.Bass("TRN2", target_bir_lowering=False)
    p = Prog(nc)
    io = _mk_common_io(nc, cfg)
    io["tag"] = "f"
    io["x_in"] = nc.dram_tensor("x_in", [T, D], F32, kind="ExternalInput").ap()
    w_in = nc.dram_tensor("w_in", win_shape, BF16, kind="ExternalInput").ap()
    w_out = nc.dram_tensor("w_out", [4096, D], BF16, kind="ExternalInput").ap()
    io["w_in_slc"] = l0_win_slc(w_in) if layer == 0 else l1_win_slc(w_in)
    io["w_in_slc2"] = l0_win_slc2(w_in) if layer == 0 else l1_win_slc2(w_in)
    io["w_out_slc"] = l0_wout_slc(w_out)
    io["w_out_slc2"] = l0_wout_slc2(w_out)
    io["S_in"] = [nc.dram_tensor(f"S_in{j}", [N, 4096], F32, kind="ExternalInput").ap() for j in range(3)]
    io["g_in"] = nc.dram_tensor("g_in", [1, 192], F32, kind="ExternalInput").ap()
    io["halo_in"] = nc.dram_tensor("halo_in", [128, 34 * 3], BF16, kind="ExternalInput").ap()
    io["mrow"] = nc.dram_tensor("mrow", [1, 3], F32, kind="ExternalInput").ap()
    io["mcol"] = nc.dram_tensor("mcol", [128, 4], F32, kind="ExternalInput").ap()
    io["x_out"] = nc.dram_tensor("x_out", [T, D], F32, kind="ExternalOutput").ap()
    io["acs_dram"] = nc.dram_tensor("acs_dram", [64, CH], F32, kind="Internal")
    io["alast_dram"] = nc.dram_tensor("alast_dram", [64], F32, kind="Internal")
    idb = p.sbuf("idb", [128, 128], BF16)
    idf = p.sbuf("idf", [128, 128], F32)
    emit_idents(p, idb, idf)
    emit_mamba(p, cfg, io, idb, idf, "full")
    p.realize()

    bf = _bf16()
    in_maps = []
    for c in range(NCORES):
        b, q = c // 4, c % 4
        m = {
            "x_in": np.ascontiguousarray(x_full[b, q * 1024:(q + 1) * 1024]),
            "w_in": w_in_bf, "w_out": w_out_bf,
            "conv_w": conv_w, "conv_b": conv_b, "dt_bias": dt_bias,
            "A_log": A_log, "Dv": Dv, "norm_w": norm_w,
            "g_in": np.concatenate([gs[b * 4 + j].reshape(1, 64) for j in range(3)], axis=1).astype(np.float32),
            "halo_in": halos[b * 4 + q - 1] if q > 0 else np.zeros((128, 34 * 3), bf),
            "mrow": np.array([[1.0 if j < q else 0.0 for j in range(3)]], np.float32),
            "mcol": np.tile(np.array([[1.0 if j < q else 0.0 for j in range(3)] + [1.0 if q > 0 else 0.0]], np.float32), (128, 1)),
        }
        for j in range(3):
            m[f"S_in{j}"] = Ss[b * 4 + j]
        in_maps.append(m)
    res = run_bass_kernel_spmd(nc, in_maps, core_ids=list(range(NCORES)))
    out = np.empty((2, 4096, D), np.float32)
    for c in range(NCORES):
        b, q = c // 4, c % 4
        out[b, q * 1024:(q + 1) * 1024] = res.results[c]["x_out"]
    return out


def kernel(**inputs):
    inp = {k: np.asarray(v) for k, v in inputs.items()}
    return run_fused(inp).astype(np.float32)


def kernel_unfused(**inputs):
    inp = {k: np.asarray(v) for k, v in inputs.items()}
    x = np.ascontiguousarray(inp["L0_attn_x"].astype(np.float32))

    prep = run_prep_unfused(inp)

    x1 = run_mamba_unfused(0, x, prep["W0in"], prep["W0out"],
                           inp["L0_attn_conv_w"], inp["L0_attn_conv_b"], inp["L0_attn_dt_bias"],
                           inp["L0_attn_A_log"], inp["L0_attn_D"], inp["L0_attn_norm_w"])
    x2 = run_ffn0_unfused(x1, prep["GgT"], prep["GuT"], prep["GdT"])
    x3 = run_mamba_unfused(1, x2, prep["W1inT"], prep["W1outT"],
                           inp["L1_attn_conv_w"], inp["L1_attn_conv_b"], inp["L1_attn_dt_bias"],
                           inp["L1_attn_A_log"], inp["L1_attn_D"], inp["L1_attn_norm_w"])
    x4 = run_moe_unfused(x3, prep["Wgate"], prep["Wsup"], prep["Wsdn"], prep["Mup"], prep["Mdn"])
    return x4.astype(np.float32)


def kernel_fused(**inputs):
    inp = {k: np.asarray(v) for k, v in inputs.items()}
    return run_fused(inp).astype(np.float32)




def emit_prep(p, io, idb, idf):
    pk = p.sbuf("pk_pp", [128, 1024], I32)
    xi = p.sbuf("xi_pp", [128, 1024], I32)
    xf = p.sbuf("xf_pp", [128, 1024], BF16)
    t1 = p.sbuf("t1_pp", [128, 1024], BF16)
    mm_ = p.sbuf("mm_pp", [128, 1024], BF16)
    r1 = p.sbuf("r1_pp", [128, 1024], BF16)
    r2 = p.sbuf("r2_pp", [128, 1024], BF16)
    vv = p.sbuf("vv_pp", [128, 1024], BF16)
    sg = p.sbuf("sg_pp", [128, 1024], BF16)
    sc = p.sbuf("sc_pp", [128, 64], F32)
    nib2 = p.sbuf("nib2_pp", [128, 2048], BF16)
    bufB = p.sbuf("bufB_pp", [128, 8512], BF16)
    stg = p.sbuf("stg_pp", [128, 512], BF16)
    pbt = [p.psum(f"pbt{i}_pp", [128, 512], F32) for i in range(2)]
    pbtb = [t.ap().bitcast(BF16) for t in pbt]

    cst2 = p.sbuf("cstb_pp", [128, 2], F32)
    p.op("vector", lambda e: e.memset(cst2[:, 0:1], -2.0))
    p.op("vector", lambda e: e.memset(cst2[:, 1:2], -6.0), inc="dve")
    p.nc.const_aps.register(F32, -2.0, cst2[:, 0:1])
    p.nc.const_aps.register(F32, -6.0, cst2[:, 1:2])
    p.wait("scalar", "dve", p.val("dve"))

    for (src, dst, rows, cols) in io["casts"]:
        for i0 in range(0, rows, 128):
            n = min(128, rows - i0)
            v = p.dma("gpsimd", bufB[0:n, 0:cols], src[i0:i0 + n, :], inc="qa")
            p.wait("gpsimd", "qa", v)
            p.dma("gpsimd", dst[i0:i0 + n, :], bufB[0:n, 0:cols], inc="qa")
            p.wait("gpsimd", "qa", p.val("qa"))

    def transpose_out(src_sbuf, dst, ncb, i0, nr_valid=128):
        for jb0 in range(0, ncb, 4):
            nb = min(4, ncb - jb0)
            for q in range(nb):
                p.op("tensor", lambda e, jb0=jb0, q=q: e.transpose(
                    out=pbtb[0][:, q * 128:(q + 1) * 128], in_=src_sbuf[:, (jb0 + q) * 128:(jb0 + q + 1) * 128],
                    identity=idb[:, :]), inc=("pe" if q == nb - 1 else None))
            v2 = p.val("pe")
            p.wait("scalar", "pe", v2)
            p.op("scalar", lambda e, nb=nb: e.copy(out=stg[:, 0:nb * 128], in_=pbtb[0][:, 0:nb * 128]), inc="act")
            va = p.val("act")
            p.wait("gpsimd", "act", va)
            p.dma("gpsimd",
                  dst[jb0 * 128:(jb0 + nb) * 128, i0:i0 + 128].rearrange("(q c) r -> c q r", q=nb),
                  stg.ap()[:, 0:nb * 128].rearrange("c (q r) -> c q r", q=nb), inc="qa")
            p.wait("tensor", "qa", p.val("qa"))

    for (src, dst, rows, cols) in io["ctrans"]:
        ncb = cols // 128
        for i in range(0, rows, 128):
            v = p.dma("gpsimd", bufB[:, 0:cols], src[i:i + 128, :], inc="qa")
            p.wait("tensor", "qa", v)
            transpose_out(bufB, dst, ncb, i)

    for (packed, scales, dst) in io["decode"]:
        R = packed.shape[0]
        PC = packed.shape[1]
        for i in range(0, R, 128):
            v = p.dma("gpsimd", pk[:, 0:PC], packed[i:i + 128, :], inc="qa")
            v2 = p.dma("gpsimd", sc[:, 0:PC // 16], scales[i:i + 128, :], inc="qa")
            p.wait("vector", "qa", p.val("qa"))
            for off, shift in ((0, 0), (1, 4)):
                if shift:
                    p.op("vector", lambda e: e.tensor_scalar(out=xi[:, 0:PC], in0=pk[:, 0:PC], scalar1=4, scalar2=15, op0=ALU.logical_shift_right, op1=ALU.bitwise_and))
                else:
                    p.op("vector", lambda e: e.tensor_scalar(out=xi[:, 0:PC], in0=pk[:, 0:PC], scalar1=15, scalar2=None, op0=ALU.bitwise_and))
                p.op("vector", lambda e: e.tensor_copy(out=xf[:, 0:PC], in_=xi[:, 0:PC]))
                p.op("vector", lambda e: e.tensor_scalar(out=t1[:, 0:PC], in0=xf[:, 0:PC], scalar1=8.0, scalar2=-8.0, op0=ALU.is_ge, op1=ALU.mult))
                p.op("vector", lambda e: e.tensor_add(out=mm_[:, 0:PC], in0=xf[:, 0:PC], in1=t1[:, 0:PC]), inc="dve")
                vd = p.val("dve")
                p.wait("scalar", "dve", vd)
                p.op("scalar", lambda e: e.activation(out=r1[:, 0:PC], in_=mm_[:, 0:PC], func=AF.Relu, scale=0.5, bias=-2.0), inc="act")
                p.op("scalar", lambda e: e.activation(out=r2[:, 0:PC], in_=mm_[:, 0:PC], func=AF.Relu, bias=-6.0), inc="act")
                va = p.val("act")
                p.wait("vector", "act", va)
                p.op("vector", lambda e: e.scalar_tensor_tensor(out=vv[:, 0:PC], in0=mm_[:, 0:PC], scalar=0.5, in1=r1[:, 0:PC], op0=ALU.mult, op1=ALU.add))
                p.op("vector", lambda e: e.tensor_add(out=vv[:, 0:PC], in0=vv[:, 0:PC], in1=r2[:, 0:PC]))
                p.op("vector", lambda e: e.tensor_scalar(out=sg[:, 0:PC], in0=t1[:, 0:PC], scalar1=0.25, scalar2=1.0, op0=ALU.mult, op1=ALU.add))
                p.op("vector", lambda e: e.tensor_mul(out=vv[:, 0:PC], in0=vv[:, 0:PC], in1=sg[:, 0:PC]))
                p.op("vector", lambda e, off=off: e.tensor_tensor(
                    out=nib2.ap()[:, 0:2 * PC].rearrange("p (a b t) -> p a b t", b=16, t=2)[:, :, :, off],
                    in0=vv.ap()[:, 0:PC].rearrange("p (a b) -> p a b", b=16),
                    in1=sc[:, 0:PC // 16].unsqueeze(2).to_broadcast([128, PC // 16, 16]),
                    op=ALU.mult), inc="dve")
            vd = p.val("dve")
            p.wait("tensor", "dve", vd)
            transpose_out(nib2, dst, (2 * PC) // 128, i)
    p.wait("gpsimd", "qa", p.val("qa"))


def emit_ffn0(p, io, idb, idf):
    sfx = "_f0"
    uTb = p.sbuf("uT" + sfx, [128, 16 * T], BF16)
    uT = lambda k, a, b: uTb[:, k * T + a: k * T + b]
    inter = p.sbuf("inter" + sfx, [128, 16 * T], BF16)
    dacc = p.sbuf("dacc" + sfx, [128, 8 * D], F32)
    gbuf = p.sbuf("gbuf" + sfx, [128, 3 * T], BF16)
    s1 = p.sbuf("s1" + sfx, [128, T], BF16)
    s2 = p.sbuf("s2" + sfx, [128, T], BF16)
    lnS = p.sbuf("lnS" + sfx, [128, 8], F32)
    xtl = p.sbuf("xtl" + sfx, [128, D], F32)
    utok = p.sbuf("utok" + sfx, [128, D], BF16)
    sq2 = p.sbuf("sq2" + sfx, [128, D], BF16)
    wbf2 = [p.sbuf(f"wbf{i}" + sfx, [128, 768], BF16) for i in range(2)]
    wrf2 = [p.sbuf(f"wrf{i}" + sfx, [128, 1024], BF16) for i in range(2)]
    pb = [p.psum(f"pb{i}" + sfx, [128, 512], F32) for i in range(8)]
    pbb = [t.ap().bitcast(BF16) for t in pb]

    if not getattr(p, "_consts_done", False):
        p._consts_done = True
        cstt = p.sbuf("cstt" + sfx, [128, 2], F32)
        p.op("vector", lambda e: e.memset(cstt[:, 0:1], EPS), inc="dve")
        p.op("vector", lambda e: e.memset(cstt[:, 1:2], 1.0), inc="dve")
        p.nc.const_aps.register(F32, EPS, cstt[:, 0:1])
        p.nc.const_aps.register(F32, 1.0, cstt[:, 1:2])

    for it in range(8):
        p.wait("gpsimd", "dve", p.val("dve"))
        p.wait("gpsimd", "act", p.val("act"))
        vx = p.dma("gpsimd", xtl[:, :], io["x_in"][it * 128:(it + 1) * 128, :], inc="qa")
        p.wait("vector", "qa", vx)
        p.op("vector", lambda e: e.tensor_reduce(out=lnS[:, 0:1], in_=xtl[:, :], axis=AXX, op=ALU.add), inc="dve")
        p.wait("scalar", "qa", vx)
        p.op("scalar", lambda e: e.activation(out=sq2[:, :], in_=xtl[:, :], func=AF.Square), inc="act")
        p.wait("vector", "act", p.val("act"))
        p.op("vector", lambda e: e.tensor_reduce(out=lnS[:, 1:2], in_=sq2[:, :], axis=AXX, op=ALU.add), inc="dve")
        p.op("vector", lambda e: e.tensor_scalar(out=lnS[:, 1:2], in0=lnS[:, 1:2], scalar1=1.0 / D, scalar2=None, op0=ALU.mult), inc="dve")
        p.op("vector", lambda e: e.tensor_scalar(out=lnS[:, 2:3], in0=lnS[:, 0:1], scalar1=1.0 / D, scalar2=None, op0=ALU.mult), inc="dve")
        p.op("vector", lambda e: e.tensor_scalar(out=lnS[:, 4:5], in0=lnS[:, 2:3], scalar1=lnS[:, 2:3], scalar2=None, op0=ALU.mult), inc="dve")
        p.op("vector", lambda e: e.tensor_sub(out=lnS[:, 4:5], in0=lnS[:, 1:2], in1=lnS[:, 4:5]), inc="dve")
        vd = p.val("dve")
        p.wait("scalar", "dve", vd)
        p.op("scalar", lambda e: e.activation(out=lnS[:, 5:6], in_=lnS[:, 4:5], func=AF.Ln, bias=EPS), inc="act")
        p.op("scalar", lambda e: e.activation(out=lnS[:, 6:7], in_=lnS[:, 5:6], func=AF.Exp, scale=-0.5), inc="act")
        va = p.val("act")
        p.wait("vector", "act", va)
        p.op("vector", lambda e: e.tensor_scalar(out=utok[:, :], in0=xtl[:, :], scalar1=lnS[:, 2:3], scalar2=lnS[:, 6:7], op0=ALU.subtract, op1=ALU.mult), inc="dve")
        vd = p.val("dve")
        p.wait("tensor", "dve", vd)
        for j in range(16):
            p.op("tensor", lambda e, j=j: e.transpose(
                out=pbb[j % 4][:, 0:128], in_=utok[:, j * 128:(j + 1) * 128],
                identity=idb[:, :]), inc=("pe" if j % 4 == 3 else None))
            if j % 4 == 3:
                v = p.val("pe")
                p.wait("scalar", "pe", v)
                for jj in range(j - 3, j + 1):
                    p.op("scalar", lambda e, jj=jj, it=it: e.copy(
                        out=uT(jj, it * 128, it * 128 + 128), in_=pbb[jj % 4][:, 0:128]), inc="act")
                p.wait("tensor", "act", p.val("act"))

    C1, C2 = 0.7978845608028654, 0.044715
    for s in range(8):
        groups = [list(range(i, min(i + 3, 16))) for i in range(0, 16, 3)]
        for grp in groups:
            ncol = 128 * len(grp)
            for which, wsrc in (("g", io["g_gT"]), ("u", io["g_uT"])):
                slc2 = lambda kk, wsrc=wsrc, s=s, grp=grp, ncol=ncol: wsrc[
                    2048 * s + 256 * kk: 2048 * s + 256 * (kk + 1),
                    128 * grp[0]:128 * grp[0] + ncol].rearrange("(i p) c -> p i c", p=128)

                def fbody(k, buf, grp=grp):
                    for mi, m in enumerate(grp):
                        for nf in range(2):
                            p.op("tensor", lambda e, k=k, mi=mi, nf=nf, buf=buf: e.matmul(
                                out=pb[2 * mi + nf][:, :],
                                lhsT=buf[:, mi * 128:(mi + 1) * 128], rhs=uT(k, nf * 512, (nf + 1) * 512),
                                start=(k == 0), stop=(k == 15)), inc=("pe" if (mi == len(grp) - 1 and nf == 1) else None))
                stream_mm(p, 16, wbf2, ncol, slc2, fbody)
                if which == "g":
                    vpe = p.val("pe")
                    p.wait("scalar", "pe", vpe)
                    for mi, m in enumerate(grp):
                        for nf in range(2):
                            p.op("scalar", lambda e, mi=mi, nf=nf: e.copy(
                                out=gbuf[:, mi * T + nf * 512: mi * T + (nf + 1) * 512], in_=pb[2 * mi + nf][:, :]), inc="act")
                    p.wait("tensor", "act", p.val("act"))
            vpe = p.val("pe")
            p.wait("scalar", "pe", vpe)
            p.wait("vector", "pe", vpe)
            for mi, m in enumerate(grp):
                gm = lambda mi=mi: gbuf[:, mi * T:(mi + 1) * T]
                p.op("scalar", lambda e, mi=mi: e.activation(out=s1[:, :], in_=gbuf[:, mi * T:(mi + 1) * T], func=AF.Square), inc="act")
                va = p.val("act")
                p.wait("vector", "act", va)
                p.op("vector", lambda e: e.tensor_scalar(out=s1[:, :], in0=s1[:, :], scalar1=C2, scalar2=1.0, op0=ALU.mult, op1=ALU.add), inc="dve")
                p.op("vector", lambda e, mi=mi: e.tensor_mul(out=s1[:, :], in0=s1[:, :], in1=gbuf[:, mi * T:(mi + 1) * T]), inc="dve")
                vd = p.val("dve")
                p.wait("scalar", "dve", vd)
                p.op("scalar", lambda e: e.activation(out=s2[:, :], in_=s1[:, :], func=AF.Tanh, scale=C1), inc="act")
                va = p.val("act")
                p.wait("vector", "act", va)
                p.op("vector", lambda e: e.tensor_scalar(out=s2[:, :], in0=s2[:, :], scalar1=1.0, scalar2=0.5, op0=ALU.add, op1=ALU.mult), inc="dve")
                p.op("vector", lambda e, mi=mi: e.tensor_mul(out=s2[:, :], in0=s2[:, :], in1=gbuf[:, mi * T:(mi + 1) * T]), inc="dve")
                for nf in range(2):
                    p.op("vector", lambda e, m=m, mi=mi, nf=nf: e.tensor_tensor(
                        out=inter[:, m * T + nf * 512: m * T + (nf + 1) * 512],
                        in0=s2[:, nf * 512:(nf + 1) * 512], in1=pb[2 * mi + nf][:, :], op=ALU.mult), inc="dve")
            vd = p.val("dve")
            p.wait("tensor", "dve", vd)
        for dchunk in range(4):
            slc2 = lambda kk, s=s, dchunk=dchunk: io["g_dT"][
                2048 * s + 256 * kk: 2048 * s + 256 * (kk + 1),
                512 * dchunk:512 * (dchunk + 1)].rearrange("(i p) c -> p i c", p=128)

            def dbody(k, buf):
                for tt in range(8):
                    p.op("tensor", lambda e, k=k, tt=tt, buf=buf: e.matmul(
                        out=pb[tt][:, :], lhsT=inter[:, k * T + tt * 128: k * T + (tt + 1) * 128],
                        rhs=buf, start=(k == 0), stop=(k == 15)), inc=("pe" if tt == 7 else None))
            stream_mm(p, 16, wrf2, 512, slc2, dbody)
            vpe = p.val("pe")
            p.wait("vector", "pe", vpe)
            for tt in range(8):
                if s == 0:
                    p.op("vector", lambda e, tt=tt, dchunk=dchunk: e.tensor_copy(
                        out=dacc[:, tt * D + dchunk * 512: tt * D + (dchunk + 1) * 512], in_=pb[tt][:, :]), inc="dve")
                else:
                    p.op("vector", lambda e, tt=tt, dchunk=dchunk: e.tensor_add(
                        out=dacc[:, tt * D + dchunk * 512: tt * D + (dchunk + 1) * 512],
                        in0=dacc[:, tt * D + dchunk * 512: tt * D + (dchunk + 1) * 512], in1=pb[tt][:, :]), inc="dve")
            vd = p.val("dve")
            p.wait("tensor", "dve", vd)

    for tt in range(8):
        p.wait("gpsimd", "dve", p.val("dve"))
        vx = p.dma("gpsimd", xtl[:, :], io["x_in"][tt * 128:(tt + 1) * 128, :], inc="qa")
        p.op("vector", lambda e, tt=tt: e.tensor_reduce(out=lnS[:, 0:1], in_=dacc[:, tt * D:(tt + 1) * D], axis=AXX, op=ALU.add), inc="dve")
        p.op("scalar", lambda e, tt=tt: e.activation(out=sq2[:, :], in_=dacc[:, tt * D:(tt + 1) * D], func=AF.Square), inc="act")
        p.wait("vector", "act", p.val("act"))
        p.op("vector", lambda e: e.tensor_reduce(out=lnS[:, 1:2], in_=sq2[:, :], axis=AXX, op=ALU.add), inc="dve")
        p.op("vector", lambda e: e.tensor_scalar(out=lnS[:, 1:2], in0=lnS[:, 1:2], scalar1=1.0 / D, scalar2=None, op0=ALU.mult), inc="dve")
        p.op("vector", lambda e: e.tensor_scalar(out=lnS[:, 2:3], in0=lnS[:, 0:1], scalar1=1.0 / D, scalar2=None, op0=ALU.mult), inc="dve")
        p.op("vector", lambda e: e.tensor_scalar(out=lnS[:, 4:5], in0=lnS[:, 2:3], scalar1=lnS[:, 2:3], scalar2=None, op0=ALU.mult), inc="dve")
        p.op("vector", lambda e: e.tensor_sub(out=lnS[:, 4:5], in0=lnS[:, 1:2], in1=lnS[:, 4:5]), inc="dve")
        vd = p.val("dve")
        p.wait("scalar", "dve", vd)
        p.op("scalar", lambda e: e.activation(out=lnS[:, 5:6], in_=lnS[:, 4:5], func=AF.Ln, bias=EPS), inc="act")
        p.op("scalar", lambda e: e.activation(out=lnS[:, 6:7], in_=lnS[:, 5:6], func=AF.Exp, scale=-0.5), inc="act")
        va = p.val("act")
        p.wait("vector", "act", va)
        p.wait("vector", "qa", vx)
        p.op("vector", lambda e, tt=tt: e.tensor_scalar(
            out=dacc[:, tt * D:(tt + 1) * D], in0=dacc[:, tt * D:(tt + 1) * D],
            scalar1=lnS[:, 2:3], scalar2=lnS[:, 6:7], op0=ALU.subtract, op1=ALU.mult), inc="dve")
        p.op("vector", lambda e, tt=tt: e.tensor_add(out=xtl[:, :], in0=xtl[:, :], in1=dacc[:, tt * D:(tt + 1) * D]), inc="dve")
        vd = p.val("dve")
        p.wait("gpsimd", "dve", vd)
        p.dma("gpsimd", io["x_out"][tt * 128:(tt + 1) * 128, :], xtl[:, :], inc="qa")
        p.wait("gpsimd", "qa", p.val("qa"))
    p.wait("gpsimd", "qw", p.val("qw"))


def run_prep_unfused(inp):
    nc = bass.Bass("TRN2", target_bir_lowering=False)
    p = Prog(nc)

    def ein(name, shape, dtype=F32):
        return nc.dram_tensor(name, shape, dtype, kind="ExternalInput").ap()

    def eout(name, shape, dtype=BF16):
        return nc.dram_tensor(name, shape, dtype, kind="ExternalOutput").ap()

    W0in_s = ein("W0in_s", [256, 8512]); W0in_o = eout("W0in_o", [256, 8512])
    W0out_s = ein("W0out_s", [512, 2048]); W0out_o = eout("W0out_o", [512, 2048])
    Mup_s = ein("Mup_s", [16384, 512]); Mup_o = eout("Mup_o", [16384, 512])
    Mdn_s = ein("Mdn_s", [2048, 2048]); Mdn_o = eout("Mdn_o", [2048, 2048])
    Wsup_s = ein("Wsup_s", [256, 1024]); Wsup_o = eout("Wsup_o", [256, 1024])
    Wsdn_s = ein("Wsdn_s", [64, 2048]); Wsdn_o = eout("Wsdn_o", [64, 2048])
    Wgate_s = ein("Wgate_s", [256, 64]); Wgate_o = eout("Wgate_o", [256, 64])
    W1in_s = ein("W1in_s", [1152, 2048]); W1inT_o = eout("W1inT_o", [2048, 1152])
    W1out_s = ein("W1out_s", [2048, 512]); W1outT_o = eout("W1outT_o", [512, 2048])
    gP_s = ein("gP_s", [2048, 1024], I32); gS_s = ein("gS_s", [2048, 64]); gT_o = eout("gT_o", [2048, 2048])
    uP_s = ein("uP_s", [2048, 1024], I32); uS_s = ein("uS_s", [2048, 64]); uT_o = eout("uT_o", [2048, 2048])
    dP_s = ein("dP_s", [2048, 1024], I32); dS_s = ein("dS_s", [2048, 64]); dT_o = eout("dT_o", [2048, 2048])

    io = {
        "casts": [
            (W0in_s, W0in_o, 256, 8512), (W0out_s, W0out_o, 512, 2048),
            (Mup_s, Mup_o, 16384, 512), (Mdn_s, Mdn_o, 2048, 2048),
            (Wsup_s, Wsup_o, 256, 1024), (Wsdn_s, Wsdn_o, 64, 2048),
            (Wgate_s, Wgate_o, 256, 64),
        ],
        "ctrans": [(W1in_s, W1inT_o, 1152, 2048), (W1out_s, W1outT_o, 2048, 512)],
        "decode": [(gP_s, gS_s, gT_o), (uP_s, uS_s, uT_o), (dP_s, dS_s, dT_o)],
    }
    idb = p.sbuf("idb", [128, 128], BF16)
    idf = p.sbuf("idf", [128, 128], F32)
    emit_idents(p, idb, idf)
    emit_prep(p, io, idb, idf)
    p.realize()

    in_maps = []
    w1in_pad = np.zeros((9216, 2048), np.float32)
    w1in_pad[:8384] = inp["L1_attn_W_in"]
    mup_flat = np.ascontiguousarray(inp["L1_ffn_w_up"].reshape(131072, 512))
    mdn_flat = np.ascontiguousarray(inp["L1_ffn_w_down"].reshape(16384, 2048))
    for c in range(NCORES):
        in_maps.append({
            "W0in_s": np.ascontiguousarray(inp["L0_attn_in_proj_w"][256 * c:256 * (c + 1)]),
            "W0out_s": np.ascontiguousarray(inp["L0_attn_out_proj_w"][512 * c:512 * (c + 1)]),
            "Mup_s": np.ascontiguousarray(mup_flat[16384 * c:16384 * (c + 1)]),
            "Mdn_s": np.ascontiguousarray(mdn_flat[2048 * c:2048 * (c + 1)]),
            "Wsup_s": np.ascontiguousarray(inp["L1_ffn_w_shared_up"][256 * c:256 * (c + 1)]),
            "Wsdn_s": np.ascontiguousarray(inp["L1_ffn_w_shared_down"][64 * c:64 * (c + 1)]),
            "Wgate_s": np.ascontiguousarray(inp["L1_ffn_w_gate"][256 * c:256 * (c + 1)]),
            "W1in_s": np.ascontiguousarray(w1in_pad[1152 * c:1152 * (c + 1)]),
            "W1out_s": np.ascontiguousarray(inp["L1_attn_W_out"][:, 512 * c:512 * (c + 1)]),
            "gP_s": np.ascontiguousarray(inp["L0_ffn_gate_W_packed"][2048 * c:2048 * (c + 1)]),
            "gS_s": np.ascontiguousarray(inp["L0_ffn_gate_scales"][2048 * c:2048 * (c + 1)]),
            "uP_s": np.ascontiguousarray(inp["L0_ffn_up_W_packed"][2048 * c:2048 * (c + 1)]),
            "uS_s": np.ascontiguousarray(inp["L0_ffn_up_scales"][2048 * c:2048 * (c + 1)]),
            "dP_s": np.ascontiguousarray(inp["L0_ffn_down_W_packed"][:, 1024 * c:1024 * (c + 1)]),
            "dS_s": np.ascontiguousarray(inp["L0_ffn_down_scales"][:, 64 * c:64 * (c + 1)]),
        })
    res = run_bass_kernel_spmd(nc, in_maps, core_ids=list(range(NCORES)))
    r = res.results
    cat = lambda key, ax=0: np.concatenate([r[c][key] for c in range(NCORES)], axis=ax)
    return {
        "W0in": cat("W0in_o"), "W0out": cat("W0out_o"),
        "Mup": cat("Mup_o"), "Mdn": cat("Mdn_o"),
        "Wsup": cat("Wsup_o"), "Wsdn": cat("Wsdn_o"), "Wgate": cat("Wgate_o"),
        "W1inT": cat("W1inT_o"), "W1outT": cat("W1outT_o"),
        "GgT": cat("gT_o"), "GuT": cat("uT_o"), "GdT": cat("dT_o"),
    }


def run_ffn0_unfused(x_full, GgT, GuT, GdT):
    nc = bass.Bass("TRN2", target_bir_lowering=False)
    p = Prog(nc)
    io = {
        "x_in": nc.dram_tensor("x_in", [T, D], F32, kind="ExternalInput").ap(),
        "g_gT": nc.dram_tensor("g_gT", [16384, 2048], BF16, kind="ExternalInput").ap(),
        "g_uT": nc.dram_tensor("g_uT", [16384, 2048], BF16, kind="ExternalInput").ap(),
        "g_dT": nc.dram_tensor("g_dT", [16384, 2048], BF16, kind="ExternalInput").ap(),
        "x_out": nc.dram_tensor("x_out", [T, D], F32, kind="ExternalOutput").ap(),
    }
    idb = p.sbuf("idb", [128, 128], BF16)
    idf = p.sbuf("idf", [128, 128], F32)
    emit_idents(p, idb, idf)
    emit_ffn0(p, io, idb, idf)
    p.realize()
    in_maps = []
    for c in range(NCORES):
        b, q = c // 4, c % 4
        in_maps.append({
            "x_in": np.ascontiguousarray(x_full[b, q * 1024:(q + 1) * 1024]),
            "g_gT": GgT, "g_uT": GuT, "g_dT": GdT,
        })
    res = run_bass_kernel_spmd(nc, in_maps, core_ids=list(range(NCORES)))
    out = np.empty((2, 4096, D), np.float32)
    for c in range(NCORES):
        b, q = c // 4, c % 4
        out[b, q * 1024:(q + 1) * 1024] = res.results[c]["x_out"]
    return out



def emit_moe(p, io, idb, idf):
    sfx = "_mo"
    uTb = p.sbuf("uT" + sfx, [128, 16 * T], BF16)
    uT = lambda k, a, b: uTb[:, k * T + a: k * T + b]
    XT = p.sbuf("XT" + sfx, [128, 16 * 640], BF16)
    ybuf = p.sbuf("yb" + sfx, [128, 4 * T], BF16)
    gsil = p.sbuf("gs" + sfx, [128, T], BF16)
    actS = p.sbuf("aS" + sfx, [128, 4 * T], BF16)
    wsdn_sb = p.sbuf("wsd" + sfx, [128, 4 * 2048], BF16)
    accf = p.sbuf("acc" + sfx, [128, 2048], F32)
    yg = p.sbuf("yg" + sfx, [128, 2048], BF16)
    do0 = p.sbuf("do0" + sfx, [128, 2048], BF16)
    do1 = p.sbuf("do1" + sfx, [32, 2048], BF16)
    ab0 = p.sbuf("ab0" + sfx, [128, 512], BF16)
    ab1 = p.sbuf("ab1" + sfx, [32, 512], BF16)
    aTa = p.sbuf("aTa" + sfx, [128, 160], BF16)
    aTb = p.sbuf("aTb" + sfx, [128, 160], BF16)
    scT = p.sbuf("scT" + sfx, [64, T], F32)
    sctok = p.sbuf("sct" + sfx, [128, 64], F32)
    mx = p.sbuf("mx" + sfx, [128, 8], F32)
    ixu = p.sbuf("ixu" + sfx, [128, 8], U32)
    ixf = p.sbuf("ixf" + sfx, [128, 8 * 8], F32)
    wno = p.sbuf("wno" + sfx, [128, 8], F32)
    wcols = p.sbuf("wc" + sfx, [128, 48], F32)
    slotf = p.sbuf("sf" + sfx, [128, 48], F32)
    sloti = p.sbuf("si" + sfx, [128, 48], I32)
    selsum = p.sbuf("ss" + sfx, [128, 64], F32)
    ohb = p.sbuf("oh" + sfx, [128, 64], F32)
    pscr = p.sbuf("pscr" + sfx, [128, 64], F32)
    postok = p.sbuf("pt" + sfx, [128, 64], F32)
    pcol = p.sbuf("pc" + sfx, [128, 4], F32)
    selT = p.sbuf("seT" + sfx, [64, T], F32)
    posT = p.sbuf("poT" + sfx, [64, T], F32)
    iota64 = p.sbuf("io64" + sfx, [128, 64], F32)
    onesN = p.sbuf("onN" + sfx, [64, 1], F32)
    tokf = p.sbuf("tkf" + sfx, [128, 1], F32)
    fill80 = p.sbuf("f80" + sfx, [80, 128], F32)
    stab80 = p.sbuf("st80" + sfx, [80, 128], F32)
    stabT = p.sbuf("stT" + sfx, [128, 80], F32)
    stabI = p.sbuf("stI" + sfx, [128, 80], I32)
    lnS = p.sbuf("lnS" + sfx, [128, 8], F32)
    xtl = p.sbuf("xtl" + sfx, [128, D], F32)
    utok = p.sbuf("utok" + sfx, [128, D], BF16)
    sq2 = p.sbuf("sq2" + sfx, [128, D], BF16)
    wbf = p.sbuf("wbf" + sfx, [128, 512], BF16)
    wbm = [p.sbuf(f"wbm{i}" + sfx, [128, 1024], BF16) for i in range(2)]
    pb = [p.psum(f"pb{i}" + sfx, [128, 512], F32) for i in range(8)]
    pbb = [t.ap().bitcast(BF16) for t in pb]

    if not getattr(p, "_consts_done", False):
        p._consts_done = True
        cstt = p.sbuf("cstt" + sfx, [128, 2], F32)
        p.op("vector", lambda e: e.memset(cstt[:, 0:1], EPS), inc="dve")
        p.op("vector", lambda e: e.memset(cstt[:, 1:2], 1.0), inc="dve")
        p.nc.const_aps.register(F32, EPS, cstt[:, 0:1])
        p.nc.const_aps.register(F32, 1.0, cstt[:, 1:2])

    regs = {}

    def _mkregs(e):
        regs["b10239"] = e.alloc_register("bchk_a")
        e.reg_mov(regs["b10239"], 10239)
        regs["b1023"] = e.alloc_register("bchk_b")
        return e.reg_mov(regs["b1023"], 1023)

    p.op("gpsimd", _mkregs, inc="gps")
    p.op("gpsimd", lambda e: e.iota(out=iota64[:, :], pattern=[[1, 64]], base=0, channel_multiplier=0, allow_small_or_imprecise_dtypes=True), inc="gps")
    p.op("vector", lambda e: e.memset(onesN[:, :], 1.0), inc="dve")
    p.op("vector", lambda e: e.memset(fill80[:, :], 100000.0), inc="dve")
    vd0 = p.val("dve")
    p.wait("gpsimd", "dve", vd0)
    p.dma("gpsimd", io["slot_dram"].ap().rearrange("(g q) o -> g (q o)", q=128), fill80[:, :], inc="qa")

    for it in range(8):
        p.wait("gpsimd", "dve", p.val("dve"))
        p.wait("gpsimd", "act", p.val("act"))
        vx = p.dma("gpsimd", xtl[:, :], io["x_in"][it * 128:(it + 1) * 128, :], inc="qa")
        p.wait("vector", "qa", vx)
        p.op("vector", lambda e: e.tensor_reduce(out=lnS[:, 0:1], in_=xtl[:, :], axis=AXX, op=ALU.add), inc="dve")
        p.wait("scalar", "qa", vx)
        p.op("scalar", lambda e: e.activation(out=sq2[:, :], in_=xtl[:, :], func=AF.Square), inc="act")
        p.wait("vector", "act", p.val("act"))
        p.op("vector", lambda e: e.tensor_reduce(out=lnS[:, 1:2], in_=sq2[:, :], axis=AXX, op=ALU.add), inc="dve")
        p.op("vector", lambda e: e.tensor_scalar(out=lnS[:, 1:2], in0=lnS[:, 1:2], scalar1=1.0 / D, scalar2=None, op0=ALU.mult), inc="dve")
        p.op("vector", lambda e: e.tensor_scalar(out=lnS[:, 2:3], in0=lnS[:, 0:1], scalar1=1.0 / D, scalar2=None, op0=ALU.mult), inc="dve")
        p.op("vector", lambda e: e.tensor_scalar(out=lnS[:, 4:5], in0=lnS[:, 2:3], scalar1=lnS[:, 2:3], scalar2=None, op0=ALU.mult), inc="dve")
        p.op("vector", lambda e: e.tensor_sub(out=lnS[:, 4:5], in0=lnS[:, 1:2], in1=lnS[:, 4:5]), inc="dve")
        vd = p.val("dve")
        p.wait("scalar", "dve", vd)
        p.op("scalar", lambda e: e.activation(out=lnS[:, 5:6], in_=lnS[:, 4:5], func=AF.Ln, bias=EPS), inc="act")
        p.op("scalar", lambda e: e.activation(out=lnS[:, 6:7], in_=lnS[:, 5:6], func=AF.Exp, scale=-0.5), inc="act")
        va = p.val("act")
        p.wait("vector", "act", va)
        p.op("vector", lambda e: e.tensor_scalar(out=utok[:, :], in0=xtl[:, :], scalar1=lnS[:, 2:3], scalar2=lnS[:, 6:7], op0=ALU.subtract, op1=ALU.mult), inc="dve")
        vd = p.val("dve")
        p.wait("gpsimd", "dve", vd)
        p.dma("gpsimd", io["u4_tm"].ap()[it * 128:(it + 1) * 128, :], utok[:, :], inc="qa")
        p.wait("tensor", "dve", vd)
        for j in range(16):
            p.op("tensor", lambda e, j=j: e.transpose(
                out=pbb[j % 4][:, 0:128], in_=utok[:, j * 128:(j + 1) * 128],
                identity=idb[:, :]), inc=("pe" if j % 4 == 3 else None))
            if j % 4 == 3:
                v = p.val("pe")
                p.wait("scalar", "pe", v)
                for jj in range(j - 3, j + 1):
                    p.op("scalar", lambda e, jj=jj, it=it: e.copy(
                        out=uT(jj, it * 128, it * 128 + 128), in_=pbb[jj % 4][:, 0:128]), inc="act")
                p.wait("tensor", "act", p.val("act"))

    for k in range(16):
        vw = p.dma("sync", wbf[:, 0:64], io["wgate"][k * 128:(k + 1) * 128, :], inc="qw")
        p.wait("tensor", "qw", vw)
        for nf in range(2):
            p.op("tensor", lambda e, k=k, nf=nf: e.matmul(
                out=pb[nf][0:64, :], lhsT=wbf[:, 0:64], rhs=uT(k, nf * 512, (nf + 1) * 512),
                start=(k == 0), stop=(k == 15)), inc=("pe" if nf == 1 else None))
        vpe = p.val("pe")
        p.wait("sync", "pe", vpe)
    vpe = p.val("pe")
    p.wait("scalar", "pe", vpe)
    for nf in range(2):
        p.op("scalar", lambda e, nf=nf: e.activation(out=scT[0:64, nf * 512:(nf + 1) * 512], in_=pb[nf][0:64, :], func=AF.Sigmoid), inc="act")
    va_sc = p.val("act")
    p.wait("tensor", "act", va_sc)

    for tt in range(8):
        p.op("tensor", lambda e, tt=tt: e.transpose(out=pb[4][:, 0:64], in_=scT[0:64, tt * 128:(tt + 1) * 128], identity=idf[0:64, 0:64]), inc="pe")
        v = p.val("pe")
        p.wait("vector", "pe", v)
        p.op("vector", lambda e: e.tensor_copy(out=sctok[:, :], in_=pb[4][:, 0:64]), inc="dve")
        p.op("vector", lambda e: e.max(out=mx[:, :], in_=sctok[:, :]), inc="dve")
        p.op("vector", lambda e: e.max_index(out=ixu[:, :], in_max=mx[:, :], in_values=sctok[:, :]), inc="dve")
        p.op("vector", lambda e, tt=tt: e.tensor_copy(out=ixf[:, tt * 8:(tt + 1) * 8], in_=ixu[:, :]), inc="dve")
        p.op("vector", lambda e: e.tensor_reduce(out=wno[:, 6:7], in_=mx[:, 0:6], axis=AXX, op=ALU.add), inc="dve")
        p.op("vector", lambda e: e.reciprocal(out=wno[:, 7:8], in_=wno[:, 6:7]), inc="dve")
        p.op("vector", lambda e: e.tensor_scalar(out=wno[:, 0:6], in0=mx[:, 0:6], scalar1=wno[:, 7:8], scalar2=None, op0=ALU.mult), inc="dve")
        p.op("vector", lambda e, tt=tt: e.tensor_copy(out=wcols[:, tt * 6:(tt + 1) * 6], in_=wno[:, 0:6]), inc="dve")
        p.op("vector", lambda e: e.memset(selsum[:, :], 0.0), inc="dve")
        for j in range(6):
            p.op("vector", lambda e, tt=tt, j=j: e.tensor_scalar(
                out=ohb[:, :], in0=iota64[:, :], scalar1=ixf[:, tt * 8 + j:tt * 8 + j + 1], scalar2=None, op0=ALU.is_equal), inc="dve")
            p.op("vector", lambda e: e.tensor_add(out=selsum[:, :], in0=selsum[:, :], in1=ohb[:, :]), inc="dve")
        vd = p.val("dve")
        p.wait("tensor", "dve", vd)
        p.op("tensor", lambda e: e.transpose(out=pb[5][0:64, 0:128], in_=selsum[:, :], identity=idf[:, :]), inc="pe")
        v = p.val("pe")
        p.wait("vector", "pe", v)
        p.op("vector", lambda e, tt=tt: e.tensor_copy(out=selT[0:64, tt * 128:(tt + 1) * 128], in_=pb[5][0:64, 0:128]), inc="dve")
        p.wait("tensor", "dve", p.val("dve"))
    p.op("vector", lambda e: e.tensor_tensor_scan(
        out=posT[0:64, :], data0=onesN[:, 0:1].to_broadcast([64, T]), data1=selT[0:64, :],
        initial=0.0, op0=ALU.mult, op1=ALU.add), inc="dve")
    p.op("vector", lambda e: e.tensor_sub(out=posT[0:64, :], in0=posT[0:64, :], in1=selT[0:64, :]), inc="dve")
    vd = p.val("dve")
    p.wait("tensor", "dve", vd)
    for tt in range(8):
        p.op("tensor", lambda e, tt=tt: e.transpose(out=pb[4][:, 0:64], in_=posT[0:64, tt * 128:(tt + 1) * 128], identity=idf[0:64, 0:64]), inc="pe")
        v = p.val("pe")
        p.wait("vector", "pe", v)
        p.op("vector", lambda e: e.tensor_copy(out=postok[:, :], in_=pb[4][:, 0:64]), inc="dve")
        for j in range(6):
            p.op("vector", lambda e, tt=tt, j=j: e.tensor_scalar(
                out=ohb[:, :], in0=iota64[:, :], scalar1=ixf[:, tt * 8 + j:tt * 8 + j + 1], scalar2=None, op0=ALU.is_equal), inc="dve")
            p.op("vector", lambda e: e.tensor_mul(out=pscr[:, :], in0=ohb[:, :], in1=postok[:, :]), inc="dve")
            p.op("vector", lambda e: e.tensor_reduce(out=pcol[:, 0:1], in_=pscr[:, :], axis=AXX, op=ALU.add), inc="dve")
            p.op("vector", lambda e: e.tensor_scalar(out=pcol[:, 1:2], in0=pcol[:, 0:1], scalar1=159.5, scalar2=1.0e6, op0=ALU.is_gt, op1=ALU.mult), inc="dve")
            p.op("vector", lambda e, tt=tt, j=j: e.scalar_tensor_tensor(
                out=pcol[:, 2:3], in0=ixf[:, tt * 8 + j:tt * 8 + j + 1], scalar=160.0, in1=pcol[:, 0:1], op0=ALU.mult, op1=ALU.add), inc="dve")
            p.op("vector", lambda e, tt=tt, j=j: e.tensor_add(out=slotf[:, tt * 6 + j:tt * 6 + j + 1], in0=pcol[:, 2:3], in1=pcol[:, 1:2]), inc="dve")
        p.wait("tensor", "dve", p.val("dve"))
    p.op("vector", lambda e: e.tensor_copy(out=sloti[:, :], in_=slotf[:, :]), inc="dve")
    vd_slot = p.val("dve")

    p.wait("gpsimd", "dve", vd_slot)
    p.wait("gpsimd", "qa", p.val("qa"))
    for tt in range(8):
        p.wait("gpsimd", "qa", p.val("qa"))
        p.op("gpsimd", lambda e, tt=tt: e.iota(out=tokf[:, :], pattern=[[1, 1]], base=tt * 128, channel_multiplier=1, allow_small_or_imprecise_dtypes=True), inc="gps")
        p.wait("gpsimd", "gps", p.val("gps"))
        for j in range(6):
            p.op("gpsimd", lambda e, tt=tt, j=j: e.indirect_dma_start(
                out=io["slot_dram"].ap(), out_offset=IndirectOffsetOnAxis(ap=sloti[:, tt * 6 + j:tt * 6 + j + 1], axis=0),
                in_=tokf[:, :], in_offset=None, bounds_check=regs["b10239"], oob_is_err=False), inc="qa", n=16)
    p.wait("gpsimd", "qa", p.val("qa"))
    v = p.dma("gpsimd", stab80[:, :], io["slot_dram"].ap().rearrange("(g q) o -> g (q o)", q=128), inc="qa")
    p.wait("tensor", "qa", v)
    p.op("tensor", lambda e: e.transpose(out=pb[4][:, 0:80], in_=stab80[:, :], identity=idf[0:80, 0:80]), inc="pe")
    v = p.val("pe")
    p.wait("vector", "pe", v)
    p.op("vector", lambda e: e.tensor_copy(out=stabT[:, :], in_=pb[4][:, 0:80]), inc="dve")
    p.op("vector", lambda e: e.tensor_copy(out=stabI[:, :], in_=stabT[:, :]), inc="dve")
    vd_stab = p.val("dve")
    p.wait("gpsimd", "dve", vd_stab)
    xgb = p.sbuf("xgb" + sfx, [128, 2048], BF16)
    for g in range(80):
        vg = p.op("gpsimd", lambda e, g=g: e.indirect_dma_start(
            out=xgb[:, :], out_offset=None,
            in_=io["u4_tm"].ap(), in_offset=IndirectOffsetOnAxis(ap=stabI[:, g:g + 1], axis=0),
            bounds_check=regs["b1023"], oob_is_err=False), inc="qa", n=16)
        p.wait("gpsimd", "qa", vg)
        p.dma("gpsimd", io["Xall"].ap()[g * 128:(g + 1) * 128, :], xgb[:, :], inc="qa")
        p.wait("gpsimd", "qa", p.val("qa"))

    p.wait("tensor", "dve", vd_stab)
    for grp in ([0, 1, 2], [3, 4, 5], [6, 7]):
        for k in range(16):
            vw = p.dma("sync", wbf[:, 0:128 * len(grp)], io["wsup"][k * 128:(k + 1) * 128, 128 * grp[0]:128 * grp[0] + 128 * len(grp)], inc="qw")
            p.wait("tensor", "qw", vw)
            for mi, m in enumerate(grp):
                for nf in range(2):
                    p.op("tensor", lambda e, k=k, mi=mi, nf=nf: e.matmul(
                        out=pb[2 * mi + nf][:, :], lhsT=wbf[:, mi * 128:(mi + 1) * 128], rhs=uT(k, nf * 512, (nf + 1) * 512),
                        start=(k == 0), stop=(k == 15)), inc=("pe" if (mi == len(grp) - 1 and nf == 1) else None))
            vpe = p.val("pe")
            p.wait("sync", "pe", vpe)
        vpe = p.val("pe")
        p.wait("scalar", "pe", vpe)
        p.wait("vector", "pe", vpe)
        for mi, m in enumerate(grp):
            if m < 4:
                for nf in range(2):
                    p.op("scalar", lambda e, m=m, mi=mi, nf=nf: e.copy(
                        out=ybuf[:, m * T + nf * 512: m * T + (nf + 1) * 512], in_=pb[2 * mi + nf][:, :]), inc="act")
            else:
                p.wait("scalar", "dve", p.val("dve"))
                for nf in range(2):
                    p.op("scalar", lambda e, mi=mi, nf=nf: e.activation(
                        out=gsil[:, nf * 512:(nf + 1) * 512], in_=pb[2 * mi + nf][:, :], func=AF.Silu), inc="act")
                va = p.val("act")
                p.wait("vector", "act", va)
                p.op("vector", lambda e, m=m: e.tensor_mul(
                    out=actS[:, (m - 4) * T:(m - 3) * T], in0=gsil[:, :], in1=ybuf[:, (m - 4) * T:(m - 3) * T]), inc="dve")
        p.wait("tensor", "act", p.val("act"))
        p.wait("tensor", "dve", p.val("dve"))
        p.wait("scalar", "dve", p.val("dve"))
    for kd in range(4):
        p.dma("sync", wsdn_sb[:, kd * 2048:(kd + 1) * 2048], io["wsdn"][kd * 128:(kd + 1) * 128, :], inc="qw")
    v_wsdn = p.val("qw")

    for eg in range(16):
        p.wait("sync", "pe", p.val("pe"))
        p.wait("sync", "qa", p.val("qa"))
        for kc in range(16):
            p.dma("sync", XT[:, kc * 640:(kc + 1) * 640], io["Xall"].ap()[eg * 640:(eg + 1) * 640, kc * 128:(kc + 1) * 128], inc="qw", transpose=True)
        v_xt = p.val("qw")
        p.wait("tensor", "qw", v_xt)
        for le in range(4):
            e_ = eg * 4 + le
            soff = le * 160
            upslc2 = lambda kk, e_=e_: io["gup"][e_ * 2048 + 256 * kk: e_ * 2048 + 256 * (kk + 1),
                                                 :].rearrange("(i p) c -> p i c", p=128)

            def ubody(kc, buf, soff=soff):
                p.op("tensor", lambda e, kc=kc, soff=soff, buf=buf: e.matmul(
                    out=pb[0][:, :], lhsT=XT[:, kc * 640 + soff: kc * 640 + soff + 128], rhs=buf,
                    start=(kc == 0), stop=(kc == 15)))
                p.op("tensor", lambda e, kc=kc, soff=soff, buf=buf: e.matmul(
                    out=pb[1][0:32, :], lhsT=XT[:, kc * 640 + soff + 128: kc * 640 + soff + 160], rhs=buf,
                    start=(kc == 0), stop=(kc == 15)), inc="pe")
            stream_mm(p, 16, wbm, 512, upslc2, ubody)
            vpe = p.val("pe")
            p.wait("sync", "pe", vpe)
            vpe = p.val("pe")
            p.wait("scalar", "pe", vpe)
            p.wait("vector", "pe", vpe)
            p.op("scalar", lambda e: e.activation(out=ab0[:, 0:256], in_=pb[0][:, 0:256], func=AF.Silu), inc="act")
            p.op("scalar", lambda e: e.activation(out=ab1[:, 0:256], in_=pb[1][0:32, 0:256], func=AF.Silu), inc="act")
            va = p.val("act")
            p.wait("vector", "act", va)
            p.op("vector", lambda e: e.tensor_tensor(out=ab0[:, 256:512], in0=ab0[:, 0:256], in1=pb[0][:, 256:512], op=ALU.mult), inc="dve")
            p.op("vector", lambda e: e.tensor_tensor(out=ab1[:, 256:512], in0=ab1[:, 0:256], in1=pb[1][0:32, 256:512], op=ALU.mult), inc="dve")
            vd = p.val("dve")
            p.wait("tensor", "dve", vd)
            p.op("tensor", lambda e: e.transpose(out=pbb[2][:, 0:128], in_=ab0[:, 256:384], identity=idb[:, :]), inc="pe")
            p.op("tensor", lambda e: e.transpose(out=pbb[3][:, 0:128], in_=ab0[:, 384:512], identity=idb[:, :]), inc="pe")
            p.op("tensor", lambda e: e.transpose(out=pbb[4][:, 0:32], in_=ab1[0:32, 256:384], identity=idb[0:32, 0:32]), inc="pe")
            p.op("tensor", lambda e: e.transpose(out=pbb[5][:, 0:32], in_=ab1[0:32, 384:512], identity=idb[0:32, 0:32]), inc="pe")
            vpe = p.val("pe")
            p.wait("scalar", "pe", vpe)
            p.op("scalar", lambda e: e.copy(out=aTa[:, 0:128], in_=pbb[2][:, 0:128]), inc="act")
            p.op("scalar", lambda e: e.copy(out=aTb[:, 0:128], in_=pbb[3][:, 0:128]), inc="act")
            p.op("scalar", lambda e: e.copy(out=aTa[:, 128:160], in_=pbb[4][:, 0:32]), inc="act")
            p.op("scalar", lambda e: e.copy(out=aTb[:, 128:160], in_=pbb[5][:, 0:32]), inc="act")
            va = p.val("act")
            p.wait("tensor", "act", va)
            pb1map = [6, 7, 0, 1]
            dnslc2 = lambda kk, e_=e_: io["gdn"][e_ * 256 + (kk // 2) * 128: e_ * 256 + (kk // 2 + 1) * 128,
                                                 (kk % 2) * 1024:(kk % 2 + 1) * 1024].rearrange("p (i c) -> p i c", c=512)

            def dbody(i, buf):
                kd, dchunk = i // 4, i % 4
                lt = aTa if kd == 0 else aTb
                p.op("tensor", lambda e, kd=kd, dchunk=dchunk, lt=lt, buf=buf: e.matmul(
                    out=pb[2 + dchunk][:, :], lhsT=lt[:, 0:128], rhs=buf,
                    start=(kd == 0), stop=(kd == 1)))
                p.op("tensor", lambda e, kd=kd, dchunk=dchunk, lt=lt, buf=buf: e.matmul(
                    out=pb[pb1map[dchunk]][0:32, :],
                    lhsT=lt[:, 128:160], rhs=buf,
                    start=(kd == 0), stop=(kd == 1)), inc="pe")
            stream_mm(p, 8, wbm, 512, dnslc2, dbody)
            vpe = p.val("pe")
            p.wait("scalar", "pe", vpe)
            p.wait("scalar", "qa", p.val("qa"))
            for dchunk in range(4):
                p.op("scalar", lambda e, dchunk=dchunk: e.copy(out=do0[:, dchunk * 512:(dchunk + 1) * 512], in_=pb[2 + dchunk][:, :]), inc="act")
            for dchunk in range(4):
                p.op("scalar", lambda e, dchunk=dchunk: e.copy(out=do1[0:32, dchunk * 512:(dchunk + 1) * 512], in_=pb[pb1map[dchunk]][0:32, :]), inc="act")
            va = p.val("act")
            p.wait("gpsimd", "act", va)
            p.wait("tensor", "act", va)
            p.dma("gpsimd", io["down_all"].ap()[e_ * 160:e_ * 160 + 128, :], do0[:, :], inc="qa")
            p.dma("gpsimd", io["down_all"].ap()[e_ * 160 + 128:e_ * 160 + 160, :], do1[0:32, :], inc="qa")
            p.wait("tensor", "qa", p.val("qa"))
    p.wait("gpsimd", "qa", p.val("qa"))

    p.wait("tensor", "qw", v_wsdn)
    for tt in range(8):
        for kd in range(4):
            for dchunk in range(4):
                p.op("tensor", lambda e, tt=tt, kd=kd, dchunk=dchunk: e.matmul(
                    out=pb[dchunk][:, :], lhsT=actS[:, kd * T + tt * 128: kd * T + (tt + 1) * 128],
                    rhs=wsdn_sb[:, kd * 2048 + dchunk * 512: kd * 2048 + (dchunk + 1) * 512],
                    start=(kd == 0), stop=(kd == 3)), inc=("pe" if (kd == 3 and dchunk == 3) else None))
        vpe = p.val("pe")
        p.wait("vector", "pe", vpe)
        for dchunk in range(4):
            p.op("vector", lambda e, dchunk=dchunk: e.tensor_copy(out=accf[:, dchunk * 512:(dchunk + 1) * 512], in_=pb[dchunk][:, :]), inc="dve")
        vd = p.val("dve")
        p.wait("tensor", "dve", vd)
        p.wait("gpsimd", "dve", vd)
        for j in range(6):
            vg = p.op("gpsimd", lambda e, tt=tt, j=j: e.indirect_dma_start(
                out=yg[:, :], out_offset=None, in_=io["down_all"].ap(),
                in_offset=IndirectOffsetOnAxis(ap=sloti[:, tt * 6 + j:tt * 6 + j + 1], axis=0),
                bounds_check=regs["b10239"], oob_is_err=False), inc="qa", n=16)
            p.wait("vector", "qa", vg)
            p.op("vector", lambda e, tt=tt, j=j: e.scalar_tensor_tensor(
                out=accf[:, :], in0=yg[:, :], scalar=wcols[:, tt * 6 + j:tt * 6 + j + 1], in1=accf[:, :],
                op0=ALU.mult, op1=ALU.add), inc="dve")
            vd = p.val("dve")
            p.wait("gpsimd", "dve", vd)
        vx = p.dma("gpsimd", xtl[:, :], io["x_in"][tt * 128:(tt + 1) * 128, :], inc="qa")
        p.op("vector", lambda e: e.tensor_reduce(out=lnS[:, 0:1], in_=accf[:, :], axis=AXX, op=ALU.add), inc="dve")
        p.wait("scalar", "dve", p.val("dve"))
        p.op("scalar", lambda e: e.activation(out=sq2[:, :], in_=accf[:, :], func=AF.Square), inc="act")
        p.wait("vector", "act", p.val("act"))
        p.op("vector", lambda e: e.tensor_reduce(out=lnS[:, 1:2], in_=sq2[:, :], axis=AXX, op=ALU.add), inc="dve")
        p.op("vector", lambda e: e.tensor_scalar(out=lnS[:, 1:2], in0=lnS[:, 1:2], scalar1=1.0 / D, scalar2=None, op0=ALU.mult), inc="dve")
        p.op("vector", lambda e: e.tensor_scalar(out=lnS[:, 2:3], in0=lnS[:, 0:1], scalar1=1.0 / D, scalar2=None, op0=ALU.mult), inc="dve")
        p.op("vector", lambda e: e.tensor_scalar(out=lnS[:, 4:5], in0=lnS[:, 2:3], scalar1=lnS[:, 2:3], scalar2=None, op0=ALU.mult), inc="dve")
        p.op("vector", lambda e: e.tensor_sub(out=lnS[:, 4:5], in0=lnS[:, 1:2], in1=lnS[:, 4:5]), inc="dve")
        vd = p.val("dve")
        p.wait("scalar", "dve", vd)
        p.op("scalar", lambda e: e.activation(out=lnS[:, 5:6], in_=lnS[:, 4:5], func=AF.Ln, bias=EPS), inc="act")
        p.op("scalar", lambda e: e.activation(out=lnS[:, 6:7], in_=lnS[:, 5:6], func=AF.Exp, scale=-0.5), inc="act")
        va = p.val("act")
        p.wait("vector", "act", va)
        p.wait("vector", "qa", vx)
        p.op("vector", lambda e: e.tensor_scalar(
            out=accf[:, :], in0=accf[:, :], scalar1=lnS[:, 2:3], scalar2=lnS[:, 6:7], op0=ALU.subtract, op1=ALU.mult), inc="dve")
        p.op("vector", lambda e: e.tensor_add(out=xtl[:, :], in0=xtl[:, :], in1=accf[:, :]), inc="dve")
        vd = p.val("dve")
        p.wait("gpsimd", "dve", vd)
        p.dma("gpsimd", io["x_out"][tt * 128:(tt + 1) * 128, :], xtl[:, :], inc="qa")
        p.wait("gpsimd", "qa", p.val("qa"))
    p.wait("gpsimd", "qw", p.val("qw"))


def run_moe_unfused(x_full, Wgate, Wsup, Wsdn, Gup, Gdn):
    nc = bass.Bass("TRN2", target_bir_lowering=False)
    p = Prog(nc)
    io = {
        "x_in": nc.dram_tensor("x_in", [T, D], F32, kind="ExternalInput").ap(),
        "wgate": nc.dram_tensor("wgate", [2048, 64], BF16, kind="ExternalInput").ap(),
        "wsup": nc.dram_tensor("wsup", [2048, 1024], BF16, kind="ExternalInput").ap(),
        "wsdn": nc.dram_tensor("wsdn", [512, 2048], BF16, kind="ExternalInput").ap(),
        "gup": nc.dram_tensor("gup", [131072, 512], BF16, kind="ExternalInput").ap(),
        "gdn": nc.dram_tensor("gdn", [16384, 2048], BF16, kind="ExternalInput").ap(),
        "x_out": nc.dram_tensor("x_out", [T, D], F32, kind="ExternalOutput").ap(),
        "u4_tm": nc.dram_tensor("u4_tm", [T, D], BF16, kind="Internal"),
        "slot_dram": nc.dram_tensor("slot_dram", [10240, 1], F32, kind="Internal"),
        "Xall": nc.dram_tensor("Xall", [10240, 2048], BF16, kind="Internal"),
        "down_all": nc.dram_tensor("down_all", [10240, 2048], BF16, kind="Internal"),
    }
    idb = p.sbuf("idb", [128, 128], BF16)
    idf = p.sbuf("idf", [128, 128], F32)
    emit_idents(p, idb, idf)
    emit_moe(p, io, idb, idf)
    p.realize()
    in_maps = []
    for c in range(NCORES):
        b, q = c // 4, c % 4
        in_maps.append({
            "x_in": np.ascontiguousarray(x_full[b, q * 1024:(q + 1) * 1024]),
            "wgate": Wgate, "wsup": Wsup, "wsdn": Wsdn, "gup": Gup, "gdn": Gdn,
        })
    res = run_bass_kernel_spmd(nc, in_maps, core_ids=list(range(NCORES)))
    out = np.empty((2, 4096, D), np.float32)
    for c in range(NCORES):
        b, q = c // 4, c % 4
        out[b, q * 1024:(q + 1) * 1024] = res.results[c]["x_out"]
    return out



def run_fused(inp):
    nc = bass.Bass("TRN2", target_bir_lowering=False)
    p = Prog(nc)

    def ein(name, shape, dtype=F32):
        return nc.dram_tensor(name, shape, dtype, kind="ExternalInput").ap()

    def itn(name, shape, dtype):
        return nc.dram_tensor(name, shape, dtype, kind="Internal").ap()

    x_in = ein("x_in", [T, D])
    xh0 = ein("xh0", [3, D])
    mrow8 = ein("mrow8", [1, 8]); mcol8 = ein("mcol8", [128, 8])
    mh8 = ein("mh8", [128, 8]); mh8_3 = ein("mh8_3", [3, 8])
    shards = {
        "W0in_s": ein("W0in_s", [256, 8512]), "W0out_s": ein("W0out_s", [512, 2048]),
        "Mup_s": ein("Mup_s", [16384, 512]), "Mdn_s": ein("Mdn_s", [2048, 2048]),
        "Wsup_s": ein("Wsup_s", [256, 1024]), "Wsdn_s": ein("Wsdn_s", [64, 2048]),
        "Wgate_s": ein("Wgate_s", [256, 64]),
        "W1in_s": ein("W1in_s", [1152, 2048]), "W1out_s": ein("W1out_s", [2048, 512]),
        "gP_s": ein("gP_s", [2048, 1024], I32), "gS_s": ein("gS_s", [2048, 64]),
        "uP_s": ein("uP_s", [2048, 1024], I32), "uS_s": ein("uS_s", [2048, 64]),
        "dP_s": ein("dP_s", [2048, 1024], I32), "dS_s": ein("dS_s", [2048, 64]),
    }
    conv0 = {k: ein("c0_" + k, s) for k, s in
             (("conv_w", [4352, 4]), ("conv_b", [4352]), ("dt_bias", [64]), ("A_log", [64]), ("Dv", [64]), ("norm_w", [4096]))}
    conv1 = {k: ein("c1_" + k, s) for k, s in
             (("conv_w", [4224, 4]), ("conv_b", [4224]), ("dt_bias", [64]), ("A_log", [64]), ("Dv", [64]), ("norm_w", [4096]))}
    x_out = nc.dram_tensor("x_out", [T, D], F32, kind="ExternalOutput").ap()

    x1 = itn("x1", [T, D], F32)
    x2 = itn("x2", [T, D], F32)
    x3 = itn("x3", [T, D], F32)
    contrib = {
        "W0in": itn("cW0in", [256, 8512], BF16), "W0out": itn("cW0out", [512, 2048], BF16),
        "Mup": itn("cMup", [16384, 512], BF16), "Mdn": itn("cMdn", [2048, 2048], BF16),
        "Wsup": itn("cWsup", [256, 1024], BF16), "Wsdn": itn("cWsdn", [64, 2048], BF16),
        "Wgate": itn("cWgate", [256, 64], BF16),
        "W1inT": itn("cW1inT", [2048, 1152], BF16), "W1outT": itn("cW1outT", [512, 2048], BF16),
        "GgT": itn("cGgT", [2048, 2048], BF16), "GuT": itn("cGuT", [2048, 2048], BF16), "GdT": itn("cGdT", [2048, 2048], BF16),
    }
    gath = {
        "W0in": itn("GW0in", [2048, 8512], BF16), "W0out": itn("GW0out", [4096, 2048], BF16),
        "Mup": itn("GMup", [131072, 512], BF16), "Mdn": itn("GMdn", [16384, 2048], BF16),
        "Wsup": itn("GWsup", [2048, 1024], BF16), "Wsdn": itn("GWsdn", [512, 2048], BF16),
        "Wgate": itn("GWgate", [2048, 64], BF16),
        "W1inT": itn("GW1inT", [16384, 1152], BF16), "W1outT": itn("GW1outT", [4096, 2048], BF16),
        "GgT": itn("GGgT", [16384, 2048], BF16), "GuT": itn("GGuT", [16384, 2048], BF16), "GdT": itn("GGdT", [16384, 2048], BF16),
    }
    exch = {}
    for L in (0, 1):
        N = 128 if L == 0 else 64
        exch[L] = {
            "cS": itn(f"cS{L}", [N, 4096], F32), "GS": itn(f"GS{L}", [8 * N, 4096], F32),
            "cg": itn(f"cg{L}", [1, 64], F32), "Gg": itn(f"Gg{L}", [8, 64], F32),
            "ch": itn(f"ch{L}", [128, 102], BF16), "Gh": itn(f"Gh{L}", [1024, 102], BF16),
        }
    cxh = itn("cxh", [3, D], F32)
    Gxh = itn("Gxh", [24, D], F32)

    idb = p.sbuf("idb", [128, 128], BF16)
    idf = p.sbuf("idf", [128, 128], F32)
    emit_idents(p, idb, idf)
    cstG = p.sbuf("cstG", [128, 2], F32)
    p.op("vector", lambda e: e.memset(cstG[:, 0:1], EPS), inc="dve")
    p.op("vector", lambda e: e.memset(cstG[:, 1:2], 1.0), inc="dve")
    nc.const_aps.register(F32, EPS, cstG[:, 0:1])
    nc.const_aps.register(F32, 1.0, cstG[:, 1:2])
    p._consts_done = True

    def ag(cin, cout):
        v = p.op("gpsimd", lambda e, cin=cin, cout=cout: e.collective_compute(
            "AllGather", ALU.bypass, replica_groups=[list(range(8))],
            ins=[cin.opt()], outs=[cout.opt()]), inc="cc")
        return v

    m = p.push()
    io_prep = {
        "casts": [
            (shards["W0in_s"], contrib["W0in"], 256, 8512), (shards["W0out_s"], contrib["W0out"], 512, 2048),
            (shards["Mup_s"], contrib["Mup"], 16384, 512), (shards["Mdn_s"], contrib["Mdn"], 2048, 2048),
            (shards["Wsup_s"], contrib["Wsup"], 256, 1024), (shards["Wsdn_s"], contrib["Wsdn"], 64, 2048),
            (shards["Wgate_s"], contrib["Wgate"], 256, 64),
        ],
        "ctrans": [(shards["W1in_s"], contrib["W1inT"], 1152, 2048), (shards["W1out_s"], contrib["W1outT"], 2048, 512)],
        "decode": [(shards["gP_s"], shards["gS_s"], contrib["GgT"]),
                   (shards["uP_s"], shards["uS_s"], contrib["GuT"]),
                   (shards["dP_s"], shards["dS_s"], contrib["GdT"])],
    }
    emit_prep(p, io_prep, idb, idf)
    p.barrier()
    p.pop(m)
    for k in contrib:
        ag(contrib[k], gath[k])
    v_cc_prep = p.val("cc")
    p.wait("gpsimd", "cc", v_cc_prep)
    p.wait("sync", "cc", v_cc_prep)
    p.wait("tensor", "cc", v_cc_prep)
    p.wait("vector", "cc", v_cc_prep)
    p.wait("scalar", "cc", v_cc_prep)

    def mamba_io(L, tag, xin, conv):
        cfg = MambaCfg(L)
        io = dict(conv)
        io["tag"] = tag
        io["x_in"] = xin
        io["acs_dram"] = nc.dram_tensor("acsd_" + tag, [64, CH], F32, kind="Internal")
        io["alast_dram"] = nc.dram_tensor("ald_" + tag, [64], F32, kind="Internal")
        if L == 0:
            io["w_in_slc"] = l0_win_slc(gath["W0in"])
            io["w_in_slc2"] = l0_win_slc2(gath["W0in"])
            io["w_out_slc"] = l0_wout_slc(gath["W0out"])
            io["w_out_slc2"] = l0_wout_slc2(gath["W0out"])
        else:
            io["w_in_slc"] = l1_win_slc(gath["W1inT"])
            io["w_in_slc2"] = l1_win_slc2(gath["W1inT"])
            io["w_out_slc"] = l0_wout_slc(gath["W1outT"])
            io["w_out_slc2"] = l0_wout_slc2(gath["W1outT"])
        return cfg, io

    for L, xin, xout in ((0, x_in, x1), (1, x2, x3)):
        conv = conv0 if L == 0 else conv1
        N = 128 if L == 0 else 64
        cfg, io = mamba_io(L, f"s{L}", xin, {
            "conv_w": conv["conv_w"], "conv_b": conv["conv_b"], "dt_bias": conv["dt_bias"],
            "A_log": conv["A_log"], "Dv": conv["Dv"], "norm_w": conv["norm_w"]})
        if L == 0:
            io["x_halo"] = xh0
        else:
            io["xh_all"] = Gxh
            io["mh8_3"] = mh8_3
        io["out_S"] = exch[L]["cS"]
        io["out_g"] = exch[L]["cg"].transpose([1, 0])
        io["out_halo"] = exch[L]["ch"]
        m = p.push()
        emit_mamba(p, cfg, io, idb, idf, "state")
        p.barrier()
        p.pop(m)
        ag(exch[L]["cS"], exch[L]["GS"])
        ag(exch[L]["cg"], exch[L]["Gg"])
        ag(exch[L]["ch"], exch[L]["Gh"])
        vcc = p.val("cc")
        for eng in ENGS:
            p.wait(eng, "cc", vcc)

        cfg, io = mamba_io(L, f"f{L}", xin, {
            "conv_w": conv["conv_w"], "conv_b": conv["conv_b"], "dt_bias": conv["dt_bias"],
            "A_log": conv["A_log"], "Dv": conv["Dv"], "norm_w": conv["norm_w"]})
        io["fused"] = True
        io["S_all"] = exch[L]["GS"]
        io["g_all"] = exch[L]["Gg"]
        io["halo_all"] = exch[L]["Gh"]
        io["mrow8"] = mrow8
        io["mcol8"] = mcol8
        io["mh8"] = mh8
        io["x_out"] = xout
        m = p.push()
        emit_mamba(p, cfg, io, idb, idf, "full")
        p.barrier()
        p.pop(m)

        if L == 0:
            m = p.push()
            emit_ffn0(p, {"x_in": x1, "x_out": x2, "g_gT": gath["GgT"], "g_uT": gath["GuT"], "g_dT": gath["GdT"]}, idb, idf)
            p.barrier()
            p.pop(m)
            p.dma("gpsimd", cxh, x2[T - 3:T, :], inc="qa")
            p.wait("gpsimd", "qa", p.val("qa"))
            ag(cxh, Gxh)
            vcc = p.val("cc")
            for eng in ENGS:
                p.wait(eng, "cc", vcc)

    m = p.push()
    emit_moe(p, {
        "x_in": x3, "x_out": x_out,
        "wgate": gath["Wgate"], "wsup": gath["Wsup"], "wsdn": gath["Wsdn"],
        "gup": gath["Mup"], "gdn": gath["Mdn"],
        "u4_tm": nc.dram_tensor("u4_tm", [T, D], BF16, kind="Internal"),
        "slot_dram": nc.dram_tensor("slot_dram", [10240, 1], F32, kind="Internal"),
        "Xall": nc.dram_tensor("Xall", [10240, 2048], BF16, kind="Internal"),
        "down_all": nc.dram_tensor("down_all", [10240, 2048], BF16, kind="Internal"),
    }, idb, idf)
    p.pop(m)
    p.realize()

    w1in_pad = np.zeros((9216, 2048), np.float32)
    w1in_pad[:8384] = inp["L1_attn_W_in"]
    mup_flat = np.ascontiguousarray(inp["L1_ffn_w_up"].reshape(131072, 512))
    mdn_flat = np.ascontiguousarray(inp["L1_ffn_w_down"].reshape(16384, 2048))
    x_full = inp["L0_attn_x"].astype(np.float32)
    in_maps = []
    for c in range(NCORES):
        b, q = c // 4, c % 4
        mr = np.array([[1.0 if (j // 4 == b and j % 4 < q) else 0.0 for j in range(8)]], np.float32)
        mh = np.array([[1.0 if (q > 0 and j == c - 1) else 0.0 for j in range(8)]], np.float32)
        m_ = {
            "x_in": np.ascontiguousarray(x_full[b, q * 1024:(q + 1) * 1024]),
            "xh0": np.zeros((3, D), np.float32) if q == 0 else np.ascontiguousarray(x_full[b, q * 1024 - 3:q * 1024]),
            "mrow8": mr, "mcol8": np.tile(mr, (128, 1)),
            "mh8": np.tile(mh, (128, 1)), "mh8_3": np.tile(mh, (3, 1)),
            "W0in_s": np.ascontiguousarray(inp["L0_attn_in_proj_w"][256 * c:256 * (c + 1)]),
            "W0out_s": np.ascontiguousarray(inp["L0_attn_out_proj_w"][512 * c:512 * (c + 1)]),
            "Mup_s": np.ascontiguousarray(mup_flat[16384 * c:16384 * (c + 1)]),
            "Mdn_s": np.ascontiguousarray(mdn_flat[2048 * c:2048 * (c + 1)]),
            "Wsup_s": np.ascontiguousarray(inp["L1_ffn_w_shared_up"][256 * c:256 * (c + 1)]),
            "Wsdn_s": np.ascontiguousarray(inp["L1_ffn_w_shared_down"][64 * c:64 * (c + 1)]),
            "Wgate_s": np.ascontiguousarray(inp["L1_ffn_w_gate"][256 * c:256 * (c + 1)]),
            "W1in_s": np.ascontiguousarray(w1in_pad[1152 * c:1152 * (c + 1)]),
            "W1out_s": np.ascontiguousarray(inp["L1_attn_W_out"][:, 512 * c:512 * (c + 1)]),
            "gP_s": np.ascontiguousarray(inp["L0_ffn_gate_W_packed"][2048 * c:2048 * (c + 1)]),
            "gS_s": np.ascontiguousarray(inp["L0_ffn_gate_scales"][2048 * c:2048 * (c + 1)]),
            "uP_s": np.ascontiguousarray(inp["L0_ffn_up_W_packed"][2048 * c:2048 * (c + 1)]),
            "uS_s": np.ascontiguousarray(inp["L0_ffn_up_scales"][2048 * c:2048 * (c + 1)]),
            "dP_s": np.ascontiguousarray(inp["L0_ffn_down_W_packed"][:, 1024 * c:1024 * (c + 1)]),
            "dS_s": np.ascontiguousarray(inp["L0_ffn_down_scales"][:, 64 * c:64 * (c + 1)]),
        }
        for k, s in (("c0_", "L0_attn_"), ("c1_", "L1_attn_")):
            m_[k + "conv_w"] = inp[s + "conv_w"]
            m_[k + "conv_b"] = inp[s + "conv_b"]
            m_[k + "dt_bias"] = inp[s + "dt_bias"]
            m_[k + "A_log"] = inp[s + "A_log"]
            m_[k + "Dv"] = inp[s + "D"]
            m_[k + "norm_w"] = inp[s + "norm_w"]
        in_maps.append(m_)
    res = run_bass_kernel_spmd(nc, in_maps, core_ids=list(range(NCORES)))
    out = np.empty((2, 4096, D), np.float32)
    for c in range(NCORES):
        b, q = c // 4, c % 4
        out[b, q * 1024:(q + 1) * 1024] = res.results[c]["x_out"]
    return out


# version 14
# speedup vs baseline: 1.1534x; 1.1534x over previous
import numpy as np
import concourse.bass as bass
import concourse.mybir as mybir
from concourse.bass_utils import run_bass_kernel_spmd
from concourse.bass_core_rust import IndirectOffsetOnAxis

F32 = mybir.dt.float32
BF16 = mybir.dt.bfloat16
I32 = mybir.dt.int32
U32 = mybir.dt.uint32
AF = mybir.ActivationFunctionType
ALU = mybir.AluOpType
AXX = mybir.AxisListType.X

D = 2048
NCORES = 8
T = 1024
TB = 512
NTT = TB // 128
NCH = TB // 128
CH = 128
TT3 = TB + 3
EPS = 1e-5

ENGS = ("sync", "vector", "scalar", "tensor", "gpsimd")


class Prog:

    def __init__(self, nc):
        self.nc = nc
        self.steps = []
        self.sems = {}
        self.cnt = {}
        self._guards = []

    def sem(self, name):
        if name not in self.sems:
            g = self.nc.semaphore(name)
            self.sems[name] = g.__enter__()
            self._guards.append(g)
            self.cnt[name] = 0
        return self.sems[name]

    def sbuf(self, name, shape, dtype):
        g = self.nc.sbuf_tensor(name, shape, dtype)
        t = g.__enter__()
        self._guards.append(g)
        return t

    def psum(self, name, shape, dtype=F32):
        g = self.nc.psum_tensor(name, shape, dtype)
        t = g.__enter__()
        self._guards.append(g)
        return t

    def dram(self, name, shape, dtype, **kw):
        return self.nc.dram_tensor(name, shape, dtype, kind="Internal", **kw)

    AUTO = {"vector": "dve", "scalar": "act", "gpsimd": "gps"}

    def _selfsync(self, eng):
        a = self.AUTO.get(eng)
        if a is None:
            return
        cur = self.val(a)
        sw = getattr(self, "_selfwait", None)
        if sw is None:
            self._selfwait = sw = {}
        if cur > sw.get(eng, 0):
            self.wait(eng, a, cur)
            sw[eng] = cur

    def op(self, eng, fn, inc=None, n=1):
        self._selfsync(eng)
        if inc is None and eng in self.AUTO:
            inc = self.AUTO[eng]
            n = 1
        if inc:
            s = self.sem(inc)
            self.cnt[inc] += n
            self.steps.append((eng, lambda e, fn=fn, s=s, n=n: fn(e).then_inc(s, n)))
            return self.cnt[inc]
        self.steps.append((eng, fn))
        return None

    def _op_nosync(self, eng, fn, inc=None, n=1):
        if inc:
            s = self.sem(inc)
            self.cnt[inc] += n
            self.steps.append((eng, lambda e, fn=fn, s=s, n=n: fn(e).then_inc(s, n)))
            return self.cnt[inc]
        self.steps.append((eng, fn))
        return None

    DMA_SEMS = ("qa", "qw", "qw0", "qw1", "qa0", "qa1")

    def wait(self, eng, sem_name, val):
        if not val or val <= 0:
            return
        if sem_name in self.DMA_SEMS and val < self.cnt.get(sem_name, 0):
            val = self.cnt[sem_name]
        s = self.sem(sem_name)
        mw = getattr(self, "max_waited", None)
        if mw is None:
            self.max_waited = mw = {}
        mw[sem_name] = max(mw.get(sem_name, 0), val)
        self.steps.append((eng, lambda e, s=s, val=val: e.wait_ge(s, val)))

    def dma(self, q, out, in_, inc, **kw):
        mw = getattr(self, "max_waited", None) or {}
        isy = getattr(self, "issuer_synced", None)
        if isy is None:
            self.issuer_synced = isy = {}
        w = mw.get(inc, 0)
        if w > isy.get((q, inc), 0):
            s = self.sem(inc)
            self.steps.append((q, lambda e, s=s, val=w: e.wait_ge(s, val)))
            isy[(q, inc)] = w
        self._selfsync(q)
        return self._op_nosync(q, lambda e, out=out, in_=in_, kw=kw: e.dma_start(out=out, in_=in_, **kw), inc=inc, n=16)

    def val(self, s):
        return self.cnt.get(s, 0)

    def push(self):
        return len(self._guards)

    def pop(self, marker):
        for item in reversed(self._guards[marker:]):
            g = item[0] if isinstance(item, tuple) else item
            g.__exit__(None, None, None)
        del self._guards[marker:]

    def barrier(self, skip_cc=False):
        sems = [s for s in ("dve", "act", "pe", "gps", "qa", "qw", "cc") if s in self.cnt and self.cnt[s] > 0]
        if skip_cc:
            sems = [s for s in sems if s != "cc"]
        for eng in ENGS:
            for s in sems:
                self.wait(eng, s, self.cnt[s])

    def realize(self):
        with self.nc.Block() as block:
            by = {k: [fn for e, fn in self.steps if e == k] for k in ENGS}
            block.sync(lambda e: [fn(e) for fn in by["sync"]])
            block.vector(lambda e: [fn(e) for fn in by["vector"]])
            block.scalar(lambda e: [fn(e) for fn in by["scalar"]])
            block.tensor(lambda e: [fn(e) for fn in by["tensor"]])
            block.gpsimd(lambda e: [fn(e) for fn in by["gpsimd"]])




def emit_idents(p, idb, idf):
    for t_ in (idb, idf):
        v = p.op("gpsimd", lambda e, t_=t_: e.memset(t_[:, :], 1.0), inc="gps")
        p.wait("gpsimd", "gps", v)
        p.op("gpsimd", lambda e, t_=t_: e.affine_select(
            out=t_[:, :], in_=t_[:, :], pattern=[[1, 128]], compare_op=ALU.is_equal,
            fill=0.0, base=0, channel_multiplier=-1), inc="gps")
    p.wait("tensor", "gps", p.val("gps"))



def stream_mm(p, nk, bufs, ncol, srcfn2, body):
    npair = (nk + 1) // 2
    p.wait("sync", "pe", p.val("pe"))
    p.wait("scalar", "pe", p.val("pe"))
    vws = {0: p.dma("sync", bufs[0].ap()[:, 0:2 * ncol].rearrange("p (i c) -> p i c", i=2), srcfn2(0), inc="qw0")}
    vpes = {}
    for kk in range(npair):
        par = kk % 2
        if kk + 1 < npair:
            if kk - 1 >= 0:
                p.wait(("sync", "scalar")[(kk + 1) % 2], "pe", vpes[kk - 1])
            vws[kk + 1] = p.dma(("sync", "scalar")[(kk + 1) % 2],
                                bufs[(kk + 1) % 2].ap()[:, 0:2 * ncol].rearrange("p (i c) -> p i c", i=2),
                                srcfn2(kk + 1), inc=f"qw{(kk + 1) % 2}")
        p.wait("tensor", f"qw{par}", vws[kk])
        for half in range(2):
            k = 2 * kk + half
            if k >= nk:
                break
            body(k, bufs[par].ap()[:, half * ncol:(half + 1) * ncol])
        vpes[kk] = p.val("pe")


class MambaCfg:
    def __init__(self, layer):
        self.layer = layer
        if layer == 0:
            self.PROJ, self.CONV, self.N = 8512, 4352, 128
            self.conv_tiles = 34
            self.dt_mt = 66
            self.last_w = 192
        else:
            self.PROJ, self.CONV, self.N = 8384, 4224, 64
            self.conv_tiles = 33
            self.dt_mt = 65
            self.last_w = 64


def emit_mamba(p, cfg, io, idb, idf, mode):
    L = cfg.layer
    N = cfg.N
    CT = cfg.conv_tiles
    sfx = f"_m{L}{mode[0]}_{io['tag']}"
    full = (mode == "full")

    arena1 = p.sbuf("ar1" + sfx, [128, 34 * TT3], BF16)
    hs_ap = lambda j, a, b: arena1[:, j * TT3 + a: j * TT3 + b]
    yT_ap = lambda k, a, b: arena1[:, k * TB + a: k * TB + b]
    arena2 = p.sbuf("ar2" + sfx, [128, 8192], BF16)
    uT = lambda k, a, b: arena2[:, k * TB + a: k * TB + b]
    ring = p.sbuf("ring" + sfx, [128, 5 * 4096], BF16)
    rg = lambda s: ring[:, (s % 5) * 4096:(s % 5) * 4096 + 4096]
    xdt = p.sbuf("xdt" + sfx, [128, 4096], BF16)
    if mode == "full":
        outb = ring[:, 0:4 * D]
        gate_s = p.sbuf("gts" + sfx, [128, 4 * TB], BF16)
        gtok = p.sbuf("gtok" + sfx, [128, TB], BF16)
    dt_raw = p.sbuf("dtr" + sfx, [64, TB], F32)
    dts = p.sbuf("dts" + sfx, [64, TB], F32)
    adt = p.sbuf("adt" + sfx, [64, TB], F32)
    R = p.sbuf("R" + sfx, [N, 4096], F32)
    R_bf = p.sbuf("Rbf" + sfx, [N, 4096], BF16)
    acs = p.sbuf("acs" + sfx, [64, CH], F32)
    acsT = p.sbuf("acsT" + sfx, [128, 64], F32)
    dec_hm = p.sbuf("dechm" + sfx, [64, CH], F32)
    decT = p.sbuf("decT" + sfx, [128, 64], F32)
    sdecT = p.sbuf("sdecT" + sfx, [128, 64], F32)
    gambc = p.sbuf("gam" + sfx, [128, 64], F32)
    acsf = p.sbuf("acsf" + sfx, [1, 32 * CH], F32)
    grow = p.sbuf("grow" + sfx, [1, 64], F32)
    gsum = p.sbuf("gsum" + sfx, [64, 2], F32)
    Btok = p.sbuf("Btok" + sfx, [128, N], BF16)
    Bd = p.sbuf("Bdd" + sfx, [128, N], BF16)
    Gm = p.sbuf("Gm" + sfx, [128, CH], BF16)
    Dsc = p.sbuf("Dsc" + sfx, [128, 512], F32)
    W4 = p.sbuf("W4" + sfx, [128, 512], BF16)
    ysc = p.sbuf("ysc" + sfx, [128, 64], F32)
    Dbc = p.sbuf("Dbc" + sfx, [128, 64], F32)
    if mode == "full":
        nw_bc = p.sbuf("nwbc" + sfx, [128, 4096], BF16)
    ones1f = p.sbuf("on1" + sfx, [1, 128], F32)
    ones64 = p.sbuf("on64" + sfx, [64, CH], F32)
    srow = p.sbuf("srow" + sfx, [1, 4224], F32)
    convw = p.sbuf("cw" + sfx, [128, 34 * 4], F32)
    convb = p.sbuf("cb" + sfx, [128, 34], F32)
    dtbias = p.sbuf("dtb" + sfx, [64, 1], F32)
    expA = p.sbuf("expA" + sfx, [64, 1], F32)
    cacc = p.sbuf("cacc" + sfx, [128, TB], F32)
    halo_sv = p.sbuf("halo" + sfx, [128, 34 * 3], BF16)
    lnS = p.sbuf("lnS" + sfx, [128, 8], F32)
    xtl = p.sbuf("xtl" + sfx, [128, D], F32)
    utok = p.sbuf("utok" + sfx, [128, D], BF16)
    sq2 = utok
    wb2 = [p.sbuf(f"wbu{i}" + sfx, [128, 1024], BF16) for i in range(2)]
    if mode == "full":
        wr2 = [p.sbuf(f"wru{i}" + sfx, [128, 1024], BF16) for i in range(2)]
        rms = p.sbuf("rms" + sfx, [128, NCH], F32)
    if full:
        mrow = p.sbuf("mrow" + sfx, [1, 8], F32)
        mcol = p.sbuf("mcol" + sfx, [128, 8], F32)
        grow3 = p.sbuf("grw3" + sfx, [1, 192], F32)
        mhc = p.sbuf("mhc" + sfx, [128, 8], F32)
        grow8 = xdt.ap().bitcast(F32)[0:1, 0:512]
        hstg = xdt[:, 1024:1024 + 34 * 3]
    if mode == "state":
        u3 = p.sbuf("u3" + sfx, [128, 16 * 3], BF16)
        x3 = p.sbuf("x3" + sfx, [3, D], F32)
        u3tok = p.sbuf("u3t" + sfx, [3, D], BF16)

    pb = [p.psum(f"pb{i}" + sfx, [128, 512], F32) for i in range(8)]
    pbb = [t.ap().bitcast(BF16) for t in pb]

    QA, QW = "qa", "qw"

    p.op("vector", lambda e: e.memset(ones1f[:, :], 1.0), inc="dve")
    p.op("vector", lambda e: e.memset(ones64[:, :], 1.0), inc="dve")
    p.op("vector", lambda e: e.memset(gsum[:, :], 0.0), inc="dve")
    if not getattr(p, "_consts_done", False):
        p._consts_done = True
        cstt = p.sbuf("cstt" + sfx, [128, 2], F32)
        p.op("vector", lambda e: e.memset(cstt[:, 0:1], EPS), inc="dve")
        p.op("vector", lambda e: e.memset(cstt[:, 1:2], 1.0), inc="dve")
        p.nc.const_aps.register(F32, EPS, cstt[:, 0:1])
        p.nc.const_aps.register(F32, 1.0, cstt[:, 1:2])
    cbt = p.sbuf("cbt" + sfx, [34, 128], F32)
    if L == 0:
        p.dma("gpsimd", convw.ap().rearrange("p (j k) -> p j k", k=4)[:, 0:34, :], io["conv_w"].rearrange("(j p) k -> p j k", p=128), inc=QA)
        p.dma("gpsimd", cbt[:, :], io["conv_b"].rearrange("(j p) -> j p", p=128), inc=QA)
    else:
        p.dma("gpsimd", convw.ap().rearrange("p (j k) -> p j k", k=4)[:, 0:32, :], io["conv_w"][0:4096, :].rearrange("(j p) k -> p j k", p=128), inc=QA)
        p.dma("gpsimd", convw[0:64, 128:132], io["conv_w"][4096:4160, :], inc=QA)
        p.dma("gpsimd", convw[0:64, 132:136], io["conv_w"][4160:4224, :], inc=QA)
        p.dma("gpsimd", cbt[0:32, :], io["conv_b"][0:4096].rearrange("(j p) -> j p", p=128), inc=QA)
        p.dma("gpsimd", cbt[32:33, 0:64], io["conv_b"][4096:4160].unsqueeze(0), inc=QA)
        p.dma("gpsimd", cbt[33:34, 0:64], io["conv_b"][4160:4224].unsqueeze(0), inc=QA)
    p.dma("gpsimd", dtbias[:, :], io["dt_bias"].unsqueeze(1), inc=QA)
    p.dma("gpsimd", expA[:, :], io["A_log"].unsqueeze(1), inc=QA)
    p.dma("gpsimd", srow[0:1, 0:64], io["Dv"].unsqueeze(0), inc=QA)
    p.dma("gpsimd", srow[0:1, 64:64 + 4096], io["norm_w"].unsqueeze(0), inc=QA)
    if full and not io.get("fused"):
        p.dma("gpsimd", mrow[:, 0:3], io["mrow"], inc=QA)
        p.dma("gpsimd", mcol[:, 0:4], io["mcol"], inc=QA)
        p.dma("gpsimd", grow3[0:1, :], io["g_in"], inc=QA)
        p.dma("gpsimd", halo_sv[:, :], io["halo_in"], inc=QA)
    if full and io.get("fused"):
        p.dma("gpsimd", mrow[:, 0:8], io["mrow8"], inc=QA)
        p.dma("gpsimd", mcol[:, 0:8], io["mcol8"], inc=QA)
        p.dma("gpsimd", mhc[:, 0:8], io["mh8"], inc=QA)
        p.dma("gpsimd", grow8[0:1, :], io["g_all"].rearrange("a b -> (a b)").unsqueeze(0), inc=QA)
    vc = p.val(QA)
    p.wait("scalar", QA, vc)
    p.op("scalar", lambda e: e.activation(out=expA[:, :], in_=expA[:, :], func=AF.Exp), inc="act")
    p.wait("tensor", QA, vc)
    p.wait("tensor", "dve", p.val("dve"))
    v_dbcmm = p.op("tensor", lambda e: e.matmul(out=pb[0][:, 0:64], lhsT=ones1f[:, :], rhs=srow[0:1, 0:64], start=True, stop=True), inc="pe")
    if full:
        for j in range(8):
            p.wait("tensor", "dve", p.val("dve"))
            p.op("tensor", lambda e, j=j: e.matmul(out=pb[1][:, :], lhsT=ones1f[:, :], rhs=srow[0:1, 64 + 512 * j:64 + 512 * (j + 1)], start=True, stop=True), inc="pe")
            v = p.val("pe")
            p.wait("vector", "pe", v)
            p.op("vector", lambda e, j=j: e.tensor_copy(out=nw_bc[:, 512 * j:512 * (j + 1)], in_=pb[1][:, :]), inc="dve")
    p.wait("vector", "pe", v_dbcmm)
    p.op("vector", lambda e: e.tensor_copy(out=Dbc[:, :], in_=pb[0][:, 0:64]), inc="dve")
    p.op("tensor", lambda e: e.transpose(out=pb[3][:, 0:34], in_=cbt[:, :], identity=idf[0:34, 0:34]), inc="pe")
    p.wait("vector", "pe", p.val("pe"))
    p.op("vector", lambda e: e.tensor_copy(out=convb[:, 0:34], in_=pb[3][:, 0:34]), inc="dve")
    if full and io.get("fused"):
        p.wait("vector", QA, vc)
        p.op("vector", lambda e: e.memset(halo_sv[:, :], 0.0), inc="dve")
        p.op("vector", lambda e: e.memset(R[:, :], 0.0), inc="dve")
        for j in range(8):
            vh = p.dma("gpsimd", hstg[:, :], io["halo_all"][128 * j:128 * (j + 1), :], inc=QA)
            p.wait("vector", QA, vh)
            p.op("vector", lambda e, j=j: e.scalar_tensor_tensor(
                out=halo_sv[:, :], in0=hstg[:, :], scalar=mhc[:, j:j + 1], in1=halo_sv[:, :],
                op0=ALU.mult, op1=ALU.add), inc="dve")
            p.op("vector", lambda e, j=j: e.tensor_scalar(out=grow[0:1, :], in0=grow8[0:1, 64 * j:64 * (j + 1)], scalar1=mrow[0:1, j:j + 1], scalar2=None, op0=ALU.mult), inc="dve")
            p.op("vector", lambda e, j=j: e.tensor_scalar(out=srow[0:1, 4160:4224], in0=mrow[0:1, j:j + 1].to_broadcast([1, 64]), scalar1=-1.0, scalar2=1.0, op0=ALU.mult, op1=ALU.add), inc="dve")
            p.op("vector", lambda e: e.tensor_add(out=grow[0:1, :], in0=grow[0:1, :], in1=srow[0:1, 4160:4224]), inc="dve")
            vg = p.val("dve")
            p.wait("tensor", "dve", vg)
            p.op("tensor", lambda e: e.matmul(out=pb[2][:, 0:64], lhsT=ones1f[:, :], rhs=grow[0:1, :], start=True, stop=True), inc="pe")
            vpe = p.val("pe")
            p.wait("vector", "pe", vpe)
            p.op("vector", lambda e: e.tensor_copy(out=gambc[:, :], in_=pb[2][:, 0:64]), inc="dve")
            p.op("vector", lambda e: e.tensor_tensor(
                out=R.ap().rearrange("p (h d) -> p h d", h=64),
                in0=R.ap().rearrange("p (h d) -> p h d", h=64),
                in1=gambc[0:N, :].unsqueeze(2).to_broadcast([N, 64, 64]), op=ALU.mult), inc="dve")
            p.wait("gpsimd", "dve", p.val("dve"))
            vS = p.dma("gpsimd", R_bf.ap().bitcast(F32)[0:N, 0:2048], io["S_all"][N * j:N * (j + 1), 0:2048], inc=QA)
            p.wait("vector", QA, vS)
            p.op("vector", lambda e, j=j: e.scalar_tensor_tensor(
                out=R[0:N, 0:2048], in0=R_bf.ap().bitcast(F32)[0:N, 0:2048], scalar=mcol[0:N, j:j + 1],
                in1=R[0:N, 0:2048], op0=ALU.mult, op1=ALU.add), inc="dve")
            p.wait("gpsimd", "dve", p.val("dve"))
            vS2 = p.dma("gpsimd", R_bf.ap().bitcast(F32)[0:N, 0:2048], io["S_all"][N * j:N * (j + 1), 2048:4096], inc=QA)
            p.wait("vector", QA, vS2)
            p.op("vector", lambda e, j=j: e.scalar_tensor_tensor(
                out=R[0:N, 2048:4096], in0=R_bf.ap().bitcast(F32)[0:N, 0:2048], scalar=mcol[0:N, j:j + 1],
                in1=R[0:N, 2048:4096], op0=ALU.mult, op1=ALU.add), inc="dve")
    elif full:
        p.wait("vector", QA, vc)
        p.op("vector", lambda e: e.tensor_scalar(out=halo_sv[:, :], in0=halo_sv[:, :], scalar1=mcol[:, 3:4], scalar2=None, op0=ALU.mult), inc="dve")
        p.op("vector", lambda e: e.memset(R[:, :], 0.0), inc="dve")
        for j in range(3):
            p.op("vector", lambda e, j=j: e.tensor_scalar(out=grow[0:1, :], in0=grow3[0:1, 64 * j:64 * (j + 1)], scalar1=mrow[0:1, j:j + 1], scalar2=None, op0=ALU.mult), inc="dve")
            p.op("vector", lambda e, j=j: e.tensor_scalar(out=srow[0:1, 4160:4224], in0=mrow[0:1, j:j + 1].to_broadcast([1, 64]), scalar1=-1.0, scalar2=1.0, op0=ALU.mult, op1=ALU.add), inc="dve")
            p.op("vector", lambda e: e.tensor_add(out=grow[0:1, :], in0=grow[0:1, :], in1=srow[0:1, 4160:4224]), inc="dve")
            vg = p.val("dve")
            p.wait("tensor", "dve", vg)
            p.op("tensor", lambda e: e.matmul(out=pb[2][:, 0:64], lhsT=ones1f[:, :], rhs=grow[0:1, :], start=True, stop=True), inc="pe")
            vpe = p.val("pe")
            p.wait("vector", "pe", vpe)
            p.op("vector", lambda e: e.tensor_copy(out=gambc[:, :], in_=pb[2][:, 0:64]), inc="dve")
            p.op("vector", lambda e: e.tensor_tensor(
                out=R.ap().rearrange("p (h d) -> p h d", h=64),
                in0=R.ap().rearrange("p (h d) -> p h d", h=64),
                in1=gambc[0:N, :].unsqueeze(2).to_broadcast([N, 64, 64]), op=ALU.mult), inc="dve")
            p.wait("gpsimd", "dve", p.val("dve"))
            vS = p.dma("gpsimd", R_bf.ap().bitcast(F32)[0:N, 0:2048], io["S_in"][j][0:N, 0:2048], inc=QA)
            p.wait("vector", QA, vS)
            p.op("vector", lambda e, j=j: e.scalar_tensor_tensor(
                out=R[0:N, 0:2048], in0=R_bf.ap().bitcast(F32)[0:N, 0:2048], scalar=mcol[0:N, j:j + 1],
                in1=R[0:N, 0:2048], op0=ALU.mult, op1=ALU.add), inc="dve")
            p.wait("gpsimd", "dve", p.val("dve"))
            vS2 = p.dma("gpsimd", R_bf.ap().bitcast(F32)[0:N, 0:2048], io["S_in"][j][0:N, 2048:4096], inc=QA)
            p.wait("vector", QA, vS2)
            p.op("vector", lambda e, j=j: e.scalar_tensor_tensor(
                out=R[0:N, 2048:4096], in0=R_bf.ap().bitcast(F32)[0:N, 0:2048], scalar=mcol[0:N, j:j + 1],
                in1=R[0:N, 2048:4096], op0=ALU.mult, op1=ALU.add), inc="dve")
    else:
        p.op("vector", lambda e: e.memset(R[:, :], 0.0), inc="dve")
        p.op("vector", lambda e: e.memset(halo_sv[:, :], 0.0), inc="dve")
    p.wait("tensor", "dve", p.val("dve"))

    if L == 0:
        units = [(128 * (32 + j), 128, ("hs", j, 128)) for j in range(34)]
        units.append((8448, 64, ("dt",)))
        conv_info = [(j, 128) for j in range(34)]
    else:
        units = [(128 * (32 + j), 128, ("hs", j, 128)) for j in range(32)]
        units += [(8192, 64, ("hs", 32, 64)), (8256, 64, ("hs", 33, 64)), (8320, 64, ("dt",))]
        conv_info = [(j, 128) for j in range(32)] + [(32, 64), (33, 64)]
    hs_groups = []
    i = 0
    while i < len(units):
        g = [units[i]]
        while (len(g) < 4 and i + len(g) < len(units)
               and sum(u[1] for u in g) + units[i + len(g)][1] <= 512
               and (L == 0 or (units[i][0] // 1152) == ((units[i + len(g)][0] + units[i + len(g)][1] - 1) // 1152))):
            g.append(units[i + len(g)])
        hs_groups.append(g)
        i += len(g)
    gate_groups = []
    i = 0
    while i < 32:
        e2 = min(i + 4, 32)
        if L == 1:
            e2 = min(e2, (i // 9 + 1) * 9)
        gate_groups.append(list(range(i, e2)))
        i = e2

    if mode == "state":
        if io.get("xh_all") is not None:
            mh3 = lnS[0:3, 0:8]
            xh3 = xtl[0:3, :]
            p.dma("gpsimd", mh3[:, :], io["mh8_3"], inc=QA)
            p.op("vector", lambda e: e.memset(x3[:, :], 0.0), inc="dve")
            p.wait("vector", QA, p.val(QA))
            for j in range(8):
                v3 = p.dma("gpsimd", xh3[:, :], io["xh_all"][3 * j:3 * (j + 1), :], inc=QA)
                p.wait("vector", QA, v3)
                p.op("vector", lambda e, j=j: e.scalar_tensor_tensor(
                    out=x3[:, :], in0=xh3[:, :], scalar=mh3[:, j:j + 1], in1=x3[:, :],
                    op0=ALU.mult, op1=ALU.add), inc="dve")
                p.wait("gpsimd", "dve", p.val("dve"))
            v3 = p.val(QA)
        else:
            v3 = p.dma("gpsimd", x3[:, :], io["x_halo"], inc=QA)
        p.wait("vector", QA, v3)
        p.op("vector", lambda e: e.tensor_reduce(out=lnS[0:3, 0:1], in_=x3[:, :], axis=AXX, op=ALU.add), inc="dve")
        p.wait("scalar", QA, v3)
        p.op("scalar", lambda e: e.activation(out=sq2[0:3, :], in_=x3[:, :], func=AF.Square), inc="act")
        p.wait("vector", "act", p.val("act"))
        p.op("vector", lambda e: e.tensor_scalar(out=lnS[0:3, 1:2], in0=lnS[0:3, 0:1], scalar1=0.0, scalar2=None, op0=ALU.mult), inc="dve")
        p.op("vector", lambda e: e.tensor_reduce(out=lnS[0:3, 1:2], in_=sq2[0:3, :], axis=AXX, op=ALU.add), inc="dve")
        p.op("vector", lambda e: e.tensor_scalar(out=lnS[0:3, 1:2], in0=lnS[0:3, 1:2], scalar1=1.0 / D, scalar2=None, op0=ALU.mult), inc="dve")
        p.op("vector", lambda e: e.tensor_scalar(out=lnS[0:3, 2:3], in0=lnS[0:3, 0:1], scalar1=1.0 / D, scalar2=None, op0=ALU.mult), inc="dve")
        p.op("vector", lambda e: e.tensor_scalar(out=lnS[0:3, 4:5], in0=lnS[0:3, 2:3], scalar1=lnS[0:3, 2:3], scalar2=None, op0=ALU.mult), inc="dve")
        p.op("vector", lambda e: e.tensor_sub(out=lnS[0:3, 4:5], in0=lnS[0:3, 1:2], in1=lnS[0:3, 4:5]), inc="dve")
        vd = p.val("dve")
        p.wait("scalar", "dve", vd)
        p.op("scalar", lambda e: e.activation(out=lnS[0:3, 5:6], in_=lnS[0:3, 4:5], func=AF.Ln, bias=EPS), inc="act")
        p.op("scalar", lambda e: e.activation(out=lnS[0:3, 6:7], in_=lnS[0:3, 5:6], func=AF.Exp, scale=-0.5), inc="act")
        va = p.val("act")
        p.wait("vector", "act", va)
        p.op("vector", lambda e: e.tensor_scalar(out=u3tok[:, :], in0=x3[:, :], scalar1=lnS[0:3, 2:3], scalar2=lnS[0:3, 6:7], op0=ALU.subtract, op1=ALU.mult), inc="dve")
        vd = p.val("dve")
        p.wait("tensor", "dve", vd)
        for j in range(16):
            p.op("tensor", lambda e, j=j: e.transpose(
                out=pbb[0][:, (j % 4) * 128:(j % 4) * 128 + 3], in_=u3tok[:, j * 128:(j + 1) * 128],
                identity=idb[0:3, 0:3]), inc=("pe" if j % 4 == 3 else None))
            if j % 4 == 3:
                v = p.val("pe")
                p.wait("scalar", "pe", v)
                for jj in range(j - 3, j + 1):
                    p.op("scalar", lambda e, jj=jj: e.copy(out=u3[:, jj * 3:(jj + 1) * 3], in_=pbb[0][:, (jj % 4) * 128:(jj % 4) * 128 + 3]), inc="act")
                p.wait("tensor", "act", p.val("act"))
        for gi, grp in enumerate(hs_groups):
            m0 = grp[0][0]
            ncol = sum(u[1] for u in grp)
            for k in range(16):
                vw = p.dma("sync", wb2[0][:, 0:ncol], io["w_in_slc"](k, m0, ncol), inc=QW)
                p.wait("tensor", QW, vw)
                co = 0
                for mi, (c0, w_, dest) in enumerate(grp):
                    p.op("tensor", lambda e, k=k, co=co, w_=w_, mi=mi: e.matmul(
                        out=pb[mi][0:w_, 0:3], lhsT=wb2[0][:, co:co + w_], rhs=u3[:, k * 3:(k + 1) * 3],
                        start=(k == 0), stop=(k == 15)), inc=("pe" if mi == len(grp) - 1 else None))
                    co += w_
                vpe = p.val("pe")
                p.wait("sync", "pe", vpe)
            vpe = p.val("pe")
            p.wait("scalar", "pe", vpe)
            for mi, (c0, w_, dest) in enumerate(grp):
                if dest[0] == "hs":
                    j, pn = dest[1], dest[2]
                    p.op("scalar", lambda e, j=j, mi=mi, pn=pn: e.copy(out=hs_ap(j, 0, 3)[0:pn, :], in_=pb[mi][0:pn, 0:3]), inc="act")
            p.wait("tensor", "act", p.val("act"))
        va = p.val("act")
        p.wait("vector", "act", va)
        for (j, pn) in conv_info:
            p.op("vector", lambda e, j=j, pn=pn: e.tensor_copy(out=halo_sv[0:pn, 3 * j:3 * j + 3], in_=hs_ap(j, 0, 3)[0:pn, :]), inc="dve")
        p.wait("scalar", "dve", p.val("dve"))

    for tb in range(2):
        t0 = tb * TB

        for it in range(NTT):
            p.wait("gpsimd", "dve", p.val("dve"))
            vx = p.dma("gpsimd", xtl[:, :], io["x_in"][t0 + it * 128: t0 + (it + 1) * 128, :], inc=QA)
            p.wait("vector", QA, vx)
            p.op("vector", lambda e: e.tensor_reduce(out=lnS[:, 0:1], in_=xtl[:, :], axis=AXX, op=ALU.add), inc="dve")
            p.wait("scalar", QA, vx)
            p.op("scalar", lambda e: e.activation(out=sq2[:, :], in_=xtl[:, :], func=AF.Square), inc="act")
            p.wait("vector", "act", p.val("act"))
            p.op("vector", lambda e: e.tensor_reduce(out=lnS[:, 1:2], in_=sq2[:, :], axis=AXX, op=ALU.add), inc="dve")
            p.op("vector", lambda e: e.tensor_scalar(out=lnS[:, 1:2], in0=lnS[:, 1:2], scalar1=1.0 / D, scalar2=None, op0=ALU.mult), inc="dve")
            p.op("vector", lambda e: e.tensor_scalar(out=lnS[:, 2:3], in0=lnS[:, 0:1], scalar1=1.0 / D, scalar2=None, op0=ALU.mult), inc="dve")
            p.op("vector", lambda e: e.tensor_scalar(out=lnS[:, 4:5], in0=lnS[:, 2:3], scalar1=lnS[:, 2:3], scalar2=None, op0=ALU.mult), inc="dve")
            p.op("vector", lambda e: e.tensor_sub(out=lnS[:, 4:5], in0=lnS[:, 1:2], in1=lnS[:, 4:5]), inc="dve")
            vd = p.val("dve")
            p.wait("scalar", "dve", vd)
            p.op("scalar", lambda e: e.activation(out=lnS[:, 5:6], in_=lnS[:, 4:5], func=AF.Ln, bias=EPS), inc="act")
            p.op("scalar", lambda e: e.activation(out=lnS[:, 6:7], in_=lnS[:, 5:6], func=AF.Exp, scale=-0.5), inc="act")
            va = p.val("act")
            p.wait("vector", "act", va)
            p.op("vector", lambda e: e.tensor_scalar(out=utok[:, :], in0=xtl[:, :], scalar1=lnS[:, 2:3], scalar2=lnS[:, 6:7], op0=ALU.subtract, op1=ALU.mult), inc="dve")
            vd = p.val("dve")
            p.wait("tensor", "dve", vd)
            for j in range(16):
                p.op("tensor", lambda e, j=j: e.transpose(
                    out=pbb[j % 4][:, 0:128], in_=utok[:, j * 128:(j + 1) * 128],
                    identity=idb[:, :]), inc=("pe" if j % 4 == 3 else None))
                if j % 4 == 3:
                    v = p.val("pe")
                    p.wait("scalar", "pe", v)
                    for jj in range(j - 3, j + 1):
                        p.op("scalar", lambda e, jj=jj, it=it: e.copy(
                            out=uT(jj, it * 128, it * 128 + 128), in_=pbb[jj % 4][:, 0:128]), inc="act")
                    p.wait("tensor", "act", p.val("act"))

        evac_val = {0: 0, 1: 0}
        for gi, grp in enumerate(hs_groups):
            m0 = grp[0][0]
            ncol = sum(u[1] for u in grp)
            pbase = 4 * (gi % 2)
            p.wait("tensor", "act", evac_val[gi % 2])

            def body(k, buf, grp=grp, pbase=pbase):
                co = 0
                for mi, (c0, w_, dest) in enumerate(grp):
                    p.op("tensor", lambda e, k=k, co=co, w_=w_, mi=mi, buf=buf, pbase=pbase: e.matmul(
                        out=pb[pbase + mi][0:w_, :],
                        lhsT=buf[:, co:co + w_], rhs=uT(k, 0, TB),
                        start=(k == 0), stop=(k == 15)), inc=("pe" if mi == len(grp) - 1 else None))
                    co += w_

            stream_mm(p, 16, wb2, ncol,
                      lambda kk, m0=m0, ncol=ncol: io["w_in_slc2"](kk, m0, ncol), body)
            vpe = p.val("pe")
            p.wait("scalar", "pe", vpe)
            for mi, (c0, w_, dest) in enumerate(grp):
                if dest[0] == "dt":
                    p.op("scalar", lambda e, mi=mi, pbase=pbase: e.copy(out=dt_raw[:, :], in_=pb[pbase + mi][0:64, :]), inc="act")
                else:
                    j, pn = dest[1], dest[2]
                    p.op("scalar", lambda e, j=j, mi=mi, pn=pn, pbase=pbase: e.copy(out=hs_ap(j, 3, 3 + TB)[0:pn, :], in_=pb[pbase + mi][0:pn, :]), inc="act")
            evac_val[gi % 2] = p.val("act")
        p.wait("tensor", "act", p.val("act"))

        va_hs = p.val("act")
        p.wait("vector", "act", va_hs)
        p.wait("vector", QA, p.val(QA))
        for (j, pn) in conv_info:
            p.op("vector", lambda e, j=j, pn=pn: e.tensor_copy(out=hs_ap(j, 0, 3)[0:pn, :], in_=halo_sv[0:pn, 3 * j:3 * j + 3]), inc="dve")
            p.op("vector", lambda e, j=j, pn=pn: e.tensor_copy(out=halo_sv[0:pn, 3 * j:3 * j + 3], in_=hs_ap(j, TB, TB + 3)[0:pn, :]), inc="dve")
            p.op("vector", lambda e, j=j, pn=pn: e.tensor_scalar(
                out=cacc[0:pn, :], in0=hs_ap(j, 0, TB)[0:pn, :], scalar1=convw[0:pn, 4 * j:4 * j + 1], scalar2=None, op0=ALU.mult), inc="dve")
            for kk in range(1, 4):
                p.op("vector", lambda e, j=j, kk=kk, pn=pn: e.scalar_tensor_tensor(
                    out=cacc[0:pn, :], in0=hs_ap(j, kk, kk + TB)[0:pn, :], scalar=convw[0:pn, 4 * j + kk:4 * j + kk + 1],
                    in1=cacc[0:pn, :], op0=ALU.mult, op1=ALU.add), inc="dve")
            vd = p.val("dve")
            p.wait("scalar", "dve", vd)
            p.op("scalar", lambda e, j=j, pn=pn: e.activation(
                out=hs_ap(j, 3, 3 + TB)[0:pn, :], in_=cacc[0:pn, :], func=AF.Silu, bias=convb[0:pn, j:j + 1]), inc="act")
            p.wait("vector", "act", p.val("act"))

        p.op("vector", lambda e: e.tensor_scalar(out=dts[:, :], in0=dt_raw[:, :], scalar1=dtbias[:, :], scalar2=None, op0=ALU.add), inc="dve")
        vd = p.val("dve")
        p.wait("scalar", "dve", vd)
        p.op("scalar", lambda e: e.activation(out=dt_raw[:, :], in_=dts[:, :], func=AF.Abs), inc="act")
        p.op("scalar", lambda e: e.activation(out=adt[:, :], in_=dt_raw[:, :], func=AF.Exp, scale=-1.0), inc="act")
        p.op("scalar", lambda e: e.activation(out=adt[:, :], in_=adt[:, :], func=AF.Ln, bias=1.0), inc="act")
        va = p.val("act")
        p.wait("vector", "act", va)
        p.op("vector", lambda e: e.tensor_scalar(out=dts[:, :], in0=dts[:, :], scalar1=0.0, scalar2=None, op0=ALU.max), inc="dve")
        p.op("vector", lambda e: e.tensor_add(out=dts[:, :], in0=dts[:, :], in1=adt[:, :]), inc="dve")
        p.op("vector", lambda e: e.tensor_scalar(out=adt[:, :], in0=dts[:, :], scalar1=expA[:, :], scalar2=-1.0, op0=ALU.mult, op1=ALU.mult), inc="dve")
        if mode == "state":
            p.op("vector", lambda e: e.tensor_reduce(out=gsum[:, 1:2], in_=adt[:, :], axis=AXX, op=ALU.add), inc="dve")
            p.op("vector", lambda e: e.tensor_add(out=gsum[:, 0:1], in0=gsum[:, 0:1], in1=gsum[:, 1:2]), inc="dve")
        vd_dt = p.val("dve")

        for c in range(NCH):
            cs = c * CH
            slot_x = (tb * NCH + c + 1) % 5
            slot_y = (tb * NCH + c) % 5
            p.wait("vector", "dve", vd_dt)
            p.op("vector", lambda e, cs=cs: e.tensor_tensor_scan(
                out=acs[:, :], data0=ones64[:, :], data1=adt[:, cs:cs + CH],
                initial=0.0, op0=ALU.mult, op1=ALU.add), inc="dve")
            vd_acs = p.val("dve")
            p.op("vector", lambda e: e.tensor_scalar(out=dec_hm[:, :], in0=acs[:, :], scalar1=acs[:, CH - 1:CH], scalar2=-1.0, op0=ALU.subtract, op1=ALU.mult), inc="dve")
            vd2 = p.val("dve")
            p.wait("scalar", "dve", vd2)
            p.op("scalar", lambda e: e.activation(out=dec_hm[:, :], in_=dec_hm[:, :], func=AF.Exp), inc="act")
            va_dec = p.val("act")
            p.wait("gpsimd", "dve", vd_acs)
            p.dma("gpsimd", io["acs_dram"].ap(), acs[:, :], inc=QA)
            p.wait("gpsimd", QA, p.val(QA))
            if full:
                p.dma("gpsimd", acsf[:, :], io["acs_dram"].ap()[0:32, :].rearrange("h t -> (h t)").unsqueeze(0), inc=QA)
            v_acsf = p.val(QA)
            p.dma("gpsimd", io["alast_dram"].ap().unsqueeze(1), acs[:, CH - 1:CH], inc=QA)
            p.wait("gpsimd", QA, p.val(QA))
            vgr = p.dma("gpsimd", grow[0:1, :], io["alast_dram"].ap().unsqueeze(0), inc=QA)
            p.wait("tensor", "dve", vd_acs)
            p.op("tensor", lambda e: e.transpose(out=pb[4][:, 0:64], in_=acs[:, :], identity=idf[0:64, 0:64]), inc="pe")
            p.wait("tensor", "act", va_dec)
            p.op("tensor", lambda e: e.transpose(out=pb[5][:, 0:64], in_=dec_hm[:, :], identity=idf[0:64, 0:64]), inc="pe")
            p.op("tensor", lambda e, cs=cs: e.transpose(out=pb[6][:, 0:64], in_=dts[:, cs:cs + CH], identity=idf[0:64, 0:64]), inc="pe")
            vpe = p.val("pe")
            p.wait("vector", "pe", vpe)
            p.op("vector", lambda e: e.tensor_copy(out=acsT[:, :], in_=pb[4][:, 0:64]), inc="dve")
            p.op("vector", lambda e: e.tensor_copy(out=decT[:, :], in_=pb[5][:, 0:64]), inc="dve")
            p.op("vector", lambda e: e.tensor_copy(out=Dsc[:, 0:64], in_=pb[6][:, 0:64]), inc="dve")
            vd_acsT = p.val("dve")
            p.wait("scalar", "dve", vd_acsT)
            if full:
                p.op("scalar", lambda e: e.activation(out=sdecT[:, :], in_=acsT[:, :], func=AF.Exp), inc="act")
            p.wait("scalar", QA, vgr)
            p.op("scalar", lambda e: e.activation(out=grow[:, :], in_=grow[:, :], func=AF.Exp), inc="act")
            va_g = p.val("act")
            p.wait("tensor", "act", va_g)
            p.op("tensor", lambda e: e.matmul(out=pb[7][:, 0:64], lhsT=ones1f[:, :], rhs=grow[:, :], start=True, stop=True), inc="pe")
            vpe_g = p.val("pe")
            p.wait("vector", "pe", vpe_g)
            p.op("vector", lambda e: e.tensor_copy(out=gambc[:, :], in_=pb[7][:, 0:64]), inc="dve")
            vd_gam = p.val("dve")
            p.wait("tensor", "dve", vd_gam)

            for jg in range(8):
                for j4 in range(4):
                    j = jg * 4 + j4
                    p.op("tensor", lambda e, j=j, j4=j4, cs=cs: e.transpose(
                        out=pbb[j4][:, 0:128], in_=hs_ap(j, 3 + cs, 3 + cs + CH),
                        identity=idb[:, :]), inc=("pe" if j4 == 3 else None))
                v = p.val("pe")
                p.wait("scalar", "pe", v)
                for j4 in range(4):
                    j = jg * 4 + j4
                    p.op("scalar", lambda e, j=j, j4=j4, slot_x=slot_x: e.copy(
                        out=rg(slot_x)[:, j * 128:(j + 1) * 128], in_=pbb[j4][:, 0:128]), inc="act")
                p.wait("tensor", "act", p.val("act"))
            if L == 0:
                p.op("tensor", lambda e, cs=cs: e.transpose(out=pbb[0][:, 0:128], in_=hs_ap(32, 3 + cs, 3 + cs + CH), identity=idb[:, :]), inc="pe")
            else:
                p.op("tensor", lambda e, cs=cs: e.transpose(out=pbb[0][:, 0:64], in_=hs_ap(32, 3 + cs, 3 + cs + CH)[0:64, :], identity=idb[0:64, 0:64]), inc="pe")
            vpe_b = p.val("pe")
            p.wait("scalar", "pe", vpe_b)
            p.op("scalar", lambda e: e.copy(out=Btok[:, 0:N], in_=pbb[0][:, 0:N]), inc="act")
            va_bt = p.val("act")
            p.wait("tensor", "act", va_bt)
            p.wait("vector", "act", va_bt)

            p.op("vector", lambda e, slot_x=slot_x: e.tensor_tensor(
                out=xdt.ap().rearrange("p (h d) -> p h d", h=64),
                in0=rg(slot_x).rearrange("p (h d) -> p h d", h=64),
                in1=Dsc[:, 0:64].unsqueeze(2).to_broadcast([128, 64, 64]), op=ALU.mult), inc="dve")
            p.op("vector", lambda e: e.tensor_copy(out=R_bf[:, :], in_=R[:, :]), inc="dve")
            vd_xdt = p.val("dve")
            p.wait("tensor", "dve", vd_xdt)

            if full:
                if L == 0:
                    p.op("tensor", lambda e, cs=cs: e.matmul(
                        out=pb[6][:, 0:128], lhsT=hs_ap(32, 3 + cs, 3 + cs + CH),
                        rhs=hs_ap(33, 3 + cs, 3 + cs + CH), start=True, stop=True), inc="pe")
                else:
                    p.op("tensor", lambda e, cs=cs: e.matmul(
                        out=pb[6][:, 0:128], lhsT=hs_ap(32, 3 + cs, 3 + cs + CH)[0:64, :],
                        rhs=hs_ap(33, 3 + cs, 3 + cs + CH)[0:64, :], start=True, stop=True), inc="pe")
                vpe_G = p.val("pe")
                p.wait("scalar", "pe", vpe_G)
                p.op("scalar", lambda e: e.copy(out=Gm[:, :], in_=pb[6][:, 0:128]), inc="act")
                va_G = p.val("act")
                p.wait("gpsimd", "act", va_G)
                p.op("gpsimd", lambda e: e.affine_select(
                    out=Gm[:, :], in_=Gm[:, :], pattern=[[1, CH]], compare_op=ALU.is_ge,
                    fill=0.0, base=0, channel_multiplier=-1), inc="gps")
                v_gm = p.val("gps")
                p.wait("vector", "gps", v_gm)
                p.wait("tensor", "gps", v_gm)
                p.wait("vector", QA, p.val(QA))

            for g in range(16 if full else 0):
                h0 = 4 * g
                if g == 8:
                    p.wait("gpsimd", "pe", p.val("pe"))
                    p.dma("gpsimd", acsf[:, :], io["acs_dram"].ap()[32:64, :].rearrange("h t -> (h t)").unsqueeze(0), inc=QA)
                    p.wait("tensor", QA, p.val(QA))
                p.op("tensor", lambda e, h0=h0: e.matmul(
                    out=pb[4][:, :], lhsT=ones1f[:, :], rhs=acsf[0:1, (h0 % 32) * CH:((h0 % 32) + 4) * CH],
                    start=True, stop=True), inc="pe")
                v_db = p.val("pe")
                p.wait("vector", "pe", v_db)
                p.op("vector", lambda e, h0=h0: e.tensor_tensor(
                    out=Dsc.ap().rearrange("p (q l) -> p q l", q=4),
                    in0=pb[4].ap().rearrange("p (q l) -> p q l", q=4),
                    in1=acsT[:, h0:h0 + 4].unsqueeze(2).to_broadcast([128, 4, CH]), op=ALU.subtract), inc="dve")
                p.op("vector", lambda e: e.tensor_scalar(out=Dsc[:, :], in0=Dsc[:, :], scalar1=30.0, scalar2=None, op0=ALU.min), inc="dve")
                vd = p.val("dve")
                p.wait("scalar", "dve", vd)
                p.op("scalar", lambda e: e.activation(out=W4[:, :], in_=Dsc[:, :], func=AF.Exp), inc="act")
                va = p.val("act")
                p.wait("vector", "act", va)
                p.op("vector", lambda e: e.tensor_tensor(
                    out=W4.ap().rearrange("p (q l) -> p q l", q=4),
                    in0=W4.ap().rearrange("p (q l) -> p q l", q=4),
                    in1=Gm[:, :].unsqueeze(1).to_broadcast([128, 4, CH]), op=ALU.mult), inc="dve")
                vd_w4 = p.val("dve")
                p.wait("tensor", "dve", vd_w4)
                for hh in range(4):
                    h = h0 + hh
                    p.op("tensor", lambda e, hh=hh, h=h: e.matmul(
                        out=pb[5][:, hh * 64:(hh + 1) * 64], lhsT=W4[:, hh * CH:hh * CH + CH],
                        rhs=xdt[:, h * 64:(h + 1) * 64], start=True, stop=True))
                    if L == 0:
                        p.op("tensor", lambda e, hh=hh, h=h, cs=cs: e.matmul(
                            out=pb[6][:, hh * 64:(hh + 1) * 64], lhsT=hs_ap(33, 3 + cs, 3 + cs + CH),
                            rhs=R_bf[:, h * 64:(h + 1) * 64], start=True, stop=True),
                            inc=("pe" if hh == 3 else None))
                    else:
                        p.op("tensor", lambda e, hh=hh, h=h, cs=cs: e.matmul(
                            out=pb[6][:, hh * 64:(hh + 1) * 64],
                            lhsT=hs_ap(33, 3 + cs, 3 + cs + CH)[0:64, :],
                            rhs=R_bf[:, h * 64:(h + 1) * 64], start=True, stop=True),
                            inc=("pe" if hh == 3 else None))
                vpe_y = p.val("pe")
                p.wait("vector", "pe", vpe_y)
                for hh in range(4):
                    h = h0 + hh
                    p.op("vector", lambda e, h=h: e.tensor_scalar(
                        out=Bd[:, 0:N], in0=Btok[:, 0:N], scalar1=decT[:, h:h + 1], scalar2=None, op0=ALU.mult), inc="dve")
                    vb = p.val("dve")
                    p.wait("tensor", "dve", vb)
                    p.op("tensor", lambda e, hh=hh, h=h: e.matmul(
                        out=pb[7][0:N, hh * 64:(hh + 1) * 64], lhsT=Bd[:, 0:N],
                        rhs=xdt[:, h * 64:(h + 1) * 64], start=True, stop=True), inc="pe")
                    vS = p.val("pe")
                    p.op("vector", lambda e, h=h, hh=hh, slot_x=slot_x: e.scalar_tensor_tensor(
                        out=ysc[:, :], in0=rg(slot_x)[:, h * 64:(h + 1) * 64], scalar=Dbc[:, h:h + 1],
                        in1=pb[5][:, hh * 64:(hh + 1) * 64], op0=ALU.mult, op1=ALU.add), inc="dve")
                    p.op("vector", lambda e, h=h, hh=hh, slot_y=slot_y: e.scalar_tensor_tensor(
                        out=rg(slot_y)[:, h * 64:(h + 1) * 64], in0=pb[6][:, hh * 64:(hh + 1) * 64],
                        scalar=sdecT[:, h:h + 1], in1=ysc[:, :], op0=ALU.mult, op1=ALU.add), inc="dve")
                    p.wait("vector", "pe", vS)
                    p.op("vector", lambda e, h=h: e.tensor_scalar(
                        out=R[0:N, h * 64:(h + 1) * 64], in0=R[0:N, h * 64:(h + 1) * 64],
                        scalar1=gambc[0:N, h:h + 1], scalar2=None, op0=ALU.mult), inc="dve")
                    p.op("vector", lambda e, h=h, hh=hh: e.tensor_add(
                        out=R[0:N, h * 64:(h + 1) * 64], in0=R[0:N, h * 64:(h + 1) * 64],
                        in1=pb[7][0:N, hh * 64:(hh + 1) * 64]), inc="dve")
                vd_grp = p.val("dve")
                p.wait("tensor", "dve", vd_grp)

            if not full:
                for g in range(16):
                    h0 = 4 * g
                    for hh in range(4):
                        h = h0 + hh
                        p.op("vector", lambda e, h=h: e.tensor_scalar(
                            out=Bd[:, 0:N], in0=Btok[:, 0:N], scalar1=decT[:, h:h + 1], scalar2=None, op0=ALU.mult), inc="dve")
                        vb = p.val("dve")
                        p.wait("tensor", "dve", vb)
                        p.op("tensor", lambda e, hh=hh, h=h: e.matmul(
                            out=pb[7][0:N, hh * 64:(hh + 1) * 64], lhsT=Bd[:, 0:N],
                            rhs=xdt[:, h * 64:(h + 1) * 64], start=True, stop=True), inc="pe")
                        vS = p.val("pe")
                        p.wait("vector", "pe", vS)
                        p.op("vector", lambda e, h=h: e.tensor_scalar(
                            out=R[0:N, h * 64:(h + 1) * 64], in0=R[0:N, h * 64:(h + 1) * 64],
                            scalar1=gambc[0:N, h:h + 1], scalar2=None, op0=ALU.mult), inc="dve")
                        p.op("vector", lambda e, h=h, hh=hh: e.tensor_add(
                            out=R[0:N, h * 64:(h + 1) * 64], in0=R[0:N, h * 64:(h + 1) * 64],
                            in1=pb[7][0:N, hh * 64:(hh + 1) * 64]), inc="dve")
                    vd_grp = p.val("dve")
                    p.wait("tensor", "dve", vd_grp)

        if not full:
            continue

        for gid, grp in enumerate(gate_groups):
            m0 = grp[0]
            ncol = 128 * len(grp)

            def body(k, buf, grp=grp):
                for mi, mt in enumerate(grp):
                    p.op("tensor", lambda e, k=k, mi=mi, buf=buf: e.matmul(
                        out=pb[mi][:, :], lhsT=buf[:, mi * 128:mi * 128 + 128], rhs=uT(k, 0, TB),
                        start=(k == 0), stop=(k == 15)), inc=("pe" if mi == len(grp) - 1 else None))

            stream_mm(p, 16, wb2, ncol,
                      lambda kk, m0=m0, ncol=ncol: io["w_in_slc2"](kk, m0 * 128, ncol), body)
            vpe = p.val("pe")
            p.wait("scalar", "pe", vpe)
            for mi, mt in enumerate(grp):
                p.op("scalar", lambda e, mi=mi: e.activation(
                    out=gate_s[:, mi * TB:(mi + 1) * TB], in_=pb[mi][:, :], func=AF.Silu), inc="act")
            va = p.val("act")
            p.wait("tensor", "act", va)
            for mi, mt in enumerate(grp):
                for cc in range(NCH):
                    if cc >= 2:
                        p.wait("tensor", "act", p.val("act"))
                    p.op("tensor", lambda e, mi=mi, cc=cc: e.transpose(
                        out=pbb[4 + cc % 2][:, 0:128], in_=gate_s[:, mi * TB + cc * 128: mi * TB + (cc + 1) * 128],
                        identity=idb[:, :]), inc="pe")
                    v = p.val("pe")
                    p.wait("scalar", "pe", v)
                    p.op("scalar", lambda e, cc=cc: e.copy(
                        out=gtok[:, cc * 128:(cc + 1) * 128], in_=pbb[4 + cc % 2][:, 0:128]), inc="act")
                va2 = p.val("act")
                p.wait("vector", "act", va2)
                for cc in range(NCH):
                    sy = (tb * NCH + cc) % 5
                    p.op("vector", lambda e, mt=mt, cc=cc, sy=sy: e.tensor_mul(
                        out=rg(sy)[:, mt * 128:(mt + 1) * 128], in0=rg(sy)[:, mt * 128:(mt + 1) * 128],
                        in1=gtok[:, cc * 128:(cc + 1) * 128]), inc="dve")
                vd = p.val("dve")
                p.wait("scalar", "dve", vd)
                p.wait("tensor", "dve", vd)

        for c in range(NCH):
            sy = (tb * NCH + c) % 5
            sd = (tb * NCH + c + 4) % 5
            p.op("scalar", lambda e, sy=sy, sd=sd: e.activation(out=rg(sd)[:, :], in_=rg(sy)[:, :], func=AF.Square), inc="act")
            p.wait("vector", "act", p.val("act"))
            p.op("vector", lambda e, sd=sd, c=c: e.tensor_reduce(out=rms[:, c:c + 1], in_=rg(sd)[:, :], axis=AXX, op=ALU.add), inc="dve")
            p.op("vector", lambda e, c=c: e.tensor_scalar(out=rms[:, c:c + 1], in0=rms[:, c:c + 1], scalar1=1.0 / 4096, scalar2=None, op0=ALU.mult), inc="dve")
            vd = p.val("dve")
            p.wait("scalar", "dve", vd)
            p.op("scalar", lambda e, c=c: e.activation(out=rms[:, c:c + 1], in_=rms[:, c:c + 1], func=AF.Ln, bias=EPS), inc="act")
            p.op("scalar", lambda e, c=c: e.activation(out=rms[:, c:c + 1], in_=rms[:, c:c + 1], func=AF.Exp, scale=-0.5), inc="act")
            va = p.val("act")
            p.wait("vector", "act", va)
            p.op("vector", lambda e, sy=sy, sd=sd, c=c: e.scalar_tensor_tensor(
                out=rg(sd)[:, :], in0=rg(sy)[:, :], scalar=rms[:, c:c + 1], in1=nw_bc[:, :],
                op0=ALU.mult, op1=ALU.mult), inc="dve")
            vd = p.val("dve")
            p.wait("tensor", "dve", vd)
            for jg in range(8):
                for j4 in range(4):
                    j = jg * 4 + j4
                    p.op("tensor", lambda e, j=j, j4=j4, sd=sd: e.transpose(
                        out=pbb[j4][:, 0:128], in_=rg(sd)[:, j * 128:(j + 1) * 128],
                        identity=idb[:, :]), inc=("pe" if j4 == 3 else None))
                v = p.val("pe")
                p.wait("scalar", "pe", v)
                for j4 in range(4):
                    j = jg * 4 + j4
                    p.op("scalar", lambda e, j=j, j4=j4, c=c: e.copy(
                        out=yT_ap(j, c * 128, (c + 1) * 128), in_=pbb[j4][:, 0:128]), inc="act")
                p.wait("tensor", "act", p.val("act"))

        va_yT = p.val("act")
        p.wait("tensor", "act", va_yT)
        evac_h = {0: 0, 1: 0}
        for dchunk in range(4):
            pbase = 4 * (dchunk % 2)
            p.wait("tensor", "act", evac_h[dchunk % 2])

            def body(k, buf, pbase=pbase):
                for tt in range(NTT):
                    p.op("tensor", lambda e, k=k, tt=tt, buf=buf, pbase=pbase: e.matmul(
                        out=pb[pbase + tt][:, :], lhsT=yT_ap(k, tt * 128, (tt + 1) * 128),
                        rhs=buf[:, :], start=(k == 0), stop=(k == 31)), inc=("pe" if tt == NTT - 1 else None))

            stream_mm(p, 32, wr2, 512,
                      lambda kk, dchunk=dchunk: io["w_out_slc2"](kk, dchunk), body)
            vpe = p.val("pe")
            p.wait("scalar", "pe", vpe)
            for tt in range(NTT):
                p.op("scalar", lambda e, tt=tt, dchunk=dchunk, pbase=pbase: e.copy(
                    out=outb[:, tt * D + dchunk * 512: tt * D + (dchunk + 1) * 512], in_=pb[pbase + tt][:, :]), inc="act")
            evac_h[dchunk % 2] = p.val("act")
        p.wait("tensor", "act", p.val("act"))
        va_out = p.val("act")
        p.wait("vector", "act", va_out)
        for tt in range(NTT):
            p.wait("gpsimd", "dve", p.val("dve"))
            vx = p.dma("gpsimd", xtl[:, :], io["x_in"][t0 + tt * 128: t0 + (tt + 1) * 128, :], inc=QA)
            p.op("vector", lambda e, tt=tt: e.tensor_reduce(out=lnS[:, 0:1], in_=outb[:, tt * D:(tt + 1) * D], axis=AXX, op=ALU.add), inc="dve")
            p.op("scalar", lambda e, tt=tt: e.activation(out=sq2[:, :], in_=outb[:, tt * D:(tt + 1) * D], func=AF.Square), inc="act")
            p.wait("vector", "act", p.val("act"))
            p.op("vector", lambda e: e.tensor_reduce(out=lnS[:, 1:2], in_=sq2[:, :], axis=AXX, op=ALU.add), inc="dve")
            p.op("vector", lambda e: e.tensor_scalar(out=lnS[:, 1:2], in0=lnS[:, 1:2], scalar1=1.0 / D, scalar2=None, op0=ALU.mult), inc="dve")
            p.op("vector", lambda e: e.tensor_scalar(out=lnS[:, 2:3], in0=lnS[:, 0:1], scalar1=1.0 / D, scalar2=None, op0=ALU.mult), inc="dve")
            p.op("vector", lambda e: e.tensor_scalar(out=lnS[:, 4:5], in0=lnS[:, 2:3], scalar1=lnS[:, 2:3], scalar2=None, op0=ALU.mult), inc="dve")
            p.op("vector", lambda e: e.tensor_sub(out=lnS[:, 4:5], in0=lnS[:, 1:2], in1=lnS[:, 4:5]), inc="dve")
            vd = p.val("dve")
            p.wait("scalar", "dve", vd)
            p.op("scalar", lambda e: e.activation(out=lnS[:, 5:6], in_=lnS[:, 4:5], func=AF.Ln, bias=EPS), inc="act")
            p.op("scalar", lambda e: e.activation(out=lnS[:, 6:7], in_=lnS[:, 5:6], func=AF.Exp, scale=-0.5), inc="act")
            va = p.val("act")
            p.wait("vector", "act", va)
            p.wait("vector", QA, vx)
            p.op("vector", lambda e, tt=tt: e.tensor_scalar(
                out=outb[:, tt * D:(tt + 1) * D],
                in0=outb[:, tt * D:(tt + 1) * D],
                scalar1=lnS[:, 2:3], scalar2=lnS[:, 6:7], op0=ALU.subtract, op1=ALU.mult), inc="dve")
            p.op("vector", lambda e, tt=tt: e.tensor_add(out=xtl[:, :], in0=xtl[:, :], in1=outb[:, tt * D:(tt + 1) * D]), inc="dve")
            vd = p.val("dve")
            p.wait("gpsimd", "dve", vd)
            p.dma("gpsimd", io["x_out"][t0 + tt * 128:t0 + (tt + 1) * 128, :], xtl[:, :], inc=QA)
            p.wait("gpsimd", QA, p.val(QA))

    if mode == "state":
        vd = p.val("dve")
        p.wait("scalar", "dve", vd)
        p.op("scalar", lambda e: e.activation(out=gsum[:, 0:1], in_=gsum[:, 0:1], func=AF.Exp), inc="act")
        va = p.val("act")
        p.wait("gpsimd", "act", va)
        p.wait("gpsimd", "dve", vd)
        p.dma("gpsimd", io["out_S"], R[0:N, :], inc=QA)
        p.dma("gpsimd", io["out_g"], gsum[:, 0:1], inc=QA)
        p.dma("gpsimd", io["out_halo"], halo_sv[:, :], inc=QA)
        p.wait("gpsimd", QA, p.val(QA))
    p.wait("gpsimd", QW, p.val(QW))



def l0_win_slc(w):
    def f(k, c0, ncol):
        return w[k * 128:(k + 1) * 128, c0: c0 + ncol]
    return f


def l0_win_slc2(w):
    def f(kk, c0, ncol):
        return w[kk * 256:(kk + 1) * 256, c0: c0 + ncol].rearrange("(i p) c -> p i c", p=128)
    return f


def l1_win_slc2(w):
    def f(kk, c0, ncol):
        r = c0 // 1152
        pl = c0 - r * 1152
        return w[2048 * r + kk * 256: 2048 * r + (kk + 1) * 256, pl: pl + ncol].rearrange("(i p) c -> p i c", p=128)
    return f


def l0_wout_slc2(w):
    def f(kk, dchunk):
        return w[kk * 256:(kk + 1) * 256, dchunk * 512:(dchunk + 1) * 512].rearrange("(i p) c -> p i c", p=128)
    return f


def l0_wout_slc(w):
    def f(k, dchunk):
        return w[k * 128:(k + 1) * 128, dchunk * 512:(dchunk + 1) * 512]
    return f


def l1_win_slc(w):
    def f(k, c0, ncol):
        r = c0 // 1152
        pl = c0 - r * 1152
        return w[2048 * r + k * 128: 2048 * r + (k + 1) * 128, pl: pl + ncol]
    return f



def _bf16():
    import ml_dtypes
    return ml_dtypes.bfloat16





def _mk_common_io(nc, cfg, pref=""):
    CT = cfg.conv_tiles
    io = {}
    io["conv_w"] = nc.dram_tensor(pref + "conv_w", [cfg.CONV, 4], F32, kind="ExternalInput").ap()
    io["conv_b"] = nc.dram_tensor(pref + "conv_b", [cfg.CONV], F32, kind="ExternalInput").ap()
    io["dt_bias"] = nc.dram_tensor(pref + "dt_bias", [64], F32, kind="ExternalInput").ap()
    io["A_log"] = nc.dram_tensor(pref + "A_log", [64], F32, kind="ExternalInput").ap()
    io["Dv"] = nc.dram_tensor(pref + "Dv", [64], F32, kind="ExternalInput").ap()
    io["norm_w"] = nc.dram_tensor(pref + "norm_w", [4096], F32, kind="ExternalInput").ap()
    return io


def run_mamba_unfused(layer, x_full, w_in_bf, w_out_bf, conv_w, conv_b, dt_bias, A_log, Dv, norm_w):
    cfg = MambaCfg(layer)
    CT = cfg.conv_tiles
    N = cfg.N
    win_shape = list(w_in_bf.shape)

    nc = bass.Bass("TRN2", target_bir_lowering=False)
    p = Prog(nc)
    io = _mk_common_io(nc, cfg)
    io["tag"] = "s"
    io["x_in"] = nc.dram_tensor("x_in", [T, D], F32, kind="ExternalInput").ap()
    io["x_halo"] = nc.dram_tensor("x_halo", [3, D], F32, kind="ExternalInput").ap()
    w_in = nc.dram_tensor("w_in", win_shape, BF16, kind="ExternalInput").ap()
    io["w_in_slc"] = l0_win_slc(w_in) if layer == 0 else l1_win_slc(w_in)
    io["w_in_slc2"] = l0_win_slc2(w_in) if layer == 0 else l1_win_slc2(w_in)
    io["acs_dram"] = nc.dram_tensor("acs_dram", [64, CH], F32, kind="Internal")
    io["alast_dram"] = nc.dram_tensor("alast_dram", [64], F32, kind="Internal")
    io["out_S"] = nc.dram_tensor("out_S", [N, 4096], F32, kind="ExternalOutput").ap()
    io["out_g"] = nc.dram_tensor("out_g", [64, 1], F32, kind="ExternalOutput").ap()
    io["out_halo"] = nc.dram_tensor("out_halo", [128, 34 * 3], BF16, kind="ExternalOutput").ap()
    idb = p.sbuf("idb", [128, 128], BF16)
    idf = p.sbuf("idf", [128, 128], F32)
    emit_idents(p, idb, idf)
    emit_mamba(p, cfg, io, idb, idf, "state")
    p.realize()

    in_maps = []
    for c in range(NCORES):
        b, q = c // 4, c % 4
        xh = np.zeros((3, D), np.float32) if q == 0 else np.ascontiguousarray(x_full[b, q * 1024 - 3:q * 1024])
        in_maps.append({
            "x_in": np.ascontiguousarray(x_full[b, q * 1024:(q + 1) * 1024]),
            "x_halo": xh, "w_in": w_in_bf,
            "conv_w": conv_w, "conv_b": conv_b, "dt_bias": dt_bias,
            "A_log": A_log, "Dv": Dv, "norm_w": norm_w,
        })
    res = run_bass_kernel_spmd(nc, in_maps, core_ids=list(range(NCORES)))
    Ss = [res.results[c]["out_S"] for c in range(NCORES)]
    gs = [res.results[c]["out_g"] for c in range(NCORES)]
    halos = [res.results[c]["out_halo"] for c in range(NCORES)]

    nc = bass.Bass("TRN2", target_bir_lowering=False)
    p = Prog(nc)
    io = _mk_common_io(nc, cfg)
    io["tag"] = "f"
    io["x_in"] = nc.dram_tensor("x_in", [T, D], F32, kind="ExternalInput").ap()
    w_in = nc.dram_tensor("w_in", win_shape, BF16, kind="ExternalInput").ap()
    w_out = nc.dram_tensor("w_out", [4096, D], BF16, kind="ExternalInput").ap()
    io["w_in_slc"] = l0_win_slc(w_in) if layer == 0 else l1_win_slc(w_in)
    io["w_in_slc2"] = l0_win_slc2(w_in) if layer == 0 else l1_win_slc2(w_in)
    io["w_out_slc"] = l0_wout_slc(w_out)
    io["w_out_slc2"] = l0_wout_slc2(w_out)
    io["S_in"] = [nc.dram_tensor(f"S_in{j}", [N, 4096], F32, kind="ExternalInput").ap() for j in range(3)]
    io["g_in"] = nc.dram_tensor("g_in", [1, 192], F32, kind="ExternalInput").ap()
    io["halo_in"] = nc.dram_tensor("halo_in", [128, 34 * 3], BF16, kind="ExternalInput").ap()
    io["mrow"] = nc.dram_tensor("mrow", [1, 3], F32, kind="ExternalInput").ap()
    io["mcol"] = nc.dram_tensor("mcol", [128, 4], F32, kind="ExternalInput").ap()
    io["x_out"] = nc.dram_tensor("x_out", [T, D], F32, kind="ExternalOutput").ap()
    io["acs_dram"] = nc.dram_tensor("acs_dram", [64, CH], F32, kind="Internal")
    io["alast_dram"] = nc.dram_tensor("alast_dram", [64], F32, kind="Internal")
    idb = p.sbuf("idb", [128, 128], BF16)
    idf = p.sbuf("idf", [128, 128], F32)
    emit_idents(p, idb, idf)
    emit_mamba(p, cfg, io, idb, idf, "full")
    p.realize()

    bf = _bf16()
    in_maps = []
    for c in range(NCORES):
        b, q = c // 4, c % 4
        m = {
            "x_in": np.ascontiguousarray(x_full[b, q * 1024:(q + 1) * 1024]),
            "w_in": w_in_bf, "w_out": w_out_bf,
            "conv_w": conv_w, "conv_b": conv_b, "dt_bias": dt_bias,
            "A_log": A_log, "Dv": Dv, "norm_w": norm_w,
            "g_in": np.concatenate([gs[b * 4 + j].reshape(1, 64) for j in range(3)], axis=1).astype(np.float32),
            "halo_in": halos[b * 4 + q - 1] if q > 0 else np.zeros((128, 34 * 3), bf),
            "mrow": np.array([[1.0 if j < q else 0.0 for j in range(3)]], np.float32),
            "mcol": np.tile(np.array([[1.0 if j < q else 0.0 for j in range(3)] + [1.0 if q > 0 else 0.0]], np.float32), (128, 1)),
        }
        for j in range(3):
            m[f"S_in{j}"] = Ss[b * 4 + j]
        in_maps.append(m)
    res = run_bass_kernel_spmd(nc, in_maps, core_ids=list(range(NCORES)))
    out = np.empty((2, 4096, D), np.float32)
    for c in range(NCORES):
        b, q = c // 4, c % 4
        out[b, q * 1024:(q + 1) * 1024] = res.results[c]["x_out"]
    return out


def kernel(**inputs):
    inp = {k: np.asarray(v) for k, v in inputs.items()}
    return run_fused(inp).astype(np.float32)


def kernel_unfused(**inputs):
    inp = {k: np.asarray(v) for k, v in inputs.items()}
    x = np.ascontiguousarray(inp["L0_attn_x"].astype(np.float32))

    prep = run_prep_unfused(inp)

    x1 = run_mamba_unfused(0, x, prep["W0in"], prep["W0out"],
                           inp["L0_attn_conv_w"], inp["L0_attn_conv_b"], inp["L0_attn_dt_bias"],
                           inp["L0_attn_A_log"], inp["L0_attn_D"], inp["L0_attn_norm_w"])
    x2 = run_ffn0_unfused(x1, prep["GgT"], prep["GuT"], prep["GdT"])
    x3 = run_mamba_unfused(1, x2, prep["W1inT"], prep["W1outT"],
                           inp["L1_attn_conv_w"], inp["L1_attn_conv_b"], inp["L1_attn_dt_bias"],
                           inp["L1_attn_A_log"], inp["L1_attn_D"], inp["L1_attn_norm_w"])
    x4 = run_moe_unfused(x3, prep["Wgate"], prep["Wsup"], prep["Wsdn"], prep["Mup"], prep["Mdn"])
    return x4.astype(np.float32)


def kernel_fused(**inputs):
    inp = {k: np.asarray(v) for k, v in inputs.items()}
    return run_fused(inp).astype(np.float32)




def emit_prep(p, io, idb, idf, sfx=""):
    pk = p.sbuf("pk_pp" + sfx, [128, 1024], I32)
    xi = p.sbuf("xi_pp" + sfx, [128, 1024], I32)
    xf = p.sbuf("xf_pp" + sfx, [128, 1024], BF16)
    t1 = p.sbuf("t1_pp" + sfx, [128, 1024], BF16)
    mm_ = p.sbuf("mm_pp" + sfx, [128, 1024], BF16)
    r1 = p.sbuf("r1_pp" + sfx, [128, 1024], BF16)
    r2 = p.sbuf("r2_pp" + sfx, [128, 1024], BF16)
    vv = p.sbuf("vv_pp" + sfx, [128, 1024], BF16)
    sg = p.sbuf("sg_pp" + sfx, [128, 1024], BF16)
    sc = p.sbuf("sc_pp" + sfx, [128, 64], F32)
    nib2 = p.sbuf("nib2_pp" + sfx, [128, 2048], BF16)
    bufB = p.sbuf("bufB_pp" + sfx, [128, 8512], BF16)
    stg = p.sbuf("stg_pp" + sfx, [128, 512], BF16)
    pbt = [p.psum(f"pbt{i}_pp" + sfx, [128, 512], F32) for i in range(2)]
    pbtb = [t.ap().bitcast(BF16) for t in pbt]

    if not getattr(p, "_consts2_done", False):
        p._consts2_done = True
        cst2 = p.sbuf("cstb_pp" + sfx, [128, 2], F32)
        p.op("vector", lambda e: e.memset(cst2[:, 0:1], -2.0))
        p.op("vector", lambda e: e.memset(cst2[:, 1:2], -6.0), inc="dve")
        p.nc.const_aps.register(F32, -2.0, cst2[:, 0:1])
        p.nc.const_aps.register(F32, -6.0, cst2[:, 1:2])
        p.wait("scalar", "dve", p.val("dve"))

    if io["casts"]:
        bufB2 = p.sbuf("bufB2_pp" + sfx, [128, 8512], BF16)
        cbufs = (bufB, bufB2)
        vst = {}
        ti = 0
        for (src, dst, rows, cols) in io["casts"]:
            for i0 in range(0, rows, 128):
                n = min(128, rows - i0)
                par = ti % 2
                if ti >= 2:
                    p.wait("gpsimd", "qw", vst[ti - 2])
                vin = p.dma("gpsimd", cbufs[par][0:n, 0:cols], src[i0:i0 + n, :], inc="qa")
                p.wait("sync", "qa", vin)
                vst[ti] = p.dma("sync", dst[i0:i0 + n, :], cbufs[par][0:n, 0:cols], inc="qw")
                ti += 1
        p.wait("gpsimd", "qa", p.val("qa"))
        p.wait("gpsimd", "qw", p.val("qw"))

    def transpose_out(src_sbuf, dst, ncb, i0, nr_valid=128):
        for jb0 in range(0, ncb, 4):
            nb = min(4, ncb - jb0)
            for q in range(nb):
                p.op("tensor", lambda e, jb0=jb0, q=q: e.transpose(
                    out=pbtb[0][:, q * 128:(q + 1) * 128], in_=src_sbuf[:, (jb0 + q) * 128:(jb0 + q + 1) * 128],
                    identity=idb[:, :]), inc=("pe" if q == nb - 1 else None))
            v2 = p.val("pe")
            p.wait("scalar", "pe", v2)
            p.op("scalar", lambda e, nb=nb: e.copy(out=stg[:, 0:nb * 128], in_=pbtb[0][:, 0:nb * 128]), inc="act")
            va = p.val("act")
            p.wait("gpsimd", "act", va)
            p.dma("gpsimd",
                  dst[jb0 * 128:(jb0 + nb) * 128, i0:i0 + 128].rearrange("(q c) r -> c q r", q=nb),
                  stg.ap()[:, 0:nb * 128].rearrange("c (q r) -> c q r", q=nb), inc="qa")
            p.wait("tensor", "qa", p.val("qa"))

    for (src, dst, rows, cols) in io["ctrans"]:
        ncb = cols // 128
        for i in range(0, rows, 128):
            v = p.dma("gpsimd", bufB[:, 0:cols], src[i:i + 128, :], inc="qa")
            p.wait("tensor", "qa", v)
            transpose_out(bufB, dst, ncb, i)

    for (packed, scales, dst) in io["decode"]:
        R = packed.shape[0]
        PC = packed.shape[1]
        for i in range(0, R, 128):
            v = p.dma("gpsimd", pk[:, 0:PC], packed[i:i + 128, :], inc="qa")
            v2 = p.dma("gpsimd", sc[:, 0:PC // 16], scales[i:i + 128, :], inc="qa")
            p.wait("vector", "qa", p.val("qa"))
            for off, shift in ((0, 0), (1, 4)):
                if shift:
                    p.op("vector", lambda e: e.tensor_scalar(out=xi[:, 0:PC], in0=pk[:, 0:PC], scalar1=4, scalar2=15, op0=ALU.logical_shift_right, op1=ALU.bitwise_and))
                else:
                    p.op("vector", lambda e: e.tensor_scalar(out=xi[:, 0:PC], in0=pk[:, 0:PC], scalar1=15, scalar2=None, op0=ALU.bitwise_and))
                p.op("vector", lambda e: e.tensor_copy(out=xf[:, 0:PC], in_=xi[:, 0:PC]))
                p.op("vector", lambda e: e.tensor_scalar(out=t1[:, 0:PC], in0=xf[:, 0:PC], scalar1=8.0, scalar2=-8.0, op0=ALU.is_ge, op1=ALU.mult))
                p.op("vector", lambda e: e.tensor_add(out=mm_[:, 0:PC], in0=xf[:, 0:PC], in1=t1[:, 0:PC]), inc="dve")
                vd = p.val("dve")
                p.wait("scalar", "dve", vd)
                p.op("scalar", lambda e: e.activation(out=r1[:, 0:PC], in_=mm_[:, 0:PC], func=AF.Relu, scale=0.5, bias=-2.0), inc="act")
                p.op("scalar", lambda e: e.activation(out=r2[:, 0:PC], in_=mm_[:, 0:PC], func=AF.Relu, bias=-6.0), inc="act")
                va = p.val("act")
                p.wait("vector", "act", va)
                p.op("vector", lambda e: e.scalar_tensor_tensor(out=vv[:, 0:PC], in0=mm_[:, 0:PC], scalar=0.5, in1=r1[:, 0:PC], op0=ALU.mult, op1=ALU.add))
                p.op("vector", lambda e: e.tensor_add(out=vv[:, 0:PC], in0=vv[:, 0:PC], in1=r2[:, 0:PC]))
                p.op("vector", lambda e: e.tensor_scalar(out=sg[:, 0:PC], in0=t1[:, 0:PC], scalar1=0.25, scalar2=1.0, op0=ALU.mult, op1=ALU.add))
                p.op("vector", lambda e: e.tensor_mul(out=vv[:, 0:PC], in0=vv[:, 0:PC], in1=sg[:, 0:PC]))
                p.op("vector", lambda e, off=off: e.tensor_tensor(
                    out=nib2.ap()[:, 0:2 * PC].rearrange("p (a b t) -> p a b t", b=16, t=2)[:, :, :, off],
                    in0=vv.ap()[:, 0:PC].rearrange("p (a b) -> p a b", b=16),
                    in1=sc[:, 0:PC // 16].unsqueeze(2).to_broadcast([128, PC // 16, 16]),
                    op=ALU.mult), inc="dve")
            vd = p.val("dve")
            p.wait("tensor", "dve", vd)
            transpose_out(nib2, dst, (2 * PC) // 128, i)
    p.wait("gpsimd", "qa", p.val("qa"))


def emit_ffn0(p, io, idb, idf):
    sfx = "_f0"
    uTb = p.sbuf("uT" + sfx, [128, 16 * T], BF16)
    uT = lambda k, a, b: uTb[:, k * T + a: k * T + b]
    inter = p.sbuf("inter" + sfx, [128, 16 * T], BF16)
    dacc = p.sbuf("dacc" + sfx, [128, 8 * D], F32)
    gbuf = p.sbuf("gbuf" + sfx, [128, 3 * T], BF16)
    s1 = p.sbuf("s1" + sfx, [128, T], BF16)
    s2 = p.sbuf("s2" + sfx, [128, T], BF16)
    lnS = p.sbuf("lnS" + sfx, [128, 8], F32)
    xtl = p.sbuf("xtl" + sfx, [128, D], F32)
    utok = p.sbuf("utok" + sfx, [128, D], BF16)
    sq2 = p.sbuf("sq2" + sfx, [128, D], BF16)
    wbf2 = [p.sbuf(f"wbf{i}" + sfx, [128, 768], BF16) for i in range(2)]
    wrf2 = [p.sbuf(f"wrf{i}" + sfx, [128, 1024], BF16) for i in range(2)]
    pb = [p.psum(f"pb{i}" + sfx, [128, 512], F32) for i in range(8)]
    pbb = [t.ap().bitcast(BF16) for t in pb]

    if not getattr(p, "_consts_done", False):
        p._consts_done = True
        cstt = p.sbuf("cstt" + sfx, [128, 2], F32)
        p.op("vector", lambda e: e.memset(cstt[:, 0:1], EPS), inc="dve")
        p.op("vector", lambda e: e.memset(cstt[:, 1:2], 1.0), inc="dve")
        p.nc.const_aps.register(F32, EPS, cstt[:, 0:1])
        p.nc.const_aps.register(F32, 1.0, cstt[:, 1:2])

    for it in range(8):
        p.wait("gpsimd", "dve", p.val("dve"))
        p.wait("gpsimd", "act", p.val("act"))
        vx = p.dma("gpsimd", xtl[:, :], io["x_in"][it * 128:(it + 1) * 128, :], inc="qa")
        p.wait("vector", "qa", vx)
        p.op("vector", lambda e: e.tensor_reduce(out=lnS[:, 0:1], in_=xtl[:, :], axis=AXX, op=ALU.add), inc="dve")
        p.wait("scalar", "qa", vx)
        p.op("scalar", lambda e: e.activation(out=sq2[:, :], in_=xtl[:, :], func=AF.Square), inc="act")
        p.wait("vector", "act", p.val("act"))
        p.op("vector", lambda e: e.tensor_reduce(out=lnS[:, 1:2], in_=sq2[:, :], axis=AXX, op=ALU.add), inc="dve")
        p.op("vector", lambda e: e.tensor_scalar(out=lnS[:, 1:2], in0=lnS[:, 1:2], scalar1=1.0 / D, scalar2=None, op0=ALU.mult), inc="dve")
        p.op("vector", lambda e: e.tensor_scalar(out=lnS[:, 2:3], in0=lnS[:, 0:1], scalar1=1.0 / D, scalar2=None, op0=ALU.mult), inc="dve")
        p.op("vector", lambda e: e.tensor_scalar(out=lnS[:, 4:5], in0=lnS[:, 2:3], scalar1=lnS[:, 2:3], scalar2=None, op0=ALU.mult), inc="dve")
        p.op("vector", lambda e: e.tensor_sub(out=lnS[:, 4:5], in0=lnS[:, 1:2], in1=lnS[:, 4:5]), inc="dve")
        vd = p.val("dve")
        p.wait("scalar", "dve", vd)
        p.op("scalar", lambda e: e.activation(out=lnS[:, 5:6], in_=lnS[:, 4:5], func=AF.Ln, bias=EPS), inc="act")
        p.op("scalar", lambda e: e.activation(out=lnS[:, 6:7], in_=lnS[:, 5:6], func=AF.Exp, scale=-0.5), inc="act")
        va = p.val("act")
        p.wait("vector", "act", va)
        p.op("vector", lambda e: e.tensor_scalar(out=utok[:, :], in0=xtl[:, :], scalar1=lnS[:, 2:3], scalar2=lnS[:, 6:7], op0=ALU.subtract, op1=ALU.mult), inc="dve")
        vd = p.val("dve")
        p.wait("tensor", "dve", vd)
        for j in range(16):
            p.op("tensor", lambda e, j=j: e.transpose(
                out=pbb[j % 4][:, 0:128], in_=utok[:, j * 128:(j + 1) * 128],
                identity=idb[:, :]), inc=("pe" if j % 4 == 3 else None))
            if j % 4 == 3:
                v = p.val("pe")
                p.wait("scalar", "pe", v)
                for jj in range(j - 3, j + 1):
                    p.op("scalar", lambda e, jj=jj, it=it: e.copy(
                        out=uT(jj, it * 128, it * 128 + 128), in_=pbb[jj % 4][:, 0:128]), inc="act")
                p.wait("tensor", "act", p.val("act"))

    C1, C2 = 0.7978845608028654, 0.044715
    for s in range(8):
        groups = [list(range(i, min(i + 3, 16))) for i in range(0, 16, 3)]
        for grp in groups:
            ncol = 128 * len(grp)
            for which, wsrc in (("g", io["g_gT"]), ("u", io["g_uT"])):
                slc2 = lambda kk, wsrc=wsrc, s=s, grp=grp, ncol=ncol: wsrc[
                    2048 * s + 256 * kk: 2048 * s + 256 * (kk + 1),
                    128 * grp[0]:128 * grp[0] + ncol].rearrange("(i p) c -> p i c", p=128)

                def fbody(k, buf, grp=grp):
                    for mi, m in enumerate(grp):
                        for nf in range(2):
                            p.op("tensor", lambda e, k=k, mi=mi, nf=nf, buf=buf: e.matmul(
                                out=pb[2 * mi + nf][:, :],
                                lhsT=buf[:, mi * 128:(mi + 1) * 128], rhs=uT(k, nf * 512, (nf + 1) * 512),
                                start=(k == 0), stop=(k == 15)), inc=("pe" if (mi == len(grp) - 1 and nf == 1) else None))
                stream_mm(p, 16, wbf2, ncol, slc2, fbody)
                if which == "g":
                    vpe = p.val("pe")
                    p.wait("scalar", "pe", vpe)
                    for mi, m in enumerate(grp):
                        for nf in range(2):
                            p.op("scalar", lambda e, mi=mi, nf=nf: e.copy(
                                out=gbuf[:, mi * T + nf * 512: mi * T + (nf + 1) * 512], in_=pb[2 * mi + nf][:, :]), inc="act")
                    p.wait("tensor", "act", p.val("act"))
            vpe = p.val("pe")
            p.wait("scalar", "pe", vpe)
            p.wait("vector", "pe", vpe)
            for mi, m in enumerate(grp):
                gm = lambda mi=mi: gbuf[:, mi * T:(mi + 1) * T]
                p.op("scalar", lambda e, mi=mi: e.activation(out=s1[:, :], in_=gbuf[:, mi * T:(mi + 1) * T], func=AF.Square), inc="act")
                va = p.val("act")
                p.wait("vector", "act", va)
                p.op("vector", lambda e: e.tensor_scalar(out=s1[:, :], in0=s1[:, :], scalar1=C2, scalar2=1.0, op0=ALU.mult, op1=ALU.add), inc="dve")
                p.op("vector", lambda e, mi=mi: e.tensor_mul(out=s1[:, :], in0=s1[:, :], in1=gbuf[:, mi * T:(mi + 1) * T]), inc="dve")
                vd = p.val("dve")
                p.wait("scalar", "dve", vd)
                p.op("scalar", lambda e: e.activation(out=s2[:, :], in_=s1[:, :], func=AF.Tanh, scale=C1), inc="act")
                va = p.val("act")
                p.wait("vector", "act", va)
                p.op("vector", lambda e: e.tensor_scalar(out=s2[:, :], in0=s2[:, :], scalar1=1.0, scalar2=0.5, op0=ALU.add, op1=ALU.mult), inc="dve")
                p.op("vector", lambda e, mi=mi: e.tensor_mul(out=s2[:, :], in0=s2[:, :], in1=gbuf[:, mi * T:(mi + 1) * T]), inc="dve")
                for nf in range(2):
                    p.op("vector", lambda e, m=m, mi=mi, nf=nf: e.tensor_tensor(
                        out=inter[:, m * T + nf * 512: m * T + (nf + 1) * 512],
                        in0=s2[:, nf * 512:(nf + 1) * 512], in1=pb[2 * mi + nf][:, :], op=ALU.mult), inc="dve")
            vd = p.val("dve")
            p.wait("tensor", "dve", vd)
        for dchunk in range(4):
            slc2 = lambda kk, s=s, dchunk=dchunk: io["g_dT"][
                2048 * s + 256 * kk: 2048 * s + 256 * (kk + 1),
                512 * dchunk:512 * (dchunk + 1)].rearrange("(i p) c -> p i c", p=128)

            def dbody(k, buf):
                for tt in range(8):
                    p.op("tensor", lambda e, k=k, tt=tt, buf=buf: e.matmul(
                        out=pb[tt][:, :], lhsT=inter[:, k * T + tt * 128: k * T + (tt + 1) * 128],
                        rhs=buf, start=(k == 0), stop=(k == 15)), inc=("pe" if tt == 7 else None))
            stream_mm(p, 16, wrf2, 512, slc2, dbody)
            vpe = p.val("pe")
            p.wait("vector", "pe", vpe)
            for tt in range(8):
                if s == 0:
                    p.op("vector", lambda e, tt=tt, dchunk=dchunk: e.tensor_copy(
                        out=dacc[:, tt * D + dchunk * 512: tt * D + (dchunk + 1) * 512], in_=pb[tt][:, :]), inc="dve")
                else:
                    p.op("vector", lambda e, tt=tt, dchunk=dchunk: e.tensor_add(
                        out=dacc[:, tt * D + dchunk * 512: tt * D + (dchunk + 1) * 512],
                        in0=dacc[:, tt * D + dchunk * 512: tt * D + (dchunk + 1) * 512], in1=pb[tt][:, :]), inc="dve")
            vd = p.val("dve")
            p.wait("tensor", "dve", vd)

    for tt in range(8):
        p.wait("gpsimd", "dve", p.val("dve"))
        vx = p.dma("gpsimd", xtl[:, :], io["x_in"][tt * 128:(tt + 1) * 128, :], inc="qa")
        p.op("vector", lambda e, tt=tt: e.tensor_reduce(out=lnS[:, 0:1], in_=dacc[:, tt * D:(tt + 1) * D], axis=AXX, op=ALU.add), inc="dve")
        p.op("scalar", lambda e, tt=tt: e.activation(out=sq2[:, :], in_=dacc[:, tt * D:(tt + 1) * D], func=AF.Square), inc="act")
        p.wait("vector", "act", p.val("act"))
        p.op("vector", lambda e: e.tensor_reduce(out=lnS[:, 1:2], in_=sq2[:, :], axis=AXX, op=ALU.add), inc="dve")
        p.op("vector", lambda e: e.tensor_scalar(out=lnS[:, 1:2], in0=lnS[:, 1:2], scalar1=1.0 / D, scalar2=None, op0=ALU.mult), inc="dve")
        p.op("vector", lambda e: e.tensor_scalar(out=lnS[:, 2:3], in0=lnS[:, 0:1], scalar1=1.0 / D, scalar2=None, op0=ALU.mult), inc="dve")
        p.op("vector", lambda e: e.tensor_scalar(out=lnS[:, 4:5], in0=lnS[:, 2:3], scalar1=lnS[:, 2:3], scalar2=None, op0=ALU.mult), inc="dve")
        p.op("vector", lambda e: e.tensor_sub(out=lnS[:, 4:5], in0=lnS[:, 1:2], in1=lnS[:, 4:5]), inc="dve")
        vd = p.val("dve")
        p.wait("scalar", "dve", vd)
        p.op("scalar", lambda e: e.activation(out=lnS[:, 5:6], in_=lnS[:, 4:5], func=AF.Ln, bias=EPS), inc="act")
        p.op("scalar", lambda e: e.activation(out=lnS[:, 6:7], in_=lnS[:, 5:6], func=AF.Exp, scale=-0.5), inc="act")
        va = p.val("act")
        p.wait("vector", "act", va)
        p.wait("vector", "qa", vx)
        p.op("vector", lambda e, tt=tt: e.tensor_scalar(
            out=dacc[:, tt * D:(tt + 1) * D], in0=dacc[:, tt * D:(tt + 1) * D],
            scalar1=lnS[:, 2:3], scalar2=lnS[:, 6:7], op0=ALU.subtract, op1=ALU.mult), inc="dve")
        p.op("vector", lambda e, tt=tt: e.tensor_add(out=xtl[:, :], in0=xtl[:, :], in1=dacc[:, tt * D:(tt + 1) * D]), inc="dve")
        vd = p.val("dve")
        p.wait("gpsimd", "dve", vd)
        p.dma("gpsimd", io["x_out"][tt * 128:(tt + 1) * 128, :], xtl[:, :], inc="qa")
        p.wait("gpsimd", "qa", p.val("qa"))
    p.wait("gpsimd", "qw", p.val("qw"))


def run_prep_unfused(inp):
    nc = bass.Bass("TRN2", target_bir_lowering=False)
    p = Prog(nc)

    def ein(name, shape, dtype=F32):
        return nc.dram_tensor(name, shape, dtype, kind="ExternalInput").ap()

    def eout(name, shape, dtype=BF16):
        return nc.dram_tensor(name, shape, dtype, kind="ExternalOutput").ap()

    W0in_s = ein("W0in_s", [256, 8512]); W0in_o = eout("W0in_o", [256, 8512])
    W0out_s = ein("W0out_s", [512, 2048]); W0out_o = eout("W0out_o", [512, 2048])
    Mup_s = ein("Mup_s", [16384, 512]); Mup_o = eout("Mup_o", [16384, 512])
    Mdn_s = ein("Mdn_s", [2048, 2048]); Mdn_o = eout("Mdn_o", [2048, 2048])
    Wsup_s = ein("Wsup_s", [256, 1024]); Wsup_o = eout("Wsup_o", [256, 1024])
    Wsdn_s = ein("Wsdn_s", [64, 2048]); Wsdn_o = eout("Wsdn_o", [64, 2048])
    Wgate_s = ein("Wgate_s", [256, 64]); Wgate_o = eout("Wgate_o", [256, 64])
    W1in_s = ein("W1in_s", [1152, 2048]); W1inT_o = eout("W1inT_o", [2048, 1152])
    W1out_s = ein("W1out_s", [2048, 512]); W1outT_o = eout("W1outT_o", [512, 2048])
    gP_s = ein("gP_s", [2048, 1024], I32); gS_s = ein("gS_s", [2048, 64]); gT_o = eout("gT_o", [2048, 2048])
    uP_s = ein("uP_s", [2048, 1024], I32); uS_s = ein("uS_s", [2048, 64]); uT_o = eout("uT_o", [2048, 2048])
    dP_s = ein("dP_s", [2048, 1024], I32); dS_s = ein("dS_s", [2048, 64]); dT_o = eout("dT_o", [2048, 2048])

    io = {
        "casts": [
            (W0in_s, W0in_o, 256, 8512), (W0out_s, W0out_o, 512, 2048),
            (Mup_s, Mup_o, 16384, 512), (Mdn_s, Mdn_o, 2048, 2048),
            (Wsup_s, Wsup_o, 256, 1024), (Wsdn_s, Wsdn_o, 64, 2048),
            (Wgate_s, Wgate_o, 256, 64),
        ],
        "ctrans": [(W1in_s, W1inT_o, 1152, 2048), (W1out_s, W1outT_o, 2048, 512)],
        "decode": [(gP_s, gS_s, gT_o), (uP_s, uS_s, uT_o), (dP_s, dS_s, dT_o)],
    }
    idb = p.sbuf("idb", [128, 128], BF16)
    idf = p.sbuf("idf", [128, 128], F32)
    emit_idents(p, idb, idf)
    emit_prep(p, io, idb, idf)
    p.realize()

    in_maps = []
    w1in_pad = np.zeros((9216, 2048), np.float32)
    w1in_pad[:8384] = inp["L1_attn_W_in"]
    mup_flat = np.ascontiguousarray(inp["L1_ffn_w_up"].reshape(131072, 512))
    mdn_flat = np.ascontiguousarray(inp["L1_ffn_w_down"].reshape(16384, 2048))
    for c in range(NCORES):
        in_maps.append({
            "W0in_s": np.ascontiguousarray(inp["L0_attn_in_proj_w"][256 * c:256 * (c + 1)]),
            "W0out_s": np.ascontiguousarray(inp["L0_attn_out_proj_w"][512 * c:512 * (c + 1)]),
            "Mup_s": np.ascontiguousarray(mup_flat[16384 * c:16384 * (c + 1)]),
            "Mdn_s": np.ascontiguousarray(mdn_flat[2048 * c:2048 * (c + 1)]),
            "Wsup_s": np.ascontiguousarray(inp["L1_ffn_w_shared_up"][256 * c:256 * (c + 1)]),
            "Wsdn_s": np.ascontiguousarray(inp["L1_ffn_w_shared_down"][64 * c:64 * (c + 1)]),
            "Wgate_s": np.ascontiguousarray(inp["L1_ffn_w_gate"][256 * c:256 * (c + 1)]),
            "W1in_s": np.ascontiguousarray(w1in_pad[1152 * c:1152 * (c + 1)]),
            "W1out_s": np.ascontiguousarray(inp["L1_attn_W_out"][:, 512 * c:512 * (c + 1)]),
            "gP_s": np.ascontiguousarray(inp["L0_ffn_gate_W_packed"][2048 * c:2048 * (c + 1)]),
            "gS_s": np.ascontiguousarray(inp["L0_ffn_gate_scales"][2048 * c:2048 * (c + 1)]),
            "uP_s": np.ascontiguousarray(inp["L0_ffn_up_W_packed"][2048 * c:2048 * (c + 1)]),
            "uS_s": np.ascontiguousarray(inp["L0_ffn_up_scales"][2048 * c:2048 * (c + 1)]),
            "dP_s": np.ascontiguousarray(inp["L0_ffn_down_W_packed"][:, 1024 * c:1024 * (c + 1)]),
            "dS_s": np.ascontiguousarray(inp["L0_ffn_down_scales"][:, 64 * c:64 * (c + 1)]),
        })
    res = run_bass_kernel_spmd(nc, in_maps, core_ids=list(range(NCORES)))
    r = res.results
    cat = lambda key, ax=0: np.concatenate([r[c][key] for c in range(NCORES)], axis=ax)
    return {
        "W0in": cat("W0in_o"), "W0out": cat("W0out_o"),
        "Mup": cat("Mup_o"), "Mdn": cat("Mdn_o"),
        "Wsup": cat("Wsup_o"), "Wsdn": cat("Wsdn_o"), "Wgate": cat("Wgate_o"),
        "W1inT": cat("W1inT_o"), "W1outT": cat("W1outT_o"),
        "GgT": cat("gT_o"), "GuT": cat("uT_o"), "GdT": cat("dT_o"),
    }


def run_ffn0_unfused(x_full, GgT, GuT, GdT):
    nc = bass.Bass("TRN2", target_bir_lowering=False)
    p = Prog(nc)
    io = {
        "x_in": nc.dram_tensor("x_in", [T, D], F32, kind="ExternalInput").ap(),
        "g_gT": nc.dram_tensor("g_gT", [16384, 2048], BF16, kind="ExternalInput").ap(),
        "g_uT": nc.dram_tensor("g_uT", [16384, 2048], BF16, kind="ExternalInput").ap(),
        "g_dT": nc.dram_tensor("g_dT", [16384, 2048], BF16, kind="ExternalInput").ap(),
        "x_out": nc.dram_tensor("x_out", [T, D], F32, kind="ExternalOutput").ap(),
    }
    idb = p.sbuf("idb", [128, 128], BF16)
    idf = p.sbuf("idf", [128, 128], F32)
    emit_idents(p, idb, idf)
    emit_ffn0(p, io, idb, idf)
    p.realize()
    in_maps = []
    for c in range(NCORES):
        b, q = c // 4, c % 4
        in_maps.append({
            "x_in": np.ascontiguousarray(x_full[b, q * 1024:(q + 1) * 1024]),
            "g_gT": GgT, "g_uT": GuT, "g_dT": GdT,
        })
    res = run_bass_kernel_spmd(nc, in_maps, core_ids=list(range(NCORES)))
    out = np.empty((2, 4096, D), np.float32)
    for c in range(NCORES):
        b, q = c // 4, c % 4
        out[b, q * 1024:(q + 1) * 1024] = res.results[c]["x_out"]
    return out



def emit_moe(p, io, idb, idf):
    sfx = "_mo"
    uTb = p.sbuf("uT" + sfx, [128, 16 * T], BF16)
    uT = lambda k, a, b: uTb[:, k * T + a: k * T + b]
    XT = p.sbuf("XT" + sfx, [128, 16 * 640], BF16)
    ybuf = p.sbuf("yb" + sfx, [128, 4 * T], BF16)
    gsil = p.sbuf("gs" + sfx, [128, T], BF16)
    actS = p.sbuf("aS" + sfx, [128, 4 * T], BF16)
    wsdn_sb = p.sbuf("wsd" + sfx, [128, 4 * 2048], BF16)
    accf = p.sbuf("acc" + sfx, [128, 2048], F32)
    yg = p.sbuf("yg" + sfx, [128, 2048], BF16)
    do0 = p.sbuf("do0" + sfx, [128, 2048], BF16)
    do1 = p.sbuf("do1" + sfx, [32, 2048], BF16)
    ab0 = p.sbuf("ab0" + sfx, [128, 512], BF16)
    ab1 = p.sbuf("ab1" + sfx, [32, 512], BF16)
    aTa = p.sbuf("aTa" + sfx, [128, 160], BF16)
    aTb = p.sbuf("aTb" + sfx, [128, 160], BF16)
    scT = p.sbuf("scT" + sfx, [64, T], F32)
    sctok = p.sbuf("sct" + sfx, [128, 64], F32)
    mx = p.sbuf("mx" + sfx, [128, 8], F32)
    ixu = p.sbuf("ixu" + sfx, [128, 8], U32)
    ixf = p.sbuf("ixf" + sfx, [128, 8 * 8], F32)
    wno = p.sbuf("wno" + sfx, [128, 8], F32)
    wcols = p.sbuf("wc" + sfx, [128, 48], F32)
    slotf = p.sbuf("sf" + sfx, [128, 48], F32)
    sloti = p.sbuf("si" + sfx, [128, 48], I32)
    selsum = p.sbuf("ss" + sfx, [128, 64], F32)
    ohb = p.sbuf("oh" + sfx, [128, 64], F32)
    pscr = p.sbuf("pscr" + sfx, [128, 64], F32)
    postok = p.sbuf("pt" + sfx, [128, 64], F32)
    pcol = p.sbuf("pc" + sfx, [128, 4], F32)
    selT = p.sbuf("seT" + sfx, [64, T], F32)
    posT = p.sbuf("poT" + sfx, [64, T], F32)
    iota64 = p.sbuf("io64" + sfx, [128, 64], F32)
    onesN = p.sbuf("onN" + sfx, [64, 1], F32)
    tokf = p.sbuf("tkf" + sfx, [128, 1], F32)
    fill80 = p.sbuf("f80" + sfx, [80, 128], F32)
    stab80 = p.sbuf("st80" + sfx, [80, 128], F32)
    stabT = p.sbuf("stT" + sfx, [128, 80], F32)
    stabI = p.sbuf("stI" + sfx, [128, 80], I32)
    lnS = p.sbuf("lnS" + sfx, [128, 8], F32)
    xtl = p.sbuf("xtl" + sfx, [128, D], F32)
    utok = p.sbuf("utok" + sfx, [128, D], BF16)
    sq2 = p.sbuf("sq2" + sfx, [128, D], BF16)
    wbf = p.sbuf("wbf" + sfx, [128, 512], BF16)
    wbm = [p.sbuf(f"wbm{i}" + sfx, [128, 1024], BF16) for i in range(2)]
    pb = [p.psum(f"pb{i}" + sfx, [128, 512], F32) for i in range(8)]
    pbb = [t.ap().bitcast(BF16) for t in pb]

    if not getattr(p, "_consts_done", False):
        p._consts_done = True
        cstt = p.sbuf("cstt" + sfx, [128, 2], F32)
        p.op("vector", lambda e: e.memset(cstt[:, 0:1], EPS), inc="dve")
        p.op("vector", lambda e: e.memset(cstt[:, 1:2], 1.0), inc="dve")
        p.nc.const_aps.register(F32, EPS, cstt[:, 0:1])
        p.nc.const_aps.register(F32, 1.0, cstt[:, 1:2])

    regs = {}

    def _mkregs(e):
        regs["b10239"] = e.alloc_register("bchk_a")
        e.reg_mov(regs["b10239"], 10239)
        regs["b1023"] = e.alloc_register("bchk_b")
        return e.reg_mov(regs["b1023"], 1023)

    p.op("gpsimd", _mkregs, inc="gps")
    p.op("gpsimd", lambda e: e.iota(out=iota64[:, :], pattern=[[1, 64]], base=0, channel_multiplier=0, allow_small_or_imprecise_dtypes=True), inc="gps")
    p.op("vector", lambda e: e.memset(onesN[:, :], 1.0), inc="dve")
    p.op("vector", lambda e: e.memset(fill80[:, :], 100000.0), inc="dve")
    vd0 = p.val("dve")
    p.wait("gpsimd", "dve", vd0)
    p.dma("gpsimd", io["slot_dram"].ap().rearrange("(g q) o -> g (q o)", q=128), fill80[:, :], inc="qa")

    for it in range(8):
        p.wait("gpsimd", "dve", p.val("dve"))
        p.wait("gpsimd", "act", p.val("act"))
        vx = p.dma("gpsimd", xtl[:, :], io["x_in"][it * 128:(it + 1) * 128, :], inc="qa")
        p.wait("vector", "qa", vx)
        p.op("vector", lambda e: e.tensor_reduce(out=lnS[:, 0:1], in_=xtl[:, :], axis=AXX, op=ALU.add), inc="dve")
        p.wait("scalar", "qa", vx)
        p.op("scalar", lambda e: e.activation(out=sq2[:, :], in_=xtl[:, :], func=AF.Square), inc="act")
        p.wait("vector", "act", p.val("act"))
        p.op("vector", lambda e: e.tensor_reduce(out=lnS[:, 1:2], in_=sq2[:, :], axis=AXX, op=ALU.add), inc="dve")
        p.op("vector", lambda e: e.tensor_scalar(out=lnS[:, 1:2], in0=lnS[:, 1:2], scalar1=1.0 / D, scalar2=None, op0=ALU.mult), inc="dve")
        p.op("vector", lambda e: e.tensor_scalar(out=lnS[:, 2:3], in0=lnS[:, 0:1], scalar1=1.0 / D, scalar2=None, op0=ALU.mult), inc="dve")
        p.op("vector", lambda e: e.tensor_scalar(out=lnS[:, 4:5], in0=lnS[:, 2:3], scalar1=lnS[:, 2:3], scalar2=None, op0=ALU.mult), inc="dve")
        p.op("vector", lambda e: e.tensor_sub(out=lnS[:, 4:5], in0=lnS[:, 1:2], in1=lnS[:, 4:5]), inc="dve")
        vd = p.val("dve")
        p.wait("scalar", "dve", vd)
        p.op("scalar", lambda e: e.activation(out=lnS[:, 5:6], in_=lnS[:, 4:5], func=AF.Ln, bias=EPS), inc="act")
        p.op("scalar", lambda e: e.activation(out=lnS[:, 6:7], in_=lnS[:, 5:6], func=AF.Exp, scale=-0.5), inc="act")
        va = p.val("act")
        p.wait("vector", "act", va)
        p.op("vector", lambda e: e.tensor_scalar(out=utok[:, :], in0=xtl[:, :], scalar1=lnS[:, 2:3], scalar2=lnS[:, 6:7], op0=ALU.subtract, op1=ALU.mult), inc="dve")
        vd = p.val("dve")
        p.wait("gpsimd", "dve", vd)
        p.dma("gpsimd", io["u4_tm"].ap()[it * 128:(it + 1) * 128, :], utok[:, :], inc="qa")
        p.wait("tensor", "dve", vd)
        for j in range(16):
            p.op("tensor", lambda e, j=j: e.transpose(
                out=pbb[j % 4][:, 0:128], in_=utok[:, j * 128:(j + 1) * 128],
                identity=idb[:, :]), inc=("pe" if j % 4 == 3 else None))
            if j % 4 == 3:
                v = p.val("pe")
                p.wait("scalar", "pe", v)
                for jj in range(j - 3, j + 1):
                    p.op("scalar", lambda e, jj=jj, it=it: e.copy(
                        out=uT(jj, it * 128, it * 128 + 128), in_=pbb[jj % 4][:, 0:128]), inc="act")
                p.wait("tensor", "act", p.val("act"))

    for k in range(16):
        vw = p.dma("sync", wbf[:, 0:64], io["wgate"][k * 128:(k + 1) * 128, :], inc="qw")
        p.wait("tensor", "qw", vw)
        for nf in range(2):
            p.op("tensor", lambda e, k=k, nf=nf: e.matmul(
                out=pb[nf][0:64, :], lhsT=wbf[:, 0:64], rhs=uT(k, nf * 512, (nf + 1) * 512),
                start=(k == 0), stop=(k == 15)), inc=("pe" if nf == 1 else None))
        vpe = p.val("pe")
        p.wait("sync", "pe", vpe)
    vpe = p.val("pe")
    p.wait("scalar", "pe", vpe)
    for nf in range(2):
        p.op("scalar", lambda e, nf=nf: e.activation(out=scT[0:64, nf * 512:(nf + 1) * 512], in_=pb[nf][0:64, :], func=AF.Sigmoid), inc="act")
    va_sc = p.val("act")
    p.wait("tensor", "act", va_sc)

    for tt in range(8):
        p.op("tensor", lambda e, tt=tt: e.transpose(out=pb[4][:, 0:64], in_=scT[0:64, tt * 128:(tt + 1) * 128], identity=idf[0:64, 0:64]), inc="pe")
        v = p.val("pe")
        p.wait("vector", "pe", v)
        p.op("vector", lambda e: e.tensor_copy(out=sctok[:, :], in_=pb[4][:, 0:64]), inc="dve")
        p.op("vector", lambda e: e.max(out=mx[:, :], in_=sctok[:, :]), inc="dve")
        p.op("vector", lambda e: e.max_index(out=ixu[:, :], in_max=mx[:, :], in_values=sctok[:, :]), inc="dve")
        p.op("vector", lambda e, tt=tt: e.tensor_copy(out=ixf[:, tt * 8:(tt + 1) * 8], in_=ixu[:, :]), inc="dve")
        p.op("vector", lambda e: e.tensor_reduce(out=wno[:, 6:7], in_=mx[:, 0:6], axis=AXX, op=ALU.add), inc="dve")
        p.op("vector", lambda e: e.reciprocal(out=wno[:, 7:8], in_=wno[:, 6:7]), inc="dve")
        p.op("vector", lambda e: e.tensor_scalar(out=wno[:, 0:6], in0=mx[:, 0:6], scalar1=wno[:, 7:8], scalar2=None, op0=ALU.mult), inc="dve")
        p.op("vector", lambda e, tt=tt: e.tensor_copy(out=wcols[:, tt * 6:(tt + 1) * 6], in_=wno[:, 0:6]), inc="dve")
        p.op("vector", lambda e: e.memset(selsum[:, :], 0.0), inc="dve")
        for j in range(6):
            p.op("vector", lambda e, tt=tt, j=j: e.tensor_scalar(
                out=ohb[:, :], in0=iota64[:, :], scalar1=ixf[:, tt * 8 + j:tt * 8 + j + 1], scalar2=None, op0=ALU.is_equal), inc="dve")
            p.op("vector", lambda e: e.tensor_add(out=selsum[:, :], in0=selsum[:, :], in1=ohb[:, :]), inc="dve")
        vd = p.val("dve")
        p.wait("tensor", "dve", vd)
        p.op("tensor", lambda e: e.transpose(out=pb[5][0:64, 0:128], in_=selsum[:, :], identity=idf[:, :]), inc="pe")
        v = p.val("pe")
        p.wait("vector", "pe", v)
        p.op("vector", lambda e, tt=tt: e.tensor_copy(out=selT[0:64, tt * 128:(tt + 1) * 128], in_=pb[5][0:64, 0:128]), inc="dve")
        p.wait("tensor", "dve", p.val("dve"))
    p.op("vector", lambda e: e.tensor_tensor_scan(
        out=posT[0:64, :], data0=onesN[:, 0:1].to_broadcast([64, T]), data1=selT[0:64, :],
        initial=0.0, op0=ALU.mult, op1=ALU.add), inc="dve")
    p.op("vector", lambda e: e.tensor_sub(out=posT[0:64, :], in0=posT[0:64, :], in1=selT[0:64, :]), inc="dve")
    vd = p.val("dve")
    p.wait("tensor", "dve", vd)
    for tt in range(8):
        p.op("tensor", lambda e, tt=tt: e.transpose(out=pb[4][:, 0:64], in_=posT[0:64, tt * 128:(tt + 1) * 128], identity=idf[0:64, 0:64]), inc="pe")
        v = p.val("pe")
        p.wait("vector", "pe", v)
        p.op("vector", lambda e: e.tensor_copy(out=postok[:, :], in_=pb[4][:, 0:64]), inc="dve")
        for j in range(6):
            p.op("vector", lambda e, tt=tt, j=j: e.tensor_scalar(
                out=ohb[:, :], in0=iota64[:, :], scalar1=ixf[:, tt * 8 + j:tt * 8 + j + 1], scalar2=None, op0=ALU.is_equal), inc="dve")
            p.op("vector", lambda e: e.tensor_mul(out=pscr[:, :], in0=ohb[:, :], in1=postok[:, :]), inc="dve")
            p.op("vector", lambda e: e.tensor_reduce(out=pcol[:, 0:1], in_=pscr[:, :], axis=AXX, op=ALU.add), inc="dve")
            p.op("vector", lambda e: e.tensor_scalar(out=pcol[:, 1:2], in0=pcol[:, 0:1], scalar1=159.5, scalar2=1.0e6, op0=ALU.is_gt, op1=ALU.mult), inc="dve")
            p.op("vector", lambda e, tt=tt, j=j: e.scalar_tensor_tensor(
                out=pcol[:, 2:3], in0=ixf[:, tt * 8 + j:tt * 8 + j + 1], scalar=160.0, in1=pcol[:, 0:1], op0=ALU.mult, op1=ALU.add), inc="dve")
            p.op("vector", lambda e, tt=tt, j=j: e.tensor_add(out=slotf[:, tt * 6 + j:tt * 6 + j + 1], in0=pcol[:, 2:3], in1=pcol[:, 1:2]), inc="dve")
        p.wait("tensor", "dve", p.val("dve"))
    p.op("vector", lambda e: e.tensor_copy(out=sloti[:, :], in_=slotf[:, :]), inc="dve")
    vd_slot = p.val("dve")

    p.wait("gpsimd", "dve", vd_slot)
    p.wait("gpsimd", "qa", p.val("qa"))
    for tt in range(8):
        p.wait("gpsimd", "qa", p.val("qa"))
        p.op("gpsimd", lambda e, tt=tt: e.iota(out=tokf[:, :], pattern=[[1, 1]], base=tt * 128, channel_multiplier=1, allow_small_or_imprecise_dtypes=True), inc="gps")
        p.wait("gpsimd", "gps", p.val("gps"))
        for j in range(6):
            p.op("gpsimd", lambda e, tt=tt, j=j: e.indirect_dma_start(
                out=io["slot_dram"].ap(), out_offset=IndirectOffsetOnAxis(ap=sloti[:, tt * 6 + j:tt * 6 + j + 1], axis=0),
                in_=tokf[:, :], in_offset=None, bounds_check=regs["b10239"], oob_is_err=False), inc="qa", n=16)
    p.wait("gpsimd", "qa", p.val("qa"))
    v = p.dma("gpsimd", stab80[:, :], io["slot_dram"].ap().rearrange("(g q) o -> g (q o)", q=128), inc="qa")
    p.wait("tensor", "qa", v)
    p.op("tensor", lambda e: e.transpose(out=pb[4][:, 0:80], in_=stab80[:, :], identity=idf[0:80, 0:80]), inc="pe")
    v = p.val("pe")
    p.wait("vector", "pe", v)
    p.op("vector", lambda e: e.tensor_copy(out=stabT[:, :], in_=pb[4][:, 0:80]), inc="dve")
    p.op("vector", lambda e: e.tensor_copy(out=stabI[:, :], in_=stabT[:, :]), inc="dve")
    vd_stab = p.val("dve")
    p.wait("gpsimd", "dve", vd_stab)
    xgb = p.sbuf("xgb" + sfx, [128, 2048], BF16)
    for g in range(80):
        vg = p.op("gpsimd", lambda e, g=g: e.indirect_dma_start(
            out=xgb[:, :], out_offset=None,
            in_=io["u4_tm"].ap(), in_offset=IndirectOffsetOnAxis(ap=stabI[:, g:g + 1], axis=0),
            bounds_check=regs["b1023"], oob_is_err=False), inc="qa", n=16)
        p.wait("gpsimd", "qa", vg)
        p.dma("gpsimd", io["Xall"].ap()[g * 128:(g + 1) * 128, :], xgb[:, :], inc="qa")
        p.wait("gpsimd", "qa", p.val("qa"))

    p.wait("tensor", "dve", vd_stab)
    for grp in ([0, 1, 2], [3, 4, 5], [6, 7]):
        for k in range(16):
            vw = p.dma("sync", wbf[:, 0:128 * len(grp)], io["wsup"][k * 128:(k + 1) * 128, 128 * grp[0]:128 * grp[0] + 128 * len(grp)], inc="qw")
            p.wait("tensor", "qw", vw)
            for mi, m in enumerate(grp):
                for nf in range(2):
                    p.op("tensor", lambda e, k=k, mi=mi, nf=nf: e.matmul(
                        out=pb[2 * mi + nf][:, :], lhsT=wbf[:, mi * 128:(mi + 1) * 128], rhs=uT(k, nf * 512, (nf + 1) * 512),
                        start=(k == 0), stop=(k == 15)), inc=("pe" if (mi == len(grp) - 1 and nf == 1) else None))
            vpe = p.val("pe")
            p.wait("sync", "pe", vpe)
        vpe = p.val("pe")
        p.wait("scalar", "pe", vpe)
        p.wait("vector", "pe", vpe)
        for mi, m in enumerate(grp):
            if m < 4:
                for nf in range(2):
                    p.op("scalar", lambda e, m=m, mi=mi, nf=nf: e.copy(
                        out=ybuf[:, m * T + nf * 512: m * T + (nf + 1) * 512], in_=pb[2 * mi + nf][:, :]), inc="act")
            else:
                p.wait("scalar", "dve", p.val("dve"))
                for nf in range(2):
                    p.op("scalar", lambda e, mi=mi, nf=nf: e.activation(
                        out=gsil[:, nf * 512:(nf + 1) * 512], in_=pb[2 * mi + nf][:, :], func=AF.Silu), inc="act")
                va = p.val("act")
                p.wait("vector", "act", va)
                p.op("vector", lambda e, m=m: e.tensor_mul(
                    out=actS[:, (m - 4) * T:(m - 3) * T], in0=gsil[:, :], in1=ybuf[:, (m - 4) * T:(m - 3) * T]), inc="dve")
        p.wait("tensor", "act", p.val("act"))
        p.wait("tensor", "dve", p.val("dve"))
        p.wait("scalar", "dve", p.val("dve"))
    for kd in range(4):
        p.dma("sync", wsdn_sb[:, kd * 2048:(kd + 1) * 2048], io["wsdn"][kd * 128:(kd + 1) * 128, :], inc="qw")
    v_wsdn = p.val("qw")

    for eg in range(16):
        p.wait("sync", "pe", p.val("pe"))
        p.wait("sync", "qa", p.val("qa"))
        for kc in range(16):
            p.dma("sync", XT[:, kc * 640:(kc + 1) * 640], io["Xall"].ap()[eg * 640:(eg + 1) * 640, kc * 128:(kc + 1) * 128], inc="qw", transpose=True)
        v_xt = p.val("qw")
        p.wait("tensor", "qw", v_xt)
        for le in range(4):
            e_ = eg * 4 + le
            soff = le * 160
            upslc2 = lambda kk, e_=e_: io["gup"][e_ * 2048 + 256 * kk: e_ * 2048 + 256 * (kk + 1),
                                                 :].rearrange("(i p) c -> p i c", p=128)

            def ubody(kc, buf, soff=soff):
                p.op("tensor", lambda e, kc=kc, soff=soff, buf=buf: e.matmul(
                    out=pb[0][:, :], lhsT=XT[:, kc * 640 + soff: kc * 640 + soff + 128], rhs=buf,
                    start=(kc == 0), stop=(kc == 15)))
                p.op("tensor", lambda e, kc=kc, soff=soff, buf=buf: e.matmul(
                    out=pb[1][0:32, :], lhsT=XT[:, kc * 640 + soff + 128: kc * 640 + soff + 160], rhs=buf,
                    start=(kc == 0), stop=(kc == 15)), inc="pe")
            stream_mm(p, 16, wbm, 512, upslc2, ubody)
            vpe = p.val("pe")
            p.wait("sync", "pe", vpe)
            vpe = p.val("pe")
            p.wait("scalar", "pe", vpe)
            p.wait("vector", "pe", vpe)
            p.op("scalar", lambda e: e.activation(out=ab0[:, 0:256], in_=pb[0][:, 0:256], func=AF.Silu), inc="act")
            p.op("scalar", lambda e: e.activation(out=ab1[:, 0:256], in_=pb[1][0:32, 0:256], func=AF.Silu), inc="act")
            va = p.val("act")
            p.wait("vector", "act", va)
            p.op("vector", lambda e: e.tensor_tensor(out=ab0[:, 256:512], in0=ab0[:, 0:256], in1=pb[0][:, 256:512], op=ALU.mult), inc="dve")
            p.op("vector", lambda e: e.tensor_tensor(out=ab1[:, 256:512], in0=ab1[:, 0:256], in1=pb[1][0:32, 256:512], op=ALU.mult), inc="dve")
            vd = p.val("dve")
            p.wait("tensor", "dve", vd)
            p.op("tensor", lambda e: e.transpose(out=pbb[2][:, 0:128], in_=ab0[:, 256:384], identity=idb[:, :]), inc="pe")
            p.op("tensor", lambda e: e.transpose(out=pbb[3][:, 0:128], in_=ab0[:, 384:512], identity=idb[:, :]), inc="pe")
            p.op("tensor", lambda e: e.transpose(out=pbb[4][:, 0:32], in_=ab1[0:32, 256:384], identity=idb[0:32, 0:32]), inc="pe")
            p.op("tensor", lambda e: e.transpose(out=pbb[5][:, 0:32], in_=ab1[0:32, 384:512], identity=idb[0:32, 0:32]), inc="pe")
            vpe = p.val("pe")
            p.wait("scalar", "pe", vpe)
            p.op("scalar", lambda e: e.copy(out=aTa[:, 0:128], in_=pbb[2][:, 0:128]), inc="act")
            p.op("scalar", lambda e: e.copy(out=aTb[:, 0:128], in_=pbb[3][:, 0:128]), inc="act")
            p.op("scalar", lambda e: e.copy(out=aTa[:, 128:160], in_=pbb[4][:, 0:32]), inc="act")
            p.op("scalar", lambda e: e.copy(out=aTb[:, 128:160], in_=pbb[5][:, 0:32]), inc="act")
            va = p.val("act")
            p.wait("tensor", "act", va)
            pb1map = [6, 7, 0, 1]
            dnslc2 = lambda kk, e_=e_: io["gdn"][e_ * 256 + (kk // 2) * 128: e_ * 256 + (kk // 2 + 1) * 128,
                                                 (kk % 2) * 1024:(kk % 2 + 1) * 1024].rearrange("p (i c) -> p i c", c=512)

            def dbody(i, buf):
                kd, dchunk = i // 4, i % 4
                lt = aTa if kd == 0 else aTb
                p.op("tensor", lambda e, kd=kd, dchunk=dchunk, lt=lt, buf=buf: e.matmul(
                    out=pb[2 + dchunk][:, :], lhsT=lt[:, 0:128], rhs=buf,
                    start=(kd == 0), stop=(kd == 1)))
                p.op("tensor", lambda e, kd=kd, dchunk=dchunk, lt=lt, buf=buf: e.matmul(
                    out=pb[pb1map[dchunk]][0:32, :],
                    lhsT=lt[:, 128:160], rhs=buf,
                    start=(kd == 0), stop=(kd == 1)), inc="pe")
            stream_mm(p, 8, wbm, 512, dnslc2, dbody)
            vpe = p.val("pe")
            p.wait("scalar", "pe", vpe)
            p.wait("scalar", "qa", p.val("qa"))
            for dchunk in range(4):
                p.op("scalar", lambda e, dchunk=dchunk: e.copy(out=do0[:, dchunk * 512:(dchunk + 1) * 512], in_=pb[2 + dchunk][:, :]), inc="act")
            for dchunk in range(4):
                p.op("scalar", lambda e, dchunk=dchunk: e.copy(out=do1[0:32, dchunk * 512:(dchunk + 1) * 512], in_=pb[pb1map[dchunk]][0:32, :]), inc="act")
            va = p.val("act")
            p.wait("gpsimd", "act", va)
            p.wait("tensor", "act", va)
            p.dma("gpsimd", io["down_all"].ap()[e_ * 160:e_ * 160 + 128, :], do0[:, :], inc="qa")
            p.dma("gpsimd", io["down_all"].ap()[e_ * 160 + 128:e_ * 160 + 160, :], do1[0:32, :], inc="qa")
            p.wait("tensor", "qa", p.val("qa"))
    p.wait("gpsimd", "qa", p.val("qa"))

    p.wait("tensor", "qw", v_wsdn)
    for tt in range(8):
        for kd in range(4):
            for dchunk in range(4):
                p.op("tensor", lambda e, tt=tt, kd=kd, dchunk=dchunk: e.matmul(
                    out=pb[dchunk][:, :], lhsT=actS[:, kd * T + tt * 128: kd * T + (tt + 1) * 128],
                    rhs=wsdn_sb[:, kd * 2048 + dchunk * 512: kd * 2048 + (dchunk + 1) * 512],
                    start=(kd == 0), stop=(kd == 3)), inc=("pe" if (kd == 3 and dchunk == 3) else None))
        vpe = p.val("pe")
        p.wait("vector", "pe", vpe)
        for dchunk in range(4):
            p.op("vector", lambda e, dchunk=dchunk: e.tensor_copy(out=accf[:, dchunk * 512:(dchunk + 1) * 512], in_=pb[dchunk][:, :]), inc="dve")
        vd = p.val("dve")
        p.wait("tensor", "dve", vd)
        p.wait("gpsimd", "dve", vd)
        for j in range(6):
            vg = p.op("gpsimd", lambda e, tt=tt, j=j: e.indirect_dma_start(
                out=yg[:, :], out_offset=None, in_=io["down_all"].ap(),
                in_offset=IndirectOffsetOnAxis(ap=sloti[:, tt * 6 + j:tt * 6 + j + 1], axis=0),
                bounds_check=regs["b10239"], oob_is_err=False), inc="qa", n=16)
            p.wait("vector", "qa", vg)
            p.op("vector", lambda e, tt=tt, j=j: e.scalar_tensor_tensor(
                out=accf[:, :], in0=yg[:, :], scalar=wcols[:, tt * 6 + j:tt * 6 + j + 1], in1=accf[:, :],
                op0=ALU.mult, op1=ALU.add), inc="dve")
            vd = p.val("dve")
            p.wait("gpsimd", "dve", vd)
        vx = p.dma("gpsimd", xtl[:, :], io["x_in"][tt * 128:(tt + 1) * 128, :], inc="qa")
        p.op("vector", lambda e: e.tensor_reduce(out=lnS[:, 0:1], in_=accf[:, :], axis=AXX, op=ALU.add), inc="dve")
        p.wait("scalar", "dve", p.val("dve"))
        p.op("scalar", lambda e: e.activation(out=sq2[:, :], in_=accf[:, :], func=AF.Square), inc="act")
        p.wait("vector", "act", p.val("act"))
        p.op("vector", lambda e: e.tensor_reduce(out=lnS[:, 1:2], in_=sq2[:, :], axis=AXX, op=ALU.add), inc="dve")
        p.op("vector", lambda e: e.tensor_scalar(out=lnS[:, 1:2], in0=lnS[:, 1:2], scalar1=1.0 / D, scalar2=None, op0=ALU.mult), inc="dve")
        p.op("vector", lambda e: e.tensor_scalar(out=lnS[:, 2:3], in0=lnS[:, 0:1], scalar1=1.0 / D, scalar2=None, op0=ALU.mult), inc="dve")
        p.op("vector", lambda e: e.tensor_scalar(out=lnS[:, 4:5], in0=lnS[:, 2:3], scalar1=lnS[:, 2:3], scalar2=None, op0=ALU.mult), inc="dve")
        p.op("vector", lambda e: e.tensor_sub(out=lnS[:, 4:5], in0=lnS[:, 1:2], in1=lnS[:, 4:5]), inc="dve")
        vd = p.val("dve")
        p.wait("scalar", "dve", vd)
        p.op("scalar", lambda e: e.activation(out=lnS[:, 5:6], in_=lnS[:, 4:5], func=AF.Ln, bias=EPS), inc="act")
        p.op("scalar", lambda e: e.activation(out=lnS[:, 6:7], in_=lnS[:, 5:6], func=AF.Exp, scale=-0.5), inc="act")
        va = p.val("act")
        p.wait("vector", "act", va)
        p.wait("vector", "qa", vx)
        p.op("vector", lambda e: e.tensor_scalar(
            out=accf[:, :], in0=accf[:, :], scalar1=lnS[:, 2:3], scalar2=lnS[:, 6:7], op0=ALU.subtract, op1=ALU.mult), inc="dve")
        p.op("vector", lambda e: e.tensor_add(out=xtl[:, :], in0=xtl[:, :], in1=accf[:, :]), inc="dve")
        vd = p.val("dve")
        p.wait("gpsimd", "dve", vd)
        p.dma("gpsimd", io["x_out"][tt * 128:(tt + 1) * 128, :], xtl[:, :], inc="qa")
        p.wait("gpsimd", "qa", p.val("qa"))
    p.wait("gpsimd", "qw", p.val("qw"))


def run_moe_unfused(x_full, Wgate, Wsup, Wsdn, Gup, Gdn):
    nc = bass.Bass("TRN2", target_bir_lowering=False)
    p = Prog(nc)
    io = {
        "x_in": nc.dram_tensor("x_in", [T, D], F32, kind="ExternalInput").ap(),
        "wgate": nc.dram_tensor("wgate", [2048, 64], BF16, kind="ExternalInput").ap(),
        "wsup": nc.dram_tensor("wsup", [2048, 1024], BF16, kind="ExternalInput").ap(),
        "wsdn": nc.dram_tensor("wsdn", [512, 2048], BF16, kind="ExternalInput").ap(),
        "gup": nc.dram_tensor("gup", [131072, 512], BF16, kind="ExternalInput").ap(),
        "gdn": nc.dram_tensor("gdn", [16384, 2048], BF16, kind="ExternalInput").ap(),
        "x_out": nc.dram_tensor("x_out", [T, D], F32, kind="ExternalOutput").ap(),
        "u4_tm": nc.dram_tensor("u4_tm", [T, D], BF16, kind="Internal"),
        "slot_dram": nc.dram_tensor("slot_dram", [10240, 1], F32, kind="Internal"),
        "Xall": nc.dram_tensor("Xall", [10240, 2048], BF16, kind="Internal"),
        "down_all": nc.dram_tensor("down_all", [10240, 2048], BF16, kind="Internal"),
    }
    idb = p.sbuf("idb", [128, 128], BF16)
    idf = p.sbuf("idf", [128, 128], F32)
    emit_idents(p, idb, idf)
    emit_moe(p, io, idb, idf)
    p.realize()
    in_maps = []
    for c in range(NCORES):
        b, q = c // 4, c % 4
        in_maps.append({
            "x_in": np.ascontiguousarray(x_full[b, q * 1024:(q + 1) * 1024]),
            "wgate": Wgate, "wsup": Wsup, "wsdn": Wsdn, "gup": Gup, "gdn": Gdn,
        })
    res = run_bass_kernel_spmd(nc, in_maps, core_ids=list(range(NCORES)))
    out = np.empty((2, 4096, D), np.float32)
    for c in range(NCORES):
        b, q = c // 4, c % 4
        out[b, q * 1024:(q + 1) * 1024] = res.results[c]["x_out"]
    return out



def run_fused(inp):
    nc = bass.Bass("TRN2", target_bir_lowering=False)
    p = Prog(nc)

    def ein(name, shape, dtype=F32):
        return nc.dram_tensor(name, shape, dtype, kind="ExternalInput").ap()

    def itn(name, shape, dtype):
        return nc.dram_tensor(name, shape, dtype, kind="Internal").ap()

    x_in = ein("x_in", [T, D])
    xh0 = ein("xh0", [3, D])
    mrow8 = ein("mrow8", [1, 8]); mcol8 = ein("mcol8", [128, 8])
    mh8 = ein("mh8", [128, 8]); mh8_3 = ein("mh8_3", [3, 8])
    shards = {
        "W0in_s": ein("W0in_s", [256, 8512]), "W0out_s": ein("W0out_s", [512, 2048]),
        "Mup_s": ein("Mup_s", [16384, 512]), "Mdn_s": ein("Mdn_s", [2048, 2048]),
        "Wsup_s": ein("Wsup_s", [256, 1024]), "Wsdn_s": ein("Wsdn_s", [64, 2048]),
        "Wgate_s": ein("Wgate_s", [256, 64]),
        "W1in_s": ein("W1in_s", [1152, 2048]), "W1out_s": ein("W1out_s", [2048, 512]),
        "gP_s": ein("gP_s", [2048, 1024], I32), "gS_s": ein("gS_s", [2048, 64]),
        "uP_s": ein("uP_s", [2048, 1024], I32), "uS_s": ein("uS_s", [2048, 64]),
        "dP_s": ein("dP_s", [2048, 1024], I32), "dS_s": ein("dS_s", [2048, 64]),
    }
    conv0 = {k: ein("c0_" + k, s) for k, s in
             (("conv_w", [4352, 4]), ("conv_b", [4352]), ("dt_bias", [64]), ("A_log", [64]), ("Dv", [64]), ("norm_w", [4096]))}
    conv1 = {k: ein("c1_" + k, s) for k, s in
             (("conv_w", [4224, 4]), ("conv_b", [4224]), ("dt_bias", [64]), ("A_log", [64]), ("Dv", [64]), ("norm_w", [4096]))}
    x_out = nc.dram_tensor("x_out", [T, D], F32, kind="ExternalOutput").ap()

    x1 = itn("x1", [T, D], F32)
    x2 = itn("x2", [T, D], F32)
    x3 = itn("x3", [T, D], F32)
    contrib = {
        "W0in": itn("cW0in", [256, 8512], BF16), "W0out": itn("cW0out", [512, 2048], BF16),
        "Mup": itn("cMup", [16384, 512], BF16), "Mdn": itn("cMdn", [2048, 2048], BF16),
        "Wsup": itn("cWsup", [256, 1024], BF16), "Wsdn": itn("cWsdn", [64, 2048], BF16),
        "Wgate": itn("cWgate", [256, 64], BF16),
        "W1inT": itn("cW1inT", [2048, 1152], BF16), "W1outT": itn("cW1outT", [512, 2048], BF16),
        "GgT": itn("cGgT", [2048, 2048], BF16), "GuT": itn("cGuT", [2048, 2048], BF16), "GdT": itn("cGdT", [2048, 2048], BF16),
    }
    gath = {
        "W0in": itn("GW0in", [2048, 8512], BF16), "W0out": itn("GW0out", [4096, 2048], BF16),
        "Mup": itn("GMup", [131072, 512], BF16), "Mdn": itn("GMdn", [16384, 2048], BF16),
        "Wsup": itn("GWsup", [2048, 1024], BF16), "Wsdn": itn("GWsdn", [512, 2048], BF16),
        "Wgate": itn("GWgate", [2048, 64], BF16),
        "W1inT": itn("GW1inT", [16384, 1152], BF16), "W1outT": itn("GW1outT", [4096, 2048], BF16),
        "GgT": itn("GGgT", [16384, 2048], BF16), "GuT": itn("GGuT", [16384, 2048], BF16), "GdT": itn("GGdT", [16384, 2048], BF16),
    }
    exch = {}
    for L in (0, 1):
        N = 128 if L == 0 else 64
        exch[L] = {
            "cS": itn(f"cS{L}", [N, 4096], F32), "GS": itn(f"GS{L}", [8 * N, 4096], F32),
            "cg": itn(f"cg{L}", [1, 64], F32), "Gg": itn(f"Gg{L}", [8, 64], F32),
            "ch": itn(f"ch{L}", [128, 102], BF16), "Gh": itn(f"Gh{L}", [1024, 102], BF16),
        }
    cxh = itn("cxh", [3, D], F32)
    Gxh = itn("Gxh", [24, D], F32)

    idb = p.sbuf("idb", [128, 128], BF16)
    idf = p.sbuf("idf", [128, 128], F32)
    emit_idents(p, idb, idf)
    cstG = p.sbuf("cstG", [128, 4], F32)
    p.op("vector", lambda e: e.memset(cstG[:, 0:1], EPS), inc="dve")
    p.op("vector", lambda e: e.memset(cstG[:, 1:2], 1.0), inc="dve")
    p.op("vector", lambda e: e.memset(cstG[:, 2:3], -2.0), inc="dve")
    p.op("vector", lambda e: e.memset(cstG[:, 3:4], -6.0), inc="dve")
    nc.const_aps.register(F32, EPS, cstG[:, 0:1])
    nc.const_aps.register(F32, 1.0, cstG[:, 1:2])
    nc.const_aps.register(F32, -2.0, cstG[:, 2:3])
    nc.const_aps.register(F32, -6.0, cstG[:, 3:4])
    p._consts_done = True
    p._consts2_done = True
    p.wait("scalar", "dve", p.val("dve"))

    def ag(cin, cout):
        v = p.op("gpsimd", lambda e, cin=cin, cout=cout: e.collective_compute(
            "AllGather", ALU.bypass, replica_groups=[list(range(8))],
            ins=[cin.opt()], outs=[cout.opt()]), inc="cc")
        return v

    stage_n = [0]

    def prep_stage(jobs):
        stage_n[0] += 1
        m = p.push()
        emit_prep(p, jobs, idb, idf, sfx=str(stage_n[0]))
        p.barrier(skip_cc=True)
        p.pop(m)

    prep_stage({"casts": [(shards["W0in_s"], contrib["W0in"], 256, 8512),
                          (shards["W0out_s"], contrib["W0out"], 512, 2048)],
                "ctrans": [], "decode": []})
    ag(contrib["W0in"], gath["W0in"])
    v_cc_w0 = ag(contrib["W0out"], gath["W0out"])
    prep_stage({"casts": [], "ctrans": [],
                "decode": [(shards["gP_s"], shards["gS_s"], contrib["GgT"]),
                           (shards["uP_s"], shards["uS_s"], contrib["GuT"]),
                           (shards["dP_s"], shards["dS_s"], contrib["GdT"])]})
    for k in ("GgT", "GuT", "GdT"):
        ag(contrib[k], gath[k])
    prep_stage({"casts": [], "decode": [],
                "ctrans": [(shards["W1in_s"], contrib["W1inT"], 1152, 2048),
                           (shards["W1out_s"], contrib["W1outT"], 2048, 512)]})
    ag(contrib["W1inT"], gath["W1inT"])
    ag(contrib["W1outT"], gath["W1outT"])
    prep_stage({"ctrans": [], "decode": [],
                "casts": [(shards["Mup_s"], contrib["Mup"], 16384, 512),
                          (shards["Mdn_s"], contrib["Mdn"], 2048, 2048),
                          (shards["Wsup_s"], contrib["Wsup"], 256, 1024),
                          (shards["Wsdn_s"], contrib["Wsdn"], 64, 2048),
                          (shards["Wgate_s"], contrib["Wgate"], 256, 64)]})
    moe_ag_keys = ("Mup", "Mdn", "Wsup", "Wsdn", "Wgate")
    for eng in ENGS:
        p.wait(eng, "cc", v_cc_w0)

    def mamba_io(L, tag, xin, conv):
        cfg = MambaCfg(L)
        io = dict(conv)
        io["tag"] = tag
        io["x_in"] = xin
        io["acs_dram"] = nc.dram_tensor("acsd_" + tag, [64, CH], F32, kind="Internal")
        io["alast_dram"] = nc.dram_tensor("ald_" + tag, [64], F32, kind="Internal")
        if L == 0:
            io["w_in_slc"] = l0_win_slc(gath["W0in"])
            io["w_in_slc2"] = l0_win_slc2(gath["W0in"])
            io["w_out_slc"] = l0_wout_slc(gath["W0out"])
            io["w_out_slc2"] = l0_wout_slc2(gath["W0out"])
        else:
            io["w_in_slc"] = l1_win_slc(gath["W1inT"])
            io["w_in_slc2"] = l1_win_slc2(gath["W1inT"])
            io["w_out_slc"] = l0_wout_slc(gath["W1outT"])
            io["w_out_slc2"] = l0_wout_slc2(gath["W1outT"])
        return cfg, io

    for L, xin, xout in ((0, x_in, x1), (1, x2, x3)):
        conv = conv0 if L == 0 else conv1
        N = 128 if L == 0 else 64
        cfg, io = mamba_io(L, f"s{L}", xin, {
            "conv_w": conv["conv_w"], "conv_b": conv["conv_b"], "dt_bias": conv["dt_bias"],
            "A_log": conv["A_log"], "Dv": conv["Dv"], "norm_w": conv["norm_w"]})
        if L == 0:
            io["x_halo"] = xh0
        else:
            io["xh_all"] = Gxh
            io["mh8_3"] = mh8_3
        io["out_S"] = exch[L]["cS"]
        io["out_g"] = exch[L]["cg"].transpose([1, 0])
        io["out_halo"] = exch[L]["ch"]
        m = p.push()
        emit_mamba(p, cfg, io, idb, idf, "state")
        p.barrier()
        p.pop(m)
        ag(exch[L]["cS"], exch[L]["GS"])
        ag(exch[L]["cg"], exch[L]["Gg"])
        v_ex = ag(exch[L]["ch"], exch[L]["Gh"])
        if L == 0:
            for k in moe_ag_keys:
                ag(contrib[k], gath[k])
        for eng in ENGS:
            p.wait(eng, "cc", v_ex)

        cfg, io = mamba_io(L, f"f{L}", xin, {
            "conv_w": conv["conv_w"], "conv_b": conv["conv_b"], "dt_bias": conv["dt_bias"],
            "A_log": conv["A_log"], "Dv": conv["Dv"], "norm_w": conv["norm_w"]})
        io["fused"] = True
        io["S_all"] = exch[L]["GS"]
        io["g_all"] = exch[L]["Gg"]
        io["halo_all"] = exch[L]["Gh"]
        io["mrow8"] = mrow8
        io["mcol8"] = mcol8
        io["mh8"] = mh8
        io["x_out"] = xout
        m = p.push()
        emit_mamba(p, cfg, io, idb, idf, "full")
        p.barrier()
        p.pop(m)

        if L == 0:
            m = p.push()
            emit_ffn0(p, {"x_in": x1, "x_out": x2, "g_gT": gath["GgT"], "g_uT": gath["GuT"], "g_dT": gath["GdT"]}, idb, idf)
            p.barrier()
            p.pop(m)
            p.dma("gpsimd", cxh, x2[T - 3:T, :], inc="qa")
            p.wait("gpsimd", "qa", p.val("qa"))
            ag(cxh, Gxh)
            vcc = p.val("cc")
            for eng in ENGS:
                p.wait(eng, "cc", vcc)

    m = p.push()
    emit_moe(p, {
        "x_in": x3, "x_out": x_out,
        "wgate": gath["Wgate"], "wsup": gath["Wsup"], "wsdn": gath["Wsdn"],
        "gup": gath["Mup"], "gdn": gath["Mdn"],
        "u4_tm": nc.dram_tensor("u4_tm", [T, D], BF16, kind="Internal"),
        "slot_dram": nc.dram_tensor("slot_dram", [10240, 1], F32, kind="Internal"),
        "Xall": nc.dram_tensor("Xall", [10240, 2048], BF16, kind="Internal"),
        "down_all": nc.dram_tensor("down_all", [10240, 2048], BF16, kind="Internal"),
    }, idb, idf)
    p.pop(m)
    p.realize()

    w1in_pad = np.zeros((9216, 2048), np.float32)
    w1in_pad[:8384] = inp["L1_attn_W_in"]
    mup_flat = np.ascontiguousarray(inp["L1_ffn_w_up"].reshape(131072, 512))
    mdn_flat = np.ascontiguousarray(inp["L1_ffn_w_down"].reshape(16384, 2048))
    x_full = inp["L0_attn_x"].astype(np.float32)
    in_maps = []
    for c in range(NCORES):
        b, q = c // 4, c % 4
        mr = np.array([[1.0 if (j // 4 == b and j % 4 < q) else 0.0 for j in range(8)]], np.float32)
        mh = np.array([[1.0 if (q > 0 and j == c - 1) else 0.0 for j in range(8)]], np.float32)
        m_ = {
            "x_in": np.ascontiguousarray(x_full[b, q * 1024:(q + 1) * 1024]),
            "xh0": np.zeros((3, D), np.float32) if q == 0 else np.ascontiguousarray(x_full[b, q * 1024 - 3:q * 1024]),
            "mrow8": mr, "mcol8": np.tile(mr, (128, 1)),
            "mh8": np.tile(mh, (128, 1)), "mh8_3": np.tile(mh, (3, 1)),
            "W0in_s": np.ascontiguousarray(inp["L0_attn_in_proj_w"][256 * c:256 * (c + 1)]),
            "W0out_s": np.ascontiguousarray(inp["L0_attn_out_proj_w"][512 * c:512 * (c + 1)]),
            "Mup_s": np.ascontiguousarray(mup_flat[16384 * c:16384 * (c + 1)]),
            "Mdn_s": np.ascontiguousarray(mdn_flat[2048 * c:2048 * (c + 1)]),
            "Wsup_s": np.ascontiguousarray(inp["L1_ffn_w_shared_up"][256 * c:256 * (c + 1)]),
            "Wsdn_s": np.ascontiguousarray(inp["L1_ffn_w_shared_down"][64 * c:64 * (c + 1)]),
            "Wgate_s": np.ascontiguousarray(inp["L1_ffn_w_gate"][256 * c:256 * (c + 1)]),
            "W1in_s": np.ascontiguousarray(w1in_pad[1152 * c:1152 * (c + 1)]),
            "W1out_s": np.ascontiguousarray(inp["L1_attn_W_out"][:, 512 * c:512 * (c + 1)]),
            "gP_s": np.ascontiguousarray(inp["L0_ffn_gate_W_packed"][2048 * c:2048 * (c + 1)]),
            "gS_s": np.ascontiguousarray(inp["L0_ffn_gate_scales"][2048 * c:2048 * (c + 1)]),
            "uP_s": np.ascontiguousarray(inp["L0_ffn_up_W_packed"][2048 * c:2048 * (c + 1)]),
            "uS_s": np.ascontiguousarray(inp["L0_ffn_up_scales"][2048 * c:2048 * (c + 1)]),
            "dP_s": np.ascontiguousarray(inp["L0_ffn_down_W_packed"][:, 1024 * c:1024 * (c + 1)]),
            "dS_s": np.ascontiguousarray(inp["L0_ffn_down_scales"][:, 64 * c:64 * (c + 1)]),
        }
        for k, s in (("c0_", "L0_attn_"), ("c1_", "L1_attn_")):
            m_[k + "conv_w"] = inp[s + "conv_w"]
            m_[k + "conv_b"] = inp[s + "conv_b"]
            m_[k + "dt_bias"] = inp[s + "dt_bias"]
            m_[k + "A_log"] = inp[s + "A_log"]
            m_[k + "Dv"] = inp[s + "D"]
            m_[k + "norm_w"] = inp[s + "norm_w"]
        in_maps.append(m_)
    res = run_bass_kernel_spmd(nc, in_maps, core_ids=list(range(NCORES)))
    out = np.empty((2, 4096, D), np.float32)
    for c in range(NCORES):
        b, q = c // 4, c % 4
        out[b, q * 1024:(q + 1) * 1024] = res.results[c]["x_out"]
    return out


# version 15
# speedup vs baseline: 1.1642x; 1.0094x over previous
import numpy as np
import concourse.bass as bass
import concourse.mybir as mybir
from concourse.bass_utils import run_bass_kernel_spmd
from concourse.bass_core_rust import IndirectOffsetOnAxis

F32 = mybir.dt.float32
BF16 = mybir.dt.bfloat16
I32 = mybir.dt.int32
U32 = mybir.dt.uint32
AF = mybir.ActivationFunctionType
ALU = mybir.AluOpType
AXX = mybir.AxisListType.X

D = 2048
NCORES = 8
T = 1024
TB = 512
NTT = TB // 128
NCH = TB // 128
CH = 128
TT3 = TB + 3
EPS = 1e-5

ENGS = ("sync", "vector", "scalar", "tensor", "gpsimd")


class Prog:

    def __init__(self, nc):
        self.nc = nc
        self.steps = []
        self.sems = {}
        self.cnt = {}
        self._guards = []

    def sem(self, name):
        if name not in self.sems:
            g = self.nc.semaphore(name)
            self.sems[name] = g.__enter__()
            self._guards.append(g)
            self.cnt[name] = 0
        return self.sems[name]

    def sbuf(self, name, shape, dtype):
        g = self.nc.sbuf_tensor(name, shape, dtype)
        t = g.__enter__()
        self._guards.append(g)
        return t

    def psum(self, name, shape, dtype=F32):
        g = self.nc.psum_tensor(name, shape, dtype)
        t = g.__enter__()
        self._guards.append(g)
        return t

    def dram(self, name, shape, dtype, **kw):
        return self.nc.dram_tensor(name, shape, dtype, kind="Internal", **kw)

    AUTO = {"vector": "dve", "scalar": "act", "gpsimd": "gps"}

    def _selfsync(self, eng):
        a = self.AUTO.get(eng)
        if a is None:
            return
        cur = self.val(a)
        sw = getattr(self, "_selfwait", None)
        if sw is None:
            self._selfwait = sw = {}
        if cur > sw.get(eng, 0):
            self.wait(eng, a, cur)
            sw[eng] = cur

    def op(self, eng, fn, inc=None, n=1):
        self._selfsync(eng)
        if inc is None and eng in self.AUTO:
            inc = self.AUTO[eng]
            n = 1
        if inc:
            s = self.sem(inc)
            self.cnt[inc] += n
            self.steps.append((eng, lambda e, fn=fn, s=s, n=n: fn(e).then_inc(s, n)))
            return self.cnt[inc]
        self.steps.append((eng, fn))
        return None

    def _op_nosync(self, eng, fn, inc=None, n=1):
        if inc:
            s = self.sem(inc)
            self.cnt[inc] += n
            self.steps.append((eng, lambda e, fn=fn, s=s, n=n: fn(e).then_inc(s, n)))
            return self.cnt[inc]
        self.steps.append((eng, fn))
        return None

    DMA_SEMS = ("qa", "qw", "qw0", "qw1", "qa0", "qa1")

    def wait(self, eng, sem_name, val):
        if not val or val <= 0:
            return
        if sem_name in self.DMA_SEMS and val < self.cnt.get(sem_name, 0):
            val = self.cnt[sem_name]
        s = self.sem(sem_name)
        mw = getattr(self, "max_waited", None)
        if mw is None:
            self.max_waited = mw = {}
        mw[sem_name] = max(mw.get(sem_name, 0), val)
        self.steps.append((eng, lambda e, s=s, val=val: e.wait_ge(s, val)))

    def dma(self, q, out, in_, inc, **kw):
        mw = getattr(self, "max_waited", None) or {}
        isy = getattr(self, "issuer_synced", None)
        if isy is None:
            self.issuer_synced = isy = {}
        w = mw.get(inc, 0)
        if w > isy.get((q, inc), 0):
            s = self.sem(inc)
            self.steps.append((q, lambda e, s=s, val=w: e.wait_ge(s, val)))
            isy[(q, inc)] = w
        self._selfsync(q)
        return self._op_nosync(q, lambda e, out=out, in_=in_, kw=kw: e.dma_start(out=out, in_=in_, **kw), inc=inc, n=16)

    def val(self, s):
        return self.cnt.get(s, 0)

    def push(self):
        return len(self._guards)

    def pop(self, marker):
        for item in reversed(self._guards[marker:]):
            g = item[0] if isinstance(item, tuple) else item
            g.__exit__(None, None, None)
        del self._guards[marker:]

    def barrier(self, skip_cc=False):
        sems = [s for s in ("dve", "act", "pe", "gps", "qa", "qw", "cc") if s in self.cnt and self.cnt[s] > 0]
        if skip_cc:
            sems = [s for s in sems if s != "cc"]
        for eng in ENGS:
            for s in sems:
                self.wait(eng, s, self.cnt[s])

    def realize(self):
        with self.nc.Block() as block:
            by = {k: [fn for e, fn in self.steps if e == k] for k in ENGS}
            block.sync(lambda e: [fn(e) for fn in by["sync"]])
            block.vector(lambda e: [fn(e) for fn in by["vector"]])
            block.scalar(lambda e: [fn(e) for fn in by["scalar"]])
            block.tensor(lambda e: [fn(e) for fn in by["tensor"]])
            block.gpsimd(lambda e: [fn(e) for fn in by["gpsimd"]])




def emit_idents(p, idb, idf):
    for t_ in (idb, idf):
        v = p.op("gpsimd", lambda e, t_=t_: e.memset(t_[:, :], 1.0), inc="gps")
        p.wait("gpsimd", "gps", v)
        p.op("gpsimd", lambda e, t_=t_: e.affine_select(
            out=t_[:, :], in_=t_[:, :], pattern=[[1, 128]], compare_op=ALU.is_equal,
            fill=0.0, base=0, channel_multiplier=-1), inc="gps")
    p.wait("tensor", "gps", p.val("gps"))



def stream_mm(p, nk, bufs, ncol, srcfn2, body):
    npair = (nk + 1) // 2
    p.wait("sync", "pe", p.val("pe"))
    p.wait("scalar", "pe", p.val("pe"))
    vws = {0: p.dma("sync", bufs[0].ap()[:, 0:2 * ncol].rearrange("p (i c) -> p i c", i=2), srcfn2(0), inc="qw0")}
    vpes = {}
    for kk in range(npair):
        par = kk % 2
        if kk + 1 < npair:
            if kk - 1 >= 0:
                p.wait(("sync", "scalar")[(kk + 1) % 2], "pe", vpes[kk - 1])
            vws[kk + 1] = p.dma(("sync", "scalar")[(kk + 1) % 2],
                                bufs[(kk + 1) % 2].ap()[:, 0:2 * ncol].rearrange("p (i c) -> p i c", i=2),
                                srcfn2(kk + 1), inc=f"qw{(kk + 1) % 2}")
        p.wait("tensor", f"qw{par}", vws[kk])
        for half in range(2):
            k = 2 * kk + half
            if k >= nk:
                break
            body(k, bufs[par].ap()[:, half * ncol:(half + 1) * ncol])
        vpes[kk] = p.val("pe")


class MambaCfg:
    def __init__(self, layer):
        self.layer = layer
        if layer == 0:
            self.PROJ, self.CONV, self.N = 8512, 4352, 128
            self.conv_tiles = 34
            self.dt_mt = 66
            self.last_w = 192
        else:
            self.PROJ, self.CONV, self.N = 8384, 4224, 64
            self.conv_tiles = 33
            self.dt_mt = 65
            self.last_w = 64


def emit_mamba(p, cfg, io, idb, idf, mode):
    L = cfg.layer
    N = cfg.N
    CT = cfg.conv_tiles
    sfx = f"_m{L}{mode[0]}_{io['tag']}"
    full = (mode == "full")

    arena1 = p.sbuf("ar1" + sfx, [128, 34 * TT3], BF16)
    hs_ap = lambda j, a, b: arena1[:, j * TT3 + a: j * TT3 + b]
    yT_ap = lambda k, a, b: arena1[:, k * TB + a: k * TB + b]
    arena2 = p.sbuf("ar2" + sfx, [128, 8192], BF16)
    uT = lambda k, a, b: arena2[:, k * TB + a: k * TB + b]
    ring = p.sbuf("ring" + sfx, [128, 5 * 4096], BF16)
    rg = lambda s: ring[:, (s % 5) * 4096:(s % 5) * 4096 + 4096]
    xdt = p.sbuf("xdt" + sfx, [128, 4096], BF16)
    if mode == "full":
        outb = ring[:, 0:4 * D]
        gate_s = p.sbuf("gts" + sfx, [128, 4 * TB], BF16)
        gtok = p.sbuf("gtok" + sfx, [128, TB], BF16)
    dt_raw = p.sbuf("dtr" + sfx, [64, TB], F32)
    dts = p.sbuf("dts" + sfx, [64, TB], F32)
    adt = p.sbuf("adt" + sfx, [64, TB], F32)
    R = p.sbuf("R" + sfx, [N, 4096], F32)
    R_bf = p.sbuf("Rbf" + sfx, [N, 4096], BF16)
    acs = p.sbuf("acs" + sfx, [64, CH], F32)
    acsT = p.sbuf("acsT" + sfx, [128, 64], F32)
    dec_hm = p.sbuf("dechm" + sfx, [64, CH], F32)
    decT = p.sbuf("decT" + sfx, [128, 64], F32)
    sdecT = p.sbuf("sdecT" + sfx, [128, 64], F32)
    gambc = p.sbuf("gam" + sfx, [128, 64], F32)
    acsf = p.sbuf("acsf" + sfx, [1, 32 * CH], F32)
    grow = p.sbuf("grow" + sfx, [1, 64], F32)
    gsum = p.sbuf("gsum" + sfx, [64, 2], F32)
    Btok = p.sbuf("Btok" + sfx, [128, N], BF16)
    Bd = p.sbuf("Bdd" + sfx, [128, N], BF16)
    Gm = p.sbuf("Gm" + sfx, [128, CH], BF16)
    Dsc = p.sbuf("Dsc" + sfx, [128, 512], F32)
    W4 = p.sbuf("W4" + sfx, [128, 512], BF16)
    ysc = p.sbuf("ysc" + sfx, [128, 64], F32)
    Dbc = p.sbuf("Dbc" + sfx, [128, 64], F32)
    if mode == "full":
        nw_bc = p.sbuf("nwbc" + sfx, [128, 4096], BF16)
    ones1f = p.sbuf("on1" + sfx, [1, 128], F32)
    ones64 = p.sbuf("on64" + sfx, [64, CH], F32)
    srow = p.sbuf("srow" + sfx, [1, 4224], F32)
    convw = p.sbuf("cw" + sfx, [128, 34 * 4], F32)
    convb = p.sbuf("cb" + sfx, [128, 34], F32)
    dtbias = p.sbuf("dtb" + sfx, [64, 1], F32)
    expA = p.sbuf("expA" + sfx, [64, 1], F32)
    cacc = p.sbuf("cacc" + sfx, [128, TB], F32)
    halo_sv = p.sbuf("halo" + sfx, [128, 34 * 3], BF16)
    lnS = p.sbuf("lnS" + sfx, [128, 8], F32)
    xtl = p.sbuf("xtl" + sfx, [128, D], F32)
    utok = p.sbuf("utok" + sfx, [128, D], BF16)
    sq2 = utok
    wb2 = [p.sbuf(f"wbu{i}" + sfx, [128, 1024], BF16) for i in range(2)]
    if mode == "full":
        wr2 = [p.sbuf(f"wru{i}" + sfx, [128, 1024], BF16) for i in range(2)]
        rms = p.sbuf("rms" + sfx, [128, NCH], F32)
    if full:
        mrow = p.sbuf("mrow" + sfx, [1, 8], F32)
        mcol = p.sbuf("mcol" + sfx, [128, 8], F32)
        grow3 = p.sbuf("grw3" + sfx, [1, 192], F32)
        mhc = p.sbuf("mhc" + sfx, [128, 8], F32)
        grow8 = xdt.ap().bitcast(F32)[0:1, 0:512]
        hstg = xdt[:, 1024:1024 + 34 * 3]
    if mode == "state":
        u3 = p.sbuf("u3" + sfx, [128, 16 * 3], BF16)
        x3 = p.sbuf("x3" + sfx, [3, D], F32)
        u3tok = p.sbuf("u3t" + sfx, [3, D], BF16)

    pb = [p.psum(f"pb{i}" + sfx, [128, 512], F32) for i in range(8)]
    pbb = [t.ap().bitcast(BF16) for t in pb]

    QA, QW = "qa", "qw"

    p.op("vector", lambda e: e.memset(ones1f[:, :], 1.0), inc="dve")
    p.op("vector", lambda e: e.memset(ones64[:, :], 1.0), inc="dve")
    p.op("vector", lambda e: e.memset(gsum[:, :], 0.0), inc="dve")
    if not getattr(p, "_consts_done", False):
        p._consts_done = True
        cstt = p.sbuf("cstt" + sfx, [128, 2], F32)
        p.op("vector", lambda e: e.memset(cstt[:, 0:1], EPS), inc="dve")
        p.op("vector", lambda e: e.memset(cstt[:, 1:2], 1.0), inc="dve")
        p.nc.const_aps.register(F32, EPS, cstt[:, 0:1])
        p.nc.const_aps.register(F32, 1.0, cstt[:, 1:2])
    cbt = p.sbuf("cbt" + sfx, [34, 128], F32)
    if L == 0:
        p.dma("gpsimd", convw.ap().rearrange("p (j k) -> p j k", k=4)[:, 0:34, :], io["conv_w"].rearrange("(j p) k -> p j k", p=128), inc=QA)
        p.dma("gpsimd", cbt[:, :], io["conv_b"].rearrange("(j p) -> j p", p=128), inc=QA)
    else:
        p.dma("gpsimd", convw.ap().rearrange("p (j k) -> p j k", k=4)[:, 0:32, :], io["conv_w"][0:4096, :].rearrange("(j p) k -> p j k", p=128), inc=QA)
        p.dma("gpsimd", convw[0:64, 128:132], io["conv_w"][4096:4160, :], inc=QA)
        p.dma("gpsimd", convw[0:64, 132:136], io["conv_w"][4160:4224, :], inc=QA)
        p.dma("gpsimd", cbt[0:32, :], io["conv_b"][0:4096].rearrange("(j p) -> j p", p=128), inc=QA)
        p.dma("gpsimd", cbt[32:33, 0:64], io["conv_b"][4096:4160].unsqueeze(0), inc=QA)
        p.dma("gpsimd", cbt[33:34, 0:64], io["conv_b"][4160:4224].unsqueeze(0), inc=QA)
    p.dma("gpsimd", dtbias[:, :], io["dt_bias"].unsqueeze(1), inc=QA)
    p.dma("gpsimd", expA[:, :], io["A_log"].unsqueeze(1), inc=QA)
    p.dma("gpsimd", srow[0:1, 0:64], io["Dv"].unsqueeze(0), inc=QA)
    p.dma("gpsimd", srow[0:1, 64:64 + 4096], io["norm_w"].unsqueeze(0), inc=QA)
    if full and not io.get("fused"):
        p.dma("gpsimd", mrow[:, 0:3], io["mrow"], inc=QA)
        p.dma("gpsimd", mcol[:, 0:4], io["mcol"], inc=QA)
        p.dma("gpsimd", grow3[0:1, :], io["g_in"], inc=QA)
        p.dma("gpsimd", halo_sv[:, :], io["halo_in"], inc=QA)
    if full and io.get("fused"):
        p.dma("gpsimd", mrow[:, 0:8], io["mrow8"], inc=QA)
        p.dma("gpsimd", mcol[:, 0:8], io["mcol8"], inc=QA)
        p.dma("gpsimd", mhc[:, 0:8], io["mh8"], inc=QA)
        p.dma("gpsimd", grow8[0:1, :], io["g_all"].rearrange("a b -> (a b)").unsqueeze(0), inc=QA)
    vc = p.val(QA)
    p.wait("scalar", QA, vc)
    p.op("scalar", lambda e: e.activation(out=expA[:, :], in_=expA[:, :], func=AF.Exp), inc="act")
    p.wait("tensor", QA, vc)
    p.wait("tensor", "dve", p.val("dve"))
    v_dbcmm = p.op("tensor", lambda e: e.matmul(out=pb[0][:, 0:64], lhsT=ones1f[:, :], rhs=srow[0:1, 0:64], start=True, stop=True), inc="pe")
    if full:
        for j in range(8):
            p.wait("tensor", "dve", p.val("dve"))
            p.op("tensor", lambda e, j=j: e.matmul(out=pb[1][:, :], lhsT=ones1f[:, :], rhs=srow[0:1, 64 + 512 * j:64 + 512 * (j + 1)], start=True, stop=True), inc="pe")
            v = p.val("pe")
            p.wait("vector", "pe", v)
            p.op("vector", lambda e, j=j: e.tensor_copy(out=nw_bc[:, 512 * j:512 * (j + 1)], in_=pb[1][:, :]), inc="dve")
    p.wait("vector", "pe", v_dbcmm)
    p.op("vector", lambda e: e.tensor_copy(out=Dbc[:, :], in_=pb[0][:, 0:64]), inc="dve")
    p.op("tensor", lambda e: e.transpose(out=pb[3][:, 0:34], in_=cbt[:, :], identity=idf[0:34, 0:34]), inc="pe")
    p.wait("vector", "pe", p.val("pe"))
    p.op("vector", lambda e: e.tensor_copy(out=convb[:, 0:34], in_=pb[3][:, 0:34]), inc="dve")
    if full and io.get("fused"):
        p.wait("vector", QA, vc)
        p.op("vector", lambda e: e.memset(halo_sv[:, :], 0.0), inc="dve")
        p.op("vector", lambda e: e.memset(R[:, :], 0.0), inc="dve")
        for j in range(8):
            vh = p.dma("gpsimd", hstg[:, :], io["halo_all"][128 * j:128 * (j + 1), :], inc=QA)
            p.wait("vector", QA, vh)
            p.op("vector", lambda e, j=j: e.scalar_tensor_tensor(
                out=halo_sv[:, :], in0=hstg[:, :], scalar=mhc[:, j:j + 1], in1=halo_sv[:, :],
                op0=ALU.mult, op1=ALU.add), inc="dve")
            p.op("vector", lambda e, j=j: e.tensor_scalar(out=grow[0:1, :], in0=grow8[0:1, 64 * j:64 * (j + 1)], scalar1=mrow[0:1, j:j + 1], scalar2=None, op0=ALU.mult), inc="dve")
            p.op("vector", lambda e, j=j: e.tensor_scalar(out=srow[0:1, 4160:4224], in0=mrow[0:1, j:j + 1].to_broadcast([1, 64]), scalar1=-1.0, scalar2=1.0, op0=ALU.mult, op1=ALU.add), inc="dve")
            p.op("vector", lambda e: e.tensor_add(out=grow[0:1, :], in0=grow[0:1, :], in1=srow[0:1, 4160:4224]), inc="dve")
            vg = p.val("dve")
            p.wait("tensor", "dve", vg)
            p.op("tensor", lambda e: e.matmul(out=pb[2][:, 0:64], lhsT=ones1f[:, :], rhs=grow[0:1, :], start=True, stop=True), inc="pe")
            vpe = p.val("pe")
            p.wait("vector", "pe", vpe)
            p.op("vector", lambda e: e.tensor_copy(out=gambc[:, :], in_=pb[2][:, 0:64]), inc="dve")
            p.op("vector", lambda e: e.tensor_tensor(
                out=R.ap().rearrange("p (h d) -> p h d", h=64),
                in0=R.ap().rearrange("p (h d) -> p h d", h=64),
                in1=gambc[0:N, :].unsqueeze(2).to_broadcast([N, 64, 64]), op=ALU.mult), inc="dve")
            p.wait("gpsimd", "dve", p.val("dve"))
            vS = p.dma("gpsimd", R_bf.ap().bitcast(F32)[0:N, 0:2048], io["S_all"][N * j:N * (j + 1), 0:2048], inc=QA)
            p.wait("vector", QA, vS)
            p.op("vector", lambda e, j=j: e.scalar_tensor_tensor(
                out=R[0:N, 0:2048], in0=R_bf.ap().bitcast(F32)[0:N, 0:2048], scalar=mcol[0:N, j:j + 1],
                in1=R[0:N, 0:2048], op0=ALU.mult, op1=ALU.add), inc="dve")
            p.wait("gpsimd", "dve", p.val("dve"))
            vS2 = p.dma("gpsimd", R_bf.ap().bitcast(F32)[0:N, 0:2048], io["S_all"][N * j:N * (j + 1), 2048:4096], inc=QA)
            p.wait("vector", QA, vS2)
            p.op("vector", lambda e, j=j: e.scalar_tensor_tensor(
                out=R[0:N, 2048:4096], in0=R_bf.ap().bitcast(F32)[0:N, 0:2048], scalar=mcol[0:N, j:j + 1],
                in1=R[0:N, 2048:4096], op0=ALU.mult, op1=ALU.add), inc="dve")
    elif full:
        p.wait("vector", QA, vc)
        p.op("vector", lambda e: e.tensor_scalar(out=halo_sv[:, :], in0=halo_sv[:, :], scalar1=mcol[:, 3:4], scalar2=None, op0=ALU.mult), inc="dve")
        p.op("vector", lambda e: e.memset(R[:, :], 0.0), inc="dve")
        for j in range(3):
            p.op("vector", lambda e, j=j: e.tensor_scalar(out=grow[0:1, :], in0=grow3[0:1, 64 * j:64 * (j + 1)], scalar1=mrow[0:1, j:j + 1], scalar2=None, op0=ALU.mult), inc="dve")
            p.op("vector", lambda e, j=j: e.tensor_scalar(out=srow[0:1, 4160:4224], in0=mrow[0:1, j:j + 1].to_broadcast([1, 64]), scalar1=-1.0, scalar2=1.0, op0=ALU.mult, op1=ALU.add), inc="dve")
            p.op("vector", lambda e: e.tensor_add(out=grow[0:1, :], in0=grow[0:1, :], in1=srow[0:1, 4160:4224]), inc="dve")
            vg = p.val("dve")
            p.wait("tensor", "dve", vg)
            p.op("tensor", lambda e: e.matmul(out=pb[2][:, 0:64], lhsT=ones1f[:, :], rhs=grow[0:1, :], start=True, stop=True), inc="pe")
            vpe = p.val("pe")
            p.wait("vector", "pe", vpe)
            p.op("vector", lambda e: e.tensor_copy(out=gambc[:, :], in_=pb[2][:, 0:64]), inc="dve")
            p.op("vector", lambda e: e.tensor_tensor(
                out=R.ap().rearrange("p (h d) -> p h d", h=64),
                in0=R.ap().rearrange("p (h d) -> p h d", h=64),
                in1=gambc[0:N, :].unsqueeze(2).to_broadcast([N, 64, 64]), op=ALU.mult), inc="dve")
            p.wait("gpsimd", "dve", p.val("dve"))
            vS = p.dma("gpsimd", R_bf.ap().bitcast(F32)[0:N, 0:2048], io["S_in"][j][0:N, 0:2048], inc=QA)
            p.wait("vector", QA, vS)
            p.op("vector", lambda e, j=j: e.scalar_tensor_tensor(
                out=R[0:N, 0:2048], in0=R_bf.ap().bitcast(F32)[0:N, 0:2048], scalar=mcol[0:N, j:j + 1],
                in1=R[0:N, 0:2048], op0=ALU.mult, op1=ALU.add), inc="dve")
            p.wait("gpsimd", "dve", p.val("dve"))
            vS2 = p.dma("gpsimd", R_bf.ap().bitcast(F32)[0:N, 0:2048], io["S_in"][j][0:N, 2048:4096], inc=QA)
            p.wait("vector", QA, vS2)
            p.op("vector", lambda e, j=j: e.scalar_tensor_tensor(
                out=R[0:N, 2048:4096], in0=R_bf.ap().bitcast(F32)[0:N, 0:2048], scalar=mcol[0:N, j:j + 1],
                in1=R[0:N, 2048:4096], op0=ALU.mult, op1=ALU.add), inc="dve")
    else:
        p.op("vector", lambda e: e.memset(R[:, :], 0.0), inc="dve")
        p.op("vector", lambda e: e.memset(halo_sv[:, :], 0.0), inc="dve")
    p.wait("tensor", "dve", p.val("dve"))

    if L == 0:
        units = [(128 * (32 + j), 128, ("hs", j, 128)) for j in range(34)]
        units.append((8448, 64, ("dt",)))
        conv_info = [(j, 128) for j in range(34)]
    else:
        units = [(128 * (32 + j), 128, ("hs", j, 128)) for j in range(32)]
        units += [(8192, 64, ("hs", 32, 64)), (8256, 64, ("hs", 33, 64)), (8320, 64, ("dt",))]
        conv_info = [(j, 128) for j in range(32)] + [(32, 64), (33, 64)]
    hs_groups = []
    i = 0
    while i < len(units):
        g = [units[i]]
        while (len(g) < 4 and i + len(g) < len(units)
               and sum(u[1] for u in g) + units[i + len(g)][1] <= 512
               and (L == 0 or (units[i][0] // 1152) == ((units[i + len(g)][0] + units[i + len(g)][1] - 1) // 1152))):
            g.append(units[i + len(g)])
        hs_groups.append(g)
        i += len(g)
    gate_groups = []
    i = 0
    while i < 32:
        e2 = min(i + 4, 32)
        if L == 1:
            e2 = min(e2, (i // 9 + 1) * 9)
        gate_groups.append(list(range(i, e2)))
        i = e2

    if mode == "state":
        if io.get("xh_all") is not None:
            mh3 = lnS[0:3, 0:8]
            xh3 = xtl[0:3, :]
            p.dma("gpsimd", mh3[:, :], io["mh8_3"], inc=QA)
            p.op("vector", lambda e: e.memset(x3[:, :], 0.0), inc="dve")
            p.wait("vector", QA, p.val(QA))
            for j in range(8):
                v3 = p.dma("gpsimd", xh3[:, :], io["xh_all"][3 * j:3 * (j + 1), :], inc=QA)
                p.wait("vector", QA, v3)
                p.op("vector", lambda e, j=j: e.scalar_tensor_tensor(
                    out=x3[:, :], in0=xh3[:, :], scalar=mh3[:, j:j + 1], in1=x3[:, :],
                    op0=ALU.mult, op1=ALU.add), inc="dve")
                p.wait("gpsimd", "dve", p.val("dve"))
            v3 = p.val(QA)
        else:
            v3 = p.dma("gpsimd", x3[:, :], io["x_halo"], inc=QA)
        p.wait("vector", QA, v3)
        p.op("vector", lambda e: e.tensor_reduce(out=lnS[0:3, 0:1], in_=x3[:, :], axis=AXX, op=ALU.add), inc="dve")
        p.wait("scalar", QA, v3)
        p.op("scalar", lambda e: e.activation(out=sq2[0:3, :], in_=x3[:, :], func=AF.Square), inc="act")
        p.wait("vector", "act", p.val("act"))
        p.op("vector", lambda e: e.tensor_scalar(out=lnS[0:3, 1:2], in0=lnS[0:3, 0:1], scalar1=0.0, scalar2=None, op0=ALU.mult), inc="dve")
        p.op("vector", lambda e: e.tensor_reduce(out=lnS[0:3, 1:2], in_=sq2[0:3, :], axis=AXX, op=ALU.add), inc="dve")
        p.op("vector", lambda e: e.tensor_scalar(out=lnS[0:3, 1:2], in0=lnS[0:3, 1:2], scalar1=1.0 / D, scalar2=None, op0=ALU.mult), inc="dve")
        p.op("vector", lambda e: e.tensor_scalar(out=lnS[0:3, 2:3], in0=lnS[0:3, 0:1], scalar1=1.0 / D, scalar2=None, op0=ALU.mult), inc="dve")
        p.op("vector", lambda e: e.tensor_scalar(out=lnS[0:3, 4:5], in0=lnS[0:3, 2:3], scalar1=lnS[0:3, 2:3], scalar2=None, op0=ALU.mult), inc="dve")
        p.op("vector", lambda e: e.tensor_sub(out=lnS[0:3, 4:5], in0=lnS[0:3, 1:2], in1=lnS[0:3, 4:5]), inc="dve")
        vd = p.val("dve")
        p.wait("scalar", "dve", vd)
        p.op("scalar", lambda e: e.activation(out=lnS[0:3, 5:6], in_=lnS[0:3, 4:5], func=AF.Ln, bias=EPS), inc="act")
        p.op("scalar", lambda e: e.activation(out=lnS[0:3, 6:7], in_=lnS[0:3, 5:6], func=AF.Exp, scale=-0.5), inc="act")
        va = p.val("act")
        p.wait("vector", "act", va)
        p.op("vector", lambda e: e.tensor_scalar(out=u3tok[:, :], in0=x3[:, :], scalar1=lnS[0:3, 2:3], scalar2=lnS[0:3, 6:7], op0=ALU.subtract, op1=ALU.mult), inc="dve")
        vd = p.val("dve")
        p.wait("tensor", "dve", vd)
        for j in range(16):
            p.op("tensor", lambda e, j=j: e.transpose(
                out=pbb[0][:, (j % 4) * 128:(j % 4) * 128 + 3], in_=u3tok[:, j * 128:(j + 1) * 128],
                identity=idb[0:3, 0:3]), inc=("pe" if j % 4 == 3 else None))
            if j % 4 == 3:
                v = p.val("pe")
                p.wait("scalar", "pe", v)
                for jj in range(j - 3, j + 1):
                    p.op("scalar", lambda e, jj=jj: e.copy(out=u3[:, jj * 3:(jj + 1) * 3], in_=pbb[0][:, (jj % 4) * 128:(jj % 4) * 128 + 3]), inc="act")
                p.wait("tensor", "act", p.val("act"))
        for gi, grp in enumerate(hs_groups):
            m0 = grp[0][0]
            ncol = sum(u[1] for u in grp)
            for k in range(16):
                vw = p.dma("sync", wb2[0][:, 0:ncol], io["w_in_slc"](k, m0, ncol), inc=QW)
                p.wait("tensor", QW, vw)
                co = 0
                for mi, (c0, w_, dest) in enumerate(grp):
                    p.op("tensor", lambda e, k=k, co=co, w_=w_, mi=mi: e.matmul(
                        out=pb[mi][0:w_, 0:3], lhsT=wb2[0][:, co:co + w_], rhs=u3[:, k * 3:(k + 1) * 3],
                        start=(k == 0), stop=(k == 15)), inc=("pe" if mi == len(grp) - 1 else None))
                    co += w_
                vpe = p.val("pe")
                p.wait("sync", "pe", vpe)
            vpe = p.val("pe")
            p.wait("scalar", "pe", vpe)
            for mi, (c0, w_, dest) in enumerate(grp):
                if dest[0] == "hs":
                    j, pn = dest[1], dest[2]
                    p.op("scalar", lambda e, j=j, mi=mi, pn=pn: e.copy(out=hs_ap(j, 0, 3)[0:pn, :], in_=pb[mi][0:pn, 0:3]), inc="act")
            p.wait("tensor", "act", p.val("act"))
        va = p.val("act")
        p.wait("vector", "act", va)
        for (j, pn) in conv_info:
            p.op("vector", lambda e, j=j, pn=pn: e.tensor_copy(out=halo_sv[0:pn, 3 * j:3 * j + 3], in_=hs_ap(j, 0, 3)[0:pn, :]), inc="dve")
        p.wait("scalar", "dve", p.val("dve"))

    for tb in range(2):
        t0 = tb * TB

        for it in range(NTT):
            p.wait("gpsimd", "dve", p.val("dve"))
            vx = p.dma("gpsimd", xtl[:, :], io["x_in"][t0 + it * 128: t0 + (it + 1) * 128, :], inc=QA)
            p.wait("vector", QA, vx)
            p.op("vector", lambda e: e.tensor_reduce(out=lnS[:, 0:1], in_=xtl[:, :], axis=AXX, op=ALU.add), inc="dve")
            p.wait("scalar", QA, vx)
            p.op("scalar", lambda e: e.activation(out=sq2[:, :], in_=xtl[:, :], func=AF.Square), inc="act")
            p.wait("vector", "act", p.val("act"))
            p.op("vector", lambda e: e.tensor_reduce(out=lnS[:, 1:2], in_=sq2[:, :], axis=AXX, op=ALU.add), inc="dve")
            p.op("vector", lambda e: e.tensor_scalar(out=lnS[:, 1:2], in0=lnS[:, 1:2], scalar1=1.0 / D, scalar2=None, op0=ALU.mult), inc="dve")
            p.op("vector", lambda e: e.tensor_scalar(out=lnS[:, 2:3], in0=lnS[:, 0:1], scalar1=1.0 / D, scalar2=None, op0=ALU.mult), inc="dve")
            p.op("vector", lambda e: e.tensor_scalar(out=lnS[:, 4:5], in0=lnS[:, 2:3], scalar1=lnS[:, 2:3], scalar2=None, op0=ALU.mult), inc="dve")
            p.op("vector", lambda e: e.tensor_sub(out=lnS[:, 4:5], in0=lnS[:, 1:2], in1=lnS[:, 4:5]), inc="dve")
            vd = p.val("dve")
            p.wait("scalar", "dve", vd)
            p.op("scalar", lambda e: e.activation(out=lnS[:, 5:6], in_=lnS[:, 4:5], func=AF.Ln, bias=EPS), inc="act")
            p.op("scalar", lambda e: e.activation(out=lnS[:, 6:7], in_=lnS[:, 5:6], func=AF.Exp, scale=-0.5), inc="act")
            va = p.val("act")
            p.wait("vector", "act", va)
            p.op("vector", lambda e: e.tensor_scalar(out=utok[:, :], in0=xtl[:, :], scalar1=lnS[:, 2:3], scalar2=lnS[:, 6:7], op0=ALU.subtract, op1=ALU.mult), inc="dve")
            vd = p.val("dve")
            p.wait("tensor", "dve", vd)
            for j in range(16):
                p.op("tensor", lambda e, j=j: e.transpose(
                    out=pbb[j % 4][:, 0:128], in_=utok[:, j * 128:(j + 1) * 128],
                    identity=idb[:, :]), inc=("pe" if j % 4 == 3 else None))
                if j % 4 == 3:
                    v = p.val("pe")
                    p.wait("scalar", "pe", v)
                    for jj in range(j - 3, j + 1):
                        p.op("scalar", lambda e, jj=jj, it=it: e.copy(
                            out=uT(jj, it * 128, it * 128 + 128), in_=pbb[jj % 4][:, 0:128]), inc="act")
                    p.wait("tensor", "act", p.val("act"))

        evac_val = {0: 0, 1: 0}
        for gi, grp in enumerate(hs_groups):
            m0 = grp[0][0]
            ncol = sum(u[1] for u in grp)
            pbase = 4 * (gi % 2)
            p.wait("tensor", "act", evac_val[gi % 2])

            def body(k, buf, grp=grp, pbase=pbase):
                co = 0
                for mi, (c0, w_, dest) in enumerate(grp):
                    p.op("tensor", lambda e, k=k, co=co, w_=w_, mi=mi, buf=buf, pbase=pbase: e.matmul(
                        out=pb[pbase + mi][0:w_, :],
                        lhsT=buf[:, co:co + w_], rhs=uT(k, 0, TB),
                        start=(k == 0), stop=(k == 15)), inc=("pe" if mi == len(grp) - 1 else None))
                    co += w_

            stream_mm(p, 16, wb2, ncol,
                      lambda kk, m0=m0, ncol=ncol: io["w_in_slc2"](kk, m0, ncol), body)
            vpe = p.val("pe")
            p.wait("scalar", "pe", vpe)
            for mi, (c0, w_, dest) in enumerate(grp):
                if dest[0] == "dt":
                    p.op("scalar", lambda e, mi=mi, pbase=pbase: e.copy(out=dt_raw[:, :], in_=pb[pbase + mi][0:64, :]), inc="act")
                else:
                    j, pn = dest[1], dest[2]
                    p.op("scalar", lambda e, j=j, mi=mi, pn=pn, pbase=pbase: e.copy(out=hs_ap(j, 3, 3 + TB)[0:pn, :], in_=pb[pbase + mi][0:pn, :]), inc="act")
            evac_val[gi % 2] = p.val("act")
        p.wait("tensor", "act", p.val("act"))

        va_hs = p.val("act")
        p.wait("vector", "act", va_hs)
        p.wait("vector", QA, p.val(QA))
        for (j, pn) in conv_info:
            p.op("vector", lambda e, j=j, pn=pn: e.tensor_copy(out=hs_ap(j, 0, 3)[0:pn, :], in_=halo_sv[0:pn, 3 * j:3 * j + 3]), inc="dve")
            p.op("vector", lambda e, j=j, pn=pn: e.tensor_copy(out=halo_sv[0:pn, 3 * j:3 * j + 3], in_=hs_ap(j, TB, TB + 3)[0:pn, :]), inc="dve")
            p.op("vector", lambda e, j=j, pn=pn: e.tensor_scalar(
                out=cacc[0:pn, :], in0=hs_ap(j, 0, TB)[0:pn, :], scalar1=convw[0:pn, 4 * j:4 * j + 1], scalar2=None, op0=ALU.mult), inc="dve")
            for kk in range(1, 4):
                p.op("vector", lambda e, j=j, kk=kk, pn=pn: e.scalar_tensor_tensor(
                    out=cacc[0:pn, :], in0=hs_ap(j, kk, kk + TB)[0:pn, :], scalar=convw[0:pn, 4 * j + kk:4 * j + kk + 1],
                    in1=cacc[0:pn, :], op0=ALU.mult, op1=ALU.add), inc="dve")
            vd = p.val("dve")
            p.wait("scalar", "dve", vd)
            p.op("scalar", lambda e, j=j, pn=pn: e.activation(
                out=hs_ap(j, 3, 3 + TB)[0:pn, :], in_=cacc[0:pn, :], func=AF.Silu, bias=convb[0:pn, j:j + 1]), inc="act")
            p.wait("vector", "act", p.val("act"))

        p.op("vector", lambda e: e.tensor_scalar(out=dts[:, :], in0=dt_raw[:, :], scalar1=dtbias[:, :], scalar2=None, op0=ALU.add), inc="dve")
        vd = p.val("dve")
        p.wait("scalar", "dve", vd)
        p.op("scalar", lambda e: e.activation(out=dt_raw[:, :], in_=dts[:, :], func=AF.Abs), inc="act")
        p.op("scalar", lambda e: e.activation(out=adt[:, :], in_=dt_raw[:, :], func=AF.Exp, scale=-1.0), inc="act")
        p.op("scalar", lambda e: e.activation(out=adt[:, :], in_=adt[:, :], func=AF.Ln, bias=1.0), inc="act")
        va = p.val("act")
        p.wait("vector", "act", va)
        p.op("vector", lambda e: e.tensor_scalar(out=dts[:, :], in0=dts[:, :], scalar1=0.0, scalar2=None, op0=ALU.max), inc="dve")
        p.op("vector", lambda e: e.tensor_add(out=dts[:, :], in0=dts[:, :], in1=adt[:, :]), inc="dve")
        p.op("vector", lambda e: e.tensor_scalar(out=adt[:, :], in0=dts[:, :], scalar1=expA[:, :], scalar2=-1.0, op0=ALU.mult, op1=ALU.mult), inc="dve")
        if mode == "state":
            p.op("vector", lambda e: e.tensor_reduce(out=gsum[:, 1:2], in_=adt[:, :], axis=AXX, op=ALU.add), inc="dve")
            p.op("vector", lambda e: e.tensor_add(out=gsum[:, 0:1], in0=gsum[:, 0:1], in1=gsum[:, 1:2]), inc="dve")
        vd_dt = p.val("dve")

        for c in range(NCH):
            cs = c * CH
            slot_x = (tb * NCH + c + 1) % 5
            slot_y = (tb * NCH + c) % 5
            p.wait("vector", "dve", vd_dt)
            p.op("vector", lambda e, cs=cs: e.tensor_tensor_scan(
                out=acs[:, :], data0=ones64[:, :], data1=adt[:, cs:cs + CH],
                initial=0.0, op0=ALU.mult, op1=ALU.add), inc="dve")
            vd_acs = p.val("dve")
            p.op("vector", lambda e: e.tensor_scalar(out=dec_hm[:, :], in0=acs[:, :], scalar1=acs[:, CH - 1:CH], scalar2=-1.0, op0=ALU.subtract, op1=ALU.mult), inc="dve")
            vd2 = p.val("dve")
            p.wait("scalar", "dve", vd2)
            p.op("scalar", lambda e: e.activation(out=dec_hm[:, :], in_=dec_hm[:, :], func=AF.Exp), inc="act")
            va_dec = p.val("act")
            p.wait("gpsimd", "dve", vd_acs)
            p.dma("gpsimd", io["acs_dram"].ap(), acs[:, :], inc=QA)
            p.wait("gpsimd", QA, p.val(QA))
            if full:
                p.dma("gpsimd", acsf[:, :], io["acs_dram"].ap()[0:32, :].rearrange("h t -> (h t)").unsqueeze(0), inc=QA)
            v_acsf = p.val(QA)
            p.dma("gpsimd", io["alast_dram"].ap().unsqueeze(1), acs[:, CH - 1:CH], inc=QA)
            p.wait("gpsimd", QA, p.val(QA))
            vgr = p.dma("gpsimd", grow[0:1, :], io["alast_dram"].ap().unsqueeze(0), inc=QA)
            p.wait("tensor", "dve", vd_acs)
            p.op("tensor", lambda e: e.transpose(out=pb[4][:, 0:64], in_=acs[:, :], identity=idf[0:64, 0:64]), inc="pe")
            p.wait("tensor", "act", va_dec)
            p.op("tensor", lambda e: e.transpose(out=pb[5][:, 0:64], in_=dec_hm[:, :], identity=idf[0:64, 0:64]), inc="pe")
            p.op("tensor", lambda e, cs=cs: e.transpose(out=pb[6][:, 0:64], in_=dts[:, cs:cs + CH], identity=idf[0:64, 0:64]), inc="pe")
            vpe = p.val("pe")
            p.wait("vector", "pe", vpe)
            p.op("vector", lambda e: e.tensor_copy(out=acsT[:, :], in_=pb[4][:, 0:64]), inc="dve")
            p.op("vector", lambda e: e.tensor_copy(out=decT[:, :], in_=pb[5][:, 0:64]), inc="dve")
            p.op("vector", lambda e: e.tensor_copy(out=Dsc[:, 0:64], in_=pb[6][:, 0:64]), inc="dve")
            vd_acsT = p.val("dve")
            p.wait("scalar", "dve", vd_acsT)
            if full:
                p.op("scalar", lambda e: e.activation(out=sdecT[:, :], in_=acsT[:, :], func=AF.Exp), inc="act")
            p.wait("scalar", QA, vgr)
            p.op("scalar", lambda e: e.activation(out=grow[:, :], in_=grow[:, :], func=AF.Exp), inc="act")
            va_g = p.val("act")
            p.wait("tensor", "act", va_g)
            p.op("tensor", lambda e: e.matmul(out=pb[7][:, 0:64], lhsT=ones1f[:, :], rhs=grow[:, :], start=True, stop=True), inc="pe")
            vpe_g = p.val("pe")
            p.wait("vector", "pe", vpe_g)
            p.op("vector", lambda e: e.tensor_copy(out=gambc[:, :], in_=pb[7][:, 0:64]), inc="dve")
            vd_gam = p.val("dve")
            p.wait("tensor", "dve", vd_gam)

            for jg in range(8):
                for j4 in range(4):
                    j = jg * 4 + j4
                    p.op("tensor", lambda e, j=j, j4=j4, cs=cs: e.transpose(
                        out=pbb[j4][:, 0:128], in_=hs_ap(j, 3 + cs, 3 + cs + CH),
                        identity=idb[:, :]), inc=("pe" if j4 == 3 else None))
                v = p.val("pe")
                p.wait("scalar", "pe", v)
                for j4 in range(4):
                    j = jg * 4 + j4
                    p.op("scalar", lambda e, j=j, j4=j4, slot_x=slot_x: e.copy(
                        out=rg(slot_x)[:, j * 128:(j + 1) * 128], in_=pbb[j4][:, 0:128]), inc="act")
                p.wait("tensor", "act", p.val("act"))
            if L == 0:
                p.op("tensor", lambda e, cs=cs: e.transpose(out=pbb[0][:, 0:128], in_=hs_ap(32, 3 + cs, 3 + cs + CH), identity=idb[:, :]), inc="pe")
            else:
                p.op("tensor", lambda e, cs=cs: e.transpose(out=pbb[0][:, 0:64], in_=hs_ap(32, 3 + cs, 3 + cs + CH)[0:64, :], identity=idb[0:64, 0:64]), inc="pe")
            vpe_b = p.val("pe")
            p.wait("scalar", "pe", vpe_b)
            p.op("scalar", lambda e: e.copy(out=Btok[:, 0:N], in_=pbb[0][:, 0:N]), inc="act")
            va_bt = p.val("act")
            p.wait("tensor", "act", va_bt)
            p.wait("vector", "act", va_bt)

            p.op("vector", lambda e, slot_x=slot_x: e.tensor_tensor(
                out=xdt.ap().rearrange("p (h d) -> p h d", h=64),
                in0=rg(slot_x).rearrange("p (h d) -> p h d", h=64),
                in1=Dsc[:, 0:64].unsqueeze(2).to_broadcast([128, 64, 64]), op=ALU.mult), inc="dve")
            p.op("vector", lambda e: e.tensor_copy(out=R_bf[:, :], in_=R[:, :]), inc="dve")
            vd_xdt = p.val("dve")
            p.wait("tensor", "dve", vd_xdt)

            if full:
                if L == 0:
                    p.op("tensor", lambda e, cs=cs: e.matmul(
                        out=pb[6][:, 0:128], lhsT=hs_ap(32, 3 + cs, 3 + cs + CH),
                        rhs=hs_ap(33, 3 + cs, 3 + cs + CH), start=True, stop=True), inc="pe")
                else:
                    p.op("tensor", lambda e, cs=cs: e.matmul(
                        out=pb[6][:, 0:128], lhsT=hs_ap(32, 3 + cs, 3 + cs + CH)[0:64, :],
                        rhs=hs_ap(33, 3 + cs, 3 + cs + CH)[0:64, :], start=True, stop=True), inc="pe")
                vpe_G = p.val("pe")
                p.wait("scalar", "pe", vpe_G)
                p.op("scalar", lambda e: e.copy(out=Gm[:, :], in_=pb[6][:, 0:128]), inc="act")
                va_G = p.val("act")
                p.wait("gpsimd", "act", va_G)
                p.op("gpsimd", lambda e: e.affine_select(
                    out=Gm[:, :], in_=Gm[:, :], pattern=[[1, CH]], compare_op=ALU.is_ge,
                    fill=0.0, base=0, channel_multiplier=-1), inc="gps")
                v_gm = p.val("gps")
                p.wait("vector", "gps", v_gm)
                p.wait("tensor", "gps", v_gm)
                p.wait("vector", QA, p.val(QA))

            for g in range(16 if full else 0):
                h0 = 4 * g
                if g == 8:
                    p.wait("gpsimd", "pe", p.val("pe"))
                    p.dma("gpsimd", acsf[:, :], io["acs_dram"].ap()[32:64, :].rearrange("h t -> (h t)").unsqueeze(0), inc=QA)
                    p.wait("tensor", QA, p.val(QA))
                p.op("tensor", lambda e, h0=h0: e.matmul(
                    out=pb[4][:, :], lhsT=ones1f[:, :], rhs=acsf[0:1, (h0 % 32) * CH:((h0 % 32) + 4) * CH],
                    start=True, stop=True), inc="pe")
                v_db = p.val("pe")
                p.wait("vector", "pe", v_db)
                p.op("vector", lambda e, h0=h0: e.tensor_tensor(
                    out=Dsc.ap().rearrange("p (q l) -> p q l", q=4),
                    in0=pb[4].ap().rearrange("p (q l) -> p q l", q=4),
                    in1=acsT[:, h0:h0 + 4].unsqueeze(2).to_broadcast([128, 4, CH]), op=ALU.subtract), inc="dve")
                p.op("vector", lambda e: e.tensor_scalar(out=Dsc[:, :], in0=Dsc[:, :], scalar1=30.0, scalar2=None, op0=ALU.min), inc="dve")
                vd = p.val("dve")
                p.wait("scalar", "dve", vd)
                p.op("scalar", lambda e: e.activation(out=W4[:, :], in_=Dsc[:, :], func=AF.Exp), inc="act")
                va = p.val("act")
                p.wait("vector", "act", va)
                p.op("vector", lambda e: e.tensor_tensor(
                    out=W4.ap().rearrange("p (q l) -> p q l", q=4),
                    in0=W4.ap().rearrange("p (q l) -> p q l", q=4),
                    in1=Gm[:, :].unsqueeze(1).to_broadcast([128, 4, CH]), op=ALU.mult), inc="dve")
                vd_w4 = p.val("dve")
                p.wait("tensor", "dve", vd_w4)
                for hh in range(4):
                    h = h0 + hh
                    p.op("tensor", lambda e, hh=hh, h=h: e.matmul(
                        out=pb[5][:, hh * 64:(hh + 1) * 64], lhsT=W4[:, hh * CH:hh * CH + CH],
                        rhs=xdt[:, h * 64:(h + 1) * 64], start=True, stop=True))
                    if L == 0:
                        p.op("tensor", lambda e, hh=hh, h=h, cs=cs: e.matmul(
                            out=pb[6][:, hh * 64:(hh + 1) * 64], lhsT=hs_ap(33, 3 + cs, 3 + cs + CH),
                            rhs=R_bf[:, h * 64:(h + 1) * 64], start=True, stop=True),
                            inc=("pe" if hh == 3 else None))
                    else:
                        p.op("tensor", lambda e, hh=hh, h=h, cs=cs: e.matmul(
                            out=pb[6][:, hh * 64:(hh + 1) * 64],
                            lhsT=hs_ap(33, 3 + cs, 3 + cs + CH)[0:64, :],
                            rhs=R_bf[:, h * 64:(h + 1) * 64], start=True, stop=True),
                            inc=("pe" if hh == 3 else None))
                vpe_y = p.val("pe")
                p.wait("vector", "pe", vpe_y)
                for hh in range(4):
                    h = h0 + hh
                    p.op("vector", lambda e, h=h: e.tensor_scalar(
                        out=Bd[:, 0:N], in0=Btok[:, 0:N], scalar1=decT[:, h:h + 1], scalar2=None, op0=ALU.mult), inc="dve")
                    vb = p.val("dve")
                    p.wait("tensor", "dve", vb)
                    p.op("tensor", lambda e, hh=hh, h=h: e.matmul(
                        out=pb[7][0:N, hh * 64:(hh + 1) * 64], lhsT=Bd[:, 0:N],
                        rhs=xdt[:, h * 64:(h + 1) * 64], start=True, stop=True), inc="pe")
                    vS = p.val("pe")
                    p.op("vector", lambda e, h=h, hh=hh, slot_x=slot_x: e.scalar_tensor_tensor(
                        out=ysc[:, :], in0=rg(slot_x)[:, h * 64:(h + 1) * 64], scalar=Dbc[:, h:h + 1],
                        in1=pb[5][:, hh * 64:(hh + 1) * 64], op0=ALU.mult, op1=ALU.add), inc="dve")
                    p.op("vector", lambda e, h=h, hh=hh, slot_y=slot_y: e.scalar_tensor_tensor(
                        out=rg(slot_y)[:, h * 64:(h + 1) * 64], in0=pb[6][:, hh * 64:(hh + 1) * 64],
                        scalar=sdecT[:, h:h + 1], in1=ysc[:, :], op0=ALU.mult, op1=ALU.add), inc="dve")
                    p.wait("vector", "pe", vS)
                    p.op("vector", lambda e, h=h: e.tensor_scalar(
                        out=R[0:N, h * 64:(h + 1) * 64], in0=R[0:N, h * 64:(h + 1) * 64],
                        scalar1=gambc[0:N, h:h + 1], scalar2=None, op0=ALU.mult), inc="dve")
                    p.op("vector", lambda e, h=h, hh=hh: e.tensor_add(
                        out=R[0:N, h * 64:(h + 1) * 64], in0=R[0:N, h * 64:(h + 1) * 64],
                        in1=pb[7][0:N, hh * 64:(hh + 1) * 64]), inc="dve")
                vd_grp = p.val("dve")
                p.wait("tensor", "dve", vd_grp)

            if not full:
                for g in range(16):
                    h0 = 4 * g
                    for hh in range(4):
                        h = h0 + hh
                        p.op("vector", lambda e, h=h: e.tensor_scalar(
                            out=Bd[:, 0:N], in0=Btok[:, 0:N], scalar1=decT[:, h:h + 1], scalar2=None, op0=ALU.mult), inc="dve")
                        vb = p.val("dve")
                        p.wait("tensor", "dve", vb)
                        p.op("tensor", lambda e, hh=hh, h=h: e.matmul(
                            out=pb[7][0:N, hh * 64:(hh + 1) * 64], lhsT=Bd[:, 0:N],
                            rhs=xdt[:, h * 64:(h + 1) * 64], start=True, stop=True), inc="pe")
                        vS = p.val("pe")
                        p.wait("vector", "pe", vS)
                        p.op("vector", lambda e, h=h: e.tensor_scalar(
                            out=R[0:N, h * 64:(h + 1) * 64], in0=R[0:N, h * 64:(h + 1) * 64],
                            scalar1=gambc[0:N, h:h + 1], scalar2=None, op0=ALU.mult), inc="dve")
                        p.op("vector", lambda e, h=h, hh=hh: e.tensor_add(
                            out=R[0:N, h * 64:(h + 1) * 64], in0=R[0:N, h * 64:(h + 1) * 64],
                            in1=pb[7][0:N, hh * 64:(hh + 1) * 64]), inc="dve")
                    vd_grp = p.val("dve")
                    p.wait("tensor", "dve", vd_grp)

        if not full:
            continue

        for gid, grp in enumerate(gate_groups):
            m0 = grp[0]
            ncol = 128 * len(grp)

            def body(k, buf, grp=grp):
                for mi, mt in enumerate(grp):
                    p.op("tensor", lambda e, k=k, mi=mi, buf=buf: e.matmul(
                        out=pb[mi][:, :], lhsT=buf[:, mi * 128:mi * 128 + 128], rhs=uT(k, 0, TB),
                        start=(k == 0), stop=(k == 15)), inc=("pe" if mi == len(grp) - 1 else None))

            stream_mm(p, 16, wb2, ncol,
                      lambda kk, m0=m0, ncol=ncol: io["w_in_slc2"](kk, m0 * 128, ncol), body)
            vpe = p.val("pe")
            p.wait("scalar", "pe", vpe)
            for mi, mt in enumerate(grp):
                p.op("scalar", lambda e, mi=mi: e.activation(
                    out=gate_s[:, mi * TB:(mi + 1) * TB], in_=pb[mi][:, :], func=AF.Silu), inc="act")
            va = p.val("act")
            p.wait("tensor", "act", va)
            for mi, mt in enumerate(grp):
                for cc in range(NCH):
                    if cc >= 2:
                        p.wait("tensor", "act", p.val("act"))
                    p.op("tensor", lambda e, mi=mi, cc=cc: e.transpose(
                        out=pbb[4 + cc % 2][:, 0:128], in_=gate_s[:, mi * TB + cc * 128: mi * TB + (cc + 1) * 128],
                        identity=idb[:, :]), inc="pe")
                    v = p.val("pe")
                    p.wait("scalar", "pe", v)
                    p.op("scalar", lambda e, cc=cc: e.copy(
                        out=gtok[:, cc * 128:(cc + 1) * 128], in_=pbb[4 + cc % 2][:, 0:128]), inc="act")
                va2 = p.val("act")
                p.wait("vector", "act", va2)
                for cc in range(NCH):
                    sy = (tb * NCH + cc) % 5
                    p.op("vector", lambda e, mt=mt, cc=cc, sy=sy: e.tensor_mul(
                        out=rg(sy)[:, mt * 128:(mt + 1) * 128], in0=rg(sy)[:, mt * 128:(mt + 1) * 128],
                        in1=gtok[:, cc * 128:(cc + 1) * 128]), inc="dve")
                vd = p.val("dve")
                p.wait("scalar", "dve", vd)
                p.wait("tensor", "dve", vd)

        for c in range(NCH):
            sy = (tb * NCH + c) % 5
            sd = (tb * NCH + c + 4) % 5
            p.op("scalar", lambda e, sy=sy, sd=sd: e.activation(out=rg(sd)[:, :], in_=rg(sy)[:, :], func=AF.Square), inc="act")
            p.wait("vector", "act", p.val("act"))
            p.op("vector", lambda e, sd=sd, c=c: e.tensor_reduce(out=rms[:, c:c + 1], in_=rg(sd)[:, :], axis=AXX, op=ALU.add), inc="dve")
            p.op("vector", lambda e, c=c: e.tensor_scalar(out=rms[:, c:c + 1], in0=rms[:, c:c + 1], scalar1=1.0 / 4096, scalar2=None, op0=ALU.mult), inc="dve")
            vd = p.val("dve")
            p.wait("scalar", "dve", vd)
            p.op("scalar", lambda e, c=c: e.activation(out=rms[:, c:c + 1], in_=rms[:, c:c + 1], func=AF.Ln, bias=EPS), inc="act")
            p.op("scalar", lambda e, c=c: e.activation(out=rms[:, c:c + 1], in_=rms[:, c:c + 1], func=AF.Exp, scale=-0.5), inc="act")
            va = p.val("act")
            p.wait("vector", "act", va)
            p.op("vector", lambda e, sy=sy, sd=sd, c=c: e.scalar_tensor_tensor(
                out=rg(sd)[:, :], in0=rg(sy)[:, :], scalar=rms[:, c:c + 1], in1=nw_bc[:, :],
                op0=ALU.mult, op1=ALU.mult), inc="dve")
            vd = p.val("dve")
            p.wait("tensor", "dve", vd)
            for jg in range(8):
                for j4 in range(4):
                    j = jg * 4 + j4
                    p.op("tensor", lambda e, j=j, j4=j4, sd=sd: e.transpose(
                        out=pbb[j4][:, 0:128], in_=rg(sd)[:, j * 128:(j + 1) * 128],
                        identity=idb[:, :]), inc=("pe" if j4 == 3 else None))
                v = p.val("pe")
                p.wait("scalar", "pe", v)
                for j4 in range(4):
                    j = jg * 4 + j4
                    p.op("scalar", lambda e, j=j, j4=j4, c=c: e.copy(
                        out=yT_ap(j, c * 128, (c + 1) * 128), in_=pbb[j4][:, 0:128]), inc="act")
                p.wait("tensor", "act", p.val("act"))

        va_yT = p.val("act")
        p.wait("tensor", "act", va_yT)
        evac_h = {0: 0, 1: 0}
        for dchunk in range(4):
            pbase = 4 * (dchunk % 2)
            p.wait("tensor", "act", evac_h[dchunk % 2])

            def body(k, buf, pbase=pbase):
                for tt in range(NTT):
                    p.op("tensor", lambda e, k=k, tt=tt, buf=buf, pbase=pbase: e.matmul(
                        out=pb[pbase + tt][:, :], lhsT=yT_ap(k, tt * 128, (tt + 1) * 128),
                        rhs=buf[:, :], start=(k == 0), stop=(k == 31)), inc=("pe" if tt == NTT - 1 else None))

            stream_mm(p, 32, wr2, 512,
                      lambda kk, dchunk=dchunk: io["w_out_slc2"](kk, dchunk), body)
            vpe = p.val("pe")
            p.wait("scalar", "pe", vpe)
            for tt in range(NTT):
                p.op("scalar", lambda e, tt=tt, dchunk=dchunk, pbase=pbase: e.copy(
                    out=outb[:, tt * D + dchunk * 512: tt * D + (dchunk + 1) * 512], in_=pb[pbase + tt][:, :]), inc="act")
            evac_h[dchunk % 2] = p.val("act")
        p.wait("tensor", "act", p.val("act"))
        va_out = p.val("act")
        p.wait("vector", "act", va_out)
        for tt in range(NTT):
            p.wait("gpsimd", "dve", p.val("dve"))
            vx = p.dma("gpsimd", xtl[:, :], io["x_in"][t0 + tt * 128: t0 + (tt + 1) * 128, :], inc=QA)
            p.op("vector", lambda e, tt=tt: e.tensor_reduce(out=lnS[:, 0:1], in_=outb[:, tt * D:(tt + 1) * D], axis=AXX, op=ALU.add), inc="dve")
            p.op("scalar", lambda e, tt=tt: e.activation(out=sq2[:, :], in_=outb[:, tt * D:(tt + 1) * D], func=AF.Square), inc="act")
            p.wait("vector", "act", p.val("act"))
            p.op("vector", lambda e: e.tensor_reduce(out=lnS[:, 1:2], in_=sq2[:, :], axis=AXX, op=ALU.add), inc="dve")
            p.op("vector", lambda e: e.tensor_scalar(out=lnS[:, 1:2], in0=lnS[:, 1:2], scalar1=1.0 / D, scalar2=None, op0=ALU.mult), inc="dve")
            p.op("vector", lambda e: e.tensor_scalar(out=lnS[:, 2:3], in0=lnS[:, 0:1], scalar1=1.0 / D, scalar2=None, op0=ALU.mult), inc="dve")
            p.op("vector", lambda e: e.tensor_scalar(out=lnS[:, 4:5], in0=lnS[:, 2:3], scalar1=lnS[:, 2:3], scalar2=None, op0=ALU.mult), inc="dve")
            p.op("vector", lambda e: e.tensor_sub(out=lnS[:, 4:5], in0=lnS[:, 1:2], in1=lnS[:, 4:5]), inc="dve")
            vd = p.val("dve")
            p.wait("scalar", "dve", vd)
            p.op("scalar", lambda e: e.activation(out=lnS[:, 5:6], in_=lnS[:, 4:5], func=AF.Ln, bias=EPS), inc="act")
            p.op("scalar", lambda e: e.activation(out=lnS[:, 6:7], in_=lnS[:, 5:6], func=AF.Exp, scale=-0.5), inc="act")
            va = p.val("act")
            p.wait("vector", "act", va)
            p.wait("vector", QA, vx)
            p.op("vector", lambda e, tt=tt: e.tensor_scalar(
                out=outb[:, tt * D:(tt + 1) * D],
                in0=outb[:, tt * D:(tt + 1) * D],
                scalar1=lnS[:, 2:3], scalar2=lnS[:, 6:7], op0=ALU.subtract, op1=ALU.mult), inc="dve")
            p.op("vector", lambda e, tt=tt: e.tensor_add(out=xtl[:, :], in0=xtl[:, :], in1=outb[:, tt * D:(tt + 1) * D]), inc="dve")
            vd = p.val("dve")
            p.wait("gpsimd", "dve", vd)
            p.dma("gpsimd", io["x_out"][t0 + tt * 128:t0 + (tt + 1) * 128, :], xtl[:, :], inc=QA)
            p.wait("gpsimd", QA, p.val(QA))

    if mode == "state":
        vd = p.val("dve")
        p.wait("scalar", "dve", vd)
        p.op("scalar", lambda e: e.activation(out=gsum[:, 0:1], in_=gsum[:, 0:1], func=AF.Exp), inc="act")
        va = p.val("act")
        p.wait("gpsimd", "act", va)
        p.wait("gpsimd", "dve", vd)
        p.dma("gpsimd", io["out_S"], R[0:N, :], inc=QA)
        p.dma("gpsimd", io["out_g"], gsum[:, 0:1], inc=QA)
        p.dma("gpsimd", io["out_halo"], halo_sv[:, :], inc=QA)
        p.wait("gpsimd", QA, p.val(QA))
    p.wait("gpsimd", QW, p.val(QW))



def l0_win_slc(w):
    def f(k, c0, ncol):
        return w[k * 128:(k + 1) * 128, c0: c0 + ncol]
    return f


def l0_win_slc2(w):
    def f(kk, c0, ncol):
        return w[kk * 256:(kk + 1) * 256, c0: c0 + ncol].rearrange("(i p) c -> p i c", p=128)
    return f


def l1_win_slc2(w):
    def f(kk, c0, ncol):
        r = c0 // 1152
        pl = c0 - r * 1152
        return w[2048 * r + kk * 256: 2048 * r + (kk + 1) * 256, pl: pl + ncol].rearrange("(i p) c -> p i c", p=128)
    return f


def l0_wout_slc2(w):
    def f(kk, dchunk):
        return w[kk * 256:(kk + 1) * 256, dchunk * 512:(dchunk + 1) * 512].rearrange("(i p) c -> p i c", p=128)
    return f


def l0_wout_slc(w):
    def f(k, dchunk):
        return w[k * 128:(k + 1) * 128, dchunk * 512:(dchunk + 1) * 512]
    return f


def l1_win_slc(w):
    def f(k, c0, ncol):
        r = c0 // 1152
        pl = c0 - r * 1152
        return w[2048 * r + k * 128: 2048 * r + (k + 1) * 128, pl: pl + ncol]
    return f



def _bf16():
    import ml_dtypes
    return ml_dtypes.bfloat16





def _mk_common_io(nc, cfg, pref=""):
    CT = cfg.conv_tiles
    io = {}
    io["conv_w"] = nc.dram_tensor(pref + "conv_w", [cfg.CONV, 4], F32, kind="ExternalInput").ap()
    io["conv_b"] = nc.dram_tensor(pref + "conv_b", [cfg.CONV], F32, kind="ExternalInput").ap()
    io["dt_bias"] = nc.dram_tensor(pref + "dt_bias", [64], F32, kind="ExternalInput").ap()
    io["A_log"] = nc.dram_tensor(pref + "A_log", [64], F32, kind="ExternalInput").ap()
    io["Dv"] = nc.dram_tensor(pref + "Dv", [64], F32, kind="ExternalInput").ap()
    io["norm_w"] = nc.dram_tensor(pref + "norm_w", [4096], F32, kind="ExternalInput").ap()
    return io


def run_mamba_unfused(layer, x_full, w_in_bf, w_out_bf, conv_w, conv_b, dt_bias, A_log, Dv, norm_w):
    cfg = MambaCfg(layer)
    CT = cfg.conv_tiles
    N = cfg.N
    win_shape = list(w_in_bf.shape)

    nc = bass.Bass("TRN2", target_bir_lowering=False)
    p = Prog(nc)
    io = _mk_common_io(nc, cfg)
    io["tag"] = "s"
    io["x_in"] = nc.dram_tensor("x_in", [T, D], F32, kind="ExternalInput").ap()
    io["x_halo"] = nc.dram_tensor("x_halo", [3, D], F32, kind="ExternalInput").ap()
    w_in = nc.dram_tensor("w_in", win_shape, BF16, kind="ExternalInput").ap()
    io["w_in_slc"] = l0_win_slc(w_in) if layer == 0 else l1_win_slc(w_in)
    io["w_in_slc2"] = l0_win_slc2(w_in) if layer == 0 else l1_win_slc2(w_in)
    io["acs_dram"] = nc.dram_tensor("acs_dram", [64, CH], F32, kind="Internal")
    io["alast_dram"] = nc.dram_tensor("alast_dram", [64], F32, kind="Internal")
    io["out_S"] = nc.dram_tensor("out_S", [N, 4096], F32, kind="ExternalOutput").ap()
    io["out_g"] = nc.dram_tensor("out_g", [64, 1], F32, kind="ExternalOutput").ap()
    io["out_halo"] = nc.dram_tensor("out_halo", [128, 34 * 3], BF16, kind="ExternalOutput").ap()
    idb = p.sbuf("idb", [128, 128], BF16)
    idf = p.sbuf("idf", [128, 128], F32)
    emit_idents(p, idb, idf)
    emit_mamba(p, cfg, io, idb, idf, "state")
    p.realize()

    in_maps = []
    for c in range(NCORES):
        b, q = c // 4, c % 4
        xh = np.zeros((3, D), np.float32) if q == 0 else np.ascontiguousarray(x_full[b, q * 1024 - 3:q * 1024])
        in_maps.append({
            "x_in": np.ascontiguousarray(x_full[b, q * 1024:(q + 1) * 1024]),
            "x_halo": xh, "w_in": w_in_bf,
            "conv_w": conv_w, "conv_b": conv_b, "dt_bias": dt_bias,
            "A_log": A_log, "Dv": Dv, "norm_w": norm_w,
        })
    res = run_bass_kernel_spmd(nc, in_maps, core_ids=list(range(NCORES)))
    Ss = [res.results[c]["out_S"] for c in range(NCORES)]
    gs = [res.results[c]["out_g"] for c in range(NCORES)]
    halos = [res.results[c]["out_halo"] for c in range(NCORES)]

    nc = bass.Bass("TRN2", target_bir_lowering=False)
    p = Prog(nc)
    io = _mk_common_io(nc, cfg)
    io["tag"] = "f"
    io["x_in"] = nc.dram_tensor("x_in", [T, D], F32, kind="ExternalInput").ap()
    w_in = nc.dram_tensor("w_in", win_shape, BF16, kind="ExternalInput").ap()
    w_out = nc.dram_tensor("w_out", [4096, D], BF16, kind="ExternalInput").ap()
    io["w_in_slc"] = l0_win_slc(w_in) if layer == 0 else l1_win_slc(w_in)
    io["w_in_slc2"] = l0_win_slc2(w_in) if layer == 0 else l1_win_slc2(w_in)
    io["w_out_slc"] = l0_wout_slc(w_out)
    io["w_out_slc2"] = l0_wout_slc2(w_out)
    io["S_in"] = [nc.dram_tensor(f"S_in{j}", [N, 4096], F32, kind="ExternalInput").ap() for j in range(3)]
    io["g_in"] = nc.dram_tensor("g_in", [1, 192], F32, kind="ExternalInput").ap()
    io["halo_in"] = nc.dram_tensor("halo_in", [128, 34 * 3], BF16, kind="ExternalInput").ap()
    io["mrow"] = nc.dram_tensor("mrow", [1, 3], F32, kind="ExternalInput").ap()
    io["mcol"] = nc.dram_tensor("mcol", [128, 4], F32, kind="ExternalInput").ap()
    io["x_out"] = nc.dram_tensor("x_out", [T, D], F32, kind="ExternalOutput").ap()
    io["acs_dram"] = nc.dram_tensor("acs_dram", [64, CH], F32, kind="Internal")
    io["alast_dram"] = nc.dram_tensor("alast_dram", [64], F32, kind="Internal")
    idb = p.sbuf("idb", [128, 128], BF16)
    idf = p.sbuf("idf", [128, 128], F32)
    emit_idents(p, idb, idf)
    emit_mamba(p, cfg, io, idb, idf, "full")
    p.realize()

    bf = _bf16()
    in_maps = []
    for c in range(NCORES):
        b, q = c // 4, c % 4
        m = {
            "x_in": np.ascontiguousarray(x_full[b, q * 1024:(q + 1) * 1024]),
            "w_in": w_in_bf, "w_out": w_out_bf,
            "conv_w": conv_w, "conv_b": conv_b, "dt_bias": dt_bias,
            "A_log": A_log, "Dv": Dv, "norm_w": norm_w,
            "g_in": np.concatenate([gs[b * 4 + j].reshape(1, 64) for j in range(3)], axis=1).astype(np.float32),
            "halo_in": halos[b * 4 + q - 1] if q > 0 else np.zeros((128, 34 * 3), bf),
            "mrow": np.array([[1.0 if j < q else 0.0 for j in range(3)]], np.float32),
            "mcol": np.tile(np.array([[1.0 if j < q else 0.0 for j in range(3)] + [1.0 if q > 0 else 0.0]], np.float32), (128, 1)),
        }
        for j in range(3):
            m[f"S_in{j}"] = Ss[b * 4 + j]
        in_maps.append(m)
    res = run_bass_kernel_spmd(nc, in_maps, core_ids=list(range(NCORES)))
    out = np.empty((2, 4096, D), np.float32)
    for c in range(NCORES):
        b, q = c // 4, c % 4
        out[b, q * 1024:(q + 1) * 1024] = res.results[c]["x_out"]
    return out


def kernel(**inputs):
    inp = {k: np.asarray(v) for k, v in inputs.items()}
    return run_fused(inp).astype(np.float32)


def kernel_unfused(**inputs):
    inp = {k: np.asarray(v) for k, v in inputs.items()}
    x = np.ascontiguousarray(inp["L0_attn_x"].astype(np.float32))

    prep = run_prep_unfused(inp)

    x1 = run_mamba_unfused(0, x, prep["W0in"], prep["W0out"],
                           inp["L0_attn_conv_w"], inp["L0_attn_conv_b"], inp["L0_attn_dt_bias"],
                           inp["L0_attn_A_log"], inp["L0_attn_D"], inp["L0_attn_norm_w"])
    x2 = run_ffn0_unfused(x1, prep["GgT"], prep["GuT"], prep["GdT"])
    x3 = run_mamba_unfused(1, x2, prep["W1inT"], prep["W1outT"],
                           inp["L1_attn_conv_w"], inp["L1_attn_conv_b"], inp["L1_attn_dt_bias"],
                           inp["L1_attn_A_log"], inp["L1_attn_D"], inp["L1_attn_norm_w"])
    x4 = run_moe_unfused(x3, prep["Wgate"], prep["Wsup"], prep["Wsdn"], prep["Mup"], prep["Mdn"])
    return x4.astype(np.float32)


def kernel_fused(**inputs):
    inp = {k: np.asarray(v) for k, v in inputs.items()}
    return run_fused(inp).astype(np.float32)




def emit_prep(p, io, idb, idf, sfx=""):
    pk = p.sbuf("pk_pp" + sfx, [128, 1024], I32)
    xi = p.sbuf("xi_pp" + sfx, [128, 1024], I32)
    xf = p.sbuf("xf_pp" + sfx, [128, 1024], BF16)
    t1 = p.sbuf("t1_pp" + sfx, [128, 1024], BF16)
    mm_ = p.sbuf("mm_pp" + sfx, [128, 1024], BF16)
    r1 = p.sbuf("r1_pp" + sfx, [128, 1024], BF16)
    r2 = p.sbuf("r2_pp" + sfx, [128, 1024], BF16)
    vv = p.sbuf("vv_pp" + sfx, [128, 1024], BF16)
    sg = p.sbuf("sg_pp" + sfx, [128, 1024], BF16)
    sc = p.sbuf("sc_pp" + sfx, [128, 64], F32)
    nib2 = p.sbuf("nib2_pp" + sfx, [128, 2048], BF16)
    bufB = p.sbuf("bufB_pp" + sfx, [128, 8512], BF16)
    stg = p.sbuf("stg_pp" + sfx, [128, 512], BF16)
    pbt = [p.psum(f"pbt{i}_pp" + sfx, [128, 512], F32) for i in range(2)]
    pbtb = [t.ap().bitcast(BF16) for t in pbt]

    if not getattr(p, "_consts2_done", False):
        p._consts2_done = True
        cst2 = p.sbuf("cstb_pp" + sfx, [128, 2], F32)
        p.op("vector", lambda e: e.memset(cst2[:, 0:1], -2.0))
        p.op("vector", lambda e: e.memset(cst2[:, 1:2], -6.0), inc="dve")
        p.nc.const_aps.register(F32, -2.0, cst2[:, 0:1])
        p.nc.const_aps.register(F32, -6.0, cst2[:, 1:2])
        p.wait("scalar", "dve", p.val("dve"))

    if io["casts"]:
        bufB2 = p.sbuf("bufB2_pp" + sfx, [128, 8512], BF16)
        cbufs = (bufB, bufB2)
        vst = {}
        ti = 0
        for (src, dst, rows, cols) in io["casts"]:
            for i0 in range(0, rows, 128):
                n = min(128, rows - i0)
                par = ti % 2
                if ti >= 2:
                    p.wait("gpsimd", "qw", vst[ti - 2])
                vin = p.dma("gpsimd", cbufs[par][0:n, 0:cols], src[i0:i0 + n, :], inc="qa")
                p.wait("sync", "qa", vin)
                vst[ti] = p.dma("sync", dst[i0:i0 + n, :], cbufs[par][0:n, 0:cols], inc="qw")
                ti += 1
        p.wait("gpsimd", "qa", p.val("qa"))
        p.wait("gpsimd", "qw", p.val("qw"))

    def transpose_out(src_sbuf, dst, ncb, i0, nr_valid=128):
        for jb0 in range(0, ncb, 4):
            nb = min(4, ncb - jb0)
            for q in range(nb):
                p.op("tensor", lambda e, jb0=jb0, q=q: e.transpose(
                    out=pbtb[0][:, q * 128:(q + 1) * 128], in_=src_sbuf[:, (jb0 + q) * 128:(jb0 + q + 1) * 128],
                    identity=idb[:, :]), inc=("pe" if q == nb - 1 else None))
            v2 = p.val("pe")
            p.wait("scalar", "pe", v2)
            p.op("scalar", lambda e, nb=nb: e.copy(out=stg[:, 0:nb * 128], in_=pbtb[0][:, 0:nb * 128]), inc="act")
            va = p.val("act")
            p.wait("gpsimd", "act", va)
            p.dma("gpsimd",
                  dst[jb0 * 128:(jb0 + nb) * 128, i0:i0 + 128].rearrange("(q c) r -> c q r", q=nb),
                  stg.ap()[:, 0:nb * 128].rearrange("c (q r) -> c q r", q=nb), inc="qa")
            p.wait("tensor", "qa", p.val("qa"))

    for (src, dst, rows, cols) in io["ctrans"]:
        ncb = cols // 128
        for i in range(0, rows, 128):
            v = p.dma("gpsimd", bufB[:, 0:cols], src[i:i + 128, :], inc="qa")
            p.wait("tensor", "qa", v)
            transpose_out(bufB, dst, ncb, i)

    for (packed, scales, dst) in io["decode"]:
        R = packed.shape[0]
        PC = packed.shape[1]
        for i in range(0, R, 128):
            v = p.dma("gpsimd", pk[:, 0:PC], packed[i:i + 128, :], inc="qa")
            v2 = p.dma("gpsimd", sc[:, 0:PC // 16], scales[i:i + 128, :], inc="qa")
            p.wait("vector", "qa", p.val("qa"))
            for off, shift in ((0, 0), (1, 4)):
                if shift:
                    p.op("vector", lambda e: e.tensor_scalar(out=xi[:, 0:PC], in0=pk[:, 0:PC], scalar1=4, scalar2=15, op0=ALU.logical_shift_right, op1=ALU.bitwise_and))
                else:
                    p.op("vector", lambda e: e.tensor_scalar(out=xi[:, 0:PC], in0=pk[:, 0:PC], scalar1=15, scalar2=None, op0=ALU.bitwise_and))
                p.op("vector", lambda e: e.tensor_copy(out=xf[:, 0:PC], in_=xi[:, 0:PC]))
                p.op("vector", lambda e: e.tensor_scalar(out=t1[:, 0:PC], in0=xf[:, 0:PC], scalar1=8.0, scalar2=-8.0, op0=ALU.is_ge, op1=ALU.mult))
                p.op("vector", lambda e: e.tensor_add(out=mm_[:, 0:PC], in0=xf[:, 0:PC], in1=t1[:, 0:PC]), inc="dve")
                vd = p.val("dve")
                p.wait("scalar", "dve", vd)
                p.op("scalar", lambda e: e.activation(out=r1[:, 0:PC], in_=mm_[:, 0:PC], func=AF.Relu, scale=0.5, bias=-2.0), inc="act")
                p.op("scalar", lambda e: e.activation(out=r2[:, 0:PC], in_=mm_[:, 0:PC], func=AF.Relu, bias=-6.0), inc="act")
                va = p.val("act")
                p.wait("vector", "act", va)
                p.op("vector", lambda e: e.scalar_tensor_tensor(out=vv[:, 0:PC], in0=mm_[:, 0:PC], scalar=0.5, in1=r1[:, 0:PC], op0=ALU.mult, op1=ALU.add))
                p.op("vector", lambda e: e.tensor_add(out=vv[:, 0:PC], in0=vv[:, 0:PC], in1=r2[:, 0:PC]))
                p.op("vector", lambda e: e.tensor_scalar(out=sg[:, 0:PC], in0=t1[:, 0:PC], scalar1=0.25, scalar2=1.0, op0=ALU.mult, op1=ALU.add))
                p.op("vector", lambda e: e.tensor_mul(out=vv[:, 0:PC], in0=vv[:, 0:PC], in1=sg[:, 0:PC]))
                p.op("vector", lambda e, off=off: e.tensor_tensor(
                    out=nib2.ap()[:, 0:2 * PC].rearrange("p (a b t) -> p a b t", b=16, t=2)[:, :, :, off],
                    in0=vv.ap()[:, 0:PC].rearrange("p (a b) -> p a b", b=16),
                    in1=sc[:, 0:PC // 16].unsqueeze(2).to_broadcast([128, PC // 16, 16]),
                    op=ALU.mult), inc="dve")
            vd = p.val("dve")
            p.wait("tensor", "dve", vd)
            transpose_out(nib2, dst, (2 * PC) // 128, i)
    p.wait("gpsimd", "qa", p.val("qa"))


def emit_ffn0(p, io, idb, idf):
    sfx = "_f0"
    uTb = p.sbuf("uT" + sfx, [128, 16 * T], BF16)
    uT = lambda k, a, b: uTb[:, k * T + a: k * T + b]
    inter = p.sbuf("inter" + sfx, [128, 16 * T], BF16)
    dacc = p.sbuf("dacc" + sfx, [128, 8 * D], F32)
    gbuf = p.sbuf("gbuf" + sfx, [128, 3 * T], BF16)
    s1 = p.sbuf("s1" + sfx, [128, T], BF16)
    s2 = p.sbuf("s2" + sfx, [128, T], BF16)
    lnS = p.sbuf("lnS" + sfx, [128, 8], F32)
    xtl = p.sbuf("xtl" + sfx, [128, D], F32)
    utok = p.sbuf("utok" + sfx, [128, D], BF16)
    sq2 = p.sbuf("sq2" + sfx, [128, D], BF16)
    wbf2 = [p.sbuf(f"wbf{i}" + sfx, [128, 768], BF16) for i in range(2)]
    wrf2 = [p.sbuf(f"wrf{i}" + sfx, [128, 1024], BF16) for i in range(2)]
    pb = [p.psum(f"pb{i}" + sfx, [128, 512], F32) for i in range(8)]
    pbb = [t.ap().bitcast(BF16) for t in pb]

    if not getattr(p, "_consts_done", False):
        p._consts_done = True
        cstt = p.sbuf("cstt" + sfx, [128, 2], F32)
        p.op("vector", lambda e: e.memset(cstt[:, 0:1], EPS), inc="dve")
        p.op("vector", lambda e: e.memset(cstt[:, 1:2], 1.0), inc="dve")
        p.nc.const_aps.register(F32, EPS, cstt[:, 0:1])
        p.nc.const_aps.register(F32, 1.0, cstt[:, 1:2])

    for it in range(8):
        p.wait("gpsimd", "dve", p.val("dve"))
        p.wait("gpsimd", "act", p.val("act"))
        vx = p.dma("gpsimd", xtl[:, :], io["x_in"][it * 128:(it + 1) * 128, :], inc="qa")
        p.wait("vector", "qa", vx)
        p.op("vector", lambda e: e.tensor_reduce(out=lnS[:, 0:1], in_=xtl[:, :], axis=AXX, op=ALU.add), inc="dve")
        p.wait("scalar", "qa", vx)
        p.op("scalar", lambda e: e.activation(out=sq2[:, :], in_=xtl[:, :], func=AF.Square), inc="act")
        p.wait("vector", "act", p.val("act"))
        p.op("vector", lambda e: e.tensor_reduce(out=lnS[:, 1:2], in_=sq2[:, :], axis=AXX, op=ALU.add), inc="dve")
        p.op("vector", lambda e: e.tensor_scalar(out=lnS[:, 1:2], in0=lnS[:, 1:2], scalar1=1.0 / D, scalar2=None, op0=ALU.mult), inc="dve")
        p.op("vector", lambda e: e.tensor_scalar(out=lnS[:, 2:3], in0=lnS[:, 0:1], scalar1=1.0 / D, scalar2=None, op0=ALU.mult), inc="dve")
        p.op("vector", lambda e: e.tensor_scalar(out=lnS[:, 4:5], in0=lnS[:, 2:3], scalar1=lnS[:, 2:3], scalar2=None, op0=ALU.mult), inc="dve")
        p.op("vector", lambda e: e.tensor_sub(out=lnS[:, 4:5], in0=lnS[:, 1:2], in1=lnS[:, 4:5]), inc="dve")
        vd = p.val("dve")
        p.wait("scalar", "dve", vd)
        p.op("scalar", lambda e: e.activation(out=lnS[:, 5:6], in_=lnS[:, 4:5], func=AF.Ln, bias=EPS), inc="act")
        p.op("scalar", lambda e: e.activation(out=lnS[:, 6:7], in_=lnS[:, 5:6], func=AF.Exp, scale=-0.5), inc="act")
        va = p.val("act")
        p.wait("vector", "act", va)
        p.op("vector", lambda e: e.tensor_scalar(out=utok[:, :], in0=xtl[:, :], scalar1=lnS[:, 2:3], scalar2=lnS[:, 6:7], op0=ALU.subtract, op1=ALU.mult), inc="dve")
        vd = p.val("dve")
        p.wait("tensor", "dve", vd)
        for j in range(16):
            p.op("tensor", lambda e, j=j: e.transpose(
                out=pbb[j % 4][:, 0:128], in_=utok[:, j * 128:(j + 1) * 128],
                identity=idb[:, :]), inc=("pe" if j % 4 == 3 else None))
            if j % 4 == 3:
                v = p.val("pe")
                p.wait("scalar", "pe", v)
                for jj in range(j - 3, j + 1):
                    p.op("scalar", lambda e, jj=jj, it=it: e.copy(
                        out=uT(jj, it * 128, it * 128 + 128), in_=pbb[jj % 4][:, 0:128]), inc="act")
                p.wait("tensor", "act", p.val("act"))

    C1, C2 = 0.7978845608028654, 0.044715
    for s in range(8):
        groups = [list(range(i, min(i + 3, 16))) for i in range(0, 16, 3)]
        for grp in groups:
            ncol = 128 * len(grp)
            for which, wsrc in (("g", io["g_gT"]), ("u", io["g_uT"])):
                slc2 = lambda kk, wsrc=wsrc, s=s, grp=grp, ncol=ncol: wsrc[
                    2048 * s + 256 * kk: 2048 * s + 256 * (kk + 1),
                    128 * grp[0]:128 * grp[0] + ncol].rearrange("(i p) c -> p i c", p=128)

                def fbody(k, buf, grp=grp):
                    for mi, m in enumerate(grp):
                        for nf in range(2):
                            p.op("tensor", lambda e, k=k, mi=mi, nf=nf, buf=buf: e.matmul(
                                out=pb[2 * mi + nf][:, :],
                                lhsT=buf[:, mi * 128:(mi + 1) * 128], rhs=uT(k, nf * 512, (nf + 1) * 512),
                                start=(k == 0), stop=(k == 15)), inc=("pe" if (mi == len(grp) - 1 and nf == 1) else None))
                stream_mm(p, 16, wbf2, ncol, slc2, fbody)
                if which == "g":
                    vpe = p.val("pe")
                    p.wait("scalar", "pe", vpe)
                    for mi, m in enumerate(grp):
                        for nf in range(2):
                            p.op("scalar", lambda e, mi=mi, nf=nf: e.copy(
                                out=gbuf[:, mi * T + nf * 512: mi * T + (nf + 1) * 512], in_=pb[2 * mi + nf][:, :]), inc="act")
                    p.wait("tensor", "act", p.val("act"))
            vpe = p.val("pe")
            p.wait("scalar", "pe", vpe)
            p.wait("vector", "pe", vpe)
            for mi, m in enumerate(grp):
                gm = lambda mi=mi: gbuf[:, mi * T:(mi + 1) * T]
                p.op("scalar", lambda e, mi=mi: e.activation(out=s1[:, :], in_=gbuf[:, mi * T:(mi + 1) * T], func=AF.Square), inc="act")
                va = p.val("act")
                p.wait("vector", "act", va)
                p.op("vector", lambda e: e.tensor_scalar(out=s1[:, :], in0=s1[:, :], scalar1=C2, scalar2=1.0, op0=ALU.mult, op1=ALU.add), inc="dve")
                p.op("vector", lambda e, mi=mi: e.tensor_mul(out=s1[:, :], in0=s1[:, :], in1=gbuf[:, mi * T:(mi + 1) * T]), inc="dve")
                vd = p.val("dve")
                p.wait("scalar", "dve", vd)
                p.op("scalar", lambda e: e.activation(out=s2[:, :], in_=s1[:, :], func=AF.Tanh, scale=C1), inc="act")
                va = p.val("act")
                p.wait("vector", "act", va)
                p.op("vector", lambda e: e.tensor_scalar(out=s2[:, :], in0=s2[:, :], scalar1=1.0, scalar2=0.5, op0=ALU.add, op1=ALU.mult), inc="dve")
                p.op("vector", lambda e, mi=mi: e.tensor_mul(out=s2[:, :], in0=s2[:, :], in1=gbuf[:, mi * T:(mi + 1) * T]), inc="dve")
                for nf in range(2):
                    p.op("vector", lambda e, m=m, mi=mi, nf=nf: e.tensor_tensor(
                        out=inter[:, m * T + nf * 512: m * T + (nf + 1) * 512],
                        in0=s2[:, nf * 512:(nf + 1) * 512], in1=pb[2 * mi + nf][:, :], op=ALU.mult), inc="dve")
            vd = p.val("dve")
            p.wait("tensor", "dve", vd)
        for dchunk in range(4):
            slc2 = lambda kk, s=s, dchunk=dchunk: io["g_dT"][
                2048 * s + 256 * kk: 2048 * s + 256 * (kk + 1),
                512 * dchunk:512 * (dchunk + 1)].rearrange("(i p) c -> p i c", p=128)

            def dbody(k, buf):
                for tt in range(8):
                    p.op("tensor", lambda e, k=k, tt=tt, buf=buf: e.matmul(
                        out=pb[tt][:, :], lhsT=inter[:, k * T + tt * 128: k * T + (tt + 1) * 128],
                        rhs=buf, start=(k == 0), stop=(k == 15)), inc=("pe" if tt == 7 else None))
            stream_mm(p, 16, wrf2, 512, slc2, dbody)
            vpe = p.val("pe")
            p.wait("vector", "pe", vpe)
            for tt in range(8):
                if s == 0:
                    p.op("vector", lambda e, tt=tt, dchunk=dchunk: e.tensor_copy(
                        out=dacc[:, tt * D + dchunk * 512: tt * D + (dchunk + 1) * 512], in_=pb[tt][:, :]), inc="dve")
                else:
                    p.op("vector", lambda e, tt=tt, dchunk=dchunk: e.tensor_add(
                        out=dacc[:, tt * D + dchunk * 512: tt * D + (dchunk + 1) * 512],
                        in0=dacc[:, tt * D + dchunk * 512: tt * D + (dchunk + 1) * 512], in1=pb[tt][:, :]), inc="dve")
            vd = p.val("dve")
            p.wait("tensor", "dve", vd)

    for tt in range(8):
        p.wait("gpsimd", "dve", p.val("dve"))
        vx = p.dma("gpsimd", xtl[:, :], io["x_in"][tt * 128:(tt + 1) * 128, :], inc="qa")
        p.op("vector", lambda e, tt=tt: e.tensor_reduce(out=lnS[:, 0:1], in_=dacc[:, tt * D:(tt + 1) * D], axis=AXX, op=ALU.add), inc="dve")
        p.op("scalar", lambda e, tt=tt: e.activation(out=sq2[:, :], in_=dacc[:, tt * D:(tt + 1) * D], func=AF.Square), inc="act")
        p.wait("vector", "act", p.val("act"))
        p.op("vector", lambda e: e.tensor_reduce(out=lnS[:, 1:2], in_=sq2[:, :], axis=AXX, op=ALU.add), inc="dve")
        p.op("vector", lambda e: e.tensor_scalar(out=lnS[:, 1:2], in0=lnS[:, 1:2], scalar1=1.0 / D, scalar2=None, op0=ALU.mult), inc="dve")
        p.op("vector", lambda e: e.tensor_scalar(out=lnS[:, 2:3], in0=lnS[:, 0:1], scalar1=1.0 / D, scalar2=None, op0=ALU.mult), inc="dve")
        p.op("vector", lambda e: e.tensor_scalar(out=lnS[:, 4:5], in0=lnS[:, 2:3], scalar1=lnS[:, 2:3], scalar2=None, op0=ALU.mult), inc="dve")
        p.op("vector", lambda e: e.tensor_sub(out=lnS[:, 4:5], in0=lnS[:, 1:2], in1=lnS[:, 4:5]), inc="dve")
        vd = p.val("dve")
        p.wait("scalar", "dve", vd)
        p.op("scalar", lambda e: e.activation(out=lnS[:, 5:6], in_=lnS[:, 4:5], func=AF.Ln, bias=EPS), inc="act")
        p.op("scalar", lambda e: e.activation(out=lnS[:, 6:7], in_=lnS[:, 5:6], func=AF.Exp, scale=-0.5), inc="act")
        va = p.val("act")
        p.wait("vector", "act", va)
        p.wait("vector", "qa", vx)
        p.op("vector", lambda e, tt=tt: e.tensor_scalar(
            out=dacc[:, tt * D:(tt + 1) * D], in0=dacc[:, tt * D:(tt + 1) * D],
            scalar1=lnS[:, 2:3], scalar2=lnS[:, 6:7], op0=ALU.subtract, op1=ALU.mult), inc="dve")
        p.op("vector", lambda e, tt=tt: e.tensor_add(out=xtl[:, :], in0=xtl[:, :], in1=dacc[:, tt * D:(tt + 1) * D]), inc="dve")
        vd = p.val("dve")
        p.wait("gpsimd", "dve", vd)
        p.dma("gpsimd", io["x_out"][tt * 128:(tt + 1) * 128, :], xtl[:, :], inc="qa")
        p.wait("gpsimd", "qa", p.val("qa"))
    p.wait("gpsimd", "qw", p.val("qw"))


def run_prep_unfused(inp):
    nc = bass.Bass("TRN2", target_bir_lowering=False)
    p = Prog(nc)

    def ein(name, shape, dtype=F32):
        return nc.dram_tensor(name, shape, dtype, kind="ExternalInput").ap()

    def eout(name, shape, dtype=BF16):
        return nc.dram_tensor(name, shape, dtype, kind="ExternalOutput").ap()

    W0in_s = ein("W0in_s", [256, 8512]); W0in_o = eout("W0in_o", [256, 8512])
    W0out_s = ein("W0out_s", [512, 2048]); W0out_o = eout("W0out_o", [512, 2048])
    Mup_s = ein("Mup_s", [16384, 512]); Mup_o = eout("Mup_o", [16384, 512])
    Mdn_s = ein("Mdn_s", [2048, 2048]); Mdn_o = eout("Mdn_o", [2048, 2048])
    Wsup_s = ein("Wsup_s", [256, 1024]); Wsup_o = eout("Wsup_o", [256, 1024])
    Wsdn_s = ein("Wsdn_s", [64, 2048]); Wsdn_o = eout("Wsdn_o", [64, 2048])
    Wgate_s = ein("Wgate_s", [256, 64]); Wgate_o = eout("Wgate_o", [256, 64])
    W1in_s = ein("W1in_s", [1152, 2048]); W1inT_o = eout("W1inT_o", [2048, 1152])
    W1out_s = ein("W1out_s", [2048, 512]); W1outT_o = eout("W1outT_o", [512, 2048])
    gP_s = ein("gP_s", [2048, 1024], I32); gS_s = ein("gS_s", [2048, 64]); gT_o = eout("gT_o", [2048, 2048])
    uP_s = ein("uP_s", [2048, 1024], I32); uS_s = ein("uS_s", [2048, 64]); uT_o = eout("uT_o", [2048, 2048])
    dP_s = ein("dP_s", [2048, 1024], I32); dS_s = ein("dS_s", [2048, 64]); dT_o = eout("dT_o", [2048, 2048])

    io = {
        "casts": [
            (W0in_s, W0in_o, 256, 8512), (W0out_s, W0out_o, 512, 2048),
            (Mup_s, Mup_o, 16384, 512), (Mdn_s, Mdn_o, 2048, 2048),
            (Wsup_s, Wsup_o, 256, 1024), (Wsdn_s, Wsdn_o, 64, 2048),
            (Wgate_s, Wgate_o, 256, 64),
        ],
        "ctrans": [(W1in_s, W1inT_o, 1152, 2048), (W1out_s, W1outT_o, 2048, 512)],
        "decode": [(gP_s, gS_s, gT_o), (uP_s, uS_s, uT_o), (dP_s, dS_s, dT_o)],
    }
    idb = p.sbuf("idb", [128, 128], BF16)
    idf = p.sbuf("idf", [128, 128], F32)
    emit_idents(p, idb, idf)
    emit_prep(p, io, idb, idf)
    p.realize()

    in_maps = []
    w1in_pad = np.zeros((9216, 2048), np.float32)
    w1in_pad[:8384] = inp["L1_attn_W_in"]
    mup_flat = np.ascontiguousarray(inp["L1_ffn_w_up"].reshape(131072, 512))
    mdn_flat = np.ascontiguousarray(inp["L1_ffn_w_down"].reshape(16384, 2048))
    for c in range(NCORES):
        in_maps.append({
            "W0in_s": np.ascontiguousarray(inp["L0_attn_in_proj_w"][256 * c:256 * (c + 1)]),
            "W0out_s": np.ascontiguousarray(inp["L0_attn_out_proj_w"][512 * c:512 * (c + 1)]),
            "Mup_s": np.ascontiguousarray(mup_flat[16384 * c:16384 * (c + 1)]),
            "Mdn_s": np.ascontiguousarray(mdn_flat[2048 * c:2048 * (c + 1)]),
            "Wsup_s": np.ascontiguousarray(inp["L1_ffn_w_shared_up"][256 * c:256 * (c + 1)]),
            "Wsdn_s": np.ascontiguousarray(inp["L1_ffn_w_shared_down"][64 * c:64 * (c + 1)]),
            "Wgate_s": np.ascontiguousarray(inp["L1_ffn_w_gate"][256 * c:256 * (c + 1)]),
            "W1in_s": np.ascontiguousarray(w1in_pad[1152 * c:1152 * (c + 1)]),
            "W1out_s": np.ascontiguousarray(inp["L1_attn_W_out"][:, 512 * c:512 * (c + 1)]),
            "gP_s": np.ascontiguousarray(inp["L0_ffn_gate_W_packed"][2048 * c:2048 * (c + 1)]),
            "gS_s": np.ascontiguousarray(inp["L0_ffn_gate_scales"][2048 * c:2048 * (c + 1)]),
            "uP_s": np.ascontiguousarray(inp["L0_ffn_up_W_packed"][2048 * c:2048 * (c + 1)]),
            "uS_s": np.ascontiguousarray(inp["L0_ffn_up_scales"][2048 * c:2048 * (c + 1)]),
            "dP_s": np.ascontiguousarray(inp["L0_ffn_down_W_packed"][:, 1024 * c:1024 * (c + 1)]),
            "dS_s": np.ascontiguousarray(inp["L0_ffn_down_scales"][:, 64 * c:64 * (c + 1)]),
        })
    res = run_bass_kernel_spmd(nc, in_maps, core_ids=list(range(NCORES)))
    r = res.results
    cat = lambda key, ax=0: np.concatenate([r[c][key] for c in range(NCORES)], axis=ax)
    return {
        "W0in": cat("W0in_o"), "W0out": cat("W0out_o"),
        "Mup": cat("Mup_o"), "Mdn": cat("Mdn_o"),
        "Wsup": cat("Wsup_o"), "Wsdn": cat("Wsdn_o"), "Wgate": cat("Wgate_o"),
        "W1inT": cat("W1inT_o"), "W1outT": cat("W1outT_o"),
        "GgT": cat("gT_o"), "GuT": cat("uT_o"), "GdT": cat("dT_o"),
    }


def run_ffn0_unfused(x_full, GgT, GuT, GdT):
    nc = bass.Bass("TRN2", target_bir_lowering=False)
    p = Prog(nc)
    io = {
        "x_in": nc.dram_tensor("x_in", [T, D], F32, kind="ExternalInput").ap(),
        "g_gT": nc.dram_tensor("g_gT", [16384, 2048], BF16, kind="ExternalInput").ap(),
        "g_uT": nc.dram_tensor("g_uT", [16384, 2048], BF16, kind="ExternalInput").ap(),
        "g_dT": nc.dram_tensor("g_dT", [16384, 2048], BF16, kind="ExternalInput").ap(),
        "x_out": nc.dram_tensor("x_out", [T, D], F32, kind="ExternalOutput").ap(),
    }
    idb = p.sbuf("idb", [128, 128], BF16)
    idf = p.sbuf("idf", [128, 128], F32)
    emit_idents(p, idb, idf)
    emit_ffn0(p, io, idb, idf)
    p.realize()
    in_maps = []
    for c in range(NCORES):
        b, q = c // 4, c % 4
        in_maps.append({
            "x_in": np.ascontiguousarray(x_full[b, q * 1024:(q + 1) * 1024]),
            "g_gT": GgT, "g_uT": GuT, "g_dT": GdT,
        })
    res = run_bass_kernel_spmd(nc, in_maps, core_ids=list(range(NCORES)))
    out = np.empty((2, 4096, D), np.float32)
    for c in range(NCORES):
        b, q = c // 4, c % 4
        out[b, q * 1024:(q + 1) * 1024] = res.results[c]["x_out"]
    return out



def emit_moe(p, io, idb, idf):
    sfx = "_mo"
    uTb = p.sbuf("uT" + sfx, [128, 16 * T], BF16)
    uT = lambda k, a, b: uTb[:, k * T + a: k * T + b]
    XT = p.sbuf("XT" + sfx, [128, 16 * 640], BF16)
    ybuf = p.sbuf("yb" + sfx, [128, 4 * T], BF16)
    gsil = p.sbuf("gs" + sfx, [128, T], BF16)
    actS = p.sbuf("aS" + sfx, [128, 4 * T], BF16)
    wsdn_sb = p.sbuf("wsd" + sfx, [128, 4 * 2048], BF16)
    accf = p.sbuf("acc" + sfx, [128, 2048], F32)
    yg = p.sbuf("yg" + sfx, [128, 2048], BF16)
    do0 = p.sbuf("do0" + sfx, [128, 2048], BF16)
    do1 = p.sbuf("do1" + sfx, [32, 2048], BF16)
    ab0 = p.sbuf("ab0" + sfx, [128, 512], BF16)
    ab1 = p.sbuf("ab1" + sfx, [32, 512], BF16)
    aTa = p.sbuf("aTa" + sfx, [128, 160], BF16)
    aTb = p.sbuf("aTb" + sfx, [128, 160], BF16)
    scT = p.sbuf("scT" + sfx, [64, T], F32)
    sctok = p.sbuf("sct" + sfx, [128, 64], F32)
    mx = p.sbuf("mx" + sfx, [128, 8], F32)
    ixu = p.sbuf("ixu" + sfx, [128, 8], U32)
    ixf = p.sbuf("ixf" + sfx, [128, 8 * 8], F32)
    wno = p.sbuf("wno" + sfx, [128, 8], F32)
    wcols = p.sbuf("wc" + sfx, [128, 48], F32)
    slotf = p.sbuf("sf" + sfx, [128, 48], F32)
    sloti = p.sbuf("si" + sfx, [128, 48], I32)
    selsum = p.sbuf("ss" + sfx, [128, 64], F32)
    ohb = p.sbuf("oh" + sfx, [128, 64], F32)
    pscr = p.sbuf("pscr" + sfx, [128, 64], F32)
    postok = p.sbuf("pt" + sfx, [128, 64], F32)
    pcol = p.sbuf("pc" + sfx, [128, 4], F32)
    selT = p.sbuf("seT" + sfx, [64, T], F32)
    posT = p.sbuf("poT" + sfx, [64, T], F32)
    iota64 = p.sbuf("io64" + sfx, [128, 64], F32)
    onesN = p.sbuf("onN" + sfx, [64, 1], F32)
    tokf = p.sbuf("tkf" + sfx, [128, 1], F32)
    fill80 = p.sbuf("f80" + sfx, [80, 128], F32)
    stab80 = p.sbuf("st80" + sfx, [80, 128], F32)
    stabT = p.sbuf("stT" + sfx, [128, 80], F32)
    stabI = p.sbuf("stI" + sfx, [128, 80], I32)
    lnS = p.sbuf("lnS" + sfx, [128, 8], F32)
    xtl = p.sbuf("xtl" + sfx, [128, D], F32)
    utok = p.sbuf("utok" + sfx, [128, D], BF16)
    sq2 = p.sbuf("sq2" + sfx, [128, D], BF16)
    wbf = p.sbuf("wbf" + sfx, [128, 512], BF16)
    wbm = [p.sbuf(f"wbm{i}" + sfx, [128, 1024], BF16) for i in range(2)]
    pb = [p.psum(f"pb{i}" + sfx, [128, 512], F32) for i in range(8)]
    pbb = [t.ap().bitcast(BF16) for t in pb]

    if not getattr(p, "_consts_done", False):
        p._consts_done = True
        cstt = p.sbuf("cstt" + sfx, [128, 2], F32)
        p.op("vector", lambda e: e.memset(cstt[:, 0:1], EPS), inc="dve")
        p.op("vector", lambda e: e.memset(cstt[:, 1:2], 1.0), inc="dve")
        p.nc.const_aps.register(F32, EPS, cstt[:, 0:1])
        p.nc.const_aps.register(F32, 1.0, cstt[:, 1:2])

    regs = {}

    def _mkregs(e):
        regs["b10239"] = e.alloc_register("bchk_a")
        e.reg_mov(regs["b10239"], 10239)
        regs["b1023"] = e.alloc_register("bchk_b")
        return e.reg_mov(regs["b1023"], 1023)

    p.op("gpsimd", _mkregs, inc="gps")
    p.op("gpsimd", lambda e: e.iota(out=iota64[:, :], pattern=[[1, 64]], base=0, channel_multiplier=0, allow_small_or_imprecise_dtypes=True), inc="gps")
    p.op("vector", lambda e: e.memset(onesN[:, :], 1.0), inc="dve")
    p.op("vector", lambda e: e.memset(fill80[:, :], 100000.0), inc="dve")
    vd0 = p.val("dve")
    p.wait("gpsimd", "dve", vd0)
    p.dma("gpsimd", io["slot_dram"].ap().rearrange("(g q) o -> g (q o)", q=128), fill80[:, :], inc="qa")

    for it in range(8):
        p.wait("gpsimd", "dve", p.val("dve"))
        p.wait("gpsimd", "act", p.val("act"))
        vx = p.dma("gpsimd", xtl[:, :], io["x_in"][it * 128:(it + 1) * 128, :], inc="qa")
        p.wait("vector", "qa", vx)
        p.op("vector", lambda e: e.tensor_reduce(out=lnS[:, 0:1], in_=xtl[:, :], axis=AXX, op=ALU.add), inc="dve")
        p.wait("scalar", "qa", vx)
        p.op("scalar", lambda e: e.activation(out=sq2[:, :], in_=xtl[:, :], func=AF.Square), inc="act")
        p.wait("vector", "act", p.val("act"))
        p.op("vector", lambda e: e.tensor_reduce(out=lnS[:, 1:2], in_=sq2[:, :], axis=AXX, op=ALU.add), inc="dve")
        p.op("vector", lambda e: e.tensor_scalar(out=lnS[:, 1:2], in0=lnS[:, 1:2], scalar1=1.0 / D, scalar2=None, op0=ALU.mult), inc="dve")
        p.op("vector", lambda e: e.tensor_scalar(out=lnS[:, 2:3], in0=lnS[:, 0:1], scalar1=1.0 / D, scalar2=None, op0=ALU.mult), inc="dve")
        p.op("vector", lambda e: e.tensor_scalar(out=lnS[:, 4:5], in0=lnS[:, 2:3], scalar1=lnS[:, 2:3], scalar2=None, op0=ALU.mult), inc="dve")
        p.op("vector", lambda e: e.tensor_sub(out=lnS[:, 4:5], in0=lnS[:, 1:2], in1=lnS[:, 4:5]), inc="dve")
        vd = p.val("dve")
        p.wait("scalar", "dve", vd)
        p.op("scalar", lambda e: e.activation(out=lnS[:, 5:6], in_=lnS[:, 4:5], func=AF.Ln, bias=EPS), inc="act")
        p.op("scalar", lambda e: e.activation(out=lnS[:, 6:7], in_=lnS[:, 5:6], func=AF.Exp, scale=-0.5), inc="act")
        va = p.val("act")
        p.wait("vector", "act", va)
        p.op("vector", lambda e: e.tensor_scalar(out=utok[:, :], in0=xtl[:, :], scalar1=lnS[:, 2:3], scalar2=lnS[:, 6:7], op0=ALU.subtract, op1=ALU.mult), inc="dve")
        vd = p.val("dve")
        p.wait("gpsimd", "dve", vd)
        p.dma("gpsimd", io["u4_tm"].ap()[it * 128:(it + 1) * 128, :], utok[:, :], inc="qa")
        p.wait("tensor", "dve", vd)
        for j in range(16):
            p.op("tensor", lambda e, j=j: e.transpose(
                out=pbb[j % 4][:, 0:128], in_=utok[:, j * 128:(j + 1) * 128],
                identity=idb[:, :]), inc=("pe" if j % 4 == 3 else None))
            if j % 4 == 3:
                v = p.val("pe")
                p.wait("scalar", "pe", v)
                for jj in range(j - 3, j + 1):
                    p.op("scalar", lambda e, jj=jj, it=it: e.copy(
                        out=uT(jj, it * 128, it * 128 + 128), in_=pbb[jj % 4][:, 0:128]), inc="act")
                p.wait("tensor", "act", p.val("act"))

    for k in range(16):
        vw = p.dma("sync", wbf[:, 0:64], io["wgate"][k * 128:(k + 1) * 128, :], inc="qw")
        p.wait("tensor", "qw", vw)
        for nf in range(2):
            p.op("tensor", lambda e, k=k, nf=nf: e.matmul(
                out=pb[nf][0:64, :], lhsT=wbf[:, 0:64], rhs=uT(k, nf * 512, (nf + 1) * 512),
                start=(k == 0), stop=(k == 15)), inc=("pe" if nf == 1 else None))
        vpe = p.val("pe")
        p.wait("sync", "pe", vpe)
    vpe = p.val("pe")
    p.wait("scalar", "pe", vpe)
    for nf in range(2):
        p.op("scalar", lambda e, nf=nf: e.activation(out=scT[0:64, nf * 512:(nf + 1) * 512], in_=pb[nf][0:64, :], func=AF.Sigmoid), inc="act")
    va_sc = p.val("act")
    p.wait("tensor", "act", va_sc)

    for tt in range(8):
        p.op("tensor", lambda e, tt=tt: e.transpose(out=pb[4][:, 0:64], in_=scT[0:64, tt * 128:(tt + 1) * 128], identity=idf[0:64, 0:64]), inc="pe")
        v = p.val("pe")
        p.wait("vector", "pe", v)
        p.op("vector", lambda e: e.tensor_copy(out=sctok[:, :], in_=pb[4][:, 0:64]), inc="dve")
        p.op("vector", lambda e: e.max(out=mx[:, :], in_=sctok[:, :]), inc="dve")
        p.op("vector", lambda e: e.max_index(out=ixu[:, :], in_max=mx[:, :], in_values=sctok[:, :]), inc="dve")
        p.op("vector", lambda e, tt=tt: e.tensor_copy(out=ixf[:, tt * 8:(tt + 1) * 8], in_=ixu[:, :]), inc="dve")
        p.op("vector", lambda e: e.tensor_reduce(out=wno[:, 6:7], in_=mx[:, 0:6], axis=AXX, op=ALU.add), inc="dve")
        p.op("vector", lambda e: e.reciprocal(out=wno[:, 7:8], in_=wno[:, 6:7]), inc="dve")
        p.op("vector", lambda e: e.tensor_scalar(out=wno[:, 0:6], in0=mx[:, 0:6], scalar1=wno[:, 7:8], scalar2=None, op0=ALU.mult), inc="dve")
        p.op("vector", lambda e, tt=tt: e.tensor_copy(out=wcols[:, tt * 6:(tt + 1) * 6], in_=wno[:, 0:6]), inc="dve")
        p.op("vector", lambda e: e.memset(selsum[:, :], 0.0), inc="dve")
        for j in range(6):
            p.op("vector", lambda e, tt=tt, j=j: e.tensor_scalar(
                out=ohb[:, :], in0=iota64[:, :], scalar1=ixf[:, tt * 8 + j:tt * 8 + j + 1], scalar2=None, op0=ALU.is_equal), inc="dve")
            p.op("vector", lambda e: e.tensor_add(out=selsum[:, :], in0=selsum[:, :], in1=ohb[:, :]), inc="dve")
        vd = p.val("dve")
        p.wait("tensor", "dve", vd)
        p.op("tensor", lambda e: e.transpose(out=pb[5][0:64, 0:128], in_=selsum[:, :], identity=idf[:, :]), inc="pe")
        v = p.val("pe")
        p.wait("vector", "pe", v)
        p.op("vector", lambda e, tt=tt: e.tensor_copy(out=selT[0:64, tt * 128:(tt + 1) * 128], in_=pb[5][0:64, 0:128]), inc="dve")
        p.wait("tensor", "dve", p.val("dve"))
    p.op("vector", lambda e: e.tensor_tensor_scan(
        out=posT[0:64, :], data0=onesN[:, 0:1].to_broadcast([64, T]), data1=selT[0:64, :],
        initial=0.0, op0=ALU.mult, op1=ALU.add), inc="dve")
    p.op("vector", lambda e: e.tensor_sub(out=posT[0:64, :], in0=posT[0:64, :], in1=selT[0:64, :]), inc="dve")
    vd = p.val("dve")
    p.wait("tensor", "dve", vd)
    for tt in range(8):
        p.op("tensor", lambda e, tt=tt: e.transpose(out=pb[4][:, 0:64], in_=posT[0:64, tt * 128:(tt + 1) * 128], identity=idf[0:64, 0:64]), inc="pe")
        v = p.val("pe")
        p.wait("vector", "pe", v)
        p.op("vector", lambda e: e.tensor_copy(out=postok[:, :], in_=pb[4][:, 0:64]), inc="dve")
        for j in range(6):
            p.op("vector", lambda e, tt=tt, j=j: e.tensor_scalar(
                out=ohb[:, :], in0=iota64[:, :], scalar1=ixf[:, tt * 8 + j:tt * 8 + j + 1], scalar2=None, op0=ALU.is_equal), inc="dve")
            p.op("vector", lambda e: e.tensor_mul(out=pscr[:, :], in0=ohb[:, :], in1=postok[:, :]), inc="dve")
            p.op("vector", lambda e: e.tensor_reduce(out=pcol[:, 0:1], in_=pscr[:, :], axis=AXX, op=ALU.add), inc="dve")
            p.op("vector", lambda e: e.tensor_scalar(out=pcol[:, 1:2], in0=pcol[:, 0:1], scalar1=159.5, scalar2=1.0e6, op0=ALU.is_gt, op1=ALU.mult), inc="dve")
            p.op("vector", lambda e, tt=tt, j=j: e.scalar_tensor_tensor(
                out=pcol[:, 2:3], in0=ixf[:, tt * 8 + j:tt * 8 + j + 1], scalar=160.0, in1=pcol[:, 0:1], op0=ALU.mult, op1=ALU.add), inc="dve")
            p.op("vector", lambda e, tt=tt, j=j: e.tensor_add(out=slotf[:, tt * 6 + j:tt * 6 + j + 1], in0=pcol[:, 2:3], in1=pcol[:, 1:2]), inc="dve")
        p.wait("tensor", "dve", p.val("dve"))
    p.op("vector", lambda e: e.tensor_copy(out=sloti[:, :], in_=slotf[:, :]), inc="dve")
    vd_slot = p.val("dve")

    p.wait("gpsimd", "dve", vd_slot)
    p.wait("gpsimd", "qa", p.val("qa"))
    for tt in range(8):
        p.wait("gpsimd", "qa", p.val("qa"))
        p.op("gpsimd", lambda e, tt=tt: e.iota(out=tokf[:, :], pattern=[[1, 1]], base=tt * 128, channel_multiplier=1, allow_small_or_imprecise_dtypes=True), inc="gps")
        p.wait("gpsimd", "gps", p.val("gps"))
        for j in range(6):
            p.op("gpsimd", lambda e, tt=tt, j=j: e.indirect_dma_start(
                out=io["slot_dram"].ap(), out_offset=IndirectOffsetOnAxis(ap=sloti[:, tt * 6 + j:tt * 6 + j + 1], axis=0),
                in_=tokf[:, :], in_offset=None, bounds_check=regs["b10239"], oob_is_err=False), inc="qa", n=16)
    p.wait("gpsimd", "qa", p.val("qa"))
    v = p.dma("gpsimd", stab80[:, :], io["slot_dram"].ap().rearrange("(g q) o -> g (q o)", q=128), inc="qa")
    p.wait("tensor", "qa", v)
    p.op("tensor", lambda e: e.transpose(out=pb[4][:, 0:80], in_=stab80[:, :], identity=idf[0:80, 0:80]), inc="pe")
    v = p.val("pe")
    p.wait("vector", "pe", v)
    p.op("vector", lambda e: e.tensor_copy(out=stabT[:, :], in_=pb[4][:, 0:80]), inc="dve")
    p.op("vector", lambda e: e.tensor_copy(out=stabI[:, :], in_=stabT[:, :]), inc="dve")
    vd_stab = p.val("dve")
    p.wait("gpsimd", "dve", vd_stab)
    xgb = p.sbuf("xgb" + sfx, [128, 2048], BF16)
    for g in range(80):
        vg = p.op("gpsimd", lambda e, g=g: e.indirect_dma_start(
            out=xgb[:, :], out_offset=None,
            in_=io["u4_tm"].ap(), in_offset=IndirectOffsetOnAxis(ap=stabI[:, g:g + 1], axis=0),
            bounds_check=regs["b1023"], oob_is_err=False), inc="qa", n=16)
        p.wait("gpsimd", "qa", vg)
        p.dma("gpsimd", io["Xall"].ap()[g * 128:(g + 1) * 128, :], xgb[:, :], inc="qa")
        p.wait("gpsimd", "qa", p.val("qa"))

    p.wait("tensor", "dve", vd_stab)
    for grp in ([0, 1, 2], [3, 4, 5], [6, 7]):
        for k in range(16):
            vw = p.dma("sync", wbf[:, 0:128 * len(grp)], io["wsup"][k * 128:(k + 1) * 128, 128 * grp[0]:128 * grp[0] + 128 * len(grp)], inc="qw")
            p.wait("tensor", "qw", vw)
            for mi, m in enumerate(grp):
                for nf in range(2):
                    p.op("tensor", lambda e, k=k, mi=mi, nf=nf: e.matmul(
                        out=pb[2 * mi + nf][:, :], lhsT=wbf[:, mi * 128:(mi + 1) * 128], rhs=uT(k, nf * 512, (nf + 1) * 512),
                        start=(k == 0), stop=(k == 15)), inc=("pe" if (mi == len(grp) - 1 and nf == 1) else None))
            vpe = p.val("pe")
            p.wait("sync", "pe", vpe)
        vpe = p.val("pe")
        p.wait("scalar", "pe", vpe)
        p.wait("vector", "pe", vpe)
        for mi, m in enumerate(grp):
            if m < 4:
                for nf in range(2):
                    p.op("scalar", lambda e, m=m, mi=mi, nf=nf: e.copy(
                        out=ybuf[:, m * T + nf * 512: m * T + (nf + 1) * 512], in_=pb[2 * mi + nf][:, :]), inc="act")
            else:
                p.wait("scalar", "dve", p.val("dve"))
                for nf in range(2):
                    p.op("scalar", lambda e, mi=mi, nf=nf: e.activation(
                        out=gsil[:, nf * 512:(nf + 1) * 512], in_=pb[2 * mi + nf][:, :], func=AF.Silu), inc="act")
                va = p.val("act")
                p.wait("vector", "act", va)
                p.op("vector", lambda e, m=m: e.tensor_mul(
                    out=actS[:, (m - 4) * T:(m - 3) * T], in0=gsil[:, :], in1=ybuf[:, (m - 4) * T:(m - 3) * T]), inc="dve")
        p.wait("tensor", "act", p.val("act"))
        p.wait("tensor", "dve", p.val("dve"))
        p.wait("scalar", "dve", p.val("dve"))
    for kd in range(4):
        p.dma("sync", wsdn_sb[:, kd * 2048:(kd + 1) * 2048], io["wsdn"][kd * 128:(kd + 1) * 128, :], inc="qw")
    v_wsdn = p.val("qw")

    for eg in range(16):
        p.wait("sync", "pe", p.val("pe"))
        p.wait("sync", "qa", p.val("qa"))
        for kc in range(16):
            p.dma("sync", XT[:, kc * 640:(kc + 1) * 640], io["Xall"].ap()[eg * 640:(eg + 1) * 640, kc * 128:(kc + 1) * 128], inc="qw", transpose=True)
        v_xt = p.val("qw")
        p.wait("tensor", "qw", v_xt)
        for le in range(4):
            e_ = eg * 4 + le
            soff = le * 160
            upslc2 = lambda kk, e_=e_: io["gup"][e_ * 2048 + 256 * kk: e_ * 2048 + 256 * (kk + 1),
                                                 :].rearrange("(i p) c -> p i c", p=128)

            def ubody(kc, buf, soff=soff):
                p.op("tensor", lambda e, kc=kc, soff=soff, buf=buf: e.matmul(
                    out=pb[0][:, :], lhsT=XT[:, kc * 640 + soff: kc * 640 + soff + 128], rhs=buf,
                    start=(kc == 0), stop=(kc == 15)))
                p.op("tensor", lambda e, kc=kc, soff=soff, buf=buf: e.matmul(
                    out=pb[1][0:32, :], lhsT=XT[:, kc * 640 + soff + 128: kc * 640 + soff + 160], rhs=buf,
                    start=(kc == 0), stop=(kc == 15)), inc="pe")
            stream_mm(p, 16, wbm, 512, upslc2, ubody)
            vpe = p.val("pe")
            p.wait("sync", "pe", vpe)
            vpe = p.val("pe")
            p.wait("scalar", "pe", vpe)
            p.wait("vector", "pe", vpe)
            p.op("scalar", lambda e: e.activation(out=ab0[:, 0:256], in_=pb[0][:, 0:256], func=AF.Silu), inc="act")
            p.op("scalar", lambda e: e.activation(out=ab1[:, 0:256], in_=pb[1][0:32, 0:256], func=AF.Silu), inc="act")
            va = p.val("act")
            p.wait("vector", "act", va)
            p.op("vector", lambda e: e.tensor_tensor(out=ab0[:, 256:512], in0=ab0[:, 0:256], in1=pb[0][:, 256:512], op=ALU.mult), inc="dve")
            p.op("vector", lambda e: e.tensor_tensor(out=ab1[:, 256:512], in0=ab1[:, 0:256], in1=pb[1][0:32, 256:512], op=ALU.mult), inc="dve")
            vd = p.val("dve")
            p.wait("tensor", "dve", vd)
            p.op("tensor", lambda e: e.transpose(out=pbb[2][:, 0:128], in_=ab0[:, 256:384], identity=idb[:, :]), inc="pe")
            p.op("tensor", lambda e: e.transpose(out=pbb[3][:, 0:128], in_=ab0[:, 384:512], identity=idb[:, :]), inc="pe")
            p.op("tensor", lambda e: e.transpose(out=pbb[4][:, 0:32], in_=ab1[0:32, 256:384], identity=idb[0:32, 0:32]), inc="pe")
            p.op("tensor", lambda e: e.transpose(out=pbb[5][:, 0:32], in_=ab1[0:32, 384:512], identity=idb[0:32, 0:32]), inc="pe")
            vpe = p.val("pe")
            p.wait("scalar", "pe", vpe)
            p.op("scalar", lambda e: e.copy(out=aTa[:, 0:128], in_=pbb[2][:, 0:128]), inc="act")
            p.op("scalar", lambda e: e.copy(out=aTb[:, 0:128], in_=pbb[3][:, 0:128]), inc="act")
            p.op("scalar", lambda e: e.copy(out=aTa[:, 128:160], in_=pbb[4][:, 0:32]), inc="act")
            p.op("scalar", lambda e: e.copy(out=aTb[:, 128:160], in_=pbb[5][:, 0:32]), inc="act")
            va = p.val("act")
            p.wait("tensor", "act", va)
            pb1map = [6, 7, 0, 1]
            dnslc2 = lambda kk, e_=e_: io["gdn"][e_ * 256 + (kk // 2) * 128: e_ * 256 + (kk // 2 + 1) * 128,
                                                 (kk % 2) * 1024:(kk % 2 + 1) * 1024].rearrange("p (i c) -> p i c", c=512)

            def dbody(i, buf):
                kd, dchunk = i // 4, i % 4
                lt = aTa if kd == 0 else aTb
                p.op("tensor", lambda e, kd=kd, dchunk=dchunk, lt=lt, buf=buf: e.matmul(
                    out=pb[2 + dchunk][:, :], lhsT=lt[:, 0:128], rhs=buf,
                    start=(kd == 0), stop=(kd == 1)))
                p.op("tensor", lambda e, kd=kd, dchunk=dchunk, lt=lt, buf=buf: e.matmul(
                    out=pb[pb1map[dchunk]][0:32, :],
                    lhsT=lt[:, 128:160], rhs=buf,
                    start=(kd == 0), stop=(kd == 1)), inc="pe")
            stream_mm(p, 8, wbm, 512, dnslc2, dbody)
            vpe = p.val("pe")
            p.wait("scalar", "pe", vpe)
            p.wait("scalar", "qa", p.val("qa"))
            for dchunk in range(4):
                p.op("scalar", lambda e, dchunk=dchunk: e.copy(out=do0[:, dchunk * 512:(dchunk + 1) * 512], in_=pb[2 + dchunk][:, :]), inc="act")
            for dchunk in range(4):
                p.op("scalar", lambda e, dchunk=dchunk: e.copy(out=do1[0:32, dchunk * 512:(dchunk + 1) * 512], in_=pb[pb1map[dchunk]][0:32, :]), inc="act")
            va = p.val("act")
            p.wait("gpsimd", "act", va)
            p.wait("tensor", "act", va)
            p.dma("gpsimd", io["down_all"].ap()[e_ * 160:e_ * 160 + 128, :], do0[:, :], inc="qa")
            p.dma("gpsimd", io["down_all"].ap()[e_ * 160 + 128:e_ * 160 + 160, :], do1[0:32, :], inc="qa")
            p.wait("tensor", "qa", p.val("qa"))
    p.wait("gpsimd", "qa", p.val("qa"))

    p.wait("tensor", "qw", v_wsdn)
    for tt in range(8):
        for kd in range(4):
            for dchunk in range(4):
                p.op("tensor", lambda e, tt=tt, kd=kd, dchunk=dchunk: e.matmul(
                    out=pb[dchunk][:, :], lhsT=actS[:, kd * T + tt * 128: kd * T + (tt + 1) * 128],
                    rhs=wsdn_sb[:, kd * 2048 + dchunk * 512: kd * 2048 + (dchunk + 1) * 512],
                    start=(kd == 0), stop=(kd == 3)), inc=("pe" if (kd == 3 and dchunk == 3) else None))
        vpe = p.val("pe")
        p.wait("vector", "pe", vpe)
        for dchunk in range(4):
            p.op("vector", lambda e, dchunk=dchunk: e.tensor_copy(out=accf[:, dchunk * 512:(dchunk + 1) * 512], in_=pb[dchunk][:, :]), inc="dve")
        vd = p.val("dve")
        p.wait("tensor", "dve", vd)
        p.wait("gpsimd", "dve", vd)
        for j in range(6):
            vg = p.op("gpsimd", lambda e, tt=tt, j=j: e.indirect_dma_start(
                out=yg[:, :], out_offset=None, in_=io["down_all"].ap(),
                in_offset=IndirectOffsetOnAxis(ap=sloti[:, tt * 6 + j:tt * 6 + j + 1], axis=0),
                bounds_check=regs["b10239"], oob_is_err=False), inc="qa", n=16)
            p.wait("vector", "qa", vg)
            p.op("vector", lambda e, tt=tt, j=j: e.scalar_tensor_tensor(
                out=accf[:, :], in0=yg[:, :], scalar=wcols[:, tt * 6 + j:tt * 6 + j + 1], in1=accf[:, :],
                op0=ALU.mult, op1=ALU.add), inc="dve")
            vd = p.val("dve")
            p.wait("gpsimd", "dve", vd)
        vx = p.dma("gpsimd", xtl[:, :], io["x_in"][tt * 128:(tt + 1) * 128, :], inc="qa")
        p.op("vector", lambda e: e.tensor_reduce(out=lnS[:, 0:1], in_=accf[:, :], axis=AXX, op=ALU.add), inc="dve")
        p.wait("scalar", "dve", p.val("dve"))
        p.op("scalar", lambda e: e.activation(out=sq2[:, :], in_=accf[:, :], func=AF.Square), inc="act")
        p.wait("vector", "act", p.val("act"))
        p.op("vector", lambda e: e.tensor_reduce(out=lnS[:, 1:2], in_=sq2[:, :], axis=AXX, op=ALU.add), inc="dve")
        p.op("vector", lambda e: e.tensor_scalar(out=lnS[:, 1:2], in0=lnS[:, 1:2], scalar1=1.0 / D, scalar2=None, op0=ALU.mult), inc="dve")
        p.op("vector", lambda e: e.tensor_scalar(out=lnS[:, 2:3], in0=lnS[:, 0:1], scalar1=1.0 / D, scalar2=None, op0=ALU.mult), inc="dve")
        p.op("vector", lambda e: e.tensor_scalar(out=lnS[:, 4:5], in0=lnS[:, 2:3], scalar1=lnS[:, 2:3], scalar2=None, op0=ALU.mult), inc="dve")
        p.op("vector", lambda e: e.tensor_sub(out=lnS[:, 4:5], in0=lnS[:, 1:2], in1=lnS[:, 4:5]), inc="dve")
        vd = p.val("dve")
        p.wait("scalar", "dve", vd)
        p.op("scalar", lambda e: e.activation(out=lnS[:, 5:6], in_=lnS[:, 4:5], func=AF.Ln, bias=EPS), inc="act")
        p.op("scalar", lambda e: e.activation(out=lnS[:, 6:7], in_=lnS[:, 5:6], func=AF.Exp, scale=-0.5), inc="act")
        va = p.val("act")
        p.wait("vector", "act", va)
        p.wait("vector", "qa", vx)
        p.op("vector", lambda e: e.tensor_scalar(
            out=accf[:, :], in0=accf[:, :], scalar1=lnS[:, 2:3], scalar2=lnS[:, 6:7], op0=ALU.subtract, op1=ALU.mult), inc="dve")
        p.op("vector", lambda e: e.tensor_add(out=xtl[:, :], in0=xtl[:, :], in1=accf[:, :]), inc="dve")
        vd = p.val("dve")
        p.wait("gpsimd", "dve", vd)
        p.dma("gpsimd", io["x_out"][tt * 128:(tt + 1) * 128, :], xtl[:, :], inc="qa")
        p.wait("gpsimd", "qa", p.val("qa"))
    p.wait("gpsimd", "qw", p.val("qw"))


def run_moe_unfused(x_full, Wgate, Wsup, Wsdn, Gup, Gdn):
    nc = bass.Bass("TRN2", target_bir_lowering=False)
    p = Prog(nc)
    io = {
        "x_in": nc.dram_tensor("x_in", [T, D], F32, kind="ExternalInput").ap(),
        "wgate": nc.dram_tensor("wgate", [2048, 64], BF16, kind="ExternalInput").ap(),
        "wsup": nc.dram_tensor("wsup", [2048, 1024], BF16, kind="ExternalInput").ap(),
        "wsdn": nc.dram_tensor("wsdn", [512, 2048], BF16, kind="ExternalInput").ap(),
        "gup": nc.dram_tensor("gup", [131072, 512], BF16, kind="ExternalInput").ap(),
        "gdn": nc.dram_tensor("gdn", [16384, 2048], BF16, kind="ExternalInput").ap(),
        "x_out": nc.dram_tensor("x_out", [T, D], F32, kind="ExternalOutput").ap(),
        "u4_tm": nc.dram_tensor("u4_tm", [T, D], BF16, kind="Internal"),
        "slot_dram": nc.dram_tensor("slot_dram", [10240, 1], F32, kind="Internal"),
        "Xall": nc.dram_tensor("Xall", [10240, 2048], BF16, kind="Internal"),
        "down_all": nc.dram_tensor("down_all", [10240, 2048], BF16, kind="Internal"),
    }
    idb = p.sbuf("idb", [128, 128], BF16)
    idf = p.sbuf("idf", [128, 128], F32)
    emit_idents(p, idb, idf)
    emit_moe(p, io, idb, idf)
    p.realize()
    in_maps = []
    for c in range(NCORES):
        b, q = c // 4, c % 4
        in_maps.append({
            "x_in": np.ascontiguousarray(x_full[b, q * 1024:(q + 1) * 1024]),
            "wgate": Wgate, "wsup": Wsup, "wsdn": Wsdn, "gup": Gup, "gdn": Gdn,
        })
    res = run_bass_kernel_spmd(nc, in_maps, core_ids=list(range(NCORES)))
    out = np.empty((2, 4096, D), np.float32)
    for c in range(NCORES):
        b, q = c // 4, c % 4
        out[b, q * 1024:(q + 1) * 1024] = res.results[c]["x_out"]
    return out



def run_fused(inp):
    nc = bass.Bass("TRN2", target_bir_lowering=False)
    p = Prog(nc)

    def ein(name, shape, dtype=F32):
        return nc.dram_tensor(name, shape, dtype, kind="ExternalInput").ap()

    def itn(name, shape, dtype):
        return nc.dram_tensor(name, shape, dtype, kind="Internal").ap()

    x_in = ein("x_in", [T, D])
    xh0 = ein("xh0", [3, D])
    mrow8 = ein("mrow8", [1, 8]); mcol8 = ein("mcol8", [128, 8])
    mh8 = ein("mh8", [128, 8]); mh8_3 = ein("mh8_3", [3, 8])
    shards = {
        "W0in_s": ein("W0in_s", [256, 8512]), "W0out_s": ein("W0out_s", [512, 2048]),
        "Mup_s": ein("Mup_s", [16384, 512]), "Mdn_s": ein("Mdn_s", [2048, 2048]),
        "Wsup_s": ein("Wsup_s", [256, 1024]), "Wsdn_s": ein("Wsdn_s", [64, 2048]),
        "Wgate_s": ein("Wgate_s", [256, 64]),
        "W1in_s": ein("W1in_s", [1152, 2048]), "W1out_s": ein("W1out_s", [2048, 512]),
        "gP_s": ein("gP_s", [2048, 1024], I32), "gS_s": ein("gS_s", [2048, 64]),
        "uP_s": ein("uP_s", [2048, 1024], I32), "uS_s": ein("uS_s", [2048, 64]),
        "dP_s": ein("dP_s", [2048, 1024], I32), "dS_s": ein("dS_s", [2048, 64]),
    }
    conv0 = {k: ein("c0_" + k, s) for k, s in
             (("conv_w", [4352, 4]), ("conv_b", [4352]), ("dt_bias", [64]), ("A_log", [64]), ("Dv", [64]), ("norm_w", [4096]))}
    conv1 = {k: ein("c1_" + k, s) for k, s in
             (("conv_w", [4224, 4]), ("conv_b", [4224]), ("dt_bias", [64]), ("A_log", [64]), ("Dv", [64]), ("norm_w", [4096]))}
    x_out = nc.dram_tensor("x_out", [T, D], F32, kind="ExternalOutput").ap()

    x1 = itn("x1", [T, D], F32)
    x2 = itn("x2", [T, D], F32)
    x3 = itn("x3", [T, D], F32)
    contrib = {
        "W0in": itn("cW0in", [256, 8512], BF16), "W0out": itn("cW0out", [512, 2048], BF16),
        "Mup": itn("cMup", [16384, 512], BF16), "Mdn": itn("cMdn", [2048, 2048], BF16),
        "Wsup": itn("cWsup", [256, 1024], BF16), "Wsdn": itn("cWsdn", [64, 2048], BF16),
        "Wgate": itn("cWgate", [256, 64], BF16),
        "W1inT": itn("cW1inT", [2048, 1152], BF16), "W1outT": itn("cW1outT", [512, 2048], BF16),
        "GgT": itn("cGgT", [2048, 2048], BF16), "GuT": itn("cGuT", [2048, 2048], BF16), "GdT": itn("cGdT", [2048, 2048], BF16),
    }
    gath = {
        "W0in": itn("GW0in", [2048, 8512], BF16), "W0out": itn("GW0out", [4096, 2048], BF16),
        "Mup": itn("GMup", [131072, 512], BF16), "Mdn": itn("GMdn", [16384, 2048], BF16),
        "Wsup": itn("GWsup", [2048, 1024], BF16), "Wsdn": itn("GWsdn", [512, 2048], BF16),
        "Wgate": itn("GWgate", [2048, 64], BF16),
        "W1inT": itn("GW1inT", [16384, 1152], BF16), "W1outT": itn("GW1outT", [4096, 2048], BF16),
        "GgT": itn("GGgT", [16384, 2048], BF16), "GuT": itn("GGuT", [16384, 2048], BF16), "GdT": itn("GGdT", [16384, 2048], BF16),
    }
    exch = {}
    for L in (0, 1):
        N = 128 if L == 0 else 64
        exch[L] = {
            "cS": itn(f"cS{L}", [N, 4096], F32), "GS": itn(f"GS{L}", [8 * N, 4096], F32),
            "cg": itn(f"cg{L}", [1, 64], F32), "Gg": itn(f"Gg{L}", [8, 64], F32),
            "ch": itn(f"ch{L}", [128, 102], BF16), "Gh": itn(f"Gh{L}", [1024, 102], BF16),
        }
    cxh = itn("cxh", [3, D], F32)
    Gxh = itn("Gxh", [24, D], F32)

    idb = p.sbuf("idb", [128, 128], BF16)
    idf = p.sbuf("idf", [128, 128], F32)
    emit_idents(p, idb, idf)
    cstG = p.sbuf("cstG", [128, 4], F32)
    p.op("vector", lambda e: e.memset(cstG[:, 0:1], EPS), inc="dve")
    p.op("vector", lambda e: e.memset(cstG[:, 1:2], 1.0), inc="dve")
    p.op("vector", lambda e: e.memset(cstG[:, 2:3], -2.0), inc="dve")
    p.op("vector", lambda e: e.memset(cstG[:, 3:4], -6.0), inc="dve")
    nc.const_aps.register(F32, EPS, cstG[:, 0:1])
    nc.const_aps.register(F32, 1.0, cstG[:, 1:2])
    nc.const_aps.register(F32, -2.0, cstG[:, 2:3])
    nc.const_aps.register(F32, -6.0, cstG[:, 3:4])
    p._consts_done = True
    p._consts2_done = True
    p.wait("scalar", "dve", p.val("dve"))

    def ag(cin, cout):
        v = p.op("gpsimd", lambda e, cin=cin, cout=cout: e.collective_compute(
            "AllGather", ALU.bypass, replica_groups=[list(range(8))],
            ins=[cin.opt()], outs=[cout.opt()]), inc="cc")
        return v

    stage_n = [0]

    def prep_stage(jobs):
        stage_n[0] += 1
        m = p.push()
        emit_prep(p, jobs, idb, idf, sfx=str(stage_n[0]))
        p.barrier(skip_cc=True)
        p.pop(m)

    p.dma("gpsimd", contrib["W0in"], shards["W0in_s"], inc="qa0")
    v_w0cast = p.dma("gpsimd", contrib["W0out"], shards["W0out_s"], inc="qa0")
    p.wait("gpsimd", "qa0", v_w0cast)
    ag(contrib["W0in"], gath["W0in"])
    v_cc_w0 = ag(contrib["W0out"], gath["W0out"])
    for k in ("Mup", "Mdn", "Wsup", "Wsdn", "Wgate"):
        v_moecast = p.dma("gpsimd", contrib[k], shards[k + "_s"], inc="qa0")
    prep_stage({"casts": [], "ctrans": [],
                "decode": [(shards["gP_s"], shards["gS_s"], contrib["GgT"]),
                           (shards["uP_s"], shards["uS_s"], contrib["GuT"]),
                           (shards["dP_s"], shards["dS_s"], contrib["GdT"])]})
    for k in ("GgT", "GuT", "GdT"):
        ag(contrib[k], gath[k])
    prep_stage({"casts": [], "decode": [],
                "ctrans": [(shards["W1in_s"], contrib["W1inT"], 1152, 2048),
                           (shards["W1out_s"], contrib["W1outT"], 2048, 512)]})
    ag(contrib["W1inT"], gath["W1inT"])
    ag(contrib["W1outT"], gath["W1outT"])
    moe_ag_keys = ("Mup", "Mdn", "Wsup", "Wsdn", "Wgate")
    for eng in ENGS:
        p.wait(eng, "cc", v_cc_w0)

    def mamba_io(L, tag, xin, conv):
        cfg = MambaCfg(L)
        io = dict(conv)
        io["tag"] = tag
        io["x_in"] = xin
        io["acs_dram"] = nc.dram_tensor("acsd_" + tag, [64, CH], F32, kind="Internal")
        io["alast_dram"] = nc.dram_tensor("ald_" + tag, [64], F32, kind="Internal")
        if L == 0:
            io["w_in_slc"] = l0_win_slc(gath["W0in"])
            io["w_in_slc2"] = l0_win_slc2(gath["W0in"])
            io["w_out_slc"] = l0_wout_slc(gath["W0out"])
            io["w_out_slc2"] = l0_wout_slc2(gath["W0out"])
        else:
            io["w_in_slc"] = l1_win_slc(gath["W1inT"])
            io["w_in_slc2"] = l1_win_slc2(gath["W1inT"])
            io["w_out_slc"] = l0_wout_slc(gath["W1outT"])
            io["w_out_slc2"] = l0_wout_slc2(gath["W1outT"])
        return cfg, io

    for L, xin, xout in ((0, x_in, x1), (1, x2, x3)):
        conv = conv0 if L == 0 else conv1
        N = 128 if L == 0 else 64
        cfg, io = mamba_io(L, f"s{L}", xin, {
            "conv_w": conv["conv_w"], "conv_b": conv["conv_b"], "dt_bias": conv["dt_bias"],
            "A_log": conv["A_log"], "Dv": conv["Dv"], "norm_w": conv["norm_w"]})
        if L == 0:
            io["x_halo"] = xh0
        else:
            io["xh_all"] = Gxh
            io["mh8_3"] = mh8_3
        io["out_S"] = exch[L]["cS"]
        io["out_g"] = exch[L]["cg"].transpose([1, 0])
        io["out_halo"] = exch[L]["ch"]
        m = p.push()
        emit_mamba(p, cfg, io, idb, idf, "state")
        p.barrier()
        p.pop(m)
        ag(exch[L]["cS"], exch[L]["GS"])
        ag(exch[L]["cg"], exch[L]["Gg"])
        v_ex = ag(exch[L]["ch"], exch[L]["Gh"])
        if L == 0:
            p.wait("gpsimd", "qa0", v_moecast)
            for k in moe_ag_keys:
                ag(contrib[k], gath[k])
        for eng in ENGS:
            p.wait(eng, "cc", v_ex)

        cfg, io = mamba_io(L, f"f{L}", xin, {
            "conv_w": conv["conv_w"], "conv_b": conv["conv_b"], "dt_bias": conv["dt_bias"],
            "A_log": conv["A_log"], "Dv": conv["Dv"], "norm_w": conv["norm_w"]})
        io["fused"] = True
        io["S_all"] = exch[L]["GS"]
        io["g_all"] = exch[L]["Gg"]
        io["halo_all"] = exch[L]["Gh"]
        io["mrow8"] = mrow8
        io["mcol8"] = mcol8
        io["mh8"] = mh8
        io["x_out"] = xout
        m = p.push()
        emit_mamba(p, cfg, io, idb, idf, "full")
        p.barrier()
        p.pop(m)

        if L == 0:
            m = p.push()
            emit_ffn0(p, {"x_in": x1, "x_out": x2, "g_gT": gath["GgT"], "g_uT": gath["GuT"], "g_dT": gath["GdT"]}, idb, idf)
            p.barrier()
            p.pop(m)
            p.dma("gpsimd", cxh, x2[T - 3:T, :], inc="qa")
            p.wait("gpsimd", "qa", p.val("qa"))
            ag(cxh, Gxh)
            vcc = p.val("cc")
            for eng in ENGS:
                p.wait(eng, "cc", vcc)

    m = p.push()
    emit_moe(p, {
        "x_in": x3, "x_out": x_out,
        "wgate": gath["Wgate"], "wsup": gath["Wsup"], "wsdn": gath["Wsdn"],
        "gup": gath["Mup"], "gdn": gath["Mdn"],
        "u4_tm": nc.dram_tensor("u4_tm", [T, D], BF16, kind="Internal"),
        "slot_dram": nc.dram_tensor("slot_dram", [10240, 1], F32, kind="Internal"),
        "Xall": nc.dram_tensor("Xall", [10240, 2048], BF16, kind="Internal"),
        "down_all": nc.dram_tensor("down_all", [10240, 2048], BF16, kind="Internal"),
    }, idb, idf)
    p.pop(m)
    p.realize()

    w1in_pad = np.zeros((9216, 2048), np.float32)
    w1in_pad[:8384] = inp["L1_attn_W_in"]
    mup_flat = np.ascontiguousarray(inp["L1_ffn_w_up"].reshape(131072, 512))
    mdn_flat = np.ascontiguousarray(inp["L1_ffn_w_down"].reshape(16384, 2048))
    x_full = inp["L0_attn_x"].astype(np.float32)
    in_maps = []
    for c in range(NCORES):
        b, q = c // 4, c % 4
        mr = np.array([[1.0 if (j // 4 == b and j % 4 < q) else 0.0 for j in range(8)]], np.float32)
        mh = np.array([[1.0 if (q > 0 and j == c - 1) else 0.0 for j in range(8)]], np.float32)
        m_ = {
            "x_in": np.ascontiguousarray(x_full[b, q * 1024:(q + 1) * 1024]),
            "xh0": np.zeros((3, D), np.float32) if q == 0 else np.ascontiguousarray(x_full[b, q * 1024 - 3:q * 1024]),
            "mrow8": mr, "mcol8": np.tile(mr, (128, 1)),
            "mh8": np.tile(mh, (128, 1)), "mh8_3": np.tile(mh, (3, 1)),
            "W0in_s": np.ascontiguousarray(inp["L0_attn_in_proj_w"][256 * c:256 * (c + 1)]),
            "W0out_s": np.ascontiguousarray(inp["L0_attn_out_proj_w"][512 * c:512 * (c + 1)]),
            "Mup_s": np.ascontiguousarray(mup_flat[16384 * c:16384 * (c + 1)]),
            "Mdn_s": np.ascontiguousarray(mdn_flat[2048 * c:2048 * (c + 1)]),
            "Wsup_s": np.ascontiguousarray(inp["L1_ffn_w_shared_up"][256 * c:256 * (c + 1)]),
            "Wsdn_s": np.ascontiguousarray(inp["L1_ffn_w_shared_down"][64 * c:64 * (c + 1)]),
            "Wgate_s": np.ascontiguousarray(inp["L1_ffn_w_gate"][256 * c:256 * (c + 1)]),
            "W1in_s": np.ascontiguousarray(w1in_pad[1152 * c:1152 * (c + 1)]),
            "W1out_s": np.ascontiguousarray(inp["L1_attn_W_out"][:, 512 * c:512 * (c + 1)]),
            "gP_s": np.ascontiguousarray(inp["L0_ffn_gate_W_packed"][2048 * c:2048 * (c + 1)]),
            "gS_s": np.ascontiguousarray(inp["L0_ffn_gate_scales"][2048 * c:2048 * (c + 1)]),
            "uP_s": np.ascontiguousarray(inp["L0_ffn_up_W_packed"][2048 * c:2048 * (c + 1)]),
            "uS_s": np.ascontiguousarray(inp["L0_ffn_up_scales"][2048 * c:2048 * (c + 1)]),
            "dP_s": np.ascontiguousarray(inp["L0_ffn_down_W_packed"][:, 1024 * c:1024 * (c + 1)]),
            "dS_s": np.ascontiguousarray(inp["L0_ffn_down_scales"][:, 64 * c:64 * (c + 1)]),
        }
        for k, s in (("c0_", "L0_attn_"), ("c1_", "L1_attn_")):
            m_[k + "conv_w"] = inp[s + "conv_w"]
            m_[k + "conv_b"] = inp[s + "conv_b"]
            m_[k + "dt_bias"] = inp[s + "dt_bias"]
            m_[k + "A_log"] = inp[s + "A_log"]
            m_[k + "Dv"] = inp[s + "D"]
            m_[k + "norm_w"] = inp[s + "norm_w"]
        in_maps.append(m_)
    res = run_bass_kernel_spmd(nc, in_maps, core_ids=list(range(NCORES)))
    out = np.empty((2, 4096, D), np.float32)
    for c in range(NCORES):
        b, q = c // 4, c % 4
        out[b, q * 1024:(q + 1) * 1024] = res.results[c]["x_out"]
    return out
